# version 01
# baseline (speedup 1.0000x reference)
import math
import numpy as np
from contextlib import ExitStack

import concourse.bass as bass
import concourse.mybir as mybir
from concourse.bass_utils import run_bass_kernel_spmd

F32 = mybir.dt.float32
BF16 = mybir.dt.bfloat16
I32 = mybir.dt.int32
U8 = mybir.dt.uint8
ALU = mybir.AluOpType
AF = mybir.ActivationFunctionType
AX = mybir.AxisListType

P = 128
T = 4096
D = 1024
HID = 2048
DI = 3072
NL = 4
TC = 512
NCH = T // TC
TC5 = 256
NCH5 = T // TC5
NH = 16
HD = 64
EPS = 1e-5
SQD = float(np.float32(math.sqrt(HID)))

ENGINES = ("sync", "gpsimd", "tensor", "vector", "scalar")
NWL = 6
SEMS = ("gd", "wfr", "tmm", "vop", "sop", "ph") + tuple(f"wl{j}" for j in range(NWL))


def vec_layout():
    lay, off = {}, 0

    def add(name, n):
        nonlocal off
        lay[name] = (off, n)
        off += n

    add("ones", 1)
    add("bp", 16)
    for i in range(NL):
        add(f"g1_{i}", 16)
        add(f"bhid_{i}", 24)
        add(f"bgate_{i}", 24)
        add(f"bo_{i}", 16)
        add(f"g2_{i}", 16)
    add("gfv", 16)
    add("bf", 8)
    add("cnb1", 8)
    add("cnb2", 4)
    add("cnb3", 2)
    add("cnb4", 1)
    add("gcb1", 4)
    add("gcb2", 2)
    add("gcb3", 1)
    add("ar", 1)
    add("cnw4", 2)
    add("gcw3", 2)
    add("bq", 16)
    add("bk", 16)
    add("bv", 16)
    add("bo_at", 8)
    add("c_eps", 1)
    return lay, off


def dvec_layout():
    lay, off = {}, 0

    def add(name, n):
        nonlocal off
        lay[name] = (off, n)
        off += n

    for i in range(NL):
        add(f"g1p1_{i}", 16)
        add(f"g2p1_{i}", 16)
        add(f"nbg_{i}", 24)
    add("gfp1", 16)
    add("cgA", 1)
    add("cgB", 1)
    add("qsc_g", 1)
    add("qsc_u", 1)
    add("qsc_d", 1)
    add("qrs_g", 1)
    add("qrs_u", 1)
    add("qrs_d", 1)
    return lay, off


def col_pack(v, ncols):
    return np.ascontiguousarray(v.reshape(ncols, P).T)


def stage_shared(I):
    f32 = np.float32
    out = {}

    def blk(w, njb):
        K, M = w.shape
        nk = K // P
        r = w.reshape(nk, P, njb, M // njb)
        return np.ascontiguousarray(r.transpose(2, 1, 0, 3).reshape(njb, P, nk * (M // njb)))

    out["wp_r"] = blk(np.asarray(I["L0_attn_Wp"], f32), 16)
    whg = np.asarray(I["L0_attn_Whg"], f32)
    whg_r = np.empty((NL, 24, P, 4096), f32)
    for i in range(NL):
        hidb = blk(whg[i][:, :DI], 24)
        gateb = blk(whg[i][:, DI:], 24)
        whg_r[i][:, :, :2048] = hidb
        whg_r[i][:, :, 2048:] = gateb
    out["whg_r"] = whg_r
    wo = np.asarray(I["L0_attn_Wo"], f32)
    out["wo_r"] = np.stack([blk(wo[i], 16) for i in range(NL)])
    out["wf_r"] = blk(np.asarray(I["L0_attn_Wf"], f32), 8)
    out["cn1_r"] = blk(np.asarray(I["L0_ffn_cn_w1"], f32), 8)
    out["cn2_r"] = blk(np.asarray(I["L0_ffn_cn_w2"], f32), 4)
    out["cn3_r"] = blk(np.asarray(I["L0_ffn_cn_w3"], f32), 2)
    out["gc1_r"] = blk(np.asarray(I["L0_ffn_gc_w1"], f32), 4)
    out["gc2_r"] = blk(np.asarray(I["L0_ffn_gc_w2"], f32), 2)

    def headT(w):
        return np.ascontiguousarray(
            w.transpose(2, 0, 1).reshape(8, P, NH * HD))

    out["wqt_r"] = headT(np.asarray(I["L1_attn_Wq"], f32))
    out["wkt_r"] = headT(np.asarray(I["L1_attn_Wk"], f32))
    out["wvt_r"] = headT(np.asarray(I["L1_attn_Wv"], f32))
    out["wot_r"] = blk(np.ascontiguousarray(np.asarray(I["L1_attn_Wo"], f32).T), 8)
    out["wg_r"] = blk(np.ascontiguousarray(np.asarray(I["L1_ffn_w_gate"], f32).T), 32)
    out["wu_r"] = blk(np.ascontiguousarray(np.asarray(I["L1_ffn_w_up"], f32).T), 32)
    out["wd_r"] = blk(np.ascontiguousarray(np.asarray(I["L1_ffn_w_down"], f32).T), 8)
    out["ident"] = np.eye(P, dtype=f32)

    lay, nv = vec_layout()
    vecs = np.zeros((P, nv), f32)

    def put(name, arr):
        o, n = lay[name]
        vecs[:, o:o + n] = arr

    put("ones", np.ones((P, 1), f32))
    put("bp", col_pack(np.asarray(I["L0_attn_bp"], f32), 16))
    for i in range(NL):
        put(f"g1_{i}", col_pack(np.asarray(I["L0_attn_g1"], f32)[i], 16))
        bhg = np.asarray(I["L0_attn_bhg"], f32)[i]
        put(f"bhid_{i}", col_pack(bhg[:DI], 24))
        put(f"bgate_{i}", col_pack(bhg[DI:], 24))
        put(f"bo_{i}", col_pack(np.asarray(I["L0_attn_bo"], f32)[i], 16))
        put(f"g2_{i}", col_pack(np.asarray(I["L0_attn_g2"], f32)[i], 16))
    put("gfv", col_pack(np.asarray(I["L0_attn_gf"], f32), 16))
    put("bf", col_pack(np.asarray(I["L0_attn_bf"], f32), 8))
    put("cnb1", col_pack(np.asarray(I["L0_ffn_cn_b1"], f32), 8))
    put("cnb2", col_pack(np.asarray(I["L0_ffn_cn_b2"], f32), 4))
    put("cnb3", col_pack(np.asarray(I["L0_ffn_cn_b3"], f32), 2))
    b4 = np.zeros((P, 1), f32); b4[0, 0] = np.asarray(I["L0_ffn_cn_b4"], f32)[0]
    put("cnb4", b4)
    put("gcb1", col_pack(np.asarray(I["L0_ffn_gc_b1"], f32), 4))
    put("gcb2", col_pack(np.asarray(I["L0_ffn_gc_b2"], f32), 2))
    g3 = np.zeros((P, 1), f32); g3[0, 0] = np.asarray(I["L0_ffn_gc_b3"], f32)[0]
    put("gcb3", g3)
    arr = np.zeros((P, 1), f32); arr[:, 0] = np.asarray(I["L0_ffn_adaptive_range"], f32)[0]
    put("ar", arr)
    put("cnw4", col_pack(np.asarray(I["L0_ffn_cn_w4"], f32)[:, 0], 2))
    put("gcw3", col_pack(np.asarray(I["L0_ffn_gc_w3"], f32)[:, 0], 2))
    for nm, key in (("bq", "L1_attn_bq"), ("bk", "L1_attn_bk"), ("bv", "L1_attn_bv")):
        a = np.zeros((P, NH), f32)
        a[0:HD, :] = np.asarray(I[key], f32).T
        put(nm, a)
    put("bo_at", col_pack(np.asarray(I["L1_attn_bo"], f32), 8))
    put("c_eps", np.full((P, 1), 1e-5, f32))
    out["vecs"] = vecs

    rowvecs = np.zeros((1, 1152), f32)
    rowvecs[0, 0:P] = 1.0
    rowvecs[0, P:P + D] = np.asarray(I["L1_attn_bv"], f32).reshape(-1)
    out["rowvecs"] = rowvecs
    return out


class BK:
    def __init__(self, dbg=()):
        self.nc = bass.Bass("TRN2", target_bir_lowering=False)
        self.q = {e: [] for e in ENGINES}
        self.cnt = {s: 0 for s in SEMS}
        self.maxwait = {s: 0 for s in SEMS}
        self.sem = {}
        self.dbg = set(dbg)
        self.es = ExitStack()
        self.banks = None
        self.bank_free = {}

    def E(self, eng, f, sem=None, k=1):
        if sem is None:
            self.q[eng].append(f)
            return None
        self.cnt[sem] += k
        v = self.cnt[sem]
        name = sem

        def g(e, f=f, name=name, k=k):
            f(e).then_inc(self.sem[name], k)
        self.q[eng].append(g)
        return v

    def W(self, eng, sem, v):
        if v is None or v <= 0:
            return
        self.maxwait[sem] = max(self.maxwait[sem], v)

        def g(e, name=sem, v=v):
            e.wait_ge(self.sem[name], v)
        self.q[eng].append(g)

    def dma(self, eng, out_ap, in_ap, sem):
        return self.E(eng, lambda e, o=out_ap, i=in_ap: e.dma_start(out=o, in_=i),
                      sem=sem, k=16)

    def gdma(self, out_ap, in_ap):
        v = self.dma("gpsimd", out_ap, in_ap, "gd")
        self.W("gpsimd", "gd", v)
        return v

    def mm(self, out, lhsT, rhs, start, stop, sem=None):
        return self.E("tensor",
                      lambda e, o=out, a=lhsT, b=rhs, s=start, p=stop:
                      e.matmul(o, a, b, start=s, stop=p), sem=sem)

    def tr(self, out, in_, ident, sem=None):
        return self.E("tensor",
                      lambda e, o=out, i=in_, d=ident: e.transpose(o, i, d),
                      sem=sem)

    def act(self, out, in_, func, bias=0.0, scale=1.0, accum=None, sem=None):
        def f(e, o=out, i=in_, fn=func, b=bias, s=scale, a=accum):
            return e.activation(o, i, fn, bias=b, scale=s, accum_out=a)
        v = self.E("scalar", f, sem=sem)
        self.q["scalar"].append(lambda e: e.drain())
        return v

    def vop(self, meth, *args, sem=None, **kw):
        def f(e, m=meth, aa=args, kk=kw):
            return getattr(e, m)(*aa, **kk)
        v = self.E("vector", f, sem=sem)
        self.q["vector"].append(lambda e: e.drain())
        return v

    def sop(self, meth, *args, sem=None, **kw):
        def f(e, m=meth, aa=args, kk=kw):
            return getattr(e, m)(*aa, **kk)
        v = self.E("scalar", f, sem=sem)
        self.q["scalar"].append(lambda e: e.drain())
        return v

    def bank_wait(self, b):
        fw = self.bank_free.get(b)
        if fw is not None:
            self.W("tensor", fw[0], fw[1])
            self.bank_free[b] = None

    def bank_set_free(self, b, sem, val):
        self.bank_free[b] = (sem, val)

    def barrier(self, idx):
        for j in range(NWL):
            self.W("sync", f"wl{j}", self.cnt[f"wl{j}"])
        self.W("gpsimd", "gd", self.cnt["gd"])
        for e in ENGINES:
            self.E(e, lambda en: en.drain(), sem="ph")
        target = self.cnt["ph"]
        for e in ENGINES:
            self.W(e, "ph", target)
        self.bank_free = {}

    def finalize(self):
        for s in SEMS:
            assert self.maxwait[s] <= self.cnt[s], \
                f"sem {s}: wait {self.maxwait[s]} > incs {self.cnt[s]}"
        es = self.es
        for s in SEMS:
            self.sem[s] = es.enter_context(self.nc.semaphore(s))
        block = es.enter_context(self.nc.Block())
        with es:
            for eng in ENGINES:
                ops = self.q[eng]

                def body(e, ops=ops):
                    for f in ops:
                        f(e)
                getattr(block, eng)(body)
        return self.nc


class Ring:

    def __init__(self, bk, pool_ap, ncols, gate=None):
        self.bk = bk
        self.pool = pool_ap
        self.n = ncols
        self.pos = 0
        self.owner = np.full(ncols, -1, dtype=np.int64)
        self.blocks = []
        self.base = bk.cnt["wfr"]
        self.gate = gate
        self.gated = gate is None

    def load(self, dram_ap, ncols, bf16=False):
        bk = self.bk
        if not self.gated:
            bk.W("sync", self.gate[0], self.gate[1])
            self.gated = True
        if self.pos + ncols > self.n:
            self.pos = 0
        sl = slice(self.pos, self.pos + ncols)
        idx = len(self.blocks)
        prev = int(self.owner[sl].max())
        need = max(prev + 1, idx - NWL + 1)
        if need > 0:
            bk.W("sync", "wfr", self.base + need)
        self.owner[sl] = idx
        ap = self.pool[:, sl]
        sem = f"wl{idx % NWL}"
        v = bk.dma("sync", ap.bitcast(BF16) if bf16 else ap, dram_ap, sem)
        self.blocks.append((sem, v))
        self.pos += ncols
        return ap, idx

    def consume_wait(self, idx):
        sem, v = self.blocks[idx]
        self.bk.W("tensor", sem, v)

    def free(self, idx):
        v = self.bk.E("tensor", lambda e: e.drain(), sem="wfr")
        assert v == self.base + idx + 1, (v, self.base, idx)


def build_program(dbg=()):
    bk = BK(dbg)
    nc = bk.nc
    es = bk.es

    x_in = nc.declare_dram_parameter("x", [T, D], F32, isOutput=False)
    y_out = nc.declare_dram_parameter("y", [T, D], F32, isOutput=True)
    dr = {}
    for name, shape in (
        ("wp_r", (16, P, 1024)), ("whg_r", (NL, 24, P, 4096)),
        ("wo_r", (NL, 16, P, 3072)), ("wf_r", (8, P, 2048)),
        ("cn1_r", (8, P, 1024)), ("cn2_r", (4, P, 1024)), ("cn3_r", (2, P, 512)),
        ("gc1_r", (4, P, 1024)), ("gc2_r", (2, P, 512)),
        ("wqt_r", (8, P, 1024)), ("wkt_r", (8, P, 1024)), ("wvt_r", (8, P, 1024)),
        ("wot_r", (8, P, 1024)), ("wg_r", (32, P, 1024)), ("wu_r", (32, P, 1024)),
        ("wd_r", (8, P, 4096)), ("ident", (P, P)), ("vecs", (P, vec_layout()[1])),
        ("rowvecs", (1, 1152)),
    ):
        dr[name] = nc.declare_dram_parameter(name, list(shape), F32, isOutput=False)

    sc = {}
    for name, shape, dt in (
        ("xn_buf", (8, P, T), F32), ("xt_buf", (8, P, T), F32),
        ("s_buf", (8, P, T), F32), ("x1_buf", (8, P, T), F32),
        ("xn2_buf", (8, P, T), F32),
        ("qt_buf", (NH, HD, T), F32), ("kt_buf", (NH, HD, T), F32),
        ("v_buf", (32, P, D), F32), ("o_buf", (8, P, T), F32),
        ("wig", (32, P, 1024), BF16), ("wiu", (32, P, 1024), BF16),
        ("wid", (8, P, 4096), BF16),
    ):
        sc[name] = nc.dram_tensor(name, list(shape), dt)

    dbg_outs = {}
    for name in bk.dbg:
        shape = {"xn_buf": (8, P, T), "xt_buf": (8, P, T), "s_buf": (8, P, T),
                 "x1_buf": (8, P, T), "xn2_buf": (8, P, T),
                 "qt_buf": (NH, HD, T), "kt_buf": (NH, HD, T),
                 "v_buf": (32, P, D), "o_buf": (8, P, T),
                 "wig": (32, P, 1024), "wiu": (32, P, 1024),
                 "wid": (8, P, 4096), "gfdbg": (P, 8)}[name]
        dt = BF16 if name in ("wig", "wiu", "wid") else F32
        dbg_outs[name] = nc.declare_dram_parameter(
            "dbg_" + name, list(shape), dt, isOutput=True)

    VL, NV = vec_layout()
    DL, ND = dvec_layout()
    vecs = es.enter_context(nc.sbuf_tensor("s_vecs", [P, NV], F32))
    dv = es.enter_context(nc.sbuf_tensor("s_dv", [P, ND], F32))
    rowv = es.enter_context(nc.sbuf_tensor("s_rowv", [1, 1152], F32))
    ident = es.enter_context(nc.sbuf_tensor("identt", [P, P], F32))
    carry = es.enter_context(nc.sbuf_tensor("s_carry", [P, NL * 24], F32))
    rows = es.enter_context(nc.sbuf_tensor("s_rows", [1, 2048], F32))
    gfdbg_sb = es.enter_context(nc.sbuf_tensor("s_gfdbg", [P, 8], F32))
    POOL = 43000
    pool = es.enter_context(nc.sbuf_tensor("s_pool", [P, POOL], F32))
    banks = [es.enter_context(nc.psum_tensor(f"bank{i}", [P, 512], F32))
             for i in range(8)]

    def vc(name, j0=0, n=1):
        o, nn = VL[name]
        assert j0 + n <= nn, name
        return vecs[:, o + j0:o + j0 + n]

    def dvc(name, j0=0, n=1):
        o, nn = DL[name]
        assert j0 + n <= nn, name
        return dv[:, o + j0:o + j0 + n]

    ones_col = vc("ones")
    ones_row = rowv[0:1, 0:P]

    def row(i):
        return rows[0:1, i * 512:(i + 1) * 512]

    v0 = bk.gdma(vecs[:], dr["vecs"][:])
    bk.gdma(rowv[:], dr["rowvecs"][:])
    v1 = bk.gdma(ident[:], dr["ident"][:])
    bk.W("vector", "gd", v1)
    for i in range(NL):
        bk.vop("tensor_scalar", dvc(f"g1p1_{i}", 0, 16), vc(f"g1_{i}", 0, 16),
               1.0, None, ALU.add)
        bk.vop("tensor_scalar", dvc(f"g2p1_{i}", 0, 16), vc(f"g2_{i}", 0, 16),
               1.0, None, ALU.add)
        bk.vop("tensor_scalar", dvc(f"nbg_{i}", 0, 24), vc(f"bgate_{i}", 0, 24),
               -1.0, None, ALU.mult)
    bk.vop("tensor_scalar", dvc("gfp1", 0, 16), vc("gfv", 0, 16), 1.0, None, ALU.add)
    bk.vop("memset", carry[:], 1.0, sem="vop")
    vop_pv = bk.cnt["vop"]
    bk.W("tensor", "vop", vop_pv)
    bk.W("tensor", "gd", v1)

    WT0, WT1, ACCC, TMPC = 0, 4096, 8192, 8193
    QQ5, R05, DI5, GI5, IQ5, WB5 = 8448, 12544, 16640, 20736, 24832, 28928
    for drname, nb, ncols, scname, rsname, wbuf in (
        ("wg_r", 32, 1024, "qsc_g", "qrs_g", "wig"),
        ("wu_r", 32, 1024, "qsc_u", "qrs_u", "wiu"),
        ("wd_r", 8, 4096, "qsc_d", "qrs_d", "wid"),
    ):
        nelem = nb * P * ncols
        slot_guard = [0, 0]
        for b in range(nb):
            s = b % 2
            wt = pool[:, (WT0 if s == 0 else WT1):(WT0 if s == 0 else WT1) + ncols]
            bk.W("gpsimd", "vop", slot_guard[s])
            lv = bk.gdma(wt, dr[drname][:][b])
            bk.W("vector", "gd", lv)
            if b == 0:
                rv = bk.vop("reduce_sum", pool[:, ACCC:ACCC + 1], wt, AX.X,
                            apply_absolute_value=True, sem="vop")
            else:
                bk.vop("reduce_sum", pool[:, TMPC:TMPC + 1], wt, AX.X,
                       apply_absolute_value=True)
                rv = bk.vop("tensor_tensor", pool[:, ACCC:ACCC + 1],
                            pool[:, ACCC:ACCC + 1], pool[:, TMPC:TMPC + 1],
                            ALU.add, sem="vop")
            slot_guard[s] = rv
        bk.W("tensor", "vop", rv)
        bk.bank_wait(0)
        mv = bk.mm(banks[0][0:1, 0:1], ones_col, pool[:, ACCC:ACCC + 1],
                   start=True, stop=True, sem="tmm")
        bk.W("vector", "tmm", mv)
        r3 = row(3)
        bk.vop("tensor_scalar", r3[0:1, 0:1], banks[0][0:1, 0:1],
               1.0 / nelem, None, ALU.mult, sem="vop")
        bk.bank_set_free(0, "vop", bk.cnt["vop"])
        bk.vop("tensor_scalar", r3[0:1, 0:1], r3[0:1, 0:1], 1e-5, None, ALU.max)
        bk.vop("reciprocal", r3[0:1, 1:2], r3[0:1, 0:1])
        sv = bk.vop("reciprocal", r3[0:1, 2:3], r3[0:1, 1:2], sem="vop")
        bk.W("tensor", "vop", sv)
        bk.bank_wait(1)
        bk.mm(banks[1][:, 0:1], ones_row, r3[0:1, 1:2], start=True, stop=True)
        bv = bk.mm(banks[1][:, 1:2], ones_row, r3[0:1, 2:3], start=True,
                   stop=True, sem="tmm")
        bk.W("vector", "tmm", bv)
        bk.vop("tensor_copy", dvc(scname), banks[1][:, 0:1])
        cv2 = bk.vop("tensor_copy", dvc(rsname), banks[1][:, 1:2], sem="vop")
        bk.bank_set_free(1, "vop", cv2)
        qq = pool[:, QQ5:QQ5 + ncols]
        r0 = pool[:, R05:R05 + ncols]
        di = pool[:, DI5:DI5 + ncols]
        gi = pool[:, GI5:GI5 + ncols]
        iq = pool[:, IQ5:IQ5 + ncols].bitcast(I32)
        wb = pool[:, WB5:WB5 + ncols // 2].bitcast(BF16)
        for b in range(nb):
            s = b % 2
            wt = pool[:, (WT0 if s == 0 else WT1):(WT0 if s == 0 else WT1) + ncols]
            bk.W("gpsimd", "vop", slot_guard[s])
            lv = bk.gdma(wt, dr[drname][:][b])
            bk.W("vector", "gd", lv)
            bk.vop("tensor_scalar", qq, wt, dvc(scname), None, ALU.mult,
                   sem="vop")
            slot_guard[s] = bk.cnt["vop"]
            bk.vop("tensor_copy", iq, qq)
            bk.vop("tensor_copy", r0, iq)
            bk.vop("tensor_tensor", di, qq, r0, ALU.subtract)
            bk.vop("tensor_scalar", gi, di, 0.5, None, ALU.is_gt)
            bk.vop("tensor_tensor", r0, r0, gi, ALU.add)
            bk.vop("tensor_scalar", gi, di, -0.5, None, ALU.is_lt)
            bk.vop("tensor_tensor", r0, r0, gi, ALU.subtract)
            bk.vop("tensor_scalar", r0, r0, 1.0, None, ALU.min)
            bk.vop("tensor_scalar", r0, r0, -1.0, None, ALU.max)
            bk.W("vector", "gd", bk.cnt["gd"])
            wv = bk.vop("tensor_copy", wb, r0, sem="vop")
            bk.W("gpsimd", "vop", wv)
            bk.gdma(sc[wbuf][:][b], wb)

    bk.barrier(-1)

    pa = 0

    def alloc(n):
        nonlocal pa
        ap = pool[:, pa:pa + n]
        pa += n
        return ap

    xt_b = [alloc(1024) for _ in range(2)]
    xn_b = alloc(1024)
    sq_col = alloc(1)
    sum_col = alloc(1)
    mean_col = alloc(1)
    ex2_col = alloc(1)
    var_col = alloc(1)
    sd_col = alloc(1)
    inv_col = alloc(1)
    tcol = alloc(1)
    sqs = alloc(1024)
    xnT = alloc(1024)
    xtT = alloc(1024)
    gccols = alloc(16)
    gc1_sb = alloc(4 * 1024)
    gc2_sb = alloc(2 * 512)

    g1v = bk.gdma(gc1_sb.rearrange("p (a b) -> p a b", a=4),
                  dr["gc1_r"][:].transpose([1, 0, 2]))
    g2v = bk.gdma(gc2_sb.rearrange("p (a b) -> p a b", a=2),
                  dr["gc2_r"][:].transpose([1, 0, 2]))

    last_store = 0
    last_xt_use = [(0, 0), (0, 0)]
    prev_mmv = 0
    for tt in range(32):
        buf = xt_b[tt % 2]
        bk.W("gpsimd", "vop", last_xt_use[tt % 2][0])
        bk.W("gpsimd", "tmm", last_xt_use[tt % 2][1])
        av = bk.gdma(buf, x_in[:][tt * P:(tt + 1) * P, :])
        bk.W("vector", "gd", av)
        bk.vop("scalar_tensor_tensor", sqs, buf, 1.0, buf, ALU.mult, ALU.mult,
               accum_out=sq_col)
        bk.vop("reduce_sum", sum_col, buf, AX.X)
        bk.vop("tensor_scalar", mean_col, sum_col, 1.0 / D, None, ALU.mult)
        bk.vop("tensor_scalar", ex2_col, sq_col, 1.0 / D, None, ALU.mult)
        bk.vop("tensor_tensor", tcol, mean_col, mean_col, ALU.mult)
        bk.vop("tensor_tensor", var_col, ex2_col, tcol, ALU.subtract, sem="vop")
        vv = bk.cnt["vop"]
        bk.W("scalar", "vop", vv)
        sv = bk.act(sd_col, var_col, AF.Sqrt, bias=vc("c_eps"), sem="sop")
        bk.W("vector", "sop", sv)
        bk.vop("reciprocal", inv_col, sd_col)
        bk.W("vector", "tmm", prev_mmv)
        bk.vop("tensor_scalar", xn_b, buf, mean_col, inv_col,
               ALU.subtract, ALU.mult, sem="vop")
        xn_ready = bk.cnt["vop"]

        bk.W("tensor", "vop", xn_ready)
        for half in range(2):
            b = banks[half]
            bk.bank_wait(half)
            for k in range(4):
                bk.tr(b[:, k * P:(k + 1) * P], xn_b[:, (half * 4 + k) * P:(half * 4 + k + 1) * P], ident[:])
            tv = bk.E("tensor", lambda e: e.drain(), sem="tmm")
            bk.W("scalar", "tmm", tv)
            cv = bk.sop("copy", xnT[:, half * 512:(half + 1) * 512], b[:], sem="sop")
            bk.bank_set_free(half, "sop", cv)
        for half in range(2):
            b = banks[2 + half]
            bk.bank_wait(2 + half)
            for k in range(4):
                bk.tr(b[:, k * P:(k + 1) * P], buf[:, (half * 4 + k) * P:(half * 4 + k + 1) * P], ident[:])
            tv = bk.E("tensor", lambda e: e.drain(), sem="tmm")
            bk.W("scalar", "tmm", tv)
            cv = bk.sop("copy", xtT[:, half * 512:(half + 1) * 512], b[:], sem="sop")
            bk.bank_set_free(2 + half, "sop", cv)
        for half in range(2):
            bk.bank_wait(6 + half)
            bk.mm(banks[6 + half][0:1, :], ones_col, xn_b[:, half * 512:(half + 1) * 512],
                  start=(tt == 0), stop=(tt == 31))
        mmv = bk.E("tensor", lambda e: e.drain(), sem="tmm")
        prev_mmv = mmv
        last_xt_use[tt % 2] = (xn_ready, mmv)

        sv_ = bk.cnt["sop"]
        bk.W("gpsimd", "sop", sv_)
        bk.gdma(sc["xn_buf"][:].transpose([1, 0, 2])[:, :, tt * P:(tt + 1) * P],
                xnT.rearrange("p (dt t) -> p dt t", dt=8))
        last_store = bk.gdma(
            sc["xt_buf"][:].transpose([1, 0, 2])[:, :, tt * P:(tt + 1) * P],
            xtT.rearrange("p (dt t) -> p dt t", dt=8))
        bk.W("scalar", "gd", last_store)

    bk.W("tensor", "gd", g2v)
    ctx = gccols[:, 0:8]
    bk.W("scalar", "tmm", bk.cnt["tmm"])
    e1 = bk.act(row(0), banks[6][0:1, :], AF.Copy, scale=1.0 / T, sem="sop")
    e2 = bk.act(row(1), banks[7][0:1, :], AF.Copy, scale=1.0 / T, sem="sop")
    bk.bank_set_free(6, "sop", e1)
    bk.bank_set_free(7, "sop", e2)
    bk.W("tensor", "sop", e2)
    for k in range(8):
        b = banks[k % 2]
        src = row(k // 4)[0:1, (k % 4) * 128:((k % 4) + 1) * 128]
        bk.bank_wait(k % 2)
        bk.tr(b[:, 0:1], src, ident[0:1, 0:1])
        tv = bk.E("tensor", lambda e: e.drain(), sem="tmm")
        bk.W("vector", "tmm", tv)
        fv = bk.vop("tensor_copy", ctx[:, k:k + 1], b[:, 0:1], sem="vop")
        bk.bank_set_free(k % 2, "vop", fv)
    h1 = gccols[:, 8:12]
    bk.W("tensor", "vop", bk.cnt["vop"])
    for o in range(4):
        bk.bank_wait(4)
        for kc in range(8):
            bk.mm(banks[4][:, 0:1], gc1_sb[:, o * 1024 + kc * P: o * 1024 + (kc + 1) * P],
                  ctx[:, kc:kc + 1], start=(kc == 0), stop=(kc == 7))
        tv = bk.E("tensor", lambda e: e.drain(), sem="tmm")
        bk.W("scalar", "tmm", tv)
        av = bk.act(h1[:, o:o + 1], banks[4][:, 0:1], AF.Gelu,
                    bias=vc("gcb1", o), sem="sop")
        bk.bank_set_free(4, "sop", av)
        bk.W("tensor", "sop", av)
    h2 = gccols[:, 12:14]
    for o in range(2):
        bk.bank_wait(5)
        for kc in range(4):
            bk.mm(banks[5][:, 0:1], gc2_sb[:, o * 512 + kc * P: o * 512 + (kc + 1) * P],
                  h1[:, kc:kc + 1], start=(kc == 0), stop=(kc == 3))
        tv = bk.E("tensor", lambda e: e.drain(), sem="tmm")
        bk.W("scalar", "tmm", tv)
        av = bk.act(h2[:, o:o + 1], banks[5][:, 0:1], AF.Gelu,
                    bias=vc("gcb2", o), sem="sop")
        bk.bank_set_free(5, "sop", av)
        bk.W("tensor", "sop", av)
    bk.bank_wait(4)
    for kc in range(2):
        bk.mm(banks[4][0:1, 0:1], vc("gcw3", kc), h2[:, kc:kc + 1],
              start=(kc == 0), stop=(kc == 1))
    tv = bk.E("tensor", lambda e: e.drain(), sem="tmm")
    bk.W("scalar", "tmm", tv)
    gfs = gccols[:, 14:15]
    av = bk.act(gfs[0:1, :], banks[4][0:1, 0:1], AF.Sigmoid,
                bias=vc("gcb3")[0:1, :], sem="sop")
    bk.bank_set_free(4, "sop", av)
    bk.W("tensor", "sop", av)
    bk.bank_wait(5)
    bk.mm(banks[5][:, 0:1], ones_row, gfs[0:1, :], start=True, stop=True)
    tv = bk.E("tensor", lambda e: e.drain(), sem="tmm")
    bk.W("vector", "tmm", tv)
    gf_col = gccols[:, 15:16]
    bk.vop("tensor_copy", gf_col, banks[5][:, 0:1], sem="vop")
    bk.bank_set_free(5, "vop", bk.cnt["vop"])
    bk.vop("tensor_scalar", dvc("cgA"), vc("ar"), 0.6, None, ALU.mult)
    bk.vop("tensor_tensor", tcol, vc("ar"), gf_col, ALU.mult)
    bk.vop("tensor_scalar", tcol, tcol, 1.4, None, ALU.mult)
    bk.vop("tensor_tensor", tcol, tcol, vc("ar"), ALU.subtract)
    bk.vop("tensor_scalar", dvc("cgB"), tcol, 0.1, None, ALU.add, sem="vop")
    if "gfdbg" in bk.dbg:
        bk.vop("memset", gfdbg_sb[:], 0.0)
        bk.vop("tensor_copy", gfdbg_sb[:, 0:1], gf_col)
        bk.vop("tensor_copy", gfdbg_sb[:, 1:2], dvc("cgA"))
        bk.vop("tensor_copy", gfdbg_sb[:, 2:3], dvc("cgB"))
        bk.vop("tensor_copy", gfdbg_sb[:, 3:4], ctx[:, 0:1], sem="vop")

    bk.barrier(0)

    def xn_t(kc):
        return pool[:, kc * 512:(kc + 1) * 512]

    def hn_t(dt):
        return pool[:, dt * 512:(dt + 1) * 512]

    def h_t(o):
        return pool[:, 8192 + o * 512:8192 + (o + 1) * 512]

    def scano(j):
        return pool[:, 16384 + j * 513:16384 + (j + 1) * 513]

    def cvc(s):
        return pool[:, 28696 + s * 1026:28696 + s * 1026 + 513]

    def cvv(s):
        return pool[:, 28696 + s * 1026 + 513:28696 + s * 1026 + 1026]

    def zs_sh(s):
        return pool[:, 30748 + s * 1536:30748 + s * 1536 + 512]

    def zs_hb(s):
        return pool[:, 30748 + s * 1536 + 512:30748 + s * 1536 + 1024]

    def zs_gm(s):
        return pool[:, 30748 + s * 1536 + 1024:30748 + s * 1536 + 1536]

    def sq_t(s):
        return zs_sh(s)

    s_col_ap = pool[:, 33820:34332]
    ring = Ring(bk, pool[:, 34332:43000], 8668)

    bk.vop("memset", cvc(0)[:, 0:1], 0.0)
    bk.vop("memset", cvv(0)[:, 0:1], 1.0)
    bk.vop("memset", cvc(1)[:, 0:1], 0.0)
    bk.vop("memset", cvv(1)[:, 0:1], 1.0, sem="vop")

    def rms_stats(tiles_f, ntiles, zs_guards=(0, 0)):
        sq_guard = [None, None]
        mmv = None
        for dt in range(ntiles):
            s = dt % 2
            if sq_guard[s] is not None:
                bk.W("scalar", "tmm", sq_guard[s])
            if dt < 2:
                bk.W("scalar", "vop", zs_guards[s])
            sv = bk.act(sq_t(s), tiles_f(dt), AF.Square, sem="sop")
            bk.W("tensor", "sop", sv)
            if dt == 0:
                bk.bank_wait(6)
            mmv = bk.mm(banks[6][0:1, :], ones_col, sq_t(s),
                        start=(dt == 0), stop=(dt == ntiles - 1), sem="tmm")
            sq_guard[s] = mmv
        bk.W("scalar", "tmm", mmv)
        nv = bk.act(row(0), banks[6][0:1, :], AF.Sqrt, sem="sop")
        bk.bank_set_free(6, "sop", nv)
        bk.W("vector", "sop", nv)
        bk.vop("tensor_scalar", row(0), row(0), 1e-12, None, ALU.max)
        bk.vop("reciprocal", row(1), row(0))
        iv = bk.vop("tensor_scalar", row(1), row(1), SQD, None, ALU.mult, sem="vop")
        bk.W("tensor", "vop", iv)
        bk.bank_wait(7)
        bcv = bk.mm(banks[7][:, :], ones_row, row(1), start=True, stop=True,
                    sem="tmm")
        return bcv

    ch_guard_sop = 0
    ch_guard_zs = [0, 0]
    ch_guard_cv = [0, 0]
    scan_val = 0
    wo_mm_last = 0
    whg_mm_last = 0
    wf_mm_last = 0
    sstore_val = 0
    for c in range(NCH):
        c0 = c * TC
        bk.W("gpsimd", "tmm", wf_mm_last)
        xnv = bk.gdma(pool[:, 0:4096].rearrange("p (dt t) -> p dt t", dt=8),
                      sc["xn_buf"][:].transpose([1, 0, 2])[:, :, c0:c0 + TC])
        bk.W("tensor", "gd", xnv)
        wp_evac = None
        for o in range(16):
            ap, idx = ring.load(dr["wp_r"][:][o], 1024)
            ring.consume_wait(idx)
            bknum = 4 + o % 2
            bk.bank_wait(bknum)
            for kc in range(8):
                v = bk.mm(banks[bknum][:], ap[:, kc * P:(kc + 1) * P], xn_t(kc),
                          start=(kc == 0), stop=(kc == 7),
                          sem=("tmm" if kc == 7 else None))
            ring.free(idx)
            bk.W("vector", "tmm", v)
            if c > 0 and o == 0:
                bk.W("vector", "sop", ch_guard_sop)
            wp_evac = bk.vop("tensor_scalar", h_t(o), banks[bknum][:],
                             vc("bp", o), None, ALU.add, sem="vop")
            bk.bank_set_free(bknum, "vop", wp_evac)

        for i in range(NL):
            bk.W("scalar", "vop", bk.cnt["vop"])
            bcv = rms_stats(h_t, 16, zs_guards=tuple(ch_guard_zs))
            bk.W("vector", "tmm", bcv)
            if i > 0:
                bk.W("vector", "tmm", whg_mm_last)
            hnv = None
            for dt in range(16):
                hnv = bk.vop("scalar_tensor_tensor", hn_t(dt), h_t(dt),
                             dvc(f"g1p1_{i}", dt), banks[7][:],
                             ALU.mult, ALU.mult,
                             sem=("vop" if dt == 15 else None))
            bk.bank_set_free(7, "vop", hnv)
            bk.W("tensor", "vop", hnv)

            for j in range(24):
                ap, idx = ring.load(dr["whg_r"][:][i, j], 4096)
                ring.consume_wait(idx)
                s = j % 2
                bh, bg = (0, 1) if s == 0 else (2, 3)
                bk.bank_wait(bh)
                bk.bank_wait(bg)
                for kc in range(16):
                    bk.mm(banks[bh][:], ap[:, kc * P:(kc + 1) * P], hn_t(kc),
                          start=(kc == 0), stop=(kc == 15))
                for kc in range(16):
                    v = bk.mm(banks[bg][:], ap[:, 2048 + kc * P:2048 + (kc + 1) * P],
                              hn_t(kc), start=(kc == 0), stop=(kc == 15),
                              sem=("tmm" if kc == 15 else None))
                whg_mm_last = v
                ring.free(idx)
                bk.W("scalar", "tmm", v)
                bk.W("scalar", "vop", ch_guard_cv[s])
                cav = bk.act(cvc(s)[:, 1:513], banks[bg][:], AF.Sigmoid,
                             scale=-1.0, bias=dvc(f"nbg_{i}", j), sem="sop")
                bk.bank_set_free(bg, "sop", cav)
                bk.W("scalar", "vop", ch_guard_zs[s])
                shv = bk.act(zs_sh(s), banks[bh][:], AF.Sigmoid,
                             bias=vc(f"bhid_{i}", j), sem="sop")
                bk.W("vector", "sop", shv)
                hbv = bk.vop("tensor_scalar", zs_hb(s), banks[bh][:],
                             vc(f"bhid_{i}", j), None, ALU.add, sem="vop")
                bk.bank_set_free(bh, "vop", hbv)
                bk.vop("tensor_scalar", zs_gm(s), zs_sh(s), 0.5, None, ALU.min)
                bk.vop("scalar_tensor_tensor", zs_gm(s), zs_hb(s), 0.0,
                       zs_gm(s), ALU.max, ALU.add)
                bk.vop("tensor_scalar", zs_sh(s), cvc(s)[:, 1:513], -1.0, 1.0,
                       ALU.mult, ALU.add)
                zv = bk.vop("tensor_tensor", cvv(s)[:, 1:513], zs_sh(s),
                            zs_gm(s), ALU.mult, sem="vop")
                ch_guard_zs[s] = zv
                bk.vop("tensor_copy", cvv(s)[:, 0:1], carry[:, i * 24 + j:i * 24 + j + 1])
                if j == 0:
                    bk.W("vector", "tmm", wo_mm_last)
                bk.vop("tensor_tensor_scan", scano(j), cvc(s), cvv(s), 0.0,
                       ALU.mult, ALU.add)
                scan_val = bk.vop("tensor_copy",
                                  carry[:, i * 24 + j:i * 24 + j + 1],
                                  scano(j)[:, 512:513], sem="vop")
                ch_guard_cv[s] = scan_val

            bk.W("tensor", "vop", scan_val)
            for o in range(16):
                ap, idx = ring.load(dr["wo_r"][:][i, o], 3072)
                ring.consume_wait(idx)
                bknum = 4 + o % 2
                bk.bank_wait(bknum)
                for kc in range(24):
                    v = bk.mm(banks[bknum][:], ap[:, kc * P:(kc + 1) * P],
                              scano(kc)[:, 1:513], start=(kc == 0),
                              stop=(kc == 23), sem=("tmm" if kc == 23 else None))
                wo_mm_last = v
                ring.free(idx)
                bk.W("vector", "tmm", v)
                ev = bk.vop("scalar_tensor_tensor", h_t(o), banks[bknum][:],
                            vc(f"bo_{i}", o), h_t(o), ALU.add, ALU.add,
                            sem="vop")
                bk.bank_set_free(bknum, "vop", ev)

            bk.W("scalar", "vop", bk.cnt["vop"])
            bcv = rms_stats(h_t, 16, zs_guards=tuple(ch_guard_zs))
            ch_guard_sop = bk.cnt["sop"]
            bk.W("vector", "tmm", bcv)
            upd = None
            for dt in range(16):
                bk.vop("scalar_tensor_tensor", zs_gm(dt % 2), h_t(dt),
                       dvc(f"g2p1_{i}", dt), banks[7][:], ALU.mult, ALU.mult)
                upd = bk.vop("tensor_tensor", h_t(dt), h_t(dt), zs_gm(dt % 2),
                             ALU.add, sem=("vop" if dt == 15 else None))
            bk.bank_set_free(7, "vop", upd)
            ch_guard_zs = [upd, upd]

        bk.W("scalar", "vop", bk.cnt["vop"])
        bcv = rms_stats(h_t, 16, zs_guards=tuple(ch_guard_zs))
        ch_guard_sop = bk.cnt["sop"]
        bk.W("vector", "tmm", bcv)
        bk.W("vector", "tmm", whg_mm_last)
        hnv = None
        for dt in range(16):
            hnv = bk.vop("scalar_tensor_tensor", hn_t(dt), h_t(dt),
                         dvc("gfp1", dt), banks[7][:], ALU.mult, ALU.mult,
                         sem=("vop" if dt == 15 else None))
        bk.bank_set_free(7, "vop", hnv)
        bk.W("tensor", "vop", hnv)
        for o in range(8):
            ap, idx = ring.load(dr["wf_r"][:][o], 2048)
            ring.consume_wait(idx)
            bknum = 4 + o % 2
            bk.bank_wait(bknum)
            for kc in range(16):
                v = bk.mm(banks[bknum][:], ap[:, kc * P:(kc + 1) * P], hn_t(kc),
                          start=(kc == 0), stop=(kc == 15),
                          sem=("tmm" if kc == 15 else None))
            wf_mm_last = v
            ring.free(idx)
            bk.W("vector", "tmm", v)
            bk.W("vector", "gd", sstore_val)
            sv = bk.vop("tensor_scalar", s_col_ap, banks[bknum][:], vc("bf", o),
                        None, ALU.add, sem="vop")
            bk.bank_set_free(bknum, "vop", sv)
            bk.W("gpsimd", "vop", sv)
            sstore_val = bk.gdma(sc["s_buf"][:][o, :, c0:c0 + TC], s_col_ap)

    bk.barrier(1)

    def p2t(base, i, w=512):
        return pool[:, base + i * w:base + (i + 1) * w]

    OA, OB, XT, SX, GG, H1, H2, H3 = 0, 4096, 8192, 12288, 16384, 20480, 24576, 26624
    CPL, SQ2, CN1, CN2, CN3 = 27648, 28160, 29184, 37376, 41472
    cn1v = bk.gdma(pool[:, CN1:CN1 + 8192].rearrange("p (a b) -> p a b", a=8),
                   dr["cn1_r"][:].transpose([1, 0, 2]))
    cn2v = bk.gdma(pool[:, CN2:CN2 + 4096].rearrange("p (a b) -> p a b", a=4),
                   dr["cn2_r"][:].transpose([1, 0, 2]))
    cn3v = bk.gdma(pool[:, CN3:CN3 + 1024].rearrange("p (a b) -> p a b", a=2),
                   dr["cn3_r"][:].transpose([1, 0, 2]))
    bk.W("tensor", "gd", cn3v)

    TW_SRC = [4, 5, 6, 7, 0, 1, 2, 3]
    TW_SGN = [1, 1, -1, -1, -1, -1, 1, 1]

    for c in range(NCH):
        c0 = c * TC
        bk.W("gpsimd", "tmm", bk.cnt["tmm"])
        bk.W("gpsimd", "vop", bk.cnt["vop"])
        bk.W("gpsimd", "sop", bk.cnt["sop"])
        ldv0 = bk.gdma(pool[:, OA:OA + 4096].rearrange("p (dt t) -> p dt t", dt=8),
                       sc["xn_buf"][:].transpose([1, 0, 2])[:, :, c0:c0 + TC])
        bk.gdma(pool[:, XT:XT + 4096].rearrange("p (dt t) -> p dt t", dt=8),
                sc["xt_buf"][:].transpose([1, 0, 2])[:, :, c0:c0 + TC])
        ldv = bk.gdma(pool[:, SX:SX + 4096].rearrange("p (dt t) -> p dt t", dt=8),
                      sc["s_buf"][:].transpose([1, 0, 2])[:, :, c0:c0 + TC])
        cur_base, nxt_base = OA, OB
        cur_ready_v, cur_ready_sem = ldv, "gd"
        for cyc in range(3):
            bk.W("tensor", cur_ready_sem, cur_ready_v)
            h1_last = None
            for o in range(8):
                bknum = o % 2
                bk.bank_wait(bknum)
                for kc in range(8):
                    v = bk.mm(banks[bknum][:], pool[:, CN1 + o * 1024 + kc * P:CN1 + o * 1024 + (kc + 1) * P],
                              p2t(cur_base, kc), start=(kc == 0), stop=(kc == 7),
                              sem=("tmm" if kc == 7 else None))
                bk.W("scalar", "tmm", v)
                h1_last = bk.act(p2t(H1, o), banks[bknum][:], AF.Gelu,
                                 bias=vc("cnb1", o), sem="sop")
                bk.bank_set_free(bknum, "sop", h1_last)
            bk.W("tensor", "sop", h1_last)
            h2_last = None
            for o in range(4):
                bk.bank_wait(2)
                for kc in range(8):
                    v = bk.mm(banks[2][:], pool[:, CN2 + o * 1024 + kc * P:CN2 + o * 1024 + (kc + 1) * P],
                              p2t(H1, kc), start=(kc == 0), stop=(kc == 7),
                              sem=("tmm" if kc == 7 else None))
                bk.W("scalar", "tmm", v)
                h2_last = bk.act(p2t(H2, o), banks[2][:], AF.Gelu,
                                 bias=vc("cnb2", o), sem="sop")
                bk.bank_set_free(2, "sop", h2_last)
            bk.W("tensor", "sop", h2_last)
            h3_last = None
            for o in range(2):
                bk.bank_wait(3)
                for kc in range(4):
                    v = bk.mm(banks[3][:], pool[:, CN3 + o * 512 + kc * P:CN3 + o * 512 + (kc + 1) * P],
                              p2t(H2, kc), start=(kc == 0), stop=(kc == 3),
                              sem=("tmm" if kc == 3 else None))
                bk.W("scalar", "tmm", v)
                h3_last = bk.act(p2t(H3, o), banks[3][:], AF.Gelu,
                                 bias=vc("cnb3", o), sem="sop")
                bk.bank_set_free(3, "sop", h3_last)
            bk.W("tensor", "sop", h3_last)
            bk.bank_wait(6)
            for kc in range(2):
                v = bk.mm(banks[6][0:1, :], vc("cnw4", kc), p2t(H3, kc),
                          start=(kc == 0), stop=(kc == 1),
                          sem=("tmm" if kc == 1 else None))
            bk.W("scalar", "tmm", v)
            tfv = bk.act(row(2), banks[6][0:1, :], AF.Sigmoid,
                         bias=vc("cnb4")[0:1, :], sem="sop")
            bk.bank_set_free(6, "sop", tfv)
            bk.W("tensor", "sop", tfv)
            bk.bank_wait(7)
            bcv = bk.mm(banks[7][:], ones_row, row(2), start=True, stop=True,
                        sem="tmm")
            bk.W("vector", "tmm", bcv)
            cplv = bk.vop("tensor_scalar", pool[:, CPL:CPL + 512], banks[7][:],
                          dvc("cgA"), dvc("cgB"), ALU.mult, ALU.add, sem="vop")
            bk.bank_set_free(7, "vop", cplv)
            upd = None
            for j in range(8):
                bk.vop("tensor_tensor", p2t(SQ2, 0), pool[:, CPL:CPL + 512],
                       p2t(cur_base, TW_SRC[j]), ALU.mult)
                upd = bk.vop("tensor_tensor", p2t(nxt_base, j),
                             p2t(cur_base, j), p2t(SQ2, 0),
                             ALU.add if TW_SGN[j] > 0 else ALU.subtract,
                             sem=("vop" if j == 7 else None))
            cur_base, nxt_base = nxt_base, cur_base
            cur_ready_v, cur_ready_sem = upd, "vop"
        bk.W("scalar", "vop", cur_ready_v)
        gv = None
        for dt in range(8):
            gv = bk.act(p2t(GG, dt), p2t(cur_base, dt), AF.Silu,
                        sem=("sop" if dt == 7 else None))
        bk.W("vector", "sop", gv)
        bk.W("vector", "gd", ldv)
        x1v = None
        for dt in range(8):
            bk.vop("tensor_tensor", p2t(GG, dt), p2t(GG, dt), p2t(SX, dt),
                   ALU.mult)
            x1v = bk.vop("tensor_tensor", p2t(XT, dt), p2t(XT, dt), p2t(GG, dt),
                         ALU.add, sem=("vop" if dt == 7 else None))
        bk.W("gpsimd", "vop", x1v)
        bk.gdma(sc["x1_buf"][:].transpose([1, 0, 2])[:, :, c0:c0 + TC],
                pool[:, XT:XT + 4096].rearrange("p (dt t) -> p dt t", dt=8))
        bk.W("tensor", "vop", x1v)
        bk.W("scalar", "vop", x1v)
        sq_guard = [None, None]
        for dt in range(8):
            s = dt % 2
            if sq_guard[s] is not None:
                bk.W("scalar", "tmm", sq_guard[s])
            sv = bk.act(p2t(SQ2, s), p2t(XT, dt), AF.Square, sem="sop")
            bk.W("tensor", "sop", sv)
            if dt == 0:
                bk.bank_wait(0)
                bk.bank_wait(1)
            bk.mm(banks[0][0:1, :], ones_col, p2t(XT, dt),
                  start=(dt == 0), stop=(dt == 7))
            mv = bk.mm(banks[1][0:1, :], ones_col, p2t(SQ2, s),
                       start=(dt == 0), stop=(dt == 7), sem="tmm")
            sq_guard[s] = mv
        bk.W("vector", "tmm", mv)
        bk.vop("tensor_scalar", row(0), banks[0][0:1, :], 1.0 / D, None, ALU.mult)
        bk.vop("tensor_scalar", row(3), banks[1][0:1, :], 1.0 / D, None, ALU.mult)
        bk.vop("tensor_tensor", row(1), row(0), row(0), ALU.mult)
        vv = bk.vop("tensor_tensor", row(3), row(3), row(1), ALU.subtract,
                    sem="vop")
        bk.bank_set_free(0, "vop", vv)
        bk.bank_set_free(1, "vop", vv)
        bk.W("scalar", "vop", vv)
        sdv = bk.act(row(3), row(3), AF.Sqrt, bias=vc("c_eps")[0:1, :], sem="sop")
        bk.W("vector", "sop", sdv)
        iv = bk.vop("reciprocal", row(1), row(3), sem="vop")
        bk.W("tensor", "vop", iv)
        bk.bank_wait(2)
        bk.bank_wait(3)
        bk.mm(banks[2][:], ones_row, row(0), start=True, stop=True)
        bcv = bk.mm(banks[3][:], ones_row, row(1), start=True, stop=True,
                    sem="tmm")
        bk.W("vector", "tmm", bcv)
        xnv2 = None
        for dt in range(8):
            bk.vop("tensor_tensor", p2t(SX, dt), p2t(XT, dt), banks[2][:],
                   ALU.subtract)
            xnv2 = bk.vop("tensor_tensor", p2t(SX, dt), p2t(SX, dt),
                          banks[3][:], ALU.mult,
                          sem=("vop" if dt == 7 else None))
        bk.bank_set_free(2, "vop", xnv2)
        bk.bank_set_free(3, "vop", xnv2)
        bk.W("gpsimd", "vop", xnv2)
        bk.gdma(sc["xn2_buf"][:].transpose([1, 0, 2])[:, :, c0:c0 + TC],
                pool[:, SX:SX + 4096].rearrange("p (dt t) -> p dt t", dt=8))

    bk.barrier(2)

    WQ, WK, WV, QEV, VEV = 4096, 12288, 20480, 28672, 29696
    wq_v = bk.gdma(pool[:, WQ:WQ + 8192].rearrange("p (a b) -> p a b", a=8),
                   dr["wqt_r"][:].transpose([1, 0, 2]))
    wk_v = bk.gdma(pool[:, WK:WK + 8192].rearrange("p (a b) -> p a b", a=8),
                   dr["wkt_r"][:].transpose([1, 0, 2]))
    wv_v = bk.gdma(pool[:, WV:WV + 8192].rearrange("p (a b) -> p a b", a=8),
                   dr["wvt_r"][:].transpose([1, 0, 2]))
    bk.W("tensor", "gd", wv_v)
    for c in range(NCH):
        c0 = c * TC
        bk.W("gpsimd", "tmm", bk.cnt["tmm"])
        bk.W("gpsimd", "vop", bk.cnt["vop"])
        ld3 = bk.gdma(pool[:, 0:4096].rearrange("p (dt t) -> p dt t", dt=8),
                      sc["xn2_buf"][:].transpose([1, 0, 2])[:, :, c0:c0 + TC])
        bk.W("tensor", "gd", ld3)
        for which, wbase, bname, buf in (("q", WQ, "bq", "qt_buf"),
                                         ("k", WK, "bk", "kt_buf")):
            for h in range(NH):
                bknum = h % 4
                bk.bank_wait(bknum)
                for kc in range(8):
                    v = bk.mm(banks[bknum][0:64, :],
                              pool[:, wbase + kc * 1024 + h * HD:wbase + kc * 1024 + (h + 1) * HD],
                              pool[:, kc * 512:(kc + 1) * 512],
                              start=(kc == 0), stop=(kc == 7),
                              sem=("tmm" if kc == 7 else None))
                bk.W("vector", "tmm", v)
                ev = pool[0:64, QEV + (h % 2) * 512:QEV + (h % 2) * 512 + 512]
                bk.W("vector", "gd", bk.cnt["gd"])
                evv = bk.vop("tensor_scalar", ev, banks[bknum][0:64, :],
                             vc(bname, h)[0:64, :], None, ALU.add, sem="vop")
                bk.bank_set_free(bknum, "vop", evv)
                bk.W("gpsimd", "vop", evv)
                bk.gdma(sc[buf][:][h, :, c0:c0 + TC], ev)
        for tsub in range(4):
            tok0 = tsub * P
            for half in range(2):
                bknum = 4 + (tsub * 2 + half) % 2
                bk.bank_wait(bknum)
                for kc in range(8):
                    bk.mm(banks[bknum][:],
                          pool[:, kc * 512 + tok0:kc * 512 + tok0 + P],
                          pool[:, WV + kc * 1024 + half * 512:WV + kc * 1024 + (half + 1) * 512],
                          start=(kc == 0), stop=False)
                v = bk.mm(banks[bknum][:], ones_row,
                          rowv[0:1, P + half * 512:P + (half + 1) * 512],
                          start=False, stop=True, sem="tmm")
                bk.W("vector", "tmm", v)
                ev = pool[:, VEV + half * 512:VEV + (half + 1) * 512]
                bk.W("vector", "gd", bk.cnt["gd"])
                evv = bk.vop("tensor_copy", ev, banks[bknum][:], sem="vop")
                bk.bank_set_free(bknum, "vop", evv)
                bk.W("gpsimd", "vop", evv)
                bk.gdma(sc["v_buf"][:][c * 4 + tsub, :, half * 512:(half + 1) * 512], ev)

    bk.barrier(3)

    QB, KB, VB, PSB, PTB, OSB, OTB, MC = 0, 4096, 8192, 10240, 14336, 15360, 15424, 15552
    mcol = pool[:, MC:MC + 8]
    lcol = pool[:, MC + 8:MC + 16]
    mneg = pool[:, MC + 16:MC + 17]
    linv = pool[:, MC + 17:MC + 18]
    prev_trv = 0
    for h in range(NH):
        bk.W("gpsimd", "tmm", bk.cnt["tmm"])
        bk.W("gpsimd", "sop", bk.cnt["sop"])
        bk.gdma(pool[0:64, QB:QB + T], sc["qt_buf"][:][h])
        bk.gdma(pool[0:64, KB:KB + T], sc["kt_buf"][:][h])
        vldv = bk.gdma(
            pool[:, VB:VB + 2048].rearrange("p (tt d) -> p tt d", tt=32),
            sc["v_buf"][:].transpose([1, 0, 2])[:, :, h * HD:(h + 1) * HD])
        bk.W("tensor", "gd", vldv)
        for sb in range(32):
            for kb in range(8):
                bk.bank_wait(kb)
                v = bk.mm(banks[kb][:], pool[0:64, QB + sb * P:QB + (sb + 1) * P],
                          pool[0:64, KB + kb * 512:KB + (kb + 1) * 512],
                          start=True, stop=True,
                          sem=("tmm" if kb == 7 else None))
            bk.W("vector", "tmm", v)
            for kb in range(8):
                bk.vop("reduce_max", mcol[:, kb:kb + 1], banks[kb][:], AX.X)
            for kb in range(1, 8):
                bk.vop("tensor_tensor", mcol[:, 0:1], mcol[:, 0:1],
                       mcol[:, kb:kb + 1], ALU.max)
            mv = bk.vop("tensor_scalar", mneg, mcol[:, 0:1], -0.125, None,
                        ALU.mult, sem="vop")
            bk.W("scalar", "vop", mv)
            ev = None
            for kb in range(8):
                ev = bk.act(pool[:, PSB + kb * 512:PSB + (kb + 1) * 512],
                            banks[kb][:], AF.Exp, scale=0.125, bias=mneg,
                            accum=lcol[:, kb:kb + 1], sem="sop")
                bk.bank_set_free(kb, "sop", ev)
            bk.W("vector", "sop", ev)
            bk.vop("reduce_sum", linv, lcol, AX.X)
            lv = bk.vop("reciprocal", linv, linv, sem="vop")
            bk.bank_wait(0)
            bk.bank_wait(1)
            bk.bank_wait(2)
            ptv = [0, 0]
            for grp in range(8):
                ptb = banks[1 + grp % 2]
                bk.W("tensor", "sop", ptv[grp % 2])
                for q4 in range(4):
                    tb = grp * 4 + q4
                    bk.tr(ptb[:, q4 * P:(q4 + 1) * P],
                          pool[:, PSB + tb * P:PSB + (tb + 1) * P], ident[:],
                          sem=("tmm" if q4 == 3 else None))
                tv = bk.cnt["tmm"]
                bk.W("scalar", "tmm", tv)
                cpv = bk.sop("copy", pool[:, PTB + (grp % 2) * 512:PTB + (grp % 2) * 512 + 512],
                             ptb[:], sem="sop")
                ptv[grp % 2] = cpv
                bk.W("tensor", "sop", cpv)
                for q4 in range(4):
                    tb = grp * 4 + q4
                    v = bk.mm(banks[0][:, 0:64],
                              pool[:, PTB + (grp % 2) * 512 + q4 * P:PTB + (grp % 2) * 512 + (q4 + 1) * P],
                              pool[:, VB + tb * 64:VB + (tb + 1) * 64],
                              start=(tb == 0), stop=(tb == 31),
                              sem=("tmm" if tb == 31 else None))
            bk.W("vector", "tmm", v)
            bk.W("vector", "vop", lv)
            bk.W("vector", "tmm", prev_trv)
            ov = bk.vop("tensor_scalar", pool[:, OSB:OSB + 64], banks[0][:, 0:64],
                        linv, None, ALU.mult, sem="vop")
            bk.bank_set_free(0, "vop", ov)
            if ptv[0]:
                bk.bank_set_free(1, "sop", ptv[0])
            if ptv[1]:
                bk.bank_set_free(2, "sop", ptv[1])
            bk.W("tensor", "vop", ov)
            bk.bank_wait(3)
            trv = bk.tr(banks[3][0:64, 0:P], pool[:, OSB:OSB + 64], ident[:],
                        sem="tmm")
            prev_trv = trv
            bk.W("scalar", "tmm", trv)
            bk.W("scalar", "gd", bk.cnt["gd"])
            otv = bk.sop("copy", pool[0:64, OTB:OTB + P], banks[3][0:64, 0:P],
                         sem="sop")
            bk.bank_set_free(3, "sop", otv)
            bk.W("gpsimd", "sop", otv)
            bk.gdma(sc["o_buf"][:][h // 2, (h % 2) * HD:(h % 2 + 1) * HD,
                                   sb * P:(sb + 1) * P],
                    pool[0:64, OTB:OTB + P])

    bk.barrier(4)

    XN5, X15, OT5, S25, GP5, UP5 = 0, 2048, 4096, 6144, 8192, 16384
    XQ5, XQ25, AM5, SCB5, IVB5, IVB25 = 16896, 17920, 22016, 22272, 22528, 22784
    RND5, YSB5, WOT5 = 23040, 24576, 26624
    wotv = bk.gdma(pool[:, WOT5:WOT5 + 8192].rearrange("p (a b) -> p a b", a=8),
                   dr["wot_r"][:].transpose([1, 0, 2]))
    bk.W("tensor", "gd", wotv)
    ring5 = Ring(bk, pool[:, 34832:43000], 8168, gate=("ph", bk.cnt["ph"]))

    def t5(base, i, w=256):
        return pool[:, base + i * w:base + (i + 1) * w]

    def xq_t(dt):
        return pool[:, XQ5 + dt * 128:XQ5 + (dt + 1) * 128].bitcast(BF16)

    def xq2_t(dt):
        return pool[:, XQ25 + dt * 128:XQ25 + (dt + 1) * 128].bitcast(BF16)

    def act_quant(src_f, ntiles, xq_f, invb_off):
        acc = pool[:, AM5:AM5 + 256]
        tmp = pool[:, RND5:RND5 + 256]
        mcl = pool[:, AM5 + 256:AM5 + 258]
        for dt in range(ntiles):
            if dt == 0:
                bk.vop("scalar_tensor_tensor", acc, src_f(0), -1.0, src_f(0),
                       ALU.mult, ALU.max)
            else:
                bk.vop("scalar_tensor_tensor", tmp, src_f(dt), -1.0, src_f(dt),
                       ALU.mult, ALU.max)
                bk.vop("tensor_tensor", acc, acc, tmp, ALU.max)
        av = bk.E("vector", lambda e: e.drain(), sem="vop")
        bk.W("tensor", "vop", av)
        bk.bank_wait(6)
        for b in range(2):
            tv = bk.tr(banks[6][:, b * P:(b + 1) * P], acc[:, b * P:(b + 1) * P],
                       ident[:], sem=("tmm" if b == 1 else None))
        bk.W("vector", "tmm", tv)
        bk.vop("reduce_max", mcl[:, 0:1], banks[6][:, 0:P], AX.X)
        mv = bk.vop("reduce_max", mcl[:, 1:2], banks[6][:, P:2 * P], AX.X,
                    sem="vop")
        bk.bank_set_free(6, "vop", mv)
        bk.vop("tensor_scalar", mcl[:, 0:2], mcl[:, 0:2], 1e-5, None, ALU.max)
        bk.vop("reciprocal", tmp[:, 0:2], mcl[:, 0:2])
        bk.vop("tensor_scalar", tmp[:, 0:2], tmp[:, 0:2], 127.0, None, ALU.mult)
        rv = bk.vop("tensor_scalar", mcl[:, 0:2], mcl[:, 0:2], 1.0 / 127.0,
                    None, ALU.mult, sem="vop")
        bk.W("tensor", "vop", rv)
        bk.bank_wait(6)
        for b in range(2):
            bk.tr(banks[6][0:1, b * P:(b + 1) * P], tmp[:, b:b + 1], ident[:])
        for b in range(2):
            tv = bk.tr(banks[6][0:1, 256 + b * P:256 + (b + 1) * P],
                       mcl[:, b:b + 1], ident[:],
                       sem=("tmm" if b == 1 else None))
        bk.W("scalar", "tmm", tv)
        rcv = bk.sop("copy", row(2)[0:1, 0:512], banks[6][0:1, 0:512],
                     sem="sop")
        bk.bank_set_free(6, "sop", rcv)
        bk.W("tensor", "sop", rcv)
        bk.bank_wait(7)
        bk.mm(banks[7][:, 0:256], ones_row, row(2)[0:1, 0:256],
              start=True, stop=True)
        bv = bk.mm(banks[7][:, 256:512], ones_row, row(2)[0:1, 256:512],
                   start=True, stop=True, sem="tmm")
        bk.W("vector", "tmm", bv)
        bk.vop("tensor_copy", pool[:, SCB5:SCB5 + 256], banks[7][:, 0:256])
        cbv = bk.vop("tensor_copy", pool[:, invb_off:invb_off + 256],
                     banks[7][:, 256:512], sem="vop")
        bk.bank_set_free(7, "vop", cbv)
        qq = pool[:, RND5 + 256:RND5 + 512]
        r0 = pool[:, RND5 + 512:RND5 + 768]
        di = pool[:, RND5 + 768:RND5 + 1024]
        gi = pool[:, RND5 + 1024:RND5 + 1280]
        iq = pool[:, RND5 + 1280:RND5 + 1536].bitcast(I32)
        lastv = None
        for dt in range(ntiles):
            bk.vop("tensor_tensor", qq, src_f(dt), pool[:, SCB5:SCB5 + 256],
                   ALU.mult)
            bk.vop("tensor_copy", iq, qq)
            bk.vop("tensor_copy", r0, iq)
            bk.vop("tensor_tensor", di, qq, r0, ALU.subtract)
            bk.vop("tensor_scalar", gi, di, 0.5, None, ALU.is_gt)
            bk.vop("tensor_tensor", r0, r0, gi, ALU.add)
            bk.vop("tensor_scalar", gi, di, -0.5, None, ALU.is_lt)
            bk.vop("tensor_tensor", r0, r0, gi, ALU.subtract)
            lastv = bk.vop("tensor_copy", xq_f(dt), r0,
                           sem=("vop" if dt == ntiles - 1 else None))
        return lastv

    for cc in range(NCH5):
        c0 = cc * TC5
        bk.W("gpsimd", "tmm", bk.cnt["tmm"])
        bk.W("gpsimd", "vop", bk.cnt["vop"])
        bk.W("gpsimd", "sop", bk.cnt["sop"])
        bk.W("vector", "tmm", bk.cnt["tmm"])
        bk.gdma(pool[:, XN5:XN5 + 2048].rearrange("p (dt t) -> p dt t", dt=8),
                sc["xn2_buf"][:].transpose([1, 0, 2])[:, :, c0:c0 + TC5])
        bk.gdma(pool[:, X15:X15 + 2048].rearrange("p (dt t) -> p dt t", dt=8),
                sc["x1_buf"][:].transpose([1, 0, 2])[:, :, c0:c0 + TC5])
        ld5 = bk.gdma(pool[:, OT5:OT5 + 2048].rearrange("p (dt t) -> p dt t", dt=8),
                      sc["o_buf"][:].transpose([1, 0, 2])[:, :, c0:c0 + TC5])
        bk.W("vector", "gd", ld5)
        xqv = act_quant(lambda dt: t5(XN5, dt), 8, xq_t, IVB5)
        bk.W("tensor", "vop", xqv)
        gev = None
        for o in range(32):
            ap, idx = ring5.load(sc["wig"][:][o], 512, bf16=True)
            ring5.consume_wait(idx)
            bknum = o % 4
            bk.bank_wait(bknum)
            apb = ap.bitcast(BF16)
            for kc in range(8):
                v = bk.mm(banks[bknum][:, 0:256], apb[:, kc * P:(kc + 1) * P],
                          xq_t(kc), start=(kc == 0), stop=(kc == 7),
                          sem=("tmm" if kc == 7 else None))
            ring5.free(idx)
            bk.W("vector", "tmm", v)
            gev = bk.vop("scalar_tensor_tensor", t5(GP5, o), banks[bknum][:, 0:256],
                         dvc("qrs_g"), pool[:, IVB5:IVB5 + 256],
                         ALU.mult, ALU.mult, sem="vop")
            bk.bank_set_free(bknum, "vop", gev)
        bk.W("scalar", "vop", gev)
        sgv = None
        for o in range(32):
            sgv = bk.act(t5(GP5, o), t5(GP5, o), AF.Silu,
                         sem=("sop" if o == 31 else None))
        bk.W("vector", "sop", sgv)
        pv = None
        for o in range(32):
            ap, idx = ring5.load(sc["wiu"][:][o], 512, bf16=True)
            ring5.consume_wait(idx)
            bknum = o % 4
            bk.bank_wait(bknum)
            apb = ap.bitcast(BF16)
            for kc in range(8):
                v = bk.mm(banks[bknum][:, 0:256], apb[:, kc * P:(kc + 1) * P],
                          xq_t(kc), start=(kc == 0), stop=(kc == 7),
                          sem=("tmm" if kc == 7 else None))
            ring5.free(idx)
            bk.W("vector", "tmm", v)
            bk.vop("scalar_tensor_tensor", pool[:, UP5:UP5 + 256],
                   banks[bknum][:, 0:256], dvc("qrs_u"),
                   pool[:, IVB5:IVB5 + 256], ALU.mult, ALU.mult)
            pv = bk.vop("tensor_tensor", t5(GP5, o), t5(GP5, o),
                        pool[:, UP5:UP5 + 256], ALU.mult, sem="vop")
            bk.bank_set_free(bknum, "vop", pv)
        xq2v = act_quant(lambda dt: t5(GP5, dt), 32, xq2_t, IVB25)
        bk.W("tensor", "vop", xq2v)
        g2v = None
        for o in range(8):
            ap, idx = ring5.load(sc["wid"][:][o], 2048, bf16=True)
            ring5.consume_wait(idx)
            bknum = 4 + o % 2
            bk.bank_wait(bknum)
            apb = ap.bitcast(BF16)
            for kc in range(32):
                v = bk.mm(banks[bknum][:, 0:256], apb[:, kc * P:(kc + 1) * P],
                          xq2_t(kc), start=(kc == 0), stop=(kc == 31),
                          sem=("tmm" if kc == 31 else None))
            ring5.free(idx)
            bk.W("vector", "tmm", v)
            g2v = bk.vop("scalar_tensor_tensor", t5(XN5, o), banks[bknum][:, 0:256],
                         dvc("qrs_d"), pool[:, IVB25:IVB25 + 256],
                         ALU.mult, ALU.mult, sem="vop")
            bk.bank_set_free(bknum, "vop", g2v)
        bk.W("scalar", "vop", g2v)
        s2v = None
        for o in range(8):
            s2v = bk.act(t5(XN5, o), t5(XN5, o), AF.Silu,
                         sem=("sop" if o == 7 else None))
        for o in range(8):
            bknum = 4 + o % 2
            bk.bank_wait(bknum)
            for kc in range(8):
                v = bk.mm(banks[bknum][:, 0:256],
                          pool[:, WOT5 + o * 1024 + kc * P:WOT5 + o * 1024 + (kc + 1) * P],
                          t5(OT5, kc), start=(kc == 0), stop=(kc == 7),
                          sem=("tmm" if kc == 7 else None))
            bk.W("vector", "tmm", v)
            ev = bk.vop("tensor_scalar", t5(S25, o), banks[bknum][:, 0:256],
                        vc("bo_at", o), None, ALU.add, sem="vop")
            bk.bank_set_free(bknum, "vop", ev)
        bk.W("vector", "sop", s2v)
        yv = None
        for dt in range(8):
            bk.vop("tensor_tensor", t5(S25, dt), t5(S25, dt), t5(XN5, dt),
                   ALU.mult)
            yv = bk.vop("tensor_tensor", t5(X15, dt), t5(X15, dt), t5(S25, dt),
                        ALU.add, sem=("vop" if dt == 7 else None))
        bk.W("tensor", "vop", yv)
        for ts in range(2):
            tok0 = ts * P
            for half in range(2):
                bknum = half
                bk.bank_wait(bknum)
                for k in range(4):
                    dt = half * 4 + k
                    bk.tr(banks[bknum][:, k * P:(k + 1) * P],
                          pool[:, X15 + dt * 256 + tok0:X15 + dt * 256 + tok0 + P],
                          ident[:], sem=("tmm" if k == 3 else None))
                tv = bk.cnt["tmm"]
                bk.W("scalar", "tmm", tv)
                bk.W("scalar", "gd", bk.cnt["gd"])
                cpv = bk.sop("copy", pool[:, YSB5 + half * 512:YSB5 + (half + 1) * 512],
                             banks[bknum][:], sem="sop")
                bk.bank_set_free(bknum, "sop", cpv)
            bk.W("gpsimd", "sop", bk.cnt["sop"])
            bk.gdma(y_out[:][c0 + tok0:c0 + tok0 + P, :],
                    pool[:, YSB5:YSB5 + 1024])

    bk.barrier(5)

    for name in bk.dbg:
        if name == "gfdbg":
            bk.gdma(dbg_outs[name][:], gfdbg_sb[:])
        else:
            bk.gdma(dbg_outs[name][:], sc[name][:])
    bk.W("gpsimd", "gd", bk.cnt["gd"])
    return bk, nc


def kernel(**inputs):
    shared = stage_shared(inputs)
    x = np.ascontiguousarray(np.asarray(inputs["L0_attn_x"], np.float32))
    bk, nc = build_program(dbg=())
    nc = bk.finalize()
    in_maps = []
    for b in range(8):
        m = dict(shared)
        m["x"] = np.ascontiguousarray(x[b])
        in_maps.append(m)
    res = run_bass_kernel_spmd(nc, in_maps, core_ids=list(range(8)))
    return np.stack([r["y"] for r in res.results])


# version 02
# speedup vs baseline: 1.0333x; 1.0333x over previous
import math
import numpy as np
from contextlib import ExitStack

import concourse.bass as bass
import concourse.mybir as mybir
from concourse.bass_utils import run_bass_kernel_spmd

F32 = mybir.dt.float32
BF16 = mybir.dt.bfloat16
I32 = mybir.dt.int32
U8 = mybir.dt.uint8
ALU = mybir.AluOpType
AF = mybir.ActivationFunctionType
AX = mybir.AxisListType

P = 128
T = 4096
D = 1024
HID = 2048
DI = 3072
NL = 4
TC = 512
NCH = T // TC
TC5 = 256
NCH5 = T // TC5
NH = 16
HD = 64
EPS = 1e-5
SQD = float(np.float32(math.sqrt(HID)))

ENGINES = ("sync", "gpsimd", "tensor", "vector", "scalar")
NWL = 6
SEMS = ("gd", "wfr", "tmm", "vop", "sop", "ph") + tuple(f"wl{j}" for j in range(NWL))


def vec_layout():
    lay, off = {}, 0

    def add(name, n):
        nonlocal off
        lay[name] = (off, n)
        off += n

    add("ones", 1)
    add("bp", 16)
    for i in range(NL):
        add(f"g1_{i}", 16)
        add(f"bhid_{i}", 24)
        add(f"bgate_{i}", 24)
        add(f"bo_{i}", 16)
        add(f"g2_{i}", 16)
    add("gfv", 16)
    add("bf", 8)
    add("cnb1", 8)
    add("cnb2", 4)
    add("cnb3", 2)
    add("cnb4", 1)
    add("gcb1", 4)
    add("gcb2", 2)
    add("gcb3", 1)
    add("ar", 1)
    add("cnw4", 2)
    add("gcw3", 2)
    add("bq", 16)
    add("bk", 16)
    add("bv", 16)
    add("bo_at", 8)
    add("c_eps", 1)
    return lay, off


def dvec_layout():
    lay, off = {}, 0

    def add(name, n):
        nonlocal off
        lay[name] = (off, n)
        off += n

    for i in range(NL):
        add(f"g1p1_{i}", 16)
        add(f"g2p1_{i}", 16)
        add(f"nbg_{i}", 24)
    add("gfp1", 16)
    add("cgA", 1)
    add("cgB", 1)
    add("qsc_g", 1)
    add("qsc_u", 1)
    add("qsc_d", 1)
    add("qrs_g", 1)
    add("qrs_u", 1)
    add("qrs_d", 1)
    return lay, off


def col_pack(v, ncols):
    return np.ascontiguousarray(v.reshape(ncols, P).T)


def stage_shared(I):
    f32 = np.float32
    out = {}

    def blk(w, njb):
        K, M = w.shape
        nk = K // P
        r = w.reshape(nk, P, njb, M // njb)
        return np.ascontiguousarray(r.transpose(2, 1, 0, 3).reshape(njb, P, nk * (M // njb)))

    out["wp_r"] = blk(np.asarray(I["L0_attn_Wp"], f32), 16)
    whg = np.asarray(I["L0_attn_Whg"], f32)
    whg_r = np.empty((NL, 24, P, 4096), f32)
    for i in range(NL):
        hidb = blk(whg[i][:, :DI], 24)
        gateb = blk(whg[i][:, DI:], 24)
        whg_r[i][:, :, :2048] = hidb
        whg_r[i][:, :, 2048:] = gateb
    out["whg_r"] = whg_r
    wo = np.asarray(I["L0_attn_Wo"], f32)
    out["wo_r"] = np.stack([blk(wo[i], 16) for i in range(NL)])
    out["wf_r"] = blk(np.asarray(I["L0_attn_Wf"], f32), 8)
    out["cn1_r"] = blk(np.asarray(I["L0_ffn_cn_w1"], f32), 8)
    out["cn2_r"] = blk(np.asarray(I["L0_ffn_cn_w2"], f32), 4)
    out["cn3_r"] = blk(np.asarray(I["L0_ffn_cn_w3"], f32), 2)
    out["gc1_r"] = blk(np.asarray(I["L0_ffn_gc_w1"], f32), 4)
    out["gc2_r"] = blk(np.asarray(I["L0_ffn_gc_w2"], f32), 2)

    def headT(w):
        return np.ascontiguousarray(
            w.transpose(2, 0, 1).reshape(8, P, NH * HD))

    out["wqt_r"] = headT(np.asarray(I["L1_attn_Wq"], f32))
    out["wkt_r"] = headT(np.asarray(I["L1_attn_Wk"], f32))
    out["wvt_r"] = headT(np.asarray(I["L1_attn_Wv"], f32))
    out["wot_r"] = blk(np.ascontiguousarray(np.asarray(I["L1_attn_Wo"], f32).T), 8)
    out["wg_r"] = blk(np.ascontiguousarray(np.asarray(I["L1_ffn_w_gate"], f32).T), 32)
    out["wu_r"] = blk(np.ascontiguousarray(np.asarray(I["L1_ffn_w_up"], f32).T), 32)
    out["wd_r"] = blk(np.ascontiguousarray(np.asarray(I["L1_ffn_w_down"], f32).T), 8)
    out["ident"] = np.eye(P, dtype=f32)

    lay, nv = vec_layout()
    vecs = np.zeros((P, nv), f32)

    def put(name, arr):
        o, n = lay[name]
        vecs[:, o:o + n] = arr

    put("ones", np.ones((P, 1), f32))
    put("bp", col_pack(np.asarray(I["L0_attn_bp"], f32), 16))
    for i in range(NL):
        put(f"g1_{i}", col_pack(np.asarray(I["L0_attn_g1"], f32)[i], 16))
        bhg = np.asarray(I["L0_attn_bhg"], f32)[i]
        put(f"bhid_{i}", col_pack(bhg[:DI], 24))
        put(f"bgate_{i}", col_pack(bhg[DI:], 24))
        put(f"bo_{i}", col_pack(np.asarray(I["L0_attn_bo"], f32)[i], 16))
        put(f"g2_{i}", col_pack(np.asarray(I["L0_attn_g2"], f32)[i], 16))
    put("gfv", col_pack(np.asarray(I["L0_attn_gf"], f32), 16))
    put("bf", col_pack(np.asarray(I["L0_attn_bf"], f32), 8))
    put("cnb1", col_pack(np.asarray(I["L0_ffn_cn_b1"], f32), 8))
    put("cnb2", col_pack(np.asarray(I["L0_ffn_cn_b2"], f32), 4))
    put("cnb3", col_pack(np.asarray(I["L0_ffn_cn_b3"], f32), 2))
    b4 = np.zeros((P, 1), f32); b4[0, 0] = np.asarray(I["L0_ffn_cn_b4"], f32)[0]
    put("cnb4", b4)
    put("gcb1", col_pack(np.asarray(I["L0_ffn_gc_b1"], f32), 4))
    put("gcb2", col_pack(np.asarray(I["L0_ffn_gc_b2"], f32), 2))
    g3 = np.zeros((P, 1), f32); g3[0, 0] = np.asarray(I["L0_ffn_gc_b3"], f32)[0]
    put("gcb3", g3)
    arr = np.zeros((P, 1), f32); arr[:, 0] = np.asarray(I["L0_ffn_adaptive_range"], f32)[0]
    put("ar", arr)
    put("cnw4", col_pack(np.asarray(I["L0_ffn_cn_w4"], f32)[:, 0], 2))
    put("gcw3", col_pack(np.asarray(I["L0_ffn_gc_w3"], f32)[:, 0], 2))
    for nm, key in (("bq", "L1_attn_bq"), ("bk", "L1_attn_bk"), ("bv", "L1_attn_bv")):
        a = np.zeros((P, NH), f32)
        a[0:HD, :] = np.asarray(I[key], f32).T
        put(nm, a)
    put("bo_at", col_pack(np.asarray(I["L1_attn_bo"], f32), 8))
    put("c_eps", np.full((P, 1), 1e-5, f32))
    out["vecs"] = vecs

    rowvecs = np.zeros((1, 1152), f32)
    rowvecs[0, 0:P] = 1.0
    rowvecs[0, P:P + D] = np.asarray(I["L1_attn_bv"], f32).reshape(-1)
    out["rowvecs"] = rowvecs
    return out


class BK:
    def __init__(self, dbg=()):
        self.nc = bass.Bass("TRN2", target_bir_lowering=False)
        self.q = {e: [] for e in ENGINES}
        self.cnt = {s: 0 for s in SEMS}
        self.maxwait = {s: 0 for s in SEMS}
        self.sem = {}
        self.dbg = set(dbg)
        self.es = ExitStack()
        self.banks = None
        self.bank_free = {}

    def E(self, eng, f, sem=None, k=1):
        if sem is None:
            self.q[eng].append(f)
            return None
        self.cnt[sem] += k
        v = self.cnt[sem]
        name = sem

        def g(e, f=f, name=name, k=k):
            f(e).then_inc(self.sem[name], k)
        self.q[eng].append(g)
        return v

    def W(self, eng, sem, v):
        if v is None or v <= 0:
            return
        self.maxwait[sem] = max(self.maxwait[sem], v)

        def g(e, name=sem, v=v):
            e.wait_ge(self.sem[name], v)
        self.q[eng].append(g)

    def dma(self, eng, out_ap, in_ap, sem):
        return self.E(eng, lambda e, o=out_ap, i=in_ap: e.dma_start(out=o, in_=i),
                      sem=sem, k=16)

    def gdma(self, out_ap, in_ap):
        v = self.dma("gpsimd", out_ap, in_ap, "gd")
        self.W("gpsimd", "gd", v)
        return v

    def mm(self, out, lhsT, rhs, start, stop, sem=None):
        return self.E("tensor",
                      lambda e, o=out, a=lhsT, b=rhs, s=start, p=stop:
                      e.matmul(o, a, b, start=s, stop=p), sem=sem)

    def tr(self, out, in_, ident, sem=None):
        return self.E("tensor",
                      lambda e, o=out, i=in_, d=ident: e.transpose(o, i, d),
                      sem=sem)

    def act(self, out, in_, func, bias=0.0, scale=1.0, accum=None, sem=None):
        def f(e, o=out, i=in_, fn=func, b=bias, s=scale, a=accum):
            return e.activation(o, i, fn, bias=b, scale=s, accum_out=a)
        v = self.E("scalar", f, sem=sem)
        self.q["scalar"].append(lambda e: e.drain())
        return v

    def vop(self, meth, *args, sem=None, **kw):
        def f(e, m=meth, aa=args, kk=kw):
            return getattr(e, m)(*aa, **kk)
        v = self.E("vector", f, sem=sem)
        self.q["vector"].append(lambda e: e.drain())
        return v

    def sop(self, meth, *args, sem=None, **kw):
        def f(e, m=meth, aa=args, kk=kw):
            return getattr(e, m)(*aa, **kk)
        v = self.E("scalar", f, sem=sem)
        self.q["scalar"].append(lambda e: e.drain())
        return v

    def bank_wait(self, b):
        fw = self.bank_free.get(b)
        if fw is not None:
            self.W("tensor", fw[0], fw[1])
            self.bank_free[b] = None

    def bank_set_free(self, b, sem, val):
        self.bank_free[b] = (sem, val)

    def barrier(self, idx):
        for j in range(NWL):
            self.W("sync", f"wl{j}", self.cnt[f"wl{j}"])
        self.W("gpsimd", "gd", self.cnt["gd"])
        for e in ENGINES:
            self.E(e, lambda en: en.drain(), sem="ph")
        target = self.cnt["ph"]
        for e in ENGINES:
            self.W(e, "ph", target)
        self.bank_free = {}

    def finalize(self):
        for s in SEMS:
            assert self.maxwait[s] <= self.cnt[s], \
                f"sem {s}: wait {self.maxwait[s]} > incs {self.cnt[s]}"
        es = self.es
        for s in SEMS:
            self.sem[s] = es.enter_context(self.nc.semaphore(s))
        block = es.enter_context(self.nc.Block())
        with es:
            for eng in ENGINES:
                ops = self.q[eng]

                def body(e, ops=ops):
                    for f in ops:
                        f(e)
                getattr(block, eng)(body)
        return self.nc


class Ring:

    def __init__(self, bk, pool_ap, ncols, gate=None):
        self.bk = bk
        self.pool = pool_ap
        self.n = ncols
        self.pos = 0
        self.owner = np.full(ncols, -1, dtype=np.int64)
        self.blocks = []
        self.base = bk.cnt["wfr"]
        self.gate = gate
        self.gated = gate is None

    def load(self, dram_ap, ncols, bf16=False):
        bk = self.bk
        if not self.gated:
            bk.W("sync", self.gate[0], self.gate[1])
            self.gated = True
        if self.pos + ncols > self.n:
            self.pos = 0
        sl = slice(self.pos, self.pos + ncols)
        idx = len(self.blocks)
        prev = int(self.owner[sl].max())
        need = max(prev + 1, idx - NWL + 1)
        if need > 0:
            bk.W("sync", "wfr", self.base + need)
        self.owner[sl] = idx
        ap = self.pool[:, sl]
        sem = f"wl{idx % NWL}"
        v = bk.dma("sync", ap.bitcast(BF16) if bf16 else ap, dram_ap, sem)
        self.blocks.append((sem, v))
        self.pos += ncols
        return ap, idx

    def consume_wait(self, idx):
        sem, v = self.blocks[idx]
        self.bk.W("tensor", sem, v)

    def free(self, idx):
        v = self.bk.E("tensor", lambda e: e.drain(), sem="wfr")
        assert v == self.base + idx + 1, (v, self.base, idx)


def build_program(dbg=()):
    bk = BK(dbg)
    nc = bk.nc
    es = bk.es

    x_in = nc.declare_dram_parameter("x", [T, D], F32, isOutput=False)
    y_out = nc.declare_dram_parameter("y", [T, D], F32, isOutput=True)
    dr = {}
    for name, shape in (
        ("wp_r", (16, P, 1024)), ("whg_r", (NL, 24, P, 4096)),
        ("wo_r", (NL, 16, P, 3072)), ("wf_r", (8, P, 2048)),
        ("cn1_r", (8, P, 1024)), ("cn2_r", (4, P, 1024)), ("cn3_r", (2, P, 512)),
        ("gc1_r", (4, P, 1024)), ("gc2_r", (2, P, 512)),
        ("wqt_r", (8, P, 1024)), ("wkt_r", (8, P, 1024)), ("wvt_r", (8, P, 1024)),
        ("wot_r", (8, P, 1024)), ("wg_r", (32, P, 1024)), ("wu_r", (32, P, 1024)),
        ("wd_r", (8, P, 4096)), ("ident", (P, P)), ("vecs", (P, vec_layout()[1])),
        ("rowvecs", (1, 1152)),
    ):
        dr[name] = nc.declare_dram_parameter(name, list(shape), F32, isOutput=False)

    sc = {}
    for name, shape, dt in (
        ("xn_buf", (8, P, T), F32), ("xt_buf", (8, P, T), F32),
        ("s_buf", (8, P, T), F32), ("x1_buf", (8, P, T), F32),
        ("xn2_buf", (8, P, T), F32),
        ("qt_buf", (NH, HD, T), F32), ("kt_buf", (NH, HD, T), F32),
        ("v_buf", (32, P, D), F32), ("o_buf", (8, P, T), F32),
        ("l_buf", (NCH5, NH, TC5), F32),
        ("wig", (32, P, 1024), BF16), ("wiu", (32, P, 1024), BF16),
        ("wid", (8, P, 4096), BF16),
    ):
        sc[name] = nc.dram_tensor(name, list(shape), dt)

    dbg_outs = {}
    for name in bk.dbg:
        shape = {"xn_buf": (8, P, T), "xt_buf": (8, P, T), "s_buf": (8, P, T),
                 "x1_buf": (8, P, T), "xn2_buf": (8, P, T),
                 "qt_buf": (NH, HD, T), "kt_buf": (NH, HD, T),
                 "v_buf": (32, P, D), "o_buf": (8, P, T),
                 "wig": (32, P, 1024), "wiu": (32, P, 1024),
                 "wid": (8, P, 4096), "gfdbg": (P, 8)}[name]
        dt = BF16 if name in ("wig", "wiu", "wid") else F32
        dbg_outs[name] = nc.declare_dram_parameter(
            "dbg_" + name, list(shape), dt, isOutput=True)

    VL, NV = vec_layout()
    DL, ND = dvec_layout()
    vecs = es.enter_context(nc.sbuf_tensor("s_vecs", [P, NV], F32))
    dv = es.enter_context(nc.sbuf_tensor("s_dv", [P, ND], F32))
    rowv = es.enter_context(nc.sbuf_tensor("s_rowv", [1, 1152], F32))
    ident = es.enter_context(nc.sbuf_tensor("identt", [P, P], F32))
    carry = es.enter_context(nc.sbuf_tensor("s_carry", [P, NL * 24], F32))
    rows = es.enter_context(nc.sbuf_tensor("s_rows", [1, 2048], F32))
    gfdbg_sb = es.enter_context(nc.sbuf_tensor("s_gfdbg", [P, 8], F32))
    POOL = 43000
    pool = es.enter_context(nc.sbuf_tensor("s_pool", [P, POOL], F32))
    banks = [es.enter_context(nc.psum_tensor(f"bank{i}", [P, 512], F32))
             for i in range(8)]

    def vc(name, j0=0, n=1):
        o, nn = VL[name]
        assert j0 + n <= nn, name
        return vecs[:, o + j0:o + j0 + n]

    def dvc(name, j0=0, n=1):
        o, nn = DL[name]
        assert j0 + n <= nn, name
        return dv[:, o + j0:o + j0 + n]

    ones_col = vc("ones")
    ones_row = rowv[0:1, 0:P]

    def row(i):
        return rows[0:1, i * 512:(i + 1) * 512]

    v0 = bk.gdma(vecs[:], dr["vecs"][:])
    bk.gdma(rowv[:], dr["rowvecs"][:])
    v1 = bk.gdma(ident[:], dr["ident"][:])
    bk.W("vector", "gd", v1)
    for i in range(NL):
        bk.vop("tensor_scalar", dvc(f"g1p1_{i}", 0, 16), vc(f"g1_{i}", 0, 16),
               1.0, None, ALU.add)
        bk.vop("tensor_scalar", dvc(f"g2p1_{i}", 0, 16), vc(f"g2_{i}", 0, 16),
               1.0, None, ALU.add)
        bk.vop("tensor_scalar", dvc(f"nbg_{i}", 0, 24), vc(f"bgate_{i}", 0, 24),
               -1.0, None, ALU.mult)
    bk.vop("tensor_scalar", dvc("gfp1", 0, 16), vc("gfv", 0, 16), 1.0, None, ALU.add)
    bk.vop("memset", carry[:], 1.0, sem="vop")
    vop_pv = bk.cnt["vop"]
    bk.W("tensor", "vop", vop_pv)
    bk.W("tensor", "gd", v1)

    WT0, WT1, ACCC, TMPC = 0, 4096, 8192, 8193
    QQ5, R05, DI5, GI5, IQ5, WB5 = 8448, 12544, 16640, 20736, 24832, 28928
    for drname, nb, ncols, scname, rsname, wbuf in (
        ("wg_r", 32, 1024, "qsc_g", "qrs_g", "wig"),
        ("wu_r", 32, 1024, "qsc_u", "qrs_u", "wiu"),
        ("wd_r", 8, 4096, "qsc_d", "qrs_d", "wid"),
    ):
        nelem = nb * P * ncols
        slot_guard = [0, 0]
        for b in range(nb):
            s = b % 2
            wt = pool[:, (WT0 if s == 0 else WT1):(WT0 if s == 0 else WT1) + ncols]
            bk.W("gpsimd", "vop", slot_guard[s])
            lv = bk.gdma(wt, dr[drname][:][b])
            bk.W("vector", "gd", lv)
            if b == 0:
                rv = bk.vop("reduce_sum", pool[:, ACCC:ACCC + 1], wt, AX.X,
                            apply_absolute_value=True, sem="vop")
            else:
                bk.vop("reduce_sum", pool[:, TMPC:TMPC + 1], wt, AX.X,
                       apply_absolute_value=True)
                rv = bk.vop("tensor_tensor", pool[:, ACCC:ACCC + 1],
                            pool[:, ACCC:ACCC + 1], pool[:, TMPC:TMPC + 1],
                            ALU.add, sem="vop")
            slot_guard[s] = rv
        bk.W("tensor", "vop", rv)
        bk.bank_wait(0)
        mv = bk.mm(banks[0][0:1, 0:1], ones_col, pool[:, ACCC:ACCC + 1],
                   start=True, stop=True, sem="tmm")
        bk.W("vector", "tmm", mv)
        r3 = row(3)
        bk.vop("tensor_scalar", r3[0:1, 0:1], banks[0][0:1, 0:1],
               1.0 / nelem, None, ALU.mult, sem="vop")
        bk.bank_set_free(0, "vop", bk.cnt["vop"])
        bk.vop("tensor_scalar", r3[0:1, 0:1], r3[0:1, 0:1], 1e-5, None, ALU.max)
        bk.vop("reciprocal", r3[0:1, 1:2], r3[0:1, 0:1])
        sv = bk.vop("reciprocal", r3[0:1, 2:3], r3[0:1, 1:2], sem="vop")
        bk.W("tensor", "vop", sv)
        bk.bank_wait(1)
        bk.mm(banks[1][:, 0:1], ones_row, r3[0:1, 1:2], start=True, stop=True)
        bv = bk.mm(banks[1][:, 1:2], ones_row, r3[0:1, 2:3], start=True,
                   stop=True, sem="tmm")
        bk.W("vector", "tmm", bv)
        bk.vop("tensor_copy", dvc(scname), banks[1][:, 0:1])
        cv2 = bk.vop("tensor_copy", dvc(rsname), banks[1][:, 1:2], sem="vop")
        bk.bank_set_free(1, "vop", cv2)
        qq = pool[:, QQ5:QQ5 + ncols]
        r0 = pool[:, R05:R05 + ncols]
        di = pool[:, DI5:DI5 + ncols]
        gi = pool[:, GI5:GI5 + ncols]
        iq = pool[:, IQ5:IQ5 + ncols].bitcast(I32)
        wb = pool[:, WB5:WB5 + ncols // 2].bitcast(BF16)
        for b in range(nb):
            s = b % 2
            wt = pool[:, (WT0 if s == 0 else WT1):(WT0 if s == 0 else WT1) + ncols]
            bk.W("gpsimd", "vop", slot_guard[s])
            lv = bk.gdma(wt, dr[drname][:][b])
            bk.W("vector", "gd", lv)
            bk.vop("tensor_scalar", qq, wt, dvc(scname), None, ALU.mult,
                   sem="vop")
            slot_guard[s] = bk.cnt["vop"]
            bk.vop("tensor_copy", iq, qq)
            bk.vop("tensor_copy", r0, iq)
            bk.vop("tensor_tensor", di, qq, r0, ALU.subtract)
            bk.vop("tensor_scalar", gi, di, 0.5, None, ALU.is_gt)
            bk.vop("tensor_tensor", r0, r0, gi, ALU.add)
            bk.vop("tensor_scalar", gi, di, -0.5, None, ALU.is_lt)
            bk.vop("tensor_tensor", r0, r0, gi, ALU.subtract)
            bk.vop("tensor_scalar", r0, r0, 1.0, None, ALU.min)
            bk.vop("tensor_scalar", r0, r0, -1.0, None, ALU.max)
            bk.W("vector", "gd", bk.cnt["gd"])
            wv = bk.vop("tensor_copy", wb, r0, sem="vop")
            bk.W("gpsimd", "vop", wv)
            bk.gdma(sc[wbuf][:][b], wb)

    bk.barrier(-1)

    pa = 0

    def alloc(n):
        nonlocal pa
        ap = pool[:, pa:pa + n]
        pa += n
        return ap

    xt_b = [alloc(1024) for _ in range(2)]
    xn_b = alloc(1024)
    sq_col = alloc(1)
    sum_col = alloc(1)
    mean_col = alloc(1)
    ex2_col = alloc(1)
    var_col = alloc(1)
    sd_col = alloc(1)
    inv_col = alloc(1)
    tcol = alloc(1)
    sqs = alloc(1024)
    xnT = alloc(1024)
    xtT = alloc(1024)
    gccols = alloc(16)
    gc1_sb = alloc(4 * 1024)
    gc2_sb = alloc(2 * 512)

    g1v = bk.gdma(gc1_sb.rearrange("p (a b) -> p a b", a=4),
                  dr["gc1_r"][:].transpose([1, 0, 2]))
    g2v = bk.gdma(gc2_sb.rearrange("p (a b) -> p a b", a=2),
                  dr["gc2_r"][:].transpose([1, 0, 2]))

    last_store = 0
    last_xt_use = [(0, 0), (0, 0)]
    prev_mmv = 0
    for tt in range(32):
        buf = xt_b[tt % 2]
        bk.W("gpsimd", "vop", last_xt_use[tt % 2][0])
        bk.W("gpsimd", "tmm", last_xt_use[tt % 2][1])
        av = bk.gdma(buf, x_in[:][tt * P:(tt + 1) * P, :])
        bk.W("vector", "gd", av)
        bk.vop("scalar_tensor_tensor", sqs, buf, 1.0, buf, ALU.mult, ALU.mult,
               accum_out=sq_col)
        bk.vop("reduce_sum", sum_col, buf, AX.X)
        bk.vop("tensor_scalar", mean_col, sum_col, 1.0 / D, None, ALU.mult)
        bk.vop("tensor_scalar", ex2_col, sq_col, 1.0 / D, None, ALU.mult)
        bk.vop("tensor_tensor", tcol, mean_col, mean_col, ALU.mult)
        bk.vop("tensor_tensor", var_col, ex2_col, tcol, ALU.subtract, sem="vop")
        vv = bk.cnt["vop"]
        bk.W("scalar", "vop", vv)
        sv = bk.act(sd_col, var_col, AF.Sqrt, bias=vc("c_eps"), sem="sop")
        bk.W("vector", "sop", sv)
        bk.vop("reciprocal", inv_col, sd_col)
        bk.W("vector", "tmm", prev_mmv)
        bk.vop("tensor_scalar", xn_b, buf, mean_col, inv_col,
               ALU.subtract, ALU.mult, sem="vop")
        xn_ready = bk.cnt["vop"]

        bk.W("tensor", "vop", xn_ready)
        for half in range(2):
            b = banks[half]
            bk.bank_wait(half)
            for k in range(4):
                bk.tr(b[:, k * P:(k + 1) * P], xn_b[:, (half * 4 + k) * P:(half * 4 + k + 1) * P], ident[:])
            tv = bk.E("tensor", lambda e: e.drain(), sem="tmm")
            bk.W("scalar", "tmm", tv)
            cv = bk.sop("copy", xnT[:, half * 512:(half + 1) * 512], b[:], sem="sop")
            bk.bank_set_free(half, "sop", cv)
        for half in range(2):
            b = banks[2 + half]
            bk.bank_wait(2 + half)
            for k in range(4):
                bk.tr(b[:, k * P:(k + 1) * P], buf[:, (half * 4 + k) * P:(half * 4 + k + 1) * P], ident[:])
            tv = bk.E("tensor", lambda e: e.drain(), sem="tmm")
            bk.W("scalar", "tmm", tv)
            cv = bk.sop("copy", xtT[:, half * 512:(half + 1) * 512], b[:], sem="sop")
            bk.bank_set_free(2 + half, "sop", cv)
        for half in range(2):
            bk.bank_wait(6 + half)
            bk.mm(banks[6 + half][0:1, :], ones_col, xn_b[:, half * 512:(half + 1) * 512],
                  start=(tt == 0), stop=(tt == 31))
        mmv = bk.E("tensor", lambda e: e.drain(), sem="tmm")
        prev_mmv = mmv
        last_xt_use[tt % 2] = (xn_ready, mmv)

        sv_ = bk.cnt["sop"]
        bk.W("gpsimd", "sop", sv_)
        bk.gdma(sc["xn_buf"][:].transpose([1, 0, 2])[:, :, tt * P:(tt + 1) * P],
                xnT.rearrange("p (dt t) -> p dt t", dt=8))
        last_store = bk.gdma(
            sc["xt_buf"][:].transpose([1, 0, 2])[:, :, tt * P:(tt + 1) * P],
            xtT.rearrange("p (dt t) -> p dt t", dt=8))
        bk.W("scalar", "gd", last_store)

    bk.W("tensor", "gd", g2v)
    ctx = gccols[:, 0:8]
    bk.W("scalar", "tmm", bk.cnt["tmm"])
    e1 = bk.act(row(0), banks[6][0:1, :], AF.Copy, scale=1.0 / T, sem="sop")
    e2 = bk.act(row(1), banks[7][0:1, :], AF.Copy, scale=1.0 / T, sem="sop")
    bk.bank_set_free(6, "sop", e1)
    bk.bank_set_free(7, "sop", e2)
    bk.W("tensor", "sop", e2)
    for k in range(8):
        b = banks[k % 2]
        src = row(k // 4)[0:1, (k % 4) * 128:((k % 4) + 1) * 128]
        bk.bank_wait(k % 2)
        bk.tr(b[:, 0:1], src, ident[0:1, 0:1])
        tv = bk.E("tensor", lambda e: e.drain(), sem="tmm")
        bk.W("vector", "tmm", tv)
        fv = bk.vop("tensor_copy", ctx[:, k:k + 1], b[:, 0:1], sem="vop")
        bk.bank_set_free(k % 2, "vop", fv)
    h1 = gccols[:, 8:12]
    bk.W("tensor", "vop", bk.cnt["vop"])
    for o in range(4):
        bk.bank_wait(4)
        for kc in range(8):
            bk.mm(banks[4][:, 0:1], gc1_sb[:, o * 1024 + kc * P: o * 1024 + (kc + 1) * P],
                  ctx[:, kc:kc + 1], start=(kc == 0), stop=(kc == 7))
        tv = bk.E("tensor", lambda e: e.drain(), sem="tmm")
        bk.W("scalar", "tmm", tv)
        av = bk.act(h1[:, o:o + 1], banks[4][:, 0:1], AF.Gelu,
                    bias=vc("gcb1", o), sem="sop")
        bk.bank_set_free(4, "sop", av)
        bk.W("tensor", "sop", av)
    h2 = gccols[:, 12:14]
    for o in range(2):
        bk.bank_wait(5)
        for kc in range(4):
            bk.mm(banks[5][:, 0:1], gc2_sb[:, o * 512 + kc * P: o * 512 + (kc + 1) * P],
                  h1[:, kc:kc + 1], start=(kc == 0), stop=(kc == 3))
        tv = bk.E("tensor", lambda e: e.drain(), sem="tmm")
        bk.W("scalar", "tmm", tv)
        av = bk.act(h2[:, o:o + 1], banks[5][:, 0:1], AF.Gelu,
                    bias=vc("gcb2", o), sem="sop")
        bk.bank_set_free(5, "sop", av)
        bk.W("tensor", "sop", av)
    bk.bank_wait(4)
    for kc in range(2):
        bk.mm(banks[4][0:1, 0:1], vc("gcw3", kc), h2[:, kc:kc + 1],
              start=(kc == 0), stop=(kc == 1))
    tv = bk.E("tensor", lambda e: e.drain(), sem="tmm")
    bk.W("scalar", "tmm", tv)
    gfs = gccols[:, 14:15]
    av = bk.act(gfs[0:1, :], banks[4][0:1, 0:1], AF.Sigmoid,
                bias=vc("gcb3")[0:1, :], sem="sop")
    bk.bank_set_free(4, "sop", av)
    bk.W("tensor", "sop", av)
    bk.bank_wait(5)
    bk.mm(banks[5][:, 0:1], ones_row, gfs[0:1, :], start=True, stop=True)
    tv = bk.E("tensor", lambda e: e.drain(), sem="tmm")
    bk.W("vector", "tmm", tv)
    gf_col = gccols[:, 15:16]
    bk.vop("tensor_copy", gf_col, banks[5][:, 0:1], sem="vop")
    bk.bank_set_free(5, "vop", bk.cnt["vop"])
    bk.vop("tensor_scalar", dvc("cgA"), vc("ar"), 0.6, None, ALU.mult)
    bk.vop("tensor_tensor", tcol, vc("ar"), gf_col, ALU.mult)
    bk.vop("tensor_scalar", tcol, tcol, 1.4, None, ALU.mult)
    bk.vop("tensor_tensor", tcol, tcol, vc("ar"), ALU.subtract)
    bk.vop("tensor_scalar", dvc("cgB"), tcol, 0.1, None, ALU.add, sem="vop")
    if "gfdbg" in bk.dbg:
        bk.vop("memset", gfdbg_sb[:], 0.0)
        bk.vop("tensor_copy", gfdbg_sb[:, 0:1], gf_col)
        bk.vop("tensor_copy", gfdbg_sb[:, 1:2], dvc("cgA"))
        bk.vop("tensor_copy", gfdbg_sb[:, 2:3], dvc("cgB"))
        bk.vop("tensor_copy", gfdbg_sb[:, 3:4], ctx[:, 0:1], sem="vop")

    bk.barrier(0)

    def xn_t(kc):
        return pool[:, kc * 512:(kc + 1) * 512]

    def hn_t(dt):
        return pool[:, dt * 512:(dt + 1) * 512]

    def h_t(o):
        return pool[:, 8192 + o * 512:8192 + (o + 1) * 512]

    def scano(j):
        return pool[:, 16384 + j * 513:16384 + (j + 1) * 513]

    def cvc(s):
        return pool[:, 28696 + s * 1026:28696 + s * 1026 + 513]

    def cvv(s):
        return pool[:, 28696 + s * 1026 + 513:28696 + s * 1026 + 1026]

    def zs_sh(s):
        return pool[:, 30748 + s * 1536:30748 + s * 1536 + 512]

    def zs_hb(s):
        return pool[:, 30748 + s * 1536 + 512:30748 + s * 1536 + 1024]

    def zs_gm(s):
        return pool[:, 30748 + s * 1536 + 1024:30748 + s * 1536 + 1536]

    def sq_t(s):
        return zs_sh(s)

    s_col_ap = pool[:, 33820:34332]
    ring = Ring(bk, pool[:, 34332:43000], 8668)

    bk.vop("memset", cvc(0)[:, 0:1], 0.0)
    bk.vop("memset", cvv(0)[:, 0:1], 1.0)
    bk.vop("memset", cvc(1)[:, 0:1], 0.0)
    bk.vop("memset", cvv(1)[:, 0:1], 1.0, sem="vop")

    def rms_stats(tiles_f, ntiles, zs_guards=(0, 0)):
        sq_guard = [None, None]
        mmv = None
        for dt in range(ntiles):
            s = dt % 2
            if sq_guard[s] is not None:
                bk.W("scalar", "tmm", sq_guard[s])
            if dt < 2:
                bk.W("scalar", "vop", zs_guards[s])
            sv = bk.act(sq_t(s), tiles_f(dt), AF.Square, sem="sop")
            bk.W("tensor", "sop", sv)
            if dt == 0:
                bk.bank_wait(6)
            mmv = bk.mm(banks[6][0:1, :], ones_col, sq_t(s),
                        start=(dt == 0), stop=(dt == ntiles - 1), sem="tmm")
            sq_guard[s] = mmv
        bk.W("scalar", "tmm", mmv)
        nv = bk.act(row(0), banks[6][0:1, :], AF.Sqrt, sem="sop")
        bk.bank_set_free(6, "sop", nv)
        bk.W("vector", "sop", nv)
        bk.vop("tensor_scalar", row(0), row(0), 1e-12, None, ALU.max)
        bk.vop("reciprocal", row(1), row(0))
        iv = bk.vop("tensor_scalar", row(1), row(1), SQD, None, ALU.mult, sem="vop")
        bk.W("tensor", "vop", iv)
        bk.bank_wait(7)
        bcv = bk.mm(banks[7][:, :], ones_row, row(1), start=True, stop=True,
                    sem="tmm")
        return bcv

    ch_guard_sop = 0
    ch_guard_zs = [0, 0]
    ch_guard_cv = [0, 0]
    scan_val = 0
    wo_mm_last = 0
    whg_mm_last = 0
    wf_mm_last = 0
    sstore_val = 0
    for c in range(NCH):
        c0 = c * TC
        bk.W("gpsimd", "tmm", wf_mm_last)
        xnv = bk.gdma(pool[:, 0:4096].rearrange("p (dt t) -> p dt t", dt=8),
                      sc["xn_buf"][:].transpose([1, 0, 2])[:, :, c0:c0 + TC])
        bk.W("tensor", "gd", xnv)
        wp_evac = None
        for o in range(16):
            ap, idx = ring.load(dr["wp_r"][:][o], 1024)
            ring.consume_wait(idx)
            bknum = 4 + o % 2
            bk.bank_wait(bknum)
            for kc in range(8):
                v = bk.mm(banks[bknum][:], ap[:, kc * P:(kc + 1) * P], xn_t(kc),
                          start=(kc == 0), stop=(kc == 7),
                          sem=("tmm" if kc == 7 else None))
            ring.free(idx)
            bk.W("vector", "tmm", v)
            if c > 0 and o == 0:
                bk.W("vector", "sop", ch_guard_sop)
            wp_evac = bk.vop("tensor_scalar", h_t(o), banks[bknum][:],
                             vc("bp", o), None, ALU.add, sem="vop")
            bk.bank_set_free(bknum, "vop", wp_evac)

        for i in range(NL):
            bk.W("scalar", "vop", bk.cnt["vop"])
            bcv = rms_stats(h_t, 16, zs_guards=tuple(ch_guard_zs))
            bk.W("vector", "tmm", bcv)
            if i > 0:
                bk.W("vector", "tmm", whg_mm_last)
            hnv = None
            for dt in range(16):
                hnv = bk.vop("scalar_tensor_tensor", hn_t(dt), h_t(dt),
                             dvc(f"g1p1_{i}", dt), banks[7][:],
                             ALU.mult, ALU.mult,
                             sem=("vop" if dt == 15 else None))
            bk.bank_set_free(7, "vop", hnv)
            bk.W("tensor", "vop", hnv)

            for j in range(24):
                ap, idx = ring.load(dr["whg_r"][:][i, j], 4096)
                ring.consume_wait(idx)
                s = j % 2
                bh, bg = (0, 1) if s == 0 else (2, 3)
                bk.bank_wait(bh)
                bk.bank_wait(bg)
                for kc in range(16):
                    bk.mm(banks[bh][:], ap[:, kc * P:(kc + 1) * P], hn_t(kc),
                          start=(kc == 0), stop=(kc == 15))
                for kc in range(16):
                    v = bk.mm(banks[bg][:], ap[:, 2048 + kc * P:2048 + (kc + 1) * P],
                              hn_t(kc), start=(kc == 0), stop=(kc == 15),
                              sem=("tmm" if kc == 15 else None))
                whg_mm_last = v
                ring.free(idx)
                bk.W("scalar", "tmm", v)
                bk.W("scalar", "vop", ch_guard_cv[s])
                cav = bk.act(cvc(s)[:, 1:513], banks[bg][:], AF.Sigmoid,
                             scale=-1.0, bias=dvc(f"nbg_{i}", j), sem="sop")
                bk.bank_set_free(bg, "sop", cav)
                bk.W("scalar", "vop", ch_guard_zs[s])
                shv = bk.act(zs_sh(s), banks[bh][:], AF.Sigmoid,
                             bias=vc(f"bhid_{i}", j), sem="sop")
                bk.W("vector", "sop", shv)
                hbv = bk.vop("tensor_scalar", zs_hb(s), banks[bh][:],
                             vc(f"bhid_{i}", j), None, ALU.add, sem="vop")
                bk.bank_set_free(bh, "vop", hbv)
                bk.vop("tensor_scalar", zs_gm(s), zs_sh(s), 0.5, None, ALU.min)
                bk.vop("scalar_tensor_tensor", zs_gm(s), zs_hb(s), 0.0,
                       zs_gm(s), ALU.max, ALU.add)
                bk.vop("tensor_scalar", zs_sh(s), cvc(s)[:, 1:513], -1.0, 1.0,
                       ALU.mult, ALU.add)
                zv = bk.vop("tensor_tensor", cvv(s)[:, 1:513], zs_sh(s),
                            zs_gm(s), ALU.mult, sem="vop")
                ch_guard_zs[s] = zv
                bk.vop("tensor_copy", cvv(s)[:, 0:1], carry[:, i * 24 + j:i * 24 + j + 1])
                if j == 0:
                    bk.W("vector", "tmm", wo_mm_last)
                bk.vop("tensor_tensor_scan", scano(j), cvc(s), cvv(s), 0.0,
                       ALU.mult, ALU.add)
                scan_val = bk.vop("tensor_copy",
                                  carry[:, i * 24 + j:i * 24 + j + 1],
                                  scano(j)[:, 512:513], sem="vop")
                ch_guard_cv[s] = scan_val

            bk.W("tensor", "vop", scan_val)
            for o in range(16):
                ap, idx = ring.load(dr["wo_r"][:][i, o], 3072)
                ring.consume_wait(idx)
                bknum = 4 + o % 2
                bk.bank_wait(bknum)
                for kc in range(24):
                    v = bk.mm(banks[bknum][:], ap[:, kc * P:(kc + 1) * P],
                              scano(kc)[:, 1:513], start=(kc == 0),
                              stop=(kc == 23), sem=("tmm" if kc == 23 else None))
                wo_mm_last = v
                ring.free(idx)
                bk.W("vector", "tmm", v)
                ev = bk.vop("scalar_tensor_tensor", h_t(o), banks[bknum][:],
                            vc(f"bo_{i}", o), h_t(o), ALU.add, ALU.add,
                            sem="vop")
                bk.bank_set_free(bknum, "vop", ev)

            bk.W("scalar", "vop", bk.cnt["vop"])
            bcv = rms_stats(h_t, 16, zs_guards=tuple(ch_guard_zs))
            ch_guard_sop = bk.cnt["sop"]
            bk.W("vector", "tmm", bcv)
            upd = None
            for dt in range(16):
                bk.vop("scalar_tensor_tensor", zs_gm(dt % 2), h_t(dt),
                       dvc(f"g2p1_{i}", dt), banks[7][:], ALU.mult, ALU.mult)
                upd = bk.vop("tensor_tensor", h_t(dt), h_t(dt), zs_gm(dt % 2),
                             ALU.add, sem=("vop" if dt == 15 else None))
            bk.bank_set_free(7, "vop", upd)
            ch_guard_zs = [upd, upd]

        bk.W("scalar", "vop", bk.cnt["vop"])
        bcv = rms_stats(h_t, 16, zs_guards=tuple(ch_guard_zs))
        ch_guard_sop = bk.cnt["sop"]
        bk.W("vector", "tmm", bcv)
        bk.W("vector", "tmm", whg_mm_last)
        hnv = None
        for dt in range(16):
            hnv = bk.vop("scalar_tensor_tensor", hn_t(dt), h_t(dt),
                         dvc("gfp1", dt), banks[7][:], ALU.mult, ALU.mult,
                         sem=("vop" if dt == 15 else None))
        bk.bank_set_free(7, "vop", hnv)
        bk.W("tensor", "vop", hnv)
        for o in range(8):
            ap, idx = ring.load(dr["wf_r"][:][o], 2048)
            ring.consume_wait(idx)
            bknum = 4 + o % 2
            bk.bank_wait(bknum)
            for kc in range(16):
                v = bk.mm(banks[bknum][:], ap[:, kc * P:(kc + 1) * P], hn_t(kc),
                          start=(kc == 0), stop=(kc == 15),
                          sem=("tmm" if kc == 15 else None))
            wf_mm_last = v
            ring.free(idx)
            bk.W("vector", "tmm", v)
            bk.W("vector", "gd", sstore_val)
            sv = bk.vop("tensor_scalar", s_col_ap, banks[bknum][:], vc("bf", o),
                        None, ALU.add, sem="vop")
            bk.bank_set_free(bknum, "vop", sv)
            bk.W("gpsimd", "vop", sv)
            sstore_val = bk.gdma(sc["s_buf"][:][o, :, c0:c0 + TC], s_col_ap)

    bk.barrier(1)

    def p2t(base, i, w=512):
        return pool[:, base + i * w:base + (i + 1) * w]

    OA, OB, XT, SX, GG, H1, H2, H3 = 0, 4096, 8192, 12288, 16384, 20480, 24576, 26624
    CPL, SQ2, CN1, CN2, CN3 = 27648, 28160, 29184, 37376, 41472
    cn1v = bk.gdma(pool[:, CN1:CN1 + 8192].rearrange("p (a b) -> p a b", a=8),
                   dr["cn1_r"][:].transpose([1, 0, 2]))
    cn2v = bk.gdma(pool[:, CN2:CN2 + 4096].rearrange("p (a b) -> p a b", a=4),
                   dr["cn2_r"][:].transpose([1, 0, 2]))
    cn3v = bk.gdma(pool[:, CN3:CN3 + 1024].rearrange("p (a b) -> p a b", a=2),
                   dr["cn3_r"][:].transpose([1, 0, 2]))
    bk.W("tensor", "gd", cn3v)

    TW_SRC = [4, 5, 6, 7, 0, 1, 2, 3]
    TW_SGN = [1, 1, -1, -1, -1, -1, 1, 1]

    for c in range(NCH):
        c0 = c * TC
        bk.W("gpsimd", "tmm", bk.cnt["tmm"])
        bk.W("gpsimd", "vop", bk.cnt["vop"])
        bk.W("gpsimd", "sop", bk.cnt["sop"])
        ldv0 = bk.gdma(pool[:, OA:OA + 4096].rearrange("p (dt t) -> p dt t", dt=8),
                       sc["xn_buf"][:].transpose([1, 0, 2])[:, :, c0:c0 + TC])
        bk.gdma(pool[:, XT:XT + 4096].rearrange("p (dt t) -> p dt t", dt=8),
                sc["xt_buf"][:].transpose([1, 0, 2])[:, :, c0:c0 + TC])
        ldv = bk.gdma(pool[:, SX:SX + 4096].rearrange("p (dt t) -> p dt t", dt=8),
                      sc["s_buf"][:].transpose([1, 0, 2])[:, :, c0:c0 + TC])
        cur_base, nxt_base = OA, OB
        cur_ready_v, cur_ready_sem = ldv, "gd"
        for cyc in range(3):
            bk.W("tensor", cur_ready_sem, cur_ready_v)
            h1_last = None
            for o in range(8):
                bknum = o % 2
                bk.bank_wait(bknum)
                for kc in range(8):
                    v = bk.mm(banks[bknum][:], pool[:, CN1 + o * 1024 + kc * P:CN1 + o * 1024 + (kc + 1) * P],
                              p2t(cur_base, kc), start=(kc == 0), stop=(kc == 7),
                              sem=("tmm" if kc == 7 else None))
                bk.W("scalar", "tmm", v)
                h1_last = bk.act(p2t(H1, o), banks[bknum][:], AF.Gelu,
                                 bias=vc("cnb1", o), sem="sop")
                bk.bank_set_free(bknum, "sop", h1_last)
            bk.W("tensor", "sop", h1_last)
            h2_last = None
            for o in range(4):
                bk.bank_wait(2)
                for kc in range(8):
                    v = bk.mm(banks[2][:], pool[:, CN2 + o * 1024 + kc * P:CN2 + o * 1024 + (kc + 1) * P],
                              p2t(H1, kc), start=(kc == 0), stop=(kc == 7),
                              sem=("tmm" if kc == 7 else None))
                bk.W("scalar", "tmm", v)
                h2_last = bk.act(p2t(H2, o), banks[2][:], AF.Gelu,
                                 bias=vc("cnb2", o), sem="sop")
                bk.bank_set_free(2, "sop", h2_last)
            bk.W("tensor", "sop", h2_last)
            h3_last = None
            for o in range(2):
                bk.bank_wait(3)
                for kc in range(4):
                    v = bk.mm(banks[3][:], pool[:, CN3 + o * 512 + kc * P:CN3 + o * 512 + (kc + 1) * P],
                              p2t(H2, kc), start=(kc == 0), stop=(kc == 3),
                              sem=("tmm" if kc == 3 else None))
                bk.W("scalar", "tmm", v)
                h3_last = bk.act(p2t(H3, o), banks[3][:], AF.Gelu,
                                 bias=vc("cnb3", o), sem="sop")
                bk.bank_set_free(3, "sop", h3_last)
            bk.W("tensor", "sop", h3_last)
            bk.bank_wait(6)
            for kc in range(2):
                v = bk.mm(banks[6][0:1, :], vc("cnw4", kc), p2t(H3, kc),
                          start=(kc == 0), stop=(kc == 1),
                          sem=("tmm" if kc == 1 else None))
            bk.W("scalar", "tmm", v)
            tfv = bk.act(row(2), banks[6][0:1, :], AF.Sigmoid,
                         bias=vc("cnb4")[0:1, :], sem="sop")
            bk.bank_set_free(6, "sop", tfv)
            bk.W("tensor", "sop", tfv)
            bk.bank_wait(7)
            bcv = bk.mm(banks[7][:], ones_row, row(2), start=True, stop=True,
                        sem="tmm")
            bk.W("vector", "tmm", bcv)
            cplv = bk.vop("tensor_scalar", pool[:, CPL:CPL + 512], banks[7][:],
                          dvc("cgA"), dvc("cgB"), ALU.mult, ALU.add, sem="vop")
            bk.bank_set_free(7, "vop", cplv)
            upd = None
            for j in range(8):
                bk.vop("tensor_tensor", p2t(SQ2, 0), pool[:, CPL:CPL + 512],
                       p2t(cur_base, TW_SRC[j]), ALU.mult)
                upd = bk.vop("tensor_tensor", p2t(nxt_base, j),
                             p2t(cur_base, j), p2t(SQ2, 0),
                             ALU.add if TW_SGN[j] > 0 else ALU.subtract,
                             sem=("vop" if j == 7 else None))
            cur_base, nxt_base = nxt_base, cur_base
            cur_ready_v, cur_ready_sem = upd, "vop"
        bk.W("scalar", "vop", cur_ready_v)
        gv = None
        for dt in range(8):
            gv = bk.act(p2t(GG, dt), p2t(cur_base, dt), AF.Silu,
                        sem=("sop" if dt == 7 else None))
        bk.W("vector", "sop", gv)
        bk.W("vector", "gd", ldv)
        x1v = None
        for dt in range(8):
            bk.vop("tensor_tensor", p2t(GG, dt), p2t(GG, dt), p2t(SX, dt),
                   ALU.mult)
            x1v = bk.vop("tensor_tensor", p2t(XT, dt), p2t(XT, dt), p2t(GG, dt),
                         ALU.add, sem=("vop" if dt == 7 else None))
        bk.W("gpsimd", "vop", x1v)
        bk.gdma(sc["x1_buf"][:].transpose([1, 0, 2])[:, :, c0:c0 + TC],
                pool[:, XT:XT + 4096].rearrange("p (dt t) -> p dt t", dt=8))
        bk.W("tensor", "vop", x1v)
        bk.W("scalar", "vop", x1v)
        sq_guard = [None, None]
        for dt in range(8):
            s = dt % 2
            if sq_guard[s] is not None:
                bk.W("scalar", "tmm", sq_guard[s])
            sv = bk.act(p2t(SQ2, s), p2t(XT, dt), AF.Square, sem="sop")
            bk.W("tensor", "sop", sv)
            if dt == 0:
                bk.bank_wait(0)
                bk.bank_wait(1)
            bk.mm(banks[0][0:1, :], ones_col, p2t(XT, dt),
                  start=(dt == 0), stop=(dt == 7))
            mv = bk.mm(banks[1][0:1, :], ones_col, p2t(SQ2, s),
                       start=(dt == 0), stop=(dt == 7), sem="tmm")
            sq_guard[s] = mv
        bk.W("vector", "tmm", mv)
        bk.vop("tensor_scalar", row(0), banks[0][0:1, :], 1.0 / D, None, ALU.mult)
        bk.vop("tensor_scalar", row(3), banks[1][0:1, :], 1.0 / D, None, ALU.mult)
        bk.vop("tensor_tensor", row(1), row(0), row(0), ALU.mult)
        vv = bk.vop("tensor_tensor", row(3), row(3), row(1), ALU.subtract,
                    sem="vop")
        bk.bank_set_free(0, "vop", vv)
        bk.bank_set_free(1, "vop", vv)
        bk.W("scalar", "vop", vv)
        sdv = bk.act(row(3), row(3), AF.Sqrt, bias=vc("c_eps")[0:1, :], sem="sop")
        bk.W("vector", "sop", sdv)
        iv = bk.vop("reciprocal", row(1), row(3), sem="vop")
        bk.W("tensor", "vop", iv)
        bk.bank_wait(2)
        bk.bank_wait(3)
        bk.mm(banks[2][:], ones_row, row(0), start=True, stop=True)
        bcv = bk.mm(banks[3][:], ones_row, row(1), start=True, stop=True,
                    sem="tmm")
        bk.W("vector", "tmm", bcv)
        xnv2 = None
        for dt in range(8):
            bk.vop("tensor_tensor", p2t(SX, dt), p2t(XT, dt), banks[2][:],
                   ALU.subtract)
            xnv2 = bk.vop("tensor_tensor", p2t(SX, dt), p2t(SX, dt),
                          banks[3][:], ALU.mult,
                          sem=("vop" if dt == 7 else None))
        bk.bank_set_free(2, "vop", xnv2)
        bk.bank_set_free(3, "vop", xnv2)
        bk.W("gpsimd", "vop", xnv2)
        bk.gdma(sc["xn2_buf"][:].transpose([1, 0, 2])[:, :, c0:c0 + TC],
                pool[:, SX:SX + 4096].rearrange("p (dt t) -> p dt t", dt=8))

    bk.barrier(2)

    WQ, WK, WV, QEV, VEV = 4096, 12288, 20480, 28672, 29696
    wq_v = bk.gdma(pool[:, WQ:WQ + 8192].rearrange("p (a b) -> p a b", a=8),
                   dr["wqt_r"][:].transpose([1, 0, 2]))
    wk_v = bk.gdma(pool[:, WK:WK + 8192].rearrange("p (a b) -> p a b", a=8),
                   dr["wkt_r"][:].transpose([1, 0, 2]))
    wv_v = bk.gdma(pool[:, WV:WV + 8192].rearrange("p (a b) -> p a b", a=8),
                   dr["wvt_r"][:].transpose([1, 0, 2]))
    bk.W("tensor", "gd", wv_v)
    for c in range(NCH):
        c0 = c * TC
        bk.W("gpsimd", "tmm", bk.cnt["tmm"])
        bk.W("gpsimd", "vop", bk.cnt["vop"])
        ld3 = bk.gdma(pool[:, 0:4096].rearrange("p (dt t) -> p dt t", dt=8),
                      sc["xn2_buf"][:].transpose([1, 0, 2])[:, :, c0:c0 + TC])
        bk.W("tensor", "gd", ld3)
        for which, wbase, bname, buf in (("q", WQ, "bq", "qt_buf"),
                                         ("k", WK, "bk", "kt_buf")):
            for h in range(NH):
                bknum = h % 4
                bk.bank_wait(bknum)
                for kc in range(8):
                    v = bk.mm(banks[bknum][0:64, :],
                              pool[:, wbase + kc * 1024 + h * HD:wbase + kc * 1024 + (h + 1) * HD],
                              pool[:, kc * 512:(kc + 1) * 512],
                              start=(kc == 0), stop=(kc == 7),
                              sem=("tmm" if kc == 7 else None))
                bk.W("vector", "tmm", v)
                ev = pool[0:64, QEV + (h % 2) * 512:QEV + (h % 2) * 512 + 512]
                bk.W("vector", "gd", bk.cnt["gd"])
                evv = bk.vop("tensor_scalar", ev, banks[bknum][0:64, :],
                             vc(bname, h)[0:64, :], None, ALU.add, sem="vop")
                bk.bank_set_free(bknum, "vop", evv)
                bk.W("gpsimd", "vop", evv)
                bk.gdma(sc[buf][:][h, :, c0:c0 + TC], ev)
        for tsub in range(4):
            tok0 = tsub * P
            for half in range(2):
                bknum = 4 + (tsub * 2 + half) % 2
                bk.bank_wait(bknum)
                for kc in range(8):
                    bk.mm(banks[bknum][:],
                          pool[:, kc * 512 + tok0:kc * 512 + tok0 + P],
                          pool[:, WV + kc * 1024 + half * 512:WV + kc * 1024 + (half + 1) * 512],
                          start=(kc == 0), stop=False)
                v = bk.mm(banks[bknum][:], ones_row,
                          rowv[0:1, P + half * 512:P + (half + 1) * 512],
                          start=False, stop=True, sem="tmm")
                bk.W("vector", "tmm", v)
                ev = pool[:, VEV + half * 512:VEV + (half + 1) * 512]
                bk.W("vector", "gd", bk.cnt["gd"])
                evv = bk.vop("tensor_copy", ev, banks[bknum][:], sem="vop")
                bk.bank_set_free(bknum, "vop", evv)
                bk.W("gpsimd", "vop", evv)
                bk.gdma(sc["v_buf"][:][c * 4 + tsub, :, half * 512:(half + 1) * 512], ev)

    bk.barrier(3)

    QB, KB, VXB, PTB, LRB, OST = 0, 4096, 8192, 10272, 11296, 11808
    vext = pool[:, VXB:VXB + 2080]
    vx = bk.vop("memset", vext, 1.0, sem="vop")
    bk.W("gpsimd", "vop", vx)
    prev_pv = [0, 0]
    for h in range(NH):
        bk.W("gpsimd", "tmm", bk.cnt["tmm"])
        bk.gdma(pool[0:64, QB:QB + T], sc["qt_buf"][:][h])
        bk.gdma(pool[0:64, KB:KB + T], sc["kt_buf"][:][h])
        vldv = bk.gdma(
            vext.rearrange("p (tt c) -> p tt c", c=65)[:, :, 0:64],
            sc["v_buf"][:].transpose([1, 0, 2])[:, :, h * HD:(h + 1) * HD])
        bk.W("tensor", "gd", vldv)
        for sg in range(8):
            qs = pool[0:64, QB + sg * 512:QB + (sg + 1) * 512]
            bk.bank_wait(4)
            for tb in range(32):
                bknum = tb % 4
                bk.bank_wait(bknum)
                sv = bk.mm(banks[bknum][:],
                           pool[0:64, KB + tb * P:KB + (tb + 1) * P], qs,
                           start=True, stop=True, sem="tmm")
                bk.W("scalar", "tmm", sv)
                bk.W("scalar", "tmm", prev_pv[tb % 2])
                pt = pool[:, PTB + (tb % 2) * 512:PTB + (tb % 2) * 512 + 512]
                ev = bk.act(pt, banks[bknum][:], AF.Exp, scale=0.125, sem="sop")
                bk.bank_set_free(bknum, "sop", ev)
                bk.W("tensor", "sop", ev)
                pv = bk.mm(banks[4][0:65, :], vext[:, tb * 65:(tb + 1) * 65],
                           pt, start=(tb == 0), stop=(tb == 31), sem="tmm")
                prev_pv[tb % 2] = pv
            bk.W("scalar", "tmm", pv)
            bk.W("scalar", "gd", bk.cnt["gd"])
            lcp = bk.sop("copy", pool[64:65, LRB:LRB + 512],
                         banks[4][64:65, :], sem="sop")
            bk.W("vector", "tmm", pv)
            bk.W("vector", "gd", bk.cnt["gd"])
            ocp = bk.vop("tensor_copy", pool[64:128, OST:OST + 512],
                         banks[4][0:64, :], sem="vop")
            bk.bank_set_free(4, "vop", ocp)
            bk.W("gpsimd", "sop", lcp)
            bk.W("gpsimd", "vop", ocp)
            bk.gdma(sc["l_buf"][:][2 * sg:2 * sg + 2, h, :],
                    pool[64:65, LRB:LRB + 512])
            bk.gdma(sc["o_buf"][:][h // 2, (h % 2) * HD:(h % 2 + 1) * HD,
                                   sg * 512:(sg + 1) * 512],
                    pool[64:128, OST:OST + 512])

    bk.barrier(4)

    XN5, X15, OT5, S25, GP5, UP5 = 0, 2048, 4096, 6144, 8192, 16384
    XQ5, XQ25, AM5, SCB5, IVB5, IVB25 = 16896, 17920, 22016, 22272, 22528, 22784
    RND5, YSB5, LR5 = 23040, 24576, 26624
    lr_ap = pool[0:1, LR5:LR5 + 4096]
    ring5 = Ring(bk, pool[:, 30736:43000], 12264, gate=("ph", bk.cnt["ph"]))

    def t5(base, i, w=256):
        return pool[:, base + i * w:base + (i + 1) * w]

    def xq_t(dt):
        return pool[:, XQ5 + dt * 128:XQ5 + (dt + 1) * 128].bitcast(BF16)

    def xq2_t(dt):
        return pool[:, XQ25 + dt * 128:XQ25 + (dt + 1) * 128].bitcast(BF16)

    def act_quant(src_f, ntiles, xq_f, invb_off):
        acc = pool[:, AM5:AM5 + 256]
        tmp = pool[:, RND5:RND5 + 256]
        mcl = pool[:, AM5 + 256:AM5 + 258]
        for dt in range(ntiles):
            if dt == 0:
                bk.vop("scalar_tensor_tensor", acc, src_f(0), -1.0, src_f(0),
                       ALU.mult, ALU.max)
            else:
                bk.vop("scalar_tensor_tensor", tmp, src_f(dt), -1.0, src_f(dt),
                       ALU.mult, ALU.max)
                bk.vop("tensor_tensor", acc, acc, tmp, ALU.max)
        av = bk.E("vector", lambda e: e.drain(), sem="vop")
        bk.W("tensor", "vop", av)
        bk.bank_wait(6)
        for b in range(2):
            tv = bk.tr(banks[6][:, b * P:(b + 1) * P], acc[:, b * P:(b + 1) * P],
                       ident[:], sem=("tmm" if b == 1 else None))
        bk.W("vector", "tmm", tv)
        bk.vop("reduce_max", mcl[:, 0:1], banks[6][:, 0:P], AX.X)
        mv = bk.vop("reduce_max", mcl[:, 1:2], banks[6][:, P:2 * P], AX.X,
                    sem="vop")
        bk.bank_set_free(6, "vop", mv)
        bk.vop("tensor_scalar", mcl[:, 0:2], mcl[:, 0:2], 1e-5, None, ALU.max)
        bk.vop("reciprocal", tmp[:, 0:2], mcl[:, 0:2])
        bk.vop("tensor_scalar", tmp[:, 0:2], tmp[:, 0:2], 127.0, None, ALU.mult)
        rv = bk.vop("tensor_scalar", mcl[:, 0:2], mcl[:, 0:2], 1.0 / 127.0,
                    None, ALU.mult, sem="vop")
        bk.W("tensor", "vop", rv)
        bk.bank_wait(6)
        for b in range(2):
            bk.tr(banks[6][0:1, b * P:(b + 1) * P], tmp[:, b:b + 1], ident[:])
        for b in range(2):
            tv = bk.tr(banks[6][0:1, 256 + b * P:256 + (b + 1) * P],
                       mcl[:, b:b + 1], ident[:],
                       sem=("tmm" if b == 1 else None))
        bk.W("scalar", "tmm", tv)
        rcv = bk.sop("copy", row(2)[0:1, 0:512], banks[6][0:1, 0:512],
                     sem="sop")
        bk.bank_set_free(6, "sop", rcv)
        bk.W("tensor", "sop", rcv)
        bk.bank_wait(7)
        bk.mm(banks[7][:, 0:256], ones_row, row(2)[0:1, 0:256],
              start=True, stop=True)
        bv = bk.mm(banks[7][:, 256:512], ones_row, row(2)[0:1, 256:512],
                   start=True, stop=True, sem="tmm")
        bk.W("vector", "tmm", bv)
        bk.vop("tensor_copy", pool[:, SCB5:SCB5 + 256], banks[7][:, 0:256])
        cbv = bk.vop("tensor_copy", pool[:, invb_off:invb_off + 256],
                     banks[7][:, 256:512], sem="vop")
        bk.bank_set_free(7, "vop", cbv)
        qq = pool[:, RND5 + 256:RND5 + 512]
        r0 = pool[:, RND5 + 512:RND5 + 768]
        di = pool[:, RND5 + 768:RND5 + 1024]
        gi = pool[:, RND5 + 1024:RND5 + 1280]
        iq = pool[:, RND5 + 1280:RND5 + 1536].bitcast(I32)
        lastv = None
        for dt in range(ntiles):
            bk.vop("tensor_tensor", qq, src_f(dt), pool[:, SCB5:SCB5 + 256],
                   ALU.mult)
            bk.vop("tensor_copy", iq, qq)
            bk.vop("tensor_copy", r0, iq)
            bk.vop("tensor_tensor", di, qq, r0, ALU.subtract)
            bk.vop("tensor_scalar", gi, di, 0.5, None, ALU.is_gt)
            bk.vop("tensor_tensor", r0, r0, gi, ALU.add)
            bk.vop("tensor_scalar", gi, di, -0.5, None, ALU.is_lt)
            bk.vop("tensor_tensor", r0, r0, gi, ALU.subtract)
            lastv = bk.vop("tensor_copy", xq_f(dt), r0,
                           sem=("vop" if dt == ntiles - 1 else None))
        return lastv

    for cc in range(NCH5):
        c0 = cc * TC5
        bk.W("gpsimd", "tmm", bk.cnt["tmm"])
        bk.W("gpsimd", "vop", bk.cnt["vop"])
        bk.W("gpsimd", "sop", bk.cnt["sop"])
        bk.W("vector", "tmm", bk.cnt["tmm"])
        bk.gdma(pool[:, XN5:XN5 + 2048].rearrange("p (dt t) -> p dt t", dt=8),
                sc["xn2_buf"][:].transpose([1, 0, 2])[:, :, c0:c0 + TC5])
        bk.gdma(pool[:, X15:X15 + 2048].rearrange("p (dt t) -> p dt t", dt=8),
                sc["x1_buf"][:].transpose([1, 0, 2])[:, :, c0:c0 + TC5])
        bk.gdma(pool[:, OT5:OT5 + 2048].rearrange("p (dt t) -> p dt t", dt=8),
                sc["o_buf"][:].transpose([1, 0, 2])[:, :, c0:c0 + TC5])
        ld5 = bk.gdma(lr_ap, sc["l_buf"][:][cc])
        bk.W("vector", "gd", ld5)
        lrv = bk.vop("reciprocal", lr_ap, lr_ap, sem="vop")
        bk.W("tensor", "vop", lrv)
        onv = None
        for dt in range(8):
            bknum = 6 + dt % 2
            bk.bank_wait(bknum)
            bk.mm(banks[bknum][0:64, 0:256], ones_row[0:1, 0:64],
                  lr_ap[0:1, (2 * dt) * 256:(2 * dt + 1) * 256],
                  start=True, stop=True)
            bv = bk.mm(banks[bknum][64:128, 0:256], ones_row[0:1, 0:64],
                       lr_ap[0:1, (2 * dt + 1) * 256:(2 * dt + 2) * 256],
                       start=True, stop=True, sem="tmm")
            bk.W("vector", "tmm", bv)
            onv = bk.vop("tensor_tensor", t5(OT5, dt), t5(OT5, dt),
                         banks[bknum][:, 0:256], ALU.mult, sem="vop")
            bk.bank_set_free(bknum, "vop", onv)
        xqv = act_quant(lambda dt: t5(XN5, dt), 8, xq_t, IVB5)
        bk.W("tensor", "vop", xqv)
        gev = None
        for o in range(32):
            ap, idx = ring5.load(sc["wig"][:][o], 512, bf16=True)
            ring5.consume_wait(idx)
            bknum = o % 4
            bk.bank_wait(bknum)
            apb = ap.bitcast(BF16)
            for kc in range(8):
                v = bk.mm(banks[bknum][:, 0:256], apb[:, kc * P:(kc + 1) * P],
                          xq_t(kc), start=(kc == 0), stop=(kc == 7),
                          sem=("tmm" if kc == 7 else None))
            ring5.free(idx)
            bk.W("vector", "tmm", v)
            gev = bk.vop("scalar_tensor_tensor", t5(GP5, o), banks[bknum][:, 0:256],
                         dvc("qrs_g"), pool[:, IVB5:IVB5 + 256],
                         ALU.mult, ALU.mult, sem="vop")
            bk.bank_set_free(bknum, "vop", gev)
        bk.W("scalar", "vop", gev)
        sgv = None
        for o in range(32):
            sgv = bk.act(t5(GP5, o), t5(GP5, o), AF.Silu,
                         sem=("sop" if o == 31 else None))
        bk.W("vector", "sop", sgv)
        pv = None
        for o in range(32):
            ap, idx = ring5.load(sc["wiu"][:][o], 512, bf16=True)
            ring5.consume_wait(idx)
            bknum = o % 4
            bk.bank_wait(bknum)
            apb = ap.bitcast(BF16)
            for kc in range(8):
                v = bk.mm(banks[bknum][:, 0:256], apb[:, kc * P:(kc + 1) * P],
                          xq_t(kc), start=(kc == 0), stop=(kc == 7),
                          sem=("tmm" if kc == 7 else None))
            ring5.free(idx)
            bk.W("vector", "tmm", v)
            bk.vop("scalar_tensor_tensor", pool[:, UP5:UP5 + 256],
                   banks[bknum][:, 0:256], dvc("qrs_u"),
                   pool[:, IVB5:IVB5 + 256], ALU.mult, ALU.mult)
            pv = bk.vop("tensor_tensor", t5(GP5, o), t5(GP5, o),
                        pool[:, UP5:UP5 + 256], ALU.mult, sem="vop")
            bk.bank_set_free(bknum, "vop", pv)
        xq2v = act_quant(lambda dt: t5(GP5, dt), 32, xq2_t, IVB25)
        bk.W("tensor", "vop", xq2v)
        g2v = None
        for o in range(8):
            ap, idx = ring5.load(sc["wid"][:][o], 2048, bf16=True)
            ring5.consume_wait(idx)
            bknum = 4 + o % 2
            bk.bank_wait(bknum)
            apb = ap.bitcast(BF16)
            for kc in range(32):
                v = bk.mm(banks[bknum][:, 0:256], apb[:, kc * P:(kc + 1) * P],
                          xq2_t(kc), start=(kc == 0), stop=(kc == 31),
                          sem=("tmm" if kc == 31 else None))
            ring5.free(idx)
            bk.W("vector", "tmm", v)
            g2v = bk.vop("scalar_tensor_tensor", t5(XN5, o), banks[bknum][:, 0:256],
                         dvc("qrs_d"), pool[:, IVB25:IVB25 + 256],
                         ALU.mult, ALU.mult, sem="vop")
            bk.bank_set_free(bknum, "vop", g2v)
        bk.W("scalar", "vop", g2v)
        s2v = None
        for o in range(8):
            s2v = bk.act(t5(XN5, o), t5(XN5, o), AF.Silu,
                         sem=("sop" if o == 7 else None))
        for o in range(8):
            ap, idx = ring5.load(dr["wot_r"][:][o], 1024)
            ring5.consume_wait(idx)
            bknum = 4 + o % 2
            bk.bank_wait(bknum)
            for kc in range(8):
                v = bk.mm(banks[bknum][:, 0:256],
                          ap[:, kc * P:(kc + 1) * P],
                          t5(OT5, kc), start=(kc == 0), stop=(kc == 7),
                          sem=("tmm" if kc == 7 else None))
            ring5.free(idx)
            bk.W("vector", "tmm", v)
            ev = bk.vop("tensor_scalar", t5(S25, o), banks[bknum][:, 0:256],
                        vc("bo_at", o), None, ALU.add, sem="vop")
            bk.bank_set_free(bknum, "vop", ev)
        bk.W("vector", "sop", s2v)
        yv = None
        for dt in range(8):
            bk.vop("tensor_tensor", t5(S25, dt), t5(S25, dt), t5(XN5, dt),
                   ALU.mult)
            yv = bk.vop("tensor_tensor", t5(X15, dt), t5(X15, dt), t5(S25, dt),
                        ALU.add, sem=("vop" if dt == 7 else None))
        bk.W("tensor", "vop", yv)
        for ts in range(2):
            tok0 = ts * P
            for half in range(2):
                bknum = half
                bk.bank_wait(bknum)
                for k in range(4):
                    dt = half * 4 + k
                    bk.tr(banks[bknum][:, k * P:(k + 1) * P],
                          pool[:, X15 + dt * 256 + tok0:X15 + dt * 256 + tok0 + P],
                          ident[:], sem=("tmm" if k == 3 else None))
                tv = bk.cnt["tmm"]
                bk.W("scalar", "tmm", tv)
                bk.W("scalar", "gd", bk.cnt["gd"])
                cpv = bk.sop("copy", pool[:, YSB5 + half * 512:YSB5 + (half + 1) * 512],
                             banks[bknum][:], sem="sop")
                bk.bank_set_free(bknum, "sop", cpv)
            bk.W("gpsimd", "sop", bk.cnt["sop"])
            bk.gdma(y_out[:][c0 + tok0:c0 + tok0 + P, :],
                    pool[:, YSB5:YSB5 + 1024])

    bk.barrier(5)

    for name in bk.dbg:
        if name == "gfdbg":
            bk.gdma(dbg_outs[name][:], gfdbg_sb[:])
        else:
            bk.gdma(dbg_outs[name][:], sc[name][:])
    bk.W("gpsimd", "gd", bk.cnt["gd"])
    return bk, nc


def kernel(**inputs):
    shared = stage_shared(inputs)
    x = np.ascontiguousarray(np.asarray(inputs["L0_attn_x"], np.float32))
    bk, nc = build_program(dbg=())
    nc = bk.finalize()
    in_maps = []
    for b in range(8):
        m = dict(shared)
        m["x"] = np.ascontiguousarray(x[b])
        in_maps.append(m)
    res = run_bass_kernel_spmd(nc, in_maps, core_ids=list(range(8)))
    return np.stack([r["y"] for r in res.results])


# version 03
# speedup vs baseline: 1.0787x; 1.0439x over previous
import math
import numpy as np
from contextlib import ExitStack

import concourse.bass as bass
import concourse.mybir as mybir
from concourse.bass_utils import run_bass_kernel_spmd

F32 = mybir.dt.float32
BF16 = mybir.dt.bfloat16
I32 = mybir.dt.int32
U8 = mybir.dt.uint8
ALU = mybir.AluOpType
AF = mybir.ActivationFunctionType
AX = mybir.AxisListType

P = 128
T = 4096
D = 1024
HID = 2048
DI = 3072
NL = 4
TC = 512
NCH = T // TC
TC5 = 256
NCH5 = T // TC5
NH = 16
HD = 64
EPS = 1e-5
SQD = float(np.float32(math.sqrt(HID)))

ENGINES = ("sync", "gpsimd", "tensor", "vector", "scalar")
NWL = 6
SEMS = ("gd", "wfr", "tmm", "vop", "sop", "ph") + tuple(f"wl{j}" for j in range(NWL))


def vec_layout():
    lay, off = {}, 0

    def add(name, n):
        nonlocal off
        lay[name] = (off, n)
        off += n

    add("ones", 1)
    add("bp", 16)
    for i in range(NL):
        add(f"g1_{i}", 16)
        add(f"bhid_{i}", 24)
        add(f"bgate_{i}", 24)
        add(f"bo_{i}", 16)
        add(f"g2_{i}", 16)
    add("gfv", 16)
    add("bf", 8)
    add("cnb1", 8)
    add("cnb2", 4)
    add("cnb3", 2)
    add("cnb4", 1)
    add("gcb1", 4)
    add("gcb2", 2)
    add("gcb3", 1)
    add("ar", 1)
    add("cnw4", 2)
    add("gcw3", 2)
    add("bq", 16)
    add("bk", 16)
    add("bv", 16)
    add("bo_at", 8)
    add("c_eps", 1)
    return lay, off


def dvec_layout():
    lay, off = {}, 0

    def add(name, n):
        nonlocal off
        lay[name] = (off, n)
        off += n

    for i in range(NL):
        add(f"g1p1_{i}", 16)
        add(f"g2p1_{i}", 16)
        add(f"nbg_{i}", 24)
    add("gfp1", 16)
    add("cgA", 1)
    add("cgB", 1)
    add("qsc_g", 1)
    add("qsc_u", 1)
    add("qsc_d", 1)
    add("qrs_g", 1)
    add("qrs_u", 1)
    add("qrs_d", 1)
    return lay, off


def col_pack(v, ncols):
    return np.ascontiguousarray(v.reshape(ncols, P).T)


def stage_shared(I):
    f32 = np.float32
    out = {}

    def blk(w, njb):
        K, M = w.shape
        nk = K // P
        r = w.reshape(nk, P, njb, M // njb)
        return np.ascontiguousarray(r.transpose(2, 1, 0, 3).reshape(njb, P, nk * (M // njb)))

    out["wp_r"] = blk(np.asarray(I["L0_attn_Wp"], f32), 16)
    whg = np.asarray(I["L0_attn_Whg"], f32)
    whg_r = np.empty((NL, 24, P, 4096), f32)
    for i in range(NL):
        hidb = blk(whg[i][:, :DI], 24)
        gateb = blk(whg[i][:, DI:], 24)
        whg_r[i][:, :, :2048] = hidb
        whg_r[i][:, :, 2048:] = gateb
    out["whg_r"] = whg_r
    wo = np.asarray(I["L0_attn_Wo"], f32)
    out["wo_r"] = np.stack([blk(wo[i], 16) for i in range(NL)])
    out["wf_r"] = blk(np.asarray(I["L0_attn_Wf"], f32), 8)
    out["cn1_r"] = blk(np.asarray(I["L0_ffn_cn_w1"], f32), 8)
    out["cn2_r"] = blk(np.asarray(I["L0_ffn_cn_w2"], f32), 4)
    out["cn3_r"] = blk(np.asarray(I["L0_ffn_cn_w3"], f32), 2)
    out["gc1_r"] = blk(np.asarray(I["L0_ffn_gc_w1"], f32), 4)
    out["gc2_r"] = blk(np.asarray(I["L0_ffn_gc_w2"], f32), 2)

    def headT(w):
        return np.ascontiguousarray(
            w.transpose(2, 0, 1).reshape(8, P, NH * HD))

    out["wqt_r"] = headT(np.asarray(I["L1_attn_Wq"], f32))
    out["wkt_r"] = headT(np.asarray(I["L1_attn_Wk"], f32))
    out["wvt_r"] = headT(np.asarray(I["L1_attn_Wv"], f32))
    out["wot_r"] = blk(np.ascontiguousarray(np.asarray(I["L1_attn_Wo"], f32).T), 8)
    out["wg_r"] = blk(np.ascontiguousarray(np.asarray(I["L1_ffn_w_gate"], f32).T), 32)
    out["wu_r"] = blk(np.ascontiguousarray(np.asarray(I["L1_ffn_w_up"], f32).T), 32)
    out["wd_r"] = blk(np.ascontiguousarray(np.asarray(I["L1_ffn_w_down"], f32).T), 8)
    out["ident"] = np.eye(P, dtype=f32)

    lay, nv = vec_layout()
    vecs = np.zeros((P, nv), f32)

    def put(name, arr):
        o, n = lay[name]
        vecs[:, o:o + n] = arr

    put("ones", np.ones((P, 1), f32))
    put("bp", col_pack(np.asarray(I["L0_attn_bp"], f32), 16))
    for i in range(NL):
        put(f"g1_{i}", col_pack(np.asarray(I["L0_attn_g1"], f32)[i], 16))
        bhg = np.asarray(I["L0_attn_bhg"], f32)[i]
        put(f"bhid_{i}", col_pack(bhg[:DI], 24))
        put(f"bgate_{i}", col_pack(bhg[DI:], 24))
        put(f"bo_{i}", col_pack(np.asarray(I["L0_attn_bo"], f32)[i], 16))
        put(f"g2_{i}", col_pack(np.asarray(I["L0_attn_g2"], f32)[i], 16))
    put("gfv", col_pack(np.asarray(I["L0_attn_gf"], f32), 16))
    put("bf", col_pack(np.asarray(I["L0_attn_bf"], f32), 8))
    put("cnb1", col_pack(np.asarray(I["L0_ffn_cn_b1"], f32), 8))
    put("cnb2", col_pack(np.asarray(I["L0_ffn_cn_b2"], f32), 4))
    put("cnb3", col_pack(np.asarray(I["L0_ffn_cn_b3"], f32), 2))
    b4 = np.zeros((P, 1), f32); b4[0, 0] = np.asarray(I["L0_ffn_cn_b4"], f32)[0]
    put("cnb4", b4)
    put("gcb1", col_pack(np.asarray(I["L0_ffn_gc_b1"], f32), 4))
    put("gcb2", col_pack(np.asarray(I["L0_ffn_gc_b2"], f32), 2))
    g3 = np.zeros((P, 1), f32); g3[0, 0] = np.asarray(I["L0_ffn_gc_b3"], f32)[0]
    put("gcb3", g3)
    arr = np.zeros((P, 1), f32); arr[:, 0] = np.asarray(I["L0_ffn_adaptive_range"], f32)[0]
    put("ar", arr)
    put("cnw4", col_pack(np.asarray(I["L0_ffn_cn_w4"], f32)[:, 0], 2))
    put("gcw3", col_pack(np.asarray(I["L0_ffn_gc_w3"], f32)[:, 0], 2))
    for nm, key in (("bq", "L1_attn_bq"), ("bk", "L1_attn_bk"), ("bv", "L1_attn_bv")):
        a = np.zeros((P, NH), f32)
        a[0:HD, :] = np.asarray(I[key], f32).T
        put(nm, a)
    put("bo_at", col_pack(np.asarray(I["L1_attn_bo"], f32), 8))
    put("c_eps", np.full((P, 1), 1e-5, f32))
    out["vecs"] = vecs

    rowvecs = np.zeros((1, 1152), f32)
    rowvecs[0, 0:P] = 1.0
    rowvecs[0, P:P + D] = np.asarray(I["L1_attn_bv"], f32).reshape(-1)
    out["rowvecs"] = rowvecs
    return out


class BK:
    def __init__(self, dbg=()):
        self.nc = bass.Bass("TRN2", target_bir_lowering=False)
        self.q = {e: [] for e in ENGINES}
        self.cnt = {s: 0 for s in SEMS}
        self.maxwait = {s: 0 for s in SEMS}
        self.sem = {}
        self.dbg = set(dbg)
        self.es = ExitStack()
        self.banks = None
        self.bank_free = {}

    def E(self, eng, f, sem=None, k=1):
        if sem is None:
            self.q[eng].append(f)
            return None
        self.cnt[sem] += k
        v = self.cnt[sem]
        name = sem

        def g(e, f=f, name=name, k=k):
            f(e).then_inc(self.sem[name], k)
        self.q[eng].append(g)
        return v

    def W(self, eng, sem, v):
        if v is None or v <= 0:
            return
        self.maxwait[sem] = max(self.maxwait[sem], v)

        def g(e, name=sem, v=v):
            e.wait_ge(self.sem[name], v)
        self.q[eng].append(g)

    def dma(self, eng, out_ap, in_ap, sem):
        return self.E(eng, lambda e, o=out_ap, i=in_ap: e.dma_start(out=o, in_=i),
                      sem=sem, k=16)

    def gdma(self, out_ap, in_ap):
        v = self.dma("gpsimd", out_ap, in_ap, "gd")
        self.W("gpsimd", "gd", v)
        return v

    def mm(self, out, lhsT, rhs, start, stop, sem=None):
        return self.E("tensor",
                      lambda e, o=out, a=lhsT, b=rhs, s=start, p=stop:
                      e.matmul(o, a, b, start=s, stop=p), sem=sem)

    def tr(self, out, in_, ident, sem=None):
        return self.E("tensor",
                      lambda e, o=out, i=in_, d=ident: e.transpose(o, i, d),
                      sem=sem)

    def act(self, out, in_, func, bias=0.0, scale=1.0, accum=None, sem=None):
        def f(e, o=out, i=in_, fn=func, b=bias, s=scale, a=accum):
            return e.activation(o, i, fn, bias=b, scale=s, accum_out=a)
        v = self.E("scalar", f, sem=sem)
        self.q["scalar"].append(lambda e: e.drain())
        return v

    def vop(self, meth, *args, sem=None, **kw):
        def f(e, m=meth, aa=args, kk=kw):
            return getattr(e, m)(*aa, **kk)
        v = self.E("vector", f, sem=sem)
        self.q["vector"].append(lambda e: e.drain())
        return v

    def sop(self, meth, *args, sem=None, **kw):
        def f(e, m=meth, aa=args, kk=kw):
            return getattr(e, m)(*aa, **kk)
        v = self.E("scalar", f, sem=sem)
        self.q["scalar"].append(lambda e: e.drain())
        return v

    def bank_wait(self, b):
        fw = self.bank_free.get(b)
        if fw is not None:
            self.W("tensor", fw[0], fw[1])
            self.bank_free[b] = None

    def bank_set_free(self, b, sem, val):
        self.bank_free[b] = (sem, val)

    def barrier(self, idx):
        for j in range(NWL):
            self.W("sync", f"wl{j}", self.cnt[f"wl{j}"])
        self.W("gpsimd", "gd", self.cnt["gd"])
        for e in ENGINES:
            self.E(e, lambda en: en.drain(), sem="ph")
        target = self.cnt["ph"]
        for e in ENGINES:
            self.W(e, "ph", target)
        self.bank_free = {}

    def finalize(self):
        for s in SEMS:
            assert self.maxwait[s] <= self.cnt[s], \
                f"sem {s}: wait {self.maxwait[s]} > incs {self.cnt[s]}"
        es = self.es
        for s in SEMS:
            self.sem[s] = es.enter_context(self.nc.semaphore(s))
        block = es.enter_context(self.nc.Block())
        with es:
            for eng in ENGINES:
                ops = self.q[eng]

                def body(e, ops=ops):
                    for f in ops:
                        f(e)
                getattr(block, eng)(body)
        return self.nc


class Ring:

    def __init__(self, bk, pool_ap, ncols, gate=None):
        self.bk = bk
        self.pool = pool_ap
        self.n = ncols
        self.pos = 0
        self.owner = np.full(ncols, -1, dtype=np.int64)
        self.blocks = []
        self.base = bk.cnt["wfr"]
        self.gate = gate
        self.gated = gate is None

    def load(self, dram_ap, ncols, bf16=False):
        bk = self.bk
        if not self.gated:
            bk.W("sync", self.gate[0], self.gate[1])
            self.gated = True
        if self.pos + ncols > self.n:
            self.pos = 0
        sl = slice(self.pos, self.pos + ncols)
        idx = len(self.blocks)
        prev = int(self.owner[sl].max())
        need = max(prev + 1, idx - NWL + 1)
        if need > 0:
            bk.W("sync", "wfr", self.base + need)
        self.owner[sl] = idx
        ap = self.pool[:, sl]
        sem = f"wl{idx % NWL}"
        v = bk.dma("sync", ap.bitcast(BF16) if bf16 else ap, dram_ap, sem)
        self.blocks.append((sem, v))
        self.pos += ncols
        return ap, idx

    def consume_wait(self, idx):
        sem, v = self.blocks[idx]
        self.bk.W("tensor", sem, v)

    def free(self, idx):
        v = self.bk.E("tensor", lambda e: e.drain(), sem="wfr")
        assert v == self.base + idx + 1, (v, self.base, idx)


def build_program(dbg=()):
    bk = BK(dbg)
    nc = bk.nc
    es = bk.es

    x_in = nc.declare_dram_parameter("x", [T, D], F32, isOutput=False)
    y_out = nc.declare_dram_parameter("y", [T, D], F32, isOutput=True)
    dr = {}
    for name, shape in (
        ("wp_r", (16, P, 1024)), ("whg_r", (NL, 24, P, 4096)),
        ("wo_r", (NL, 16, P, 3072)), ("wf_r", (8, P, 2048)),
        ("cn1_r", (8, P, 1024)), ("cn2_r", (4, P, 1024)), ("cn3_r", (2, P, 512)),
        ("gc1_r", (4, P, 1024)), ("gc2_r", (2, P, 512)),
        ("wqt_r", (8, P, 1024)), ("wkt_r", (8, P, 1024)), ("wvt_r", (8, P, 1024)),
        ("wot_r", (8, P, 1024)), ("wg_r", (32, P, 1024)), ("wu_r", (32, P, 1024)),
        ("wd_r", (8, P, 4096)), ("ident", (P, P)), ("vecs", (P, vec_layout()[1])),
        ("rowvecs", (1, 1152)),
    ):
        dr[name] = nc.declare_dram_parameter(name, list(shape), F32, isOutput=False)

    sc = {}
    for name, shape, dt in (
        ("xn_buf", (8, P, T), F32), ("xt_buf", (8, P, T), F32),
        ("s_buf", (8, P, T), F32), ("x1_buf", (8, P, T), F32),
        ("xn2_buf", (8, P, T), F32),
        ("qt_buf", (NH, HD, T), F32), ("kt_buf", (NH, HD, T), F32),
        ("v_buf", (32, P, D), F32), ("o_buf", (8, P, T), F32),
        ("l_buf", (NCH5, NH, TC5), F32),
        ("wig", (32, P, 1024), BF16), ("wiu", (32, P, 1024), BF16),
        ("wid", (8, P, 4096), BF16),
    ):
        sc[name] = nc.dram_tensor(name, list(shape), dt)

    dbg_outs = {}
    for name in bk.dbg:
        shape = {"xn_buf": (8, P, T), "xt_buf": (8, P, T), "s_buf": (8, P, T),
                 "x1_buf": (8, P, T), "xn2_buf": (8, P, T),
                 "qt_buf": (NH, HD, T), "kt_buf": (NH, HD, T),
                 "v_buf": (32, P, D), "o_buf": (8, P, T),
                 "wig": (32, P, 1024), "wiu": (32, P, 1024),
                 "wid": (8, P, 4096), "gfdbg": (P, 8)}[name]
        dt = BF16 if name in ("wig", "wiu", "wid") else F32
        dbg_outs[name] = nc.declare_dram_parameter(
            "dbg_" + name, list(shape), dt, isOutput=True)

    VL, NV = vec_layout()
    DL, ND = dvec_layout()
    vecs = es.enter_context(nc.sbuf_tensor("s_vecs", [P, NV], F32))
    dv = es.enter_context(nc.sbuf_tensor("s_dv", [P, ND], F32))
    rowv = es.enter_context(nc.sbuf_tensor("s_rowv", [1, 1152], F32))
    ident = es.enter_context(nc.sbuf_tensor("identt", [P, P], F32))
    carry = es.enter_context(nc.sbuf_tensor("s_carry", [P, NL * 24], F32))
    rows = es.enter_context(nc.sbuf_tensor("s_rows", [1, 2048], F32))
    gfdbg_sb = es.enter_context(nc.sbuf_tensor("s_gfdbg", [P, 8], F32))
    POOL = 43000
    pool = es.enter_context(nc.sbuf_tensor("s_pool", [P, POOL], F32))
    banks = [es.enter_context(nc.psum_tensor(f"bank{i}", [P, 512], F32))
             for i in range(8)]

    def vc(name, j0=0, n=1):
        o, nn = VL[name]
        assert j0 + n <= nn, name
        return vecs[:, o + j0:o + j0 + n]

    def dvc(name, j0=0, n=1):
        o, nn = DL[name]
        assert j0 + n <= nn, name
        return dv[:, o + j0:o + j0 + n]

    ones_col = vc("ones")
    ones_row = rowv[0:1, 0:P]

    def row(i):
        return rows[0:1, i * 512:(i + 1) * 512]

    v0 = bk.gdma(vecs[:], dr["vecs"][:])
    bk.gdma(rowv[:], dr["rowvecs"][:])
    v1 = bk.gdma(ident[:], dr["ident"][:])
    bk.W("vector", "gd", v1)
    for i in range(NL):
        bk.vop("tensor_scalar", dvc(f"g1p1_{i}", 0, 16), vc(f"g1_{i}", 0, 16),
               1.0, None, ALU.add)
        bk.vop("tensor_scalar", dvc(f"g2p1_{i}", 0, 16), vc(f"g2_{i}", 0, 16),
               1.0, None, ALU.add)
        bk.vop("tensor_scalar", dvc(f"nbg_{i}", 0, 24), vc(f"bgate_{i}", 0, 24),
               -1.0, None, ALU.mult)
    bk.vop("tensor_scalar", dvc("gfp1", 0, 16), vc("gfv", 0, 16), 1.0, None, ALU.add)
    bk.vop("memset", carry[:], 1.0, sem="vop")
    vop_pv = bk.cnt["vop"]
    bk.W("tensor", "vop", vop_pv)
    bk.W("tensor", "gd", v1)

    WT0, WT1, ACCC, TMPC = 0, 4096, 8192, 8193
    QQ5, R05, DI5, GI5, IQ5, WB5 = 8448, 12544, 16640, 20736, 24832, 28928
    for drname, nb, ncols, scname, rsname, wbuf in (
        ("wg_r", 32, 1024, "qsc_g", "qrs_g", "wig"),
        ("wu_r", 32, 1024, "qsc_u", "qrs_u", "wiu"),
        ("wd_r", 8, 4096, "qsc_d", "qrs_d", "wid"),
    ):
        nelem = nb * P * ncols
        slot_guard = [0, 0]
        for b in range(nb):
            s = b % 2
            wt = pool[:, (WT0 if s == 0 else WT1):(WT0 if s == 0 else WT1) + ncols]
            bk.W("gpsimd", "vop", slot_guard[s])
            lv = bk.gdma(wt, dr[drname][:][b])
            bk.W("vector", "gd", lv)
            if b == 0:
                rv = bk.vop("reduce_sum", pool[:, ACCC:ACCC + 1], wt, AX.X,
                            apply_absolute_value=True, sem="vop")
            else:
                bk.vop("reduce_sum", pool[:, TMPC:TMPC + 1], wt, AX.X,
                       apply_absolute_value=True)
                rv = bk.vop("tensor_tensor", pool[:, ACCC:ACCC + 1],
                            pool[:, ACCC:ACCC + 1], pool[:, TMPC:TMPC + 1],
                            ALU.add, sem="vop")
            slot_guard[s] = rv
        bk.W("tensor", "vop", rv)
        bk.bank_wait(0)
        mv = bk.mm(banks[0][0:1, 0:1], ones_col, pool[:, ACCC:ACCC + 1],
                   start=True, stop=True, sem="tmm")
        bk.W("vector", "tmm", mv)
        r3 = row(3)
        bk.vop("tensor_scalar", r3[0:1, 0:1], banks[0][0:1, 0:1],
               1.0 / nelem, None, ALU.mult, sem="vop")
        bk.bank_set_free(0, "vop", bk.cnt["vop"])
        bk.vop("tensor_scalar", r3[0:1, 0:1], r3[0:1, 0:1], 1e-5, None, ALU.max)
        bk.vop("reciprocal", r3[0:1, 1:2], r3[0:1, 0:1])
        sv = bk.vop("reciprocal", r3[0:1, 2:3], r3[0:1, 1:2], sem="vop")
        bk.W("tensor", "vop", sv)
        bk.bank_wait(1)
        bk.mm(banks[1][:, 0:1], ones_row, r3[0:1, 1:2], start=True, stop=True)
        bv = bk.mm(banks[1][:, 1:2], ones_row, r3[0:1, 2:3], start=True,
                   stop=True, sem="tmm")
        bk.W("vector", "tmm", bv)
        bk.vop("tensor_copy", dvc(scname), banks[1][:, 0:1])
        cv2 = bk.vop("tensor_copy", dvc(rsname), banks[1][:, 1:2], sem="vop")
        bk.bank_set_free(1, "vop", cv2)
        qq = pool[:, QQ5:QQ5 + ncols]
        r0 = pool[:, R05:R05 + ncols]
        di = pool[:, DI5:DI5 + ncols]
        gi = pool[:, GI5:GI5 + ncols]
        iq = pool[:, IQ5:IQ5 + ncols].bitcast(I32)
        wb = pool[:, WB5:WB5 + ncols // 2].bitcast(BF16)
        for b in range(nb):
            s = b % 2
            wt = pool[:, (WT0 if s == 0 else WT1):(WT0 if s == 0 else WT1) + ncols]
            bk.W("gpsimd", "vop", slot_guard[s])
            lv = bk.gdma(wt, dr[drname][:][b])
            bk.W("vector", "gd", lv)
            bk.vop("tensor_scalar", qq, wt, dvc(scname), None, ALU.mult,
                   sem="vop")
            slot_guard[s] = bk.cnt["vop"]
            bk.vop("tensor_copy", iq, qq)
            bk.vop("tensor_copy", r0, iq)
            bk.vop("tensor_tensor", di, qq, r0, ALU.subtract)
            bk.vop("tensor_scalar", gi, di, 0.5, None, ALU.is_gt)
            bk.vop("tensor_tensor", r0, r0, gi, ALU.add)
            bk.vop("tensor_scalar", gi, di, -0.5, None, ALU.is_lt)
            bk.vop("tensor_tensor", r0, r0, gi, ALU.subtract)
            bk.vop("tensor_scalar", r0, r0, 1.0, None, ALU.min)
            bk.vop("tensor_scalar", r0, r0, -1.0, None, ALU.max)
            bk.W("vector", "gd", bk.cnt["gd"])
            wv = bk.vop("tensor_copy", wb, r0, sem="vop")
            bk.W("gpsimd", "vop", wv)
            bk.gdma(sc[wbuf][:][b], wb)

    bk.barrier(-1)

    pa = 0

    def alloc(n):
        nonlocal pa
        ap = pool[:, pa:pa + n]
        pa += n
        return ap

    xt_b = [alloc(1024) for _ in range(2)]
    xn_b = alloc(1024)
    sq_col = alloc(1)
    sum_col = alloc(1)
    mean_col = alloc(1)
    ex2_col = alloc(1)
    var_col = alloc(1)
    sd_col = alloc(1)
    inv_col = alloc(1)
    tcol = alloc(1)
    sqs = alloc(1024)
    xnT = alloc(1024)
    xtT = alloc(1024)
    gccols = alloc(16)
    gc1_sb = alloc(4 * 1024)
    gc2_sb = alloc(2 * 512)

    g1v = bk.gdma(gc1_sb.rearrange("p (a b) -> p a b", a=4),
                  dr["gc1_r"][:].transpose([1, 0, 2]))
    g2v = bk.gdma(gc2_sb.rearrange("p (a b) -> p a b", a=2),
                  dr["gc2_r"][:].transpose([1, 0, 2]))

    last_store = 0
    last_xt_use = [(0, 0), (0, 0)]
    prev_mmv = 0
    for tt in range(32):
        buf = xt_b[tt % 2]
        bk.W("gpsimd", "vop", last_xt_use[tt % 2][0])
        bk.W("gpsimd", "tmm", last_xt_use[tt % 2][1])
        av = bk.gdma(buf, x_in[:][tt * P:(tt + 1) * P, :])
        bk.W("vector", "gd", av)
        bk.vop("scalar_tensor_tensor", sqs, buf, 1.0, buf, ALU.mult, ALU.mult,
               accum_out=sq_col)
        bk.vop("reduce_sum", sum_col, buf, AX.X)
        bk.vop("tensor_scalar", mean_col, sum_col, 1.0 / D, None, ALU.mult)
        bk.vop("tensor_scalar", ex2_col, sq_col, 1.0 / D, None, ALU.mult)
        bk.vop("tensor_tensor", tcol, mean_col, mean_col, ALU.mult)
        bk.vop("tensor_tensor", var_col, ex2_col, tcol, ALU.subtract, sem="vop")
        vv = bk.cnt["vop"]
        bk.W("scalar", "vop", vv)
        sv = bk.act(sd_col, var_col, AF.Sqrt, bias=vc("c_eps"), sem="sop")
        bk.W("vector", "sop", sv)
        bk.vop("reciprocal", inv_col, sd_col)
        bk.W("vector", "tmm", prev_mmv)
        bk.vop("tensor_scalar", xn_b, buf, mean_col, inv_col,
               ALU.subtract, ALU.mult, sem="vop")
        xn_ready = bk.cnt["vop"]

        bk.W("tensor", "vop", xn_ready)
        for half in range(2):
            b = banks[half]
            bk.bank_wait(half)
            for k in range(4):
                bk.tr(b[:, k * P:(k + 1) * P], xn_b[:, (half * 4 + k) * P:(half * 4 + k + 1) * P], ident[:])
            tv = bk.E("tensor", lambda e: e.drain(), sem="tmm")
            bk.W("scalar", "tmm", tv)
            cv = bk.sop("copy", xnT[:, half * 512:(half + 1) * 512], b[:], sem="sop")
            bk.bank_set_free(half, "sop", cv)
        for half in range(2):
            b = banks[2 + half]
            bk.bank_wait(2 + half)
            for k in range(4):
                bk.tr(b[:, k * P:(k + 1) * P], buf[:, (half * 4 + k) * P:(half * 4 + k + 1) * P], ident[:])
            tv = bk.E("tensor", lambda e: e.drain(), sem="tmm")
            bk.W("scalar", "tmm", tv)
            cv = bk.sop("copy", xtT[:, half * 512:(half + 1) * 512], b[:], sem="sop")
            bk.bank_set_free(2 + half, "sop", cv)
        for half in range(2):
            bk.bank_wait(6 + half)
            bk.mm(banks[6 + half][0:1, :], ones_col, xn_b[:, half * 512:(half + 1) * 512],
                  start=(tt == 0), stop=(tt == 31))
        mmv = bk.E("tensor", lambda e: e.drain(), sem="tmm")
        prev_mmv = mmv
        last_xt_use[tt % 2] = (xn_ready, mmv)

        sv_ = bk.cnt["sop"]
        bk.W("gpsimd", "sop", sv_)
        bk.gdma(sc["xn_buf"][:].transpose([1, 0, 2])[:, :, tt * P:(tt + 1) * P],
                xnT.rearrange("p (dt t) -> p dt t", dt=8))
        last_store = bk.gdma(
            sc["xt_buf"][:].transpose([1, 0, 2])[:, :, tt * P:(tt + 1) * P],
            xtT.rearrange("p (dt t) -> p dt t", dt=8))
        bk.W("scalar", "gd", last_store)

    bk.W("tensor", "gd", g2v)
    ctx = gccols[:, 0:8]
    bk.W("scalar", "tmm", bk.cnt["tmm"])
    e1 = bk.act(row(0), banks[6][0:1, :], AF.Copy, scale=1.0 / T, sem="sop")
    e2 = bk.act(row(1), banks[7][0:1, :], AF.Copy, scale=1.0 / T, sem="sop")
    bk.bank_set_free(6, "sop", e1)
    bk.bank_set_free(7, "sop", e2)
    bk.W("tensor", "sop", e2)
    for k in range(8):
        b = banks[k % 2]
        src = row(k // 4)[0:1, (k % 4) * 128:((k % 4) + 1) * 128]
        bk.bank_wait(k % 2)
        bk.tr(b[:, 0:1], src, ident[0:1, 0:1])
        tv = bk.E("tensor", lambda e: e.drain(), sem="tmm")
        bk.W("vector", "tmm", tv)
        fv = bk.vop("tensor_copy", ctx[:, k:k + 1], b[:, 0:1], sem="vop")
        bk.bank_set_free(k % 2, "vop", fv)
    h1 = gccols[:, 8:12]
    bk.W("tensor", "vop", bk.cnt["vop"])
    for o in range(4):
        bk.bank_wait(4)
        for kc in range(8):
            bk.mm(banks[4][:, 0:1], gc1_sb[:, o * 1024 + kc * P: o * 1024 + (kc + 1) * P],
                  ctx[:, kc:kc + 1], start=(kc == 0), stop=(kc == 7))
        tv = bk.E("tensor", lambda e: e.drain(), sem="tmm")
        bk.W("scalar", "tmm", tv)
        av = bk.act(h1[:, o:o + 1], banks[4][:, 0:1], AF.Gelu,
                    bias=vc("gcb1", o), sem="sop")
        bk.bank_set_free(4, "sop", av)
        bk.W("tensor", "sop", av)
    h2 = gccols[:, 12:14]
    for o in range(2):
        bk.bank_wait(5)
        for kc in range(4):
            bk.mm(banks[5][:, 0:1], gc2_sb[:, o * 512 + kc * P: o * 512 + (kc + 1) * P],
                  h1[:, kc:kc + 1], start=(kc == 0), stop=(kc == 3))
        tv = bk.E("tensor", lambda e: e.drain(), sem="tmm")
        bk.W("scalar", "tmm", tv)
        av = bk.act(h2[:, o:o + 1], banks[5][:, 0:1], AF.Gelu,
                    bias=vc("gcb2", o), sem="sop")
        bk.bank_set_free(5, "sop", av)
        bk.W("tensor", "sop", av)
    bk.bank_wait(4)
    for kc in range(2):
        bk.mm(banks[4][0:1, 0:1], vc("gcw3", kc), h2[:, kc:kc + 1],
              start=(kc == 0), stop=(kc == 1))
    tv = bk.E("tensor", lambda e: e.drain(), sem="tmm")
    bk.W("scalar", "tmm", tv)
    gfs = gccols[:, 14:15]
    av = bk.act(gfs[0:1, :], banks[4][0:1, 0:1], AF.Sigmoid,
                bias=vc("gcb3")[0:1, :], sem="sop")
    bk.bank_set_free(4, "sop", av)
    bk.W("tensor", "sop", av)
    bk.bank_wait(5)
    bk.mm(banks[5][:, 0:1], ones_row, gfs[0:1, :], start=True, stop=True)
    tv = bk.E("tensor", lambda e: e.drain(), sem="tmm")
    bk.W("vector", "tmm", tv)
    gf_col = gccols[:, 15:16]
    bk.vop("tensor_copy", gf_col, banks[5][:, 0:1], sem="vop")
    bk.bank_set_free(5, "vop", bk.cnt["vop"])
    bk.vop("tensor_scalar", dvc("cgA"), vc("ar"), 0.6, None, ALU.mult)
    bk.vop("tensor_tensor", tcol, vc("ar"), gf_col, ALU.mult)
    bk.vop("tensor_scalar", tcol, tcol, 1.4, None, ALU.mult)
    bk.vop("tensor_tensor", tcol, tcol, vc("ar"), ALU.subtract)
    bk.vop("tensor_scalar", dvc("cgB"), tcol, 0.1, None, ALU.add, sem="vop")
    if "gfdbg" in bk.dbg:
        bk.vop("memset", gfdbg_sb[:], 0.0)
        bk.vop("tensor_copy", gfdbg_sb[:, 0:1], gf_col)
        bk.vop("tensor_copy", gfdbg_sb[:, 1:2], dvc("cgA"))
        bk.vop("tensor_copy", gfdbg_sb[:, 2:3], dvc("cgB"))
        bk.vop("tensor_copy", gfdbg_sb[:, 3:4], ctx[:, 0:1], sem="vop")

    bk.barrier(0)

    def xn_t(kc):
        return pool[:, kc * 512:(kc + 1) * 512]

    def hn_t(dt):
        return pool[:, dt * 512:(dt + 1) * 512]

    def h_t(o):
        return pool[:, 8192 + o * 512:8192 + (o + 1) * 512]

    def scano(j):
        return pool[:, 16384 + j * 513:16384 + (j + 1) * 513]

    def cvc(s):
        return pool[:, 28696 + s * 1026:28696 + s * 1026 + 513]

    def cvv(s):
        return pool[:, 28696 + s * 1026 + 513:28696 + s * 1026 + 1026]

    def zs_sh(s):
        return pool[:, 30748 + s * 1536:30748 + s * 1536 + 512]

    def zs_hb(s):
        return pool[:, 30748 + s * 1536 + 512:30748 + s * 1536 + 1024]

    def zs_gm(s):
        return pool[:, 30748 + s * 1536 + 1024:30748 + s * 1536 + 1536]

    def sq_t(s):
        return zs_sh(s)

    s_col_ap = pool[:, 33820:34332]
    ring = Ring(bk, pool[:, 34332:43000], 8668)

    bk.vop("memset", cvc(0)[:, 0:1], 0.0)
    bk.vop("memset", cvv(0)[:, 0:1], 1.0)
    bk.vop("memset", cvc(1)[:, 0:1], 0.0)
    bk.vop("memset", cvv(1)[:, 0:1], 1.0, sem="vop")

    def rms_stats(tiles_f, ntiles, zs_guards=(0, 0)):
        sq_guard = [None, None]
        mmv = None
        for dt in range(ntiles):
            s = dt % 2
            if sq_guard[s] is not None:
                bk.W("scalar", "tmm", sq_guard[s])
            if dt < 2:
                bk.W("scalar", "vop", zs_guards[s])
            sv = bk.act(sq_t(s), tiles_f(dt), AF.Square, sem="sop")
            bk.W("tensor", "sop", sv)
            if dt == 0:
                bk.bank_wait(6)
            mmv = bk.mm(banks[6][0:1, :], ones_col, sq_t(s),
                        start=(dt == 0), stop=(dt == ntiles - 1), sem="tmm")
            sq_guard[s] = mmv
        bk.W("scalar", "tmm", mmv)
        nv = bk.act(row(0), banks[6][0:1, :], AF.Sqrt, sem="sop")
        bk.bank_set_free(6, "sop", nv)
        bk.W("vector", "sop", nv)
        bk.vop("tensor_scalar", row(0), row(0), 1e-12, None, ALU.max)
        bk.vop("reciprocal", row(1), row(0))
        iv = bk.vop("tensor_scalar", row(1), row(1), SQD, None, ALU.mult, sem="vop")
        bk.W("tensor", "vop", iv)
        bk.bank_wait(7)
        bcv = bk.mm(banks[7][:, :], ones_row, row(1), start=True, stop=True,
                    sem="tmm")
        return bcv

    ch_guard_sop = 0
    ch_guard_zs = [0, 0]
    ch_guard_cv = [0, 0]
    scan_val = 0
    wo_mm_last = 0
    whg_mm_last = 0
    wf_mm_last = 0
    sstore_val = 0
    for c in range(NCH):
        c0 = c * TC
        bk.W("gpsimd", "tmm", wf_mm_last)
        xnv = bk.gdma(pool[:, 0:4096].rearrange("p (dt t) -> p dt t", dt=8),
                      sc["xn_buf"][:].transpose([1, 0, 2])[:, :, c0:c0 + TC])
        bk.W("tensor", "gd", xnv)
        wp_evac = None
        for o in range(16):
            ap, idx = ring.load(dr["wp_r"][:][o], 1024)
            ring.consume_wait(idx)
            bknum = 4 + o % 2
            bk.bank_wait(bknum)
            for kc in range(8):
                v = bk.mm(banks[bknum][:], ap[:, kc * P:(kc + 1) * P], xn_t(kc),
                          start=(kc == 0), stop=(kc == 7),
                          sem=("tmm" if kc == 7 else None))
            ring.free(idx)
            bk.W("vector", "tmm", v)
            if c > 0 and o == 0:
                bk.W("vector", "sop", ch_guard_sop)
            wp_evac = bk.vop("tensor_scalar", h_t(o), banks[bknum][:],
                             vc("bp", o), None, ALU.add, sem="vop")
            bk.bank_set_free(bknum, "vop", wp_evac)

        for i in range(NL):
            bk.W("scalar", "vop", bk.cnt["vop"])
            bcv = rms_stats(h_t, 16, zs_guards=tuple(ch_guard_zs))
            bk.W("vector", "tmm", bcv)
            if i > 0:
                bk.W("vector", "tmm", whg_mm_last)
            hnv = None
            for dt in range(16):
                hnv = bk.vop("scalar_tensor_tensor", hn_t(dt), h_t(dt),
                             dvc(f"g1p1_{i}", dt), banks[7][:],
                             ALU.mult, ALU.mult,
                             sem=("vop" if dt == 15 else None))
            bk.bank_set_free(7, "vop", hnv)
            bk.W("tensor", "vop", hnv)

            for j in range(24):
                ap, idx = ring.load(dr["whg_r"][:][i, j], 4096)
                ring.consume_wait(idx)
                s = j % 2
                bh, bg = (0, 1) if s == 0 else (2, 3)
                bk.bank_wait(bh)
                bk.bank_wait(bg)
                for kc in range(16):
                    bk.mm(banks[bh][:], ap[:, kc * P:(kc + 1) * P], hn_t(kc),
                          start=(kc == 0), stop=(kc == 15))
                for kc in range(16):
                    v = bk.mm(banks[bg][:], ap[:, 2048 + kc * P:2048 + (kc + 1) * P],
                              hn_t(kc), start=(kc == 0), stop=(kc == 15),
                              sem=("tmm" if kc == 15 else None))
                whg_mm_last = v
                ring.free(idx)
                bk.W("scalar", "tmm", v)
                bk.W("scalar", "vop", ch_guard_cv[s])
                cav = bk.act(cvc(s)[:, 1:513], banks[bg][:], AF.Sigmoid,
                             scale=-1.0, bias=dvc(f"nbg_{i}", j), sem="sop")
                bk.bank_set_free(bg, "sop", cav)
                bk.W("scalar", "vop", ch_guard_zs[s])
                shv = bk.act(zs_sh(s), banks[bh][:], AF.Sigmoid,
                             bias=vc(f"bhid_{i}", j), sem="sop")
                bk.W("vector", "sop", shv)
                hbv = bk.vop("tensor_scalar", zs_hb(s), banks[bh][:],
                             vc(f"bhid_{i}", j), None, ALU.add, sem="vop")
                bk.bank_set_free(bh, "vop", hbv)
                bk.vop("tensor_scalar", zs_gm(s), zs_sh(s), 0.5, None, ALU.min)
                bk.vop("scalar_tensor_tensor", zs_gm(s), zs_hb(s), 0.0,
                       zs_gm(s), ALU.max, ALU.add)
                bk.vop("tensor_scalar", zs_sh(s), cvc(s)[:, 1:513], -1.0, 1.0,
                       ALU.mult, ALU.add)
                zv = bk.vop("tensor_tensor", cvv(s)[:, 1:513], zs_sh(s),
                            zs_gm(s), ALU.mult, sem="vop")
                ch_guard_zs[s] = zv
                bk.vop("tensor_copy", cvv(s)[:, 0:1], carry[:, i * 24 + j:i * 24 + j + 1])
                if j == 0:
                    bk.W("vector", "tmm", wo_mm_last)
                bk.vop("tensor_tensor_scan", scano(j), cvc(s), cvv(s), 0.0,
                       ALU.mult, ALU.add)
                scan_val = bk.vop("tensor_copy",
                                  carry[:, i * 24 + j:i * 24 + j + 1],
                                  scano(j)[:, 512:513], sem="vop")
                ch_guard_cv[s] = scan_val

            bk.W("tensor", "vop", scan_val)
            for o in range(16):
                ap, idx = ring.load(dr["wo_r"][:][i, o], 3072)
                ring.consume_wait(idx)
                bknum = 4 + o % 2
                bk.bank_wait(bknum)
                for kc in range(24):
                    v = bk.mm(banks[bknum][:], ap[:, kc * P:(kc + 1) * P],
                              scano(kc)[:, 1:513], start=(kc == 0),
                              stop=(kc == 23), sem=("tmm" if kc == 23 else None))
                wo_mm_last = v
                ring.free(idx)
                bk.W("vector", "tmm", v)
                ev = bk.vop("scalar_tensor_tensor", h_t(o), banks[bknum][:],
                            vc(f"bo_{i}", o), h_t(o), ALU.add, ALU.add,
                            sem="vop")
                bk.bank_set_free(bknum, "vop", ev)

            bk.W("scalar", "vop", bk.cnt["vop"])
            bcv = rms_stats(h_t, 16, zs_guards=tuple(ch_guard_zs))
            ch_guard_sop = bk.cnt["sop"]
            bk.W("vector", "tmm", bcv)
            upd = None
            for dt in range(16):
                bk.vop("scalar_tensor_tensor", zs_gm(dt % 2), h_t(dt),
                       dvc(f"g2p1_{i}", dt), banks[7][:], ALU.mult, ALU.mult)
                upd = bk.vop("tensor_tensor", h_t(dt), h_t(dt), zs_gm(dt % 2),
                             ALU.add, sem=("vop" if dt == 15 else None))
            bk.bank_set_free(7, "vop", upd)
            ch_guard_zs = [upd, upd]

        bk.W("scalar", "vop", bk.cnt["vop"])
        bcv = rms_stats(h_t, 16, zs_guards=tuple(ch_guard_zs))
        ch_guard_sop = bk.cnt["sop"]
        bk.W("vector", "tmm", bcv)
        bk.W("vector", "tmm", whg_mm_last)
        hnv = None
        for dt in range(16):
            hnv = bk.vop("scalar_tensor_tensor", hn_t(dt), h_t(dt),
                         dvc("gfp1", dt), banks[7][:], ALU.mult, ALU.mult,
                         sem=("vop" if dt == 15 else None))
        bk.bank_set_free(7, "vop", hnv)
        bk.W("tensor", "vop", hnv)
        for o in range(8):
            ap, idx = ring.load(dr["wf_r"][:][o], 2048)
            ring.consume_wait(idx)
            bknum = 4 + o % 2
            bk.bank_wait(bknum)
            for kc in range(16):
                v = bk.mm(banks[bknum][:], ap[:, kc * P:(kc + 1) * P], hn_t(kc),
                          start=(kc == 0), stop=(kc == 15),
                          sem=("tmm" if kc == 15 else None))
            wf_mm_last = v
            ring.free(idx)
            bk.W("vector", "tmm", v)
            bk.W("vector", "gd", sstore_val)
            sv = bk.vop("tensor_scalar", s_col_ap, banks[bknum][:], vc("bf", o),
                        None, ALU.add, sem="vop")
            bk.bank_set_free(bknum, "vop", sv)
            bk.W("gpsimd", "vop", sv)
            sstore_val = bk.gdma(sc["s_buf"][:][o, :, c0:c0 + TC], s_col_ap)

    bk.barrier(1)

    def p2t(base, i, w=512):
        return pool[:, base + i * w:base + (i + 1) * w]

    OA, OB, XT, SX, GG, H1, H2, H3 = 0, 4096, 8192, 12288, 16384, 20480, 24576, 26624
    CPL, SQ2, CN1, CN2, CN3 = 27648, 28160, 29184, 37376, 41472
    cn1v = bk.gdma(pool[:, CN1:CN1 + 8192].rearrange("p (a b) -> p a b", a=8),
                   dr["cn1_r"][:].transpose([1, 0, 2]))
    cn2v = bk.gdma(pool[:, CN2:CN2 + 4096].rearrange("p (a b) -> p a b", a=4),
                   dr["cn2_r"][:].transpose([1, 0, 2]))
    cn3v = bk.gdma(pool[:, CN3:CN3 + 1024].rearrange("p (a b) -> p a b", a=2),
                   dr["cn3_r"][:].transpose([1, 0, 2]))
    bk.W("tensor", "gd", cn3v)

    TW_SRC = [4, 5, 6, 7, 0, 1, 2, 3]
    TW_SGN = [1, 1, -1, -1, -1, -1, 1, 1]

    for c in range(NCH):
        c0 = c * TC
        bk.W("gpsimd", "tmm", bk.cnt["tmm"])
        bk.W("gpsimd", "vop", bk.cnt["vop"])
        bk.W("gpsimd", "sop", bk.cnt["sop"])
        ldv0 = bk.gdma(pool[:, OA:OA + 4096].rearrange("p (dt t) -> p dt t", dt=8),
                       sc["xn_buf"][:].transpose([1, 0, 2])[:, :, c0:c0 + TC])
        bk.gdma(pool[:, XT:XT + 4096].rearrange("p (dt t) -> p dt t", dt=8),
                sc["xt_buf"][:].transpose([1, 0, 2])[:, :, c0:c0 + TC])
        ldv = bk.gdma(pool[:, SX:SX + 4096].rearrange("p (dt t) -> p dt t", dt=8),
                      sc["s_buf"][:].transpose([1, 0, 2])[:, :, c0:c0 + TC])
        cur_base, nxt_base = OA, OB
        cur_ready_v, cur_ready_sem = ldv, "gd"
        for cyc in range(3):
            bk.W("tensor", cur_ready_sem, cur_ready_v)
            h1_last = None
            for o in range(8):
                bknum = o % 2
                bk.bank_wait(bknum)
                for kc in range(8):
                    v = bk.mm(banks[bknum][:], pool[:, CN1 + o * 1024 + kc * P:CN1 + o * 1024 + (kc + 1) * P],
                              p2t(cur_base, kc), start=(kc == 0), stop=(kc == 7),
                              sem=("tmm" if kc == 7 else None))
                bk.W("scalar", "tmm", v)
                h1_last = bk.act(p2t(H1, o), banks[bknum][:], AF.Gelu,
                                 bias=vc("cnb1", o), sem="sop")
                bk.bank_set_free(bknum, "sop", h1_last)
            bk.W("tensor", "sop", h1_last)
            h2_last = None
            for o in range(4):
                bk.bank_wait(2)
                for kc in range(8):
                    v = bk.mm(banks[2][:], pool[:, CN2 + o * 1024 + kc * P:CN2 + o * 1024 + (kc + 1) * P],
                              p2t(H1, kc), start=(kc == 0), stop=(kc == 7),
                              sem=("tmm" if kc == 7 else None))
                bk.W("scalar", "tmm", v)
                h2_last = bk.act(p2t(H2, o), banks[2][:], AF.Gelu,
                                 bias=vc("cnb2", o), sem="sop")
                bk.bank_set_free(2, "sop", h2_last)
            bk.W("tensor", "sop", h2_last)
            h3_last = None
            for o in range(2):
                bk.bank_wait(3)
                for kc in range(4):
                    v = bk.mm(banks[3][:], pool[:, CN3 + o * 512 + kc * P:CN3 + o * 512 + (kc + 1) * P],
                              p2t(H2, kc), start=(kc == 0), stop=(kc == 3),
                              sem=("tmm" if kc == 3 else None))
                bk.W("scalar", "tmm", v)
                h3_last = bk.act(p2t(H3, o), banks[3][:], AF.Gelu,
                                 bias=vc("cnb3", o), sem="sop")
                bk.bank_set_free(3, "sop", h3_last)
            bk.W("tensor", "sop", h3_last)
            bk.bank_wait(6)
            for kc in range(2):
                v = bk.mm(banks[6][0:1, :], vc("cnw4", kc), p2t(H3, kc),
                          start=(kc == 0), stop=(kc == 1),
                          sem=("tmm" if kc == 1 else None))
            bk.W("scalar", "tmm", v)
            tfv = bk.act(row(2), banks[6][0:1, :], AF.Sigmoid,
                         bias=vc("cnb4")[0:1, :], sem="sop")
            bk.bank_set_free(6, "sop", tfv)
            bk.W("tensor", "sop", tfv)
            bk.bank_wait(7)
            bcv = bk.mm(banks[7][:], ones_row, row(2), start=True, stop=True,
                        sem="tmm")
            bk.W("vector", "tmm", bcv)
            cplv = bk.vop("tensor_scalar", pool[:, CPL:CPL + 512], banks[7][:],
                          dvc("cgA"), dvc("cgB"), ALU.mult, ALU.add, sem="vop")
            bk.bank_set_free(7, "vop", cplv)
            upd = None
            for j in range(8):
                bk.vop("tensor_tensor", p2t(SQ2, 0), pool[:, CPL:CPL + 512],
                       p2t(cur_base, TW_SRC[j]), ALU.mult)
                upd = bk.vop("tensor_tensor", p2t(nxt_base, j),
                             p2t(cur_base, j), p2t(SQ2, 0),
                             ALU.add if TW_SGN[j] > 0 else ALU.subtract,
                             sem=("vop" if j == 7 else None))
            cur_base, nxt_base = nxt_base, cur_base
            cur_ready_v, cur_ready_sem = upd, "vop"
        bk.W("scalar", "vop", cur_ready_v)
        gv = None
        for dt in range(8):
            gv = bk.act(p2t(GG, dt), p2t(cur_base, dt), AF.Silu,
                        sem=("sop" if dt == 7 else None))
        bk.W("vector", "sop", gv)
        bk.W("vector", "gd", ldv)
        x1v = None
        for dt in range(8):
            bk.vop("tensor_tensor", p2t(GG, dt), p2t(GG, dt), p2t(SX, dt),
                   ALU.mult)
            x1v = bk.vop("tensor_tensor", p2t(XT, dt), p2t(XT, dt), p2t(GG, dt),
                         ALU.add, sem=("vop" if dt == 7 else None))
        bk.W("gpsimd", "vop", x1v)
        bk.gdma(sc["x1_buf"][:].transpose([1, 0, 2])[:, :, c0:c0 + TC],
                pool[:, XT:XT + 4096].rearrange("p (dt t) -> p dt t", dt=8))
        bk.W("tensor", "vop", x1v)
        bk.W("scalar", "vop", x1v)
        sq_guard = [None, None]
        for dt in range(8):
            s = dt % 2
            if sq_guard[s] is not None:
                bk.W("scalar", "tmm", sq_guard[s])
            sv = bk.act(p2t(SQ2, s), p2t(XT, dt), AF.Square, sem="sop")
            bk.W("tensor", "sop", sv)
            if dt == 0:
                bk.bank_wait(0)
                bk.bank_wait(1)
            bk.mm(banks[0][0:1, :], ones_col, p2t(XT, dt),
                  start=(dt == 0), stop=(dt == 7))
            mv = bk.mm(banks[1][0:1, :], ones_col, p2t(SQ2, s),
                       start=(dt == 0), stop=(dt == 7), sem="tmm")
            sq_guard[s] = mv
        bk.W("vector", "tmm", mv)
        bk.vop("tensor_scalar", row(0), banks[0][0:1, :], 1.0 / D, None, ALU.mult)
        bk.vop("tensor_scalar", row(3), banks[1][0:1, :], 1.0 / D, None, ALU.mult)
        bk.vop("tensor_tensor", row(1), row(0), row(0), ALU.mult)
        vv = bk.vop("tensor_tensor", row(3), row(3), row(1), ALU.subtract,
                    sem="vop")
        bk.bank_set_free(0, "vop", vv)
        bk.bank_set_free(1, "vop", vv)
        bk.W("scalar", "vop", vv)
        sdv = bk.act(row(3), row(3), AF.Sqrt, bias=vc("c_eps")[0:1, :], sem="sop")
        bk.W("vector", "sop", sdv)
        iv = bk.vop("reciprocal", row(1), row(3), sem="vop")
        bk.W("tensor", "vop", iv)
        bk.bank_wait(2)
        bk.bank_wait(3)
        bk.mm(banks[2][:], ones_row, row(0), start=True, stop=True)
        bcv = bk.mm(banks[3][:], ones_row, row(1), start=True, stop=True,
                    sem="tmm")
        bk.W("vector", "tmm", bcv)
        xnv2 = None
        for dt in range(8):
            bk.vop("tensor_tensor", p2t(SX, dt), p2t(XT, dt), banks[2][:],
                   ALU.subtract)
            xnv2 = bk.vop("tensor_tensor", p2t(SX, dt), p2t(SX, dt),
                          banks[3][:], ALU.mult,
                          sem=("vop" if dt == 7 else None))
        bk.bank_set_free(2, "vop", xnv2)
        bk.bank_set_free(3, "vop", xnv2)
        bk.W("gpsimd", "vop", xnv2)
        bk.gdma(sc["xn2_buf"][:].transpose([1, 0, 2])[:, :, c0:c0 + TC],
                pool[:, SX:SX + 4096].rearrange("p (dt t) -> p dt t", dt=8))

    bk.barrier(2)

    WQ, WK, WV, QEV, VEV = 4096, 12288, 20480, 28672, 29696
    wq_v = bk.gdma(pool[:, WQ:WQ + 8192].rearrange("p (a b) -> p a b", a=8),
                   dr["wqt_r"][:].transpose([1, 0, 2]))
    wk_v = bk.gdma(pool[:, WK:WK + 8192].rearrange("p (a b) -> p a b", a=8),
                   dr["wkt_r"][:].transpose([1, 0, 2]))
    wv_v = bk.gdma(pool[:, WV:WV + 8192].rearrange("p (a b) -> p a b", a=8),
                   dr["wvt_r"][:].transpose([1, 0, 2]))
    bk.W("tensor", "gd", wv_v)
    for c in range(NCH):
        c0 = c * TC
        bk.W("gpsimd", "tmm", bk.cnt["tmm"])
        bk.W("gpsimd", "vop", bk.cnt["vop"])
        ld3 = bk.gdma(pool[:, 0:4096].rearrange("p (dt t) -> p dt t", dt=8),
                      sc["xn2_buf"][:].transpose([1, 0, 2])[:, :, c0:c0 + TC])
        bk.W("tensor", "gd", ld3)
        for which, wbase, bname, buf in (("q", WQ, "bq", "qt_buf"),
                                         ("k", WK, "bk", "kt_buf")):
            for h in range(NH):
                bknum = h % 4
                bk.bank_wait(bknum)
                for kc in range(8):
                    v = bk.mm(banks[bknum][0:64, :],
                              pool[:, wbase + kc * 1024 + h * HD:wbase + kc * 1024 + (h + 1) * HD],
                              pool[:, kc * 512:(kc + 1) * 512],
                              start=(kc == 0), stop=(kc == 7),
                              sem=("tmm" if kc == 7 else None))
                bk.W("vector", "tmm", v)
                ev = pool[0:64, QEV + (h % 2) * 512:QEV + (h % 2) * 512 + 512]
                bk.W("vector", "gd", bk.cnt["gd"])
                evv = bk.vop("tensor_scalar", ev, banks[bknum][0:64, :],
                             vc(bname, h)[0:64, :], None, ALU.add, sem="vop")
                bk.bank_set_free(bknum, "vop", evv)
                bk.W("gpsimd", "vop", evv)
                bk.gdma(sc[buf][:][h, :, c0:c0 + TC], ev)
        for tsub in range(4):
            tok0 = tsub * P
            for half in range(2):
                bknum = 4 + (tsub * 2 + half) % 2
                bk.bank_wait(bknum)
                for kc in range(8):
                    bk.mm(banks[bknum][:],
                          pool[:, kc * 512 + tok0:kc * 512 + tok0 + P],
                          pool[:, WV + kc * 1024 + half * 512:WV + kc * 1024 + (half + 1) * 512],
                          start=(kc == 0), stop=False)
                v = bk.mm(banks[bknum][:], ones_row,
                          rowv[0:1, P + half * 512:P + (half + 1) * 512],
                          start=False, stop=True, sem="tmm")
                bk.W("vector", "tmm", v)
                ev = pool[:, VEV + half * 512:VEV + (half + 1) * 512]
                bk.W("vector", "gd", bk.cnt["gd"])
                evv = bk.vop("tensor_copy", ev, banks[bknum][:], sem="vop")
                bk.bank_set_free(bknum, "vop", evv)
                bk.W("gpsimd", "vop", evv)
                bk.gdma(sc["v_buf"][:][c * 4 + tsub, :, half * 512:(half + 1) * 512], ev)

    bk.barrier(3)

    QA, KA, QB_, KB_ = 0, 4096, 8192, 12288
    VA, VB_, PT4, LR4, OSA, OSB4 = 16384, 18464, 20544, 22592, 23616, 24128
    vexts = [pool[:, VA:VA + 2080], pool[:, VB_:VB_ + 2080]]
    vx = bk.vop("memset", vexts[0], 1.0)
    vx = bk.vop("memset", vexts[1], 1.0, sem="vop")
    bk.W("gpsimd", "vop", vx)
    qk = [(pool[0:64, QA:QA + T], pool[0:64, KA:KA + T]),
          (pool[0:64, QB_:QB_ + T], pool[0:64, KB_:KB_ + T])]
    lrows = [pool[64:65, LR4:LR4 + 512], pool[64:65, LR4 + 512:LR4 + 1024]]
    osts = [pool[64:128, OSA:OSA + 512], pool[64:128, OSB4:OSB4 + 512]]

    def pt4(x, par):
        off = PT4 + (x * 2 + par) * 512
        return pool[:, off:off + 512]

    prev_pv = {}
    for hp in range(NH // 2):
        bk.W("gpsimd", "tmm", bk.cnt["tmm"])
        for x in range(2):
            h = hp * 2 + x
            bk.gdma(qk[x][0], sc["qt_buf"][:][h])
            bk.gdma(qk[x][1], sc["kt_buf"][:][h])
            vldv = bk.gdma(
                vexts[x].rearrange("p (tt c) -> p tt c", c=65)[:, :, 0:64],
                sc["v_buf"][:].transpose([1, 0, 2])[:, :, h * HD:(h + 1) * HD])
        bk.W("tensor", "gd", vldv)
        for sg in range(8):
            bk.bank_wait(4)
            bk.bank_wait(5)
            exv = {}
            for tb in range(33):
                for x in range(2):
                    if tb < 32:
                        bknum = 2 * x + tb % 2
                        bk.bank_wait(bknum)
                        sv = bk.mm(banks[bknum][:],
                                   qk[x][1][:, tb * P:(tb + 1) * P],
                                   qk[x][0][:, sg * 512:(sg + 1) * 512],
                                   start=True, stop=True, sem="tmm")
                        bk.W("scalar", "tmm", sv)
                        bk.W("scalar", "tmm", prev_pv.get((x, tb % 2), 0))
                        ev = bk.act(pt4(x, tb % 2), banks[bknum][:], AF.Exp,
                                    scale=0.125, sem="sop")
                        bk.bank_set_free(bknum, "sop", ev)
                        exv[(x, tb)] = ev
                    if tb > 0:
                        tp = tb - 1
                        bk.W("tensor", "sop", exv.pop((x, tp)))
                        pv = bk.mm(banks[4 + x][0:65, :],
                                   vexts[x][:, tp * 65:(tp + 1) * 65],
                                   pt4(x, tp % 2), start=(tp == 0),
                                   stop=(tp == 31), sem="tmm")
                        prev_pv[(x, tp % 2)] = pv
            for x in range(2):
                h = hp * 2 + x
                bk.W("scalar", "tmm", prev_pv[(x, 1)])
                bk.W("scalar", "gd", bk.cnt["gd"])
                lcp = bk.sop("copy", lrows[x], banks[4 + x][64:65, :],
                             sem="sop")
                bk.W("vector", "tmm", prev_pv[(x, 1)])
                bk.W("vector", "gd", bk.cnt["gd"])
                ocp = bk.vop("tensor_copy", osts[x], banks[4 + x][0:64, :],
                             sem="vop")
                bk.bank_set_free(4 + x, "vop", ocp)
                bk.W("gpsimd", "sop", lcp)
                bk.W("gpsimd", "vop", ocp)
                bk.gdma(sc["l_buf"][:][2 * sg:2 * sg + 2, h, :], lrows[x])
                bk.gdma(sc["o_buf"][:][h // 2, (h % 2) * HD:(h % 2 + 1) * HD,
                                       sg * 512:(sg + 1) * 512], osts[x])

    bk.barrier(4)

    XN5, X15, OT5, S25, GP5, UP5 = 0, 2048, 4096, 6144, 8192, 16384
    XQ5, XQ25, AM5, SCB5, IVB5, IVB25 = 16896, 17920, 22016, 22272, 22528, 22784
    RND5, YSB5, LR5 = 23040, 24576, 26624
    lr_ap = pool[0:1, LR5:LR5 + 4096]
    ring5 = Ring(bk, pool[:, 30736:43000], 12264, gate=("ph", bk.cnt["ph"]))

    def t5(base, i, w=256):
        return pool[:, base + i * w:base + (i + 1) * w]

    def xq_t(dt):
        return pool[:, XQ5 + dt * 128:XQ5 + (dt + 1) * 128].bitcast(BF16)

    def xq2_t(dt):
        return pool[:, XQ25 + dt * 128:XQ25 + (dt + 1) * 128].bitcast(BF16)

    def act_quant(src_f, ntiles, xq_f, invb_off):
        acc = pool[:, AM5:AM5 + 256]
        tmp = pool[:, RND5:RND5 + 256]
        mcl = pool[:, AM5 + 256:AM5 + 258]
        for dt in range(ntiles):
            if dt == 0:
                bk.vop("scalar_tensor_tensor", acc, src_f(0), -1.0, src_f(0),
                       ALU.mult, ALU.max)
            else:
                bk.vop("scalar_tensor_tensor", tmp, src_f(dt), -1.0, src_f(dt),
                       ALU.mult, ALU.max)
                bk.vop("tensor_tensor", acc, acc, tmp, ALU.max)
        av = bk.E("vector", lambda e: e.drain(), sem="vop")
        bk.W("tensor", "vop", av)
        bk.bank_wait(6)
        for b in range(2):
            tv = bk.tr(banks[6][:, b * P:(b + 1) * P], acc[:, b * P:(b + 1) * P],
                       ident[:], sem=("tmm" if b == 1 else None))
        bk.W("vector", "tmm", tv)
        bk.vop("reduce_max", mcl[:, 0:1], banks[6][:, 0:P], AX.X)
        mv = bk.vop("reduce_max", mcl[:, 1:2], banks[6][:, P:2 * P], AX.X,
                    sem="vop")
        bk.bank_set_free(6, "vop", mv)
        bk.vop("tensor_scalar", mcl[:, 0:2], mcl[:, 0:2], 1e-5, None, ALU.max)
        bk.vop("reciprocal", tmp[:, 0:2], mcl[:, 0:2])
        bk.vop("tensor_scalar", tmp[:, 0:2], tmp[:, 0:2], 127.0, None, ALU.mult)
        rv = bk.vop("tensor_scalar", mcl[:, 0:2], mcl[:, 0:2], 1.0 / 127.0,
                    None, ALU.mult, sem="vop")
        bk.W("tensor", "vop", rv)
        bk.bank_wait(6)
        for b in range(2):
            bk.tr(banks[6][0:1, b * P:(b + 1) * P], tmp[:, b:b + 1], ident[:])
        for b in range(2):
            tv = bk.tr(banks[6][0:1, 256 + b * P:256 + (b + 1) * P],
                       mcl[:, b:b + 1], ident[:],
                       sem=("tmm" if b == 1 else None))
        bk.W("scalar", "tmm", tv)
        rcv = bk.sop("copy", row(2)[0:1, 0:512], banks[6][0:1, 0:512],
                     sem="sop")
        bk.bank_set_free(6, "sop", rcv)
        bk.W("tensor", "sop", rcv)
        bk.bank_wait(7)
        bk.mm(banks[7][:, 0:256], ones_row, row(2)[0:1, 0:256],
              start=True, stop=True)
        bv = bk.mm(banks[7][:, 256:512], ones_row, row(2)[0:1, 256:512],
                   start=True, stop=True, sem="tmm")
        bk.W("vector", "tmm", bv)
        bk.vop("tensor_copy", pool[:, SCB5:SCB5 + 256], banks[7][:, 0:256])
        cbv = bk.vop("tensor_copy", pool[:, invb_off:invb_off + 256],
                     banks[7][:, 256:512], sem="vop")
        bk.bank_set_free(7, "vop", cbv)
        qq = pool[:, RND5 + 256:RND5 + 512]
        r0 = pool[:, RND5 + 512:RND5 + 768]
        di = pool[:, RND5 + 768:RND5 + 1024]
        gi = pool[:, RND5 + 1024:RND5 + 1280]
        iq = pool[:, RND5 + 1280:RND5 + 1536].bitcast(I32)
        lastv = None
        for dt in range(ntiles):
            bk.vop("tensor_tensor", qq, src_f(dt), pool[:, SCB5:SCB5 + 256],
                   ALU.mult)
            bk.vop("tensor_copy", iq, qq)
            bk.vop("tensor_copy", r0, iq)
            bk.vop("tensor_tensor", di, qq, r0, ALU.subtract)
            bk.vop("tensor_scalar", gi, di, 0.5, None, ALU.is_gt)
            bk.vop("tensor_tensor", r0, r0, gi, ALU.add)
            bk.vop("tensor_scalar", gi, di, -0.5, None, ALU.is_lt)
            bk.vop("tensor_tensor", r0, r0, gi, ALU.subtract)
            lastv = bk.vop("tensor_copy", xq_f(dt), r0,
                           sem=("vop" if dt == ntiles - 1 else None))
        return lastv

    for cc in range(NCH5):
        c0 = cc * TC5
        bk.W("gpsimd", "tmm", bk.cnt["tmm"])
        bk.W("gpsimd", "vop", bk.cnt["vop"])
        bk.W("gpsimd", "sop", bk.cnt["sop"])
        bk.W("vector", "tmm", bk.cnt["tmm"])
        bk.gdma(pool[:, XN5:XN5 + 2048].rearrange("p (dt t) -> p dt t", dt=8),
                sc["xn2_buf"][:].transpose([1, 0, 2])[:, :, c0:c0 + TC5])
        bk.gdma(pool[:, X15:X15 + 2048].rearrange("p (dt t) -> p dt t", dt=8),
                sc["x1_buf"][:].transpose([1, 0, 2])[:, :, c0:c0 + TC5])
        bk.gdma(pool[:, OT5:OT5 + 2048].rearrange("p (dt t) -> p dt t", dt=8),
                sc["o_buf"][:].transpose([1, 0, 2])[:, :, c0:c0 + TC5])
        ld5 = bk.gdma(lr_ap, sc["l_buf"][:][cc])
        bk.W("vector", "gd", ld5)
        lrv = bk.vop("reciprocal", lr_ap, lr_ap, sem="vop")
        bk.W("tensor", "vop", lrv)
        onv = None
        for dt in range(8):
            bknum = 6 + dt % 2
            bk.bank_wait(bknum)
            bk.mm(banks[bknum][0:64, 0:256], ones_row[0:1, 0:64],
                  lr_ap[0:1, (2 * dt) * 256:(2 * dt + 1) * 256],
                  start=True, stop=True)
            bv = bk.mm(banks[bknum][64:128, 0:256], ones_row[0:1, 0:64],
                       lr_ap[0:1, (2 * dt + 1) * 256:(2 * dt + 2) * 256],
                       start=True, stop=True, sem="tmm")
            bk.W("vector", "tmm", bv)
            onv = bk.vop("tensor_tensor", t5(OT5, dt), t5(OT5, dt),
                         banks[bknum][:, 0:256], ALU.mult, sem="vop")
            bk.bank_set_free(bknum, "vop", onv)
        xqv = act_quant(lambda dt: t5(XN5, dt), 8, xq_t, IVB5)
        bk.W("tensor", "vop", xqv)
        gev = None
        for o in range(32):
            ap, idx = ring5.load(sc["wig"][:][o], 512, bf16=True)
            ring5.consume_wait(idx)
            bknum = o % 4
            bk.bank_wait(bknum)
            apb = ap.bitcast(BF16)
            for kc in range(8):
                v = bk.mm(banks[bknum][:, 0:256], apb[:, kc * P:(kc + 1) * P],
                          xq_t(kc), start=(kc == 0), stop=(kc == 7),
                          sem=("tmm" if kc == 7 else None))
            ring5.free(idx)
            bk.W("vector", "tmm", v)
            gev = bk.vop("scalar_tensor_tensor", t5(GP5, o), banks[bknum][:, 0:256],
                         dvc("qrs_g"), pool[:, IVB5:IVB5 + 256],
                         ALU.mult, ALU.mult, sem="vop")
            bk.bank_set_free(bknum, "vop", gev)
        bk.W("scalar", "vop", gev)
        sgv = None
        for o in range(32):
            sgv = bk.act(t5(GP5, o), t5(GP5, o), AF.Silu,
                         sem=("sop" if o == 31 else None))
        bk.W("vector", "sop", sgv)
        pv = None
        for o in range(32):
            ap, idx = ring5.load(sc["wiu"][:][o], 512, bf16=True)
            ring5.consume_wait(idx)
            bknum = o % 4
            bk.bank_wait(bknum)
            apb = ap.bitcast(BF16)
            for kc in range(8):
                v = bk.mm(banks[bknum][:, 0:256], apb[:, kc * P:(kc + 1) * P],
                          xq_t(kc), start=(kc == 0), stop=(kc == 7),
                          sem=("tmm" if kc == 7 else None))
            ring5.free(idx)
            bk.W("vector", "tmm", v)
            bk.vop("scalar_tensor_tensor", pool[:, UP5:UP5 + 256],
                   banks[bknum][:, 0:256], dvc("qrs_u"),
                   pool[:, IVB5:IVB5 + 256], ALU.mult, ALU.mult)
            pv = bk.vop("tensor_tensor", t5(GP5, o), t5(GP5, o),
                        pool[:, UP5:UP5 + 256], ALU.mult, sem="vop")
            bk.bank_set_free(bknum, "vop", pv)
        xq2v = act_quant(lambda dt: t5(GP5, dt), 32, xq2_t, IVB25)
        bk.W("tensor", "vop", xq2v)
        g2v = None
        for o in range(8):
            ap, idx = ring5.load(sc["wid"][:][o], 2048, bf16=True)
            ring5.consume_wait(idx)
            bknum = 4 + o % 2
            bk.bank_wait(bknum)
            apb = ap.bitcast(BF16)
            for kc in range(32):
                v = bk.mm(banks[bknum][:, 0:256], apb[:, kc * P:(kc + 1) * P],
                          xq2_t(kc), start=(kc == 0), stop=(kc == 31),
                          sem=("tmm" if kc == 31 else None))
            ring5.free(idx)
            bk.W("vector", "tmm", v)
            g2v = bk.vop("scalar_tensor_tensor", t5(XN5, o), banks[bknum][:, 0:256],
                         dvc("qrs_d"), pool[:, IVB25:IVB25 + 256],
                         ALU.mult, ALU.mult, sem="vop")
            bk.bank_set_free(bknum, "vop", g2v)
        bk.W("scalar", "vop", g2v)
        s2v = None
        for o in range(8):
            s2v = bk.act(t5(XN5, o), t5(XN5, o), AF.Silu,
                         sem=("sop" if o == 7 else None))
        for o in range(8):
            ap, idx = ring5.load(dr["wot_r"][:][o], 1024)
            ring5.consume_wait(idx)
            bknum = 4 + o % 2
            bk.bank_wait(bknum)
            for kc in range(8):
                v = bk.mm(banks[bknum][:, 0:256],
                          ap[:, kc * P:(kc + 1) * P],
                          t5(OT5, kc), start=(kc == 0), stop=(kc == 7),
                          sem=("tmm" if kc == 7 else None))
            ring5.free(idx)
            bk.W("vector", "tmm", v)
            ev = bk.vop("tensor_scalar", t5(S25, o), banks[bknum][:, 0:256],
                        vc("bo_at", o), None, ALU.add, sem="vop")
            bk.bank_set_free(bknum, "vop", ev)
        bk.W("vector", "sop", s2v)
        yv = None
        for dt in range(8):
            bk.vop("tensor_tensor", t5(S25, dt), t5(S25, dt), t5(XN5, dt),
                   ALU.mult)
            yv = bk.vop("tensor_tensor", t5(X15, dt), t5(X15, dt), t5(S25, dt),
                        ALU.add, sem=("vop" if dt == 7 else None))
        bk.W("tensor", "vop", yv)
        for ts in range(2):
            tok0 = ts * P
            for half in range(2):
                bknum = half
                bk.bank_wait(bknum)
                for k in range(4):
                    dt = half * 4 + k
                    bk.tr(banks[bknum][:, k * P:(k + 1) * P],
                          pool[:, X15 + dt * 256 + tok0:X15 + dt * 256 + tok0 + P],
                          ident[:], sem=("tmm" if k == 3 else None))
                tv = bk.cnt["tmm"]
                bk.W("scalar", "tmm", tv)
                bk.W("scalar", "gd", bk.cnt["gd"])
                cpv = bk.sop("copy", pool[:, YSB5 + half * 512:YSB5 + (half + 1) * 512],
                             banks[bknum][:], sem="sop")
                bk.bank_set_free(bknum, "sop", cpv)
            bk.W("gpsimd", "sop", bk.cnt["sop"])
            bk.gdma(y_out[:][c0 + tok0:c0 + tok0 + P, :],
                    pool[:, YSB5:YSB5 + 1024])

    bk.barrier(5)

    for name in bk.dbg:
        if name == "gfdbg":
            bk.gdma(dbg_outs[name][:], gfdbg_sb[:])
        else:
            bk.gdma(dbg_outs[name][:], sc[name][:])
    bk.W("gpsimd", "gd", bk.cnt["gd"])
    return bk, nc


def kernel(**inputs):
    shared = stage_shared(inputs)
    x = np.ascontiguousarray(np.asarray(inputs["L0_attn_x"], np.float32))
    bk, nc = build_program(dbg=())
    nc = bk.finalize()
    in_maps = []
    for b in range(8):
        m = dict(shared)
        m["x"] = np.ascontiguousarray(x[b])
        in_maps.append(m)
    res = run_bass_kernel_spmd(nc, in_maps, core_ids=list(range(8)))
    return np.stack([r["y"] for r in res.results])


# version 04
# speedup vs baseline: 1.2125x; 1.1240x over previous
import math
import numpy as np
from contextlib import ExitStack

import concourse.bass as bass
import concourse.mybir as mybir
from concourse.bass_utils import run_bass_kernel_spmd

F32 = mybir.dt.float32
BF16 = mybir.dt.bfloat16
I32 = mybir.dt.int32
U8 = mybir.dt.uint8
ALU = mybir.AluOpType
AF = mybir.ActivationFunctionType
AX = mybir.AxisListType

P = 128
T = 4096
D = 1024
HID = 2048
DI = 3072
NL = 4
TC = 512
NCH = T // TC
TC5 = 256
NCH5 = T // TC5
NH = 16
HD = 64
EPS = 1e-5
SQD = float(np.float32(math.sqrt(HID)))

ENGINES = ("sync", "gpsimd", "tensor", "vector", "scalar")
NWL = 6
SEMS = ("gd", "wfr", "tmm", "vop", "sop", "ph") + tuple(f"wl{j}" for j in range(NWL))


def vec_layout():
    lay, off = {}, 0

    def add(name, n):
        nonlocal off
        lay[name] = (off, n)
        off += n

    add("ones", 1)
    add("bp", 16)
    for i in range(NL):
        add(f"g1_{i}", 16)
        add(f"bhid_{i}", 24)
        add(f"bgate_{i}", 24)
        add(f"bo_{i}", 16)
        add(f"g2_{i}", 16)
    add("gfv", 16)
    add("bf", 8)
    add("cnb1", 8)
    add("cnb2", 4)
    add("cnb3", 2)
    add("cnb4", 1)
    add("gcb1", 4)
    add("gcb2", 2)
    add("gcb3", 1)
    add("ar", 1)
    add("cnw4", 2)
    add("gcw3", 2)
    add("bq", 16)
    add("bk", 16)
    add("bv", 16)
    add("bo_at", 8)
    add("c_eps", 1)
    return lay, off


def dvec_layout():
    lay, off = {}, 0

    def add(name, n):
        nonlocal off
        lay[name] = (off, n)
        off += n

    for i in range(NL):
        add(f"g1p1_{i}", 16)
        add(f"g2p1_{i}", 16)
        add(f"nbg_{i}", 24)
    add("gfp1", 16)
    add("cgA", 1)
    add("cgB", 1)
    add("qsc_g", 1)
    add("qsc_u", 1)
    add("qsc_d", 1)
    add("qrs_g", 1)
    add("qrs_u", 1)
    add("qrs_d", 1)
    return lay, off


def col_pack(v, ncols):
    return np.ascontiguousarray(v.reshape(ncols, P).T)


def stage_shared(I):
    f32 = np.float32
    out = {}

    def blk(w, njb):
        K, M = w.shape
        nk = K // P
        r = w.reshape(nk, P, njb, M // njb)
        return np.ascontiguousarray(r.transpose(2, 1, 0, 3).reshape(njb, P, nk * (M // njb)))

    out["wp_r"] = blk(np.asarray(I["L0_attn_Wp"], f32), 16)
    whg = np.asarray(I["L0_attn_Whg"], f32)
    whg_r = np.empty((NL, 24, P, 4096), f32)
    for i in range(NL):
        hidb = blk(whg[i][:, :DI], 24)
        gateb = blk(whg[i][:, DI:], 24)
        whg_r[i][:, :, :2048] = hidb
        whg_r[i][:, :, 2048:] = gateb
    out["whg_r"] = whg_r
    wo = np.asarray(I["L0_attn_Wo"], f32)
    out["wo_r"] = np.stack([blk(wo[i], 16) for i in range(NL)])
    out["wf_r"] = blk(np.asarray(I["L0_attn_Wf"], f32), 8)
    out["cn1_r"] = blk(np.asarray(I["L0_ffn_cn_w1"], f32), 8)
    out["cn2_r"] = blk(np.asarray(I["L0_ffn_cn_w2"], f32), 4)
    out["cn3_r"] = blk(np.asarray(I["L0_ffn_cn_w3"], f32), 2)
    out["gc1_r"] = blk(np.asarray(I["L0_ffn_gc_w1"], f32), 4)
    out["gc2_r"] = blk(np.asarray(I["L0_ffn_gc_w2"], f32), 2)

    def headT(w):
        return np.ascontiguousarray(
            w.transpose(2, 0, 1).reshape(8, P, NH * HD))

    out["wqt_r"] = headT(np.asarray(I["L1_attn_Wq"], f32))
    out["wkt_r"] = headT(np.asarray(I["L1_attn_Wk"], f32))
    out["wvt_r"] = headT(np.asarray(I["L1_attn_Wv"], f32))
    out["wot_r"] = blk(np.ascontiguousarray(np.asarray(I["L1_attn_Wo"], f32).T), 8)
    out["wg_r"] = blk(np.ascontiguousarray(np.asarray(I["L1_ffn_w_gate"], f32).T), 32)
    out["wu_r"] = blk(np.ascontiguousarray(np.asarray(I["L1_ffn_w_up"], f32).T), 32)
    out["wd_r"] = blk(np.ascontiguousarray(np.asarray(I["L1_ffn_w_down"], f32).T), 8)
    out["ident"] = np.eye(P, dtype=f32)

    lay, nv = vec_layout()
    vecs = np.zeros((P, nv), f32)

    def put(name, arr):
        o, n = lay[name]
        vecs[:, o:o + n] = arr

    put("ones", np.ones((P, 1), f32))
    put("bp", col_pack(np.asarray(I["L0_attn_bp"], f32), 16))
    for i in range(NL):
        put(f"g1_{i}", col_pack(np.asarray(I["L0_attn_g1"], f32)[i], 16))
        bhg = np.asarray(I["L0_attn_bhg"], f32)[i]
        put(f"bhid_{i}", col_pack(bhg[:DI], 24))
        put(f"bgate_{i}", col_pack(bhg[DI:], 24))
        put(f"bo_{i}", col_pack(np.asarray(I["L0_attn_bo"], f32)[i], 16))
        put(f"g2_{i}", col_pack(np.asarray(I["L0_attn_g2"], f32)[i], 16))
    put("gfv", col_pack(np.asarray(I["L0_attn_gf"], f32), 16))
    put("bf", col_pack(np.asarray(I["L0_attn_bf"], f32), 8))
    put("cnb1", col_pack(np.asarray(I["L0_ffn_cn_b1"], f32), 8))
    put("cnb2", col_pack(np.asarray(I["L0_ffn_cn_b2"], f32), 4))
    put("cnb3", col_pack(np.asarray(I["L0_ffn_cn_b3"], f32), 2))
    b4 = np.zeros((P, 1), f32); b4[0, 0] = np.asarray(I["L0_ffn_cn_b4"], f32)[0]
    put("cnb4", b4)
    put("gcb1", col_pack(np.asarray(I["L0_ffn_gc_b1"], f32), 4))
    put("gcb2", col_pack(np.asarray(I["L0_ffn_gc_b2"], f32), 2))
    g3 = np.zeros((P, 1), f32); g3[0, 0] = np.asarray(I["L0_ffn_gc_b3"], f32)[0]
    put("gcb3", g3)
    arr = np.zeros((P, 1), f32); arr[:, 0] = np.asarray(I["L0_ffn_adaptive_range"], f32)[0]
    put("ar", arr)
    put("cnw4", col_pack(np.asarray(I["L0_ffn_cn_w4"], f32)[:, 0], 2))
    put("gcw3", col_pack(np.asarray(I["L0_ffn_gc_w3"], f32)[:, 0], 2))
    for nm, key in (("bq", "L1_attn_bq"), ("bk", "L1_attn_bk"), ("bv", "L1_attn_bv")):
        a = np.zeros((P, NH), f32)
        a[0:HD, :] = np.asarray(I[key], f32).T
        put(nm, a)
    put("bo_at", col_pack(np.asarray(I["L1_attn_bo"], f32), 8))
    put("c_eps", np.full((P, 1), 1e-5, f32))
    out["vecs"] = vecs

    rowvecs = np.zeros((1, 1152), f32)
    rowvecs[0, 0:P] = 1.0
    rowvecs[0, P:P + D] = np.asarray(I["L1_attn_bv"], f32).reshape(-1)
    out["rowvecs"] = rowvecs
    return out


class BK:
    def __init__(self, dbg=()):
        self.nc = bass.Bass("TRN2", target_bir_lowering=False)
        self.q = {e: [] for e in ENGINES}
        self.cnt = {s: 0 for s in SEMS}
        self.maxwait = {s: 0 for s in SEMS}
        self.sem = {}
        self.dbg = set(dbg)
        self.es = ExitStack()
        self.banks = None
        self.bank_free = {}

    def E(self, eng, f, sem=None, k=1):
        if sem is None:
            self.q[eng].append(f)
            return None
        self.cnt[sem] += k
        v = self.cnt[sem]
        name = sem

        def g(e, f=f, name=name, k=k):
            f(e).then_inc(self.sem[name], k)
        self.q[eng].append(g)
        return v

    def W(self, eng, sem, v):
        if v is None or v <= 0:
            return
        self.maxwait[sem] = max(self.maxwait[sem], v)

        def g(e, name=sem, v=v):
            e.wait_ge(self.sem[name], v)
        self.q[eng].append(g)

    def dma(self, eng, out_ap, in_ap, sem):
        return self.E(eng, lambda e, o=out_ap, i=in_ap: e.dma_start(out=o, in_=i),
                      sem=sem, k=16)

    def gdma(self, out_ap, in_ap):
        v = self.dma("gpsimd", out_ap, in_ap, "gd")
        self.W("gpsimd", "gd", v)
        return v

    def mm(self, out, lhsT, rhs, start, stop, sem=None):
        return self.E("tensor",
                      lambda e, o=out, a=lhsT, b=rhs, s=start, p=stop:
                      e.matmul(o, a, b, start=s, stop=p), sem=sem)

    def tr(self, out, in_, ident, sem=None):
        return self.E("tensor",
                      lambda e, o=out, i=in_, d=ident: e.transpose(o, i, d),
                      sem=sem)

    def act(self, out, in_, func, bias=0.0, scale=1.0, accum=None, sem=None):
        def f(e, o=out, i=in_, fn=func, b=bias, s=scale, a=accum):
            return e.activation(o, i, fn, bias=b, scale=s, accum_out=a)
        v = self.E("scalar", f, sem=sem)
        self.q["scalar"].append(lambda e: e.drain())
        return v

    def vop(self, meth, *args, sem=None, **kw):
        def f(e, m=meth, aa=args, kk=kw):
            return getattr(e, m)(*aa, **kk)
        v = self.E("vector", f, sem=sem)
        self.q["vector"].append(lambda e: e.drain())
        return v

    def sop(self, meth, *args, sem=None, **kw):
        def f(e, m=meth, aa=args, kk=kw):
            return getattr(e, m)(*aa, **kk)
        v = self.E("scalar", f, sem=sem)
        self.q["scalar"].append(lambda e: e.drain())
        return v

    def bank_wait(self, b):
        fw = self.bank_free.get(b)
        if fw is not None:
            self.W("tensor", fw[0], fw[1])
            self.bank_free[b] = None

    def bank_set_free(self, b, sem, val):
        self.bank_free[b] = (sem, val)

    def barrier(self, idx):
        for j in range(NWL):
            self.W("sync", f"wl{j}", self.cnt[f"wl{j}"])
        self.W("gpsimd", "gd", self.cnt["gd"])
        for e in ENGINES:
            self.E(e, lambda en: en.drain(), sem="ph")
        target = self.cnt["ph"]
        for e in ENGINES:
            self.W(e, "ph", target)
        self.bank_free = {}

    def finalize(self):
        for s in SEMS:
            assert self.maxwait[s] <= self.cnt[s], \
                f"sem {s}: wait {self.maxwait[s]} > incs {self.cnt[s]}"
        es = self.es
        for s in SEMS:
            self.sem[s] = es.enter_context(self.nc.semaphore(s))
        block = es.enter_context(self.nc.Block())
        with es:
            for eng in ENGINES:
                ops = self.q[eng]

                def body(e, ops=ops):
                    for f in ops:
                        f(e)
                getattr(block, eng)(body)
        return self.nc


class Ring:

    def __init__(self, bk, pool_ap, ncols, gate=None):
        self.bk = bk
        self.pool = pool_ap
        self.n = ncols
        self.pos = 0
        self.owner = np.full(ncols, -1, dtype=np.int64)
        self.blocks = []
        self.base = bk.cnt["wfr"]
        self.gate = gate
        self.gated = gate is None

    def load(self, dram_ap, ncols, bf16=False):
        bk = self.bk
        if not self.gated:
            bk.W("sync", self.gate[0], self.gate[1])
            self.gated = True
        if self.pos + ncols > self.n:
            self.pos = 0
        sl = slice(self.pos, self.pos + ncols)
        idx = len(self.blocks)
        prev = int(self.owner[sl].max())
        need = max(prev + 1, idx - NWL + 1)
        if need > 0:
            bk.W("sync", "wfr", self.base + need)
        self.owner[sl] = idx
        ap = self.pool[:, sl]
        sem = f"wl{idx % NWL}"
        v = bk.dma("sync", ap.bitcast(BF16) if bf16 else ap, dram_ap, sem)
        self.blocks.append((sem, v))
        self.pos += ncols
        return ap, idx

    def consume_wait(self, idx):
        sem, v = self.blocks[idx]
        self.bk.W("tensor", sem, v)

    def free(self, idx):
        v = self.bk.E("tensor", lambda e: e.drain(), sem="wfr")
        assert v == self.base + idx + 1, (v, self.base, idx)


def build_program(dbg=()):
    bk = BK(dbg)
    nc = bk.nc
    es = bk.es

    x_in = nc.declare_dram_parameter("x", [T, D], F32, isOutput=False)
    y_out = nc.declare_dram_parameter("y", [T, D], F32, isOutput=True)
    dr = {}
    for name, shape in (
        ("wp_r", (16, P, 1024)), ("whg_r", (NL, 24, P, 4096)),
        ("wo_r", (NL, 16, P, 3072)), ("wf_r", (8, P, 2048)),
        ("cn1_r", (8, P, 1024)), ("cn2_r", (4, P, 1024)), ("cn3_r", (2, P, 512)),
        ("gc1_r", (4, P, 1024)), ("gc2_r", (2, P, 512)),
        ("wqt_r", (8, P, 1024)), ("wkt_r", (8, P, 1024)), ("wvt_r", (8, P, 1024)),
        ("wot_r", (8, P, 1024)), ("wg_r", (32, P, 1024)), ("wu_r", (32, P, 1024)),
        ("wd_r", (8, P, 4096)), ("ident", (P, P)), ("vecs", (P, vec_layout()[1])),
        ("rowvecs", (1, 1152)),
    ):
        dr[name] = nc.declare_dram_parameter(name, list(shape), F32, isOutput=False)

    sc = {}
    for name, shape, dt in (
        ("xn_buf", (8, P, T), F32), ("xt_buf", (8, P, T), F32),
        ("s_buf", (8, P, T), F32), ("x1_buf", (8, P, T), F32),
        ("xn2_buf", (8, P, T), F32),
        ("qt_buf", (NH, HD, T), F32), ("kt_buf", (NH, HD, T), F32),
        ("v_buf", (32, P, D), F32), ("o_buf", (8, P, T), F32),
        ("l_buf", (NCH5, NH, TC5), F32),
        ("wig", (32, P, 1024), BF16), ("wiu", (32, P, 1024), BF16),
        ("wid", (8, P, 4096), BF16),
    ):
        sc[name] = nc.dram_tensor(name, list(shape), dt)

    dbg_outs = {}
    for name in bk.dbg:
        shape = {"xn_buf": (8, P, T), "xt_buf": (8, P, T), "s_buf": (8, P, T),
                 "x1_buf": (8, P, T), "xn2_buf": (8, P, T),
                 "qt_buf": (NH, HD, T), "kt_buf": (NH, HD, T),
                 "v_buf": (32, P, D), "o_buf": (8, P, T),
                 "wig": (32, P, 1024), "wiu": (32, P, 1024),
                 "wid": (8, P, 4096), "gfdbg": (P, 8)}[name]
        dt = BF16 if name in ("wig", "wiu", "wid") else F32
        dbg_outs[name] = nc.declare_dram_parameter(
            "dbg_" + name, list(shape), dt, isOutput=True)

    VL, NV = vec_layout()
    DL, ND = dvec_layout()
    vecs = es.enter_context(nc.sbuf_tensor("s_vecs", [P, NV], F32))
    dv = es.enter_context(nc.sbuf_tensor("s_dv", [P, ND], F32))
    rowv = es.enter_context(nc.sbuf_tensor("s_rowv", [1, 1152], F32))
    ident = es.enter_context(nc.sbuf_tensor("identt", [P, P], F32))
    carry = es.enter_context(nc.sbuf_tensor("s_carry", [P, NL * 24], F32))
    rows = es.enter_context(nc.sbuf_tensor("s_rows", [1, 2048], F32))
    gfdbg_sb = es.enter_context(nc.sbuf_tensor("s_gfdbg", [P, 8], F32))
    POOL = 43000
    pool = es.enter_context(nc.sbuf_tensor("s_pool", [P, POOL], F32))
    banks = [es.enter_context(nc.psum_tensor(f"bank{i}", [P, 512], F32))
             for i in range(8)]

    def vc(name, j0=0, n=1):
        o, nn = VL[name]
        assert j0 + n <= nn, name
        return vecs[:, o + j0:o + j0 + n]

    def dvc(name, j0=0, n=1):
        o, nn = DL[name]
        assert j0 + n <= nn, name
        return dv[:, o + j0:o + j0 + n]

    ones_col = vc("ones")
    ones_row = rowv[0:1, 0:P]

    def row(i):
        return rows[0:1, i * 512:(i + 1) * 512]

    v0 = bk.gdma(vecs[:], dr["vecs"][:])
    bk.gdma(rowv[:], dr["rowvecs"][:])
    v1 = bk.gdma(ident[:], dr["ident"][:])
    bk.W("vector", "gd", v1)
    for i in range(NL):
        bk.vop("tensor_scalar", dvc(f"g1p1_{i}", 0, 16), vc(f"g1_{i}", 0, 16),
               1.0, None, ALU.add)
        bk.vop("tensor_scalar", dvc(f"g2p1_{i}", 0, 16), vc(f"g2_{i}", 0, 16),
               1.0, None, ALU.add)
        bk.vop("tensor_scalar", dvc(f"nbg_{i}", 0, 24), vc(f"bgate_{i}", 0, 24),
               -1.0, None, ALU.mult)
    bk.vop("tensor_scalar", dvc("gfp1", 0, 16), vc("gfv", 0, 16), 1.0, None, ALU.add)
    bk.vop("memset", carry[:], 1.0, sem="vop")
    vop_pv = bk.cnt["vop"]
    bk.W("tensor", "vop", vop_pv)
    bk.W("tensor", "gd", v1)

    WT0, WT1, ACCC, TMPC = 0, 4096, 8192, 8193
    QQ5, R05, DI5, GI5, IQ5, WB5 = 8448, 12544, 16640, 20736, 24832, 28928
    for drname, nb, ncols, scname, rsname, wbuf in (
        ("wg_r", 32, 1024, "qsc_g", "qrs_g", "wig"),
        ("wu_r", 32, 1024, "qsc_u", "qrs_u", "wiu"),
        ("wd_r", 8, 4096, "qsc_d", "qrs_d", "wid"),
    ):
        nelem = nb * P * ncols
        slot_guard = [0, 0]
        for b in range(nb):
            s = b % 2
            wt = pool[:, (WT0 if s == 0 else WT1):(WT0 if s == 0 else WT1) + ncols]
            bk.W("gpsimd", "vop", slot_guard[s])
            lv = bk.gdma(wt, dr[drname][:][b])
            bk.W("vector", "gd", lv)
            if b == 0:
                rv = bk.vop("reduce_sum", pool[:, ACCC:ACCC + 1], wt, AX.X,
                            apply_absolute_value=True, sem="vop")
            else:
                bk.vop("reduce_sum", pool[:, TMPC:TMPC + 1], wt, AX.X,
                       apply_absolute_value=True)
                rv = bk.vop("tensor_tensor", pool[:, ACCC:ACCC + 1],
                            pool[:, ACCC:ACCC + 1], pool[:, TMPC:TMPC + 1],
                            ALU.add, sem="vop")
            slot_guard[s] = rv
        bk.W("tensor", "vop", rv)
        bk.bank_wait(0)
        mv = bk.mm(banks[0][0:1, 0:1], ones_col, pool[:, ACCC:ACCC + 1],
                   start=True, stop=True, sem="tmm")
        bk.W("vector", "tmm", mv)
        r3 = row(3)
        bk.vop("tensor_scalar", r3[0:1, 0:1], banks[0][0:1, 0:1],
               1.0 / nelem, None, ALU.mult, sem="vop")
        bk.bank_set_free(0, "vop", bk.cnt["vop"])
        bk.vop("tensor_scalar", r3[0:1, 0:1], r3[0:1, 0:1], 1e-5, None, ALU.max)
        bk.vop("reciprocal", r3[0:1, 1:2], r3[0:1, 0:1])
        sv = bk.vop("reciprocal", r3[0:1, 2:3], r3[0:1, 1:2], sem="vop")
        bk.W("tensor", "vop", sv)
        bk.bank_wait(1)
        bk.mm(banks[1][:, 0:1], ones_row, r3[0:1, 1:2], start=True, stop=True)
        bv = bk.mm(banks[1][:, 1:2], ones_row, r3[0:1, 2:3], start=True,
                   stop=True, sem="tmm")
        bk.W("vector", "tmm", bv)
        bk.vop("tensor_copy", dvc(scname), banks[1][:, 0:1])
        cv2 = bk.vop("tensor_copy", dvc(rsname), banks[1][:, 1:2], sem="vop")
        bk.bank_set_free(1, "vop", cv2)
        qq = pool[:, QQ5:QQ5 + ncols]
        r0 = pool[:, R05:R05 + ncols]
        di = pool[:, DI5:DI5 + ncols]
        gi = pool[:, GI5:GI5 + ncols]
        iq = pool[:, IQ5:IQ5 + ncols].bitcast(I32)
        wb = pool[:, WB5:WB5 + ncols // 2].bitcast(BF16)
        for b in range(nb):
            s = b % 2
            wt = pool[:, (WT0 if s == 0 else WT1):(WT0 if s == 0 else WT1) + ncols]
            bk.W("gpsimd", "vop", slot_guard[s])
            lv = bk.gdma(wt, dr[drname][:][b])
            bk.W("vector", "gd", lv)
            bk.vop("tensor_scalar", qq, wt, dvc(scname), None, ALU.mult,
                   sem="vop")
            slot_guard[s] = bk.cnt["vop"]
            bk.vop("tensor_copy", iq, qq)
            bk.vop("tensor_copy", r0, iq)
            bk.vop("tensor_tensor", di, qq, r0, ALU.subtract)
            bk.vop("tensor_scalar", gi, di, 0.5, None, ALU.is_gt)
            bk.vop("tensor_tensor", r0, r0, gi, ALU.add)
            bk.vop("tensor_scalar", gi, di, -0.5, None, ALU.is_lt)
            bk.vop("tensor_tensor", r0, r0, gi, ALU.subtract)
            bk.vop("tensor_scalar", r0, r0, 1.0, None, ALU.min)
            bk.vop("tensor_scalar", r0, r0, -1.0, None, ALU.max)
            bk.W("vector", "gd", bk.cnt["gd"])
            wv = bk.vop("tensor_copy", wb, r0, sem="vop")
            bk.W("gpsimd", "vop", wv)
            bk.gdma(sc[wbuf][:][b], wb)

    bk.barrier(-1)

    pa = 0

    def alloc(n):
        nonlocal pa
        ap = pool[:, pa:pa + n]
        pa += n
        return ap

    xt_b = [alloc(1024) for _ in range(2)]
    xn_b = alloc(1024)
    sq_col = alloc(1)
    sum_col = alloc(1)
    mean_col = alloc(1)
    ex2_col = alloc(1)
    var_col = alloc(1)
    sd_col = alloc(1)
    inv_col = alloc(1)
    tcol = alloc(1)
    sqs = alloc(1024)
    xnT = alloc(1024)
    xtT = alloc(1024)
    gccols = alloc(16)
    gc1_sb = alloc(4 * 1024)
    gc2_sb = alloc(2 * 512)

    g1v = bk.gdma(gc1_sb.rearrange("p (a b) -> p a b", a=4),
                  dr["gc1_r"][:].transpose([1, 0, 2]))
    g2v = bk.gdma(gc2_sb.rearrange("p (a b) -> p a b", a=2),
                  dr["gc2_r"][:].transpose([1, 0, 2]))

    last_store = 0
    last_xt_use = [(0, 0), (0, 0)]
    prev_mmv = 0
    for tt in range(32):
        buf = xt_b[tt % 2]
        bk.W("gpsimd", "vop", last_xt_use[tt % 2][0])
        bk.W("gpsimd", "tmm", last_xt_use[tt % 2][1])
        av = bk.gdma(buf, x_in[:][tt * P:(tt + 1) * P, :])
        bk.W("vector", "gd", av)
        bk.vop("scalar_tensor_tensor", sqs, buf, 1.0, buf, ALU.mult, ALU.mult,
               accum_out=sq_col)
        bk.vop("reduce_sum", sum_col, buf, AX.X)
        bk.vop("tensor_scalar", mean_col, sum_col, 1.0 / D, None, ALU.mult)
        bk.vop("tensor_scalar", ex2_col, sq_col, 1.0 / D, None, ALU.mult)
        bk.vop("tensor_tensor", tcol, mean_col, mean_col, ALU.mult)
        bk.vop("tensor_tensor", var_col, ex2_col, tcol, ALU.subtract, sem="vop")
        vv = bk.cnt["vop"]
        bk.W("scalar", "vop", vv)
        sv = bk.act(sd_col, var_col, AF.Sqrt, bias=vc("c_eps"), sem="sop")
        bk.W("vector", "sop", sv)
        bk.vop("reciprocal", inv_col, sd_col)
        bk.W("vector", "tmm", prev_mmv)
        bk.vop("tensor_scalar", xn_b, buf, mean_col, inv_col,
               ALU.subtract, ALU.mult, sem="vop")
        xn_ready = bk.cnt["vop"]

        bk.W("tensor", "vop", xn_ready)
        for half in range(2):
            b = banks[half]
            bk.bank_wait(half)
            for k in range(4):
                bk.tr(b[:, k * P:(k + 1) * P], xn_b[:, (half * 4 + k) * P:(half * 4 + k + 1) * P], ident[:])
            tv = bk.E("tensor", lambda e: e.drain(), sem="tmm")
            bk.W("scalar", "tmm", tv)
            cv = bk.sop("copy", xnT[:, half * 512:(half + 1) * 512], b[:], sem="sop")
            bk.bank_set_free(half, "sop", cv)
        for half in range(2):
            b = banks[2 + half]
            bk.bank_wait(2 + half)
            for k in range(4):
                bk.tr(b[:, k * P:(k + 1) * P], buf[:, (half * 4 + k) * P:(half * 4 + k + 1) * P], ident[:])
            tv = bk.E("tensor", lambda e: e.drain(), sem="tmm")
            bk.W("scalar", "tmm", tv)
            cv = bk.sop("copy", xtT[:, half * 512:(half + 1) * 512], b[:], sem="sop")
            bk.bank_set_free(2 + half, "sop", cv)
        for half in range(2):
            bk.bank_wait(6 + half)
            bk.mm(banks[6 + half][0:1, :], ones_col, xn_b[:, half * 512:(half + 1) * 512],
                  start=(tt == 0), stop=(tt == 31))
        mmv = bk.E("tensor", lambda e: e.drain(), sem="tmm")
        prev_mmv = mmv
        last_xt_use[tt % 2] = (xn_ready, mmv)

        sv_ = bk.cnt["sop"]
        bk.W("gpsimd", "sop", sv_)
        bk.gdma(sc["xn_buf"][:].transpose([1, 0, 2])[:, :, tt * P:(tt + 1) * P],
                xnT.rearrange("p (dt t) -> p dt t", dt=8))
        last_store = bk.gdma(
            sc["xt_buf"][:].transpose([1, 0, 2])[:, :, tt * P:(tt + 1) * P],
            xtT.rearrange("p (dt t) -> p dt t", dt=8))
        bk.W("scalar", "gd", last_store)

    bk.W("tensor", "gd", g2v)
    ctx = gccols[:, 0:8]
    bk.W("scalar", "tmm", bk.cnt["tmm"])
    e1 = bk.act(row(0), banks[6][0:1, :], AF.Copy, scale=1.0 / T, sem="sop")
    e2 = bk.act(row(1), banks[7][0:1, :], AF.Copy, scale=1.0 / T, sem="sop")
    bk.bank_set_free(6, "sop", e1)
    bk.bank_set_free(7, "sop", e2)
    bk.W("tensor", "sop", e2)
    for k in range(8):
        b = banks[k % 2]
        src = row(k // 4)[0:1, (k % 4) * 128:((k % 4) + 1) * 128]
        bk.bank_wait(k % 2)
        bk.tr(b[:, 0:1], src, ident[0:1, 0:1])
        tv = bk.E("tensor", lambda e: e.drain(), sem="tmm")
        bk.W("vector", "tmm", tv)
        fv = bk.vop("tensor_copy", ctx[:, k:k + 1], b[:, 0:1], sem="vop")
        bk.bank_set_free(k % 2, "vop", fv)
    h1 = gccols[:, 8:12]
    bk.W("tensor", "vop", bk.cnt["vop"])
    for o in range(4):
        bk.bank_wait(4)
        for kc in range(8):
            bk.mm(banks[4][:, 0:1], gc1_sb[:, o * 1024 + kc * P: o * 1024 + (kc + 1) * P],
                  ctx[:, kc:kc + 1], start=(kc == 0), stop=(kc == 7))
        tv = bk.E("tensor", lambda e: e.drain(), sem="tmm")
        bk.W("scalar", "tmm", tv)
        av = bk.act(h1[:, o:o + 1], banks[4][:, 0:1], AF.Gelu,
                    bias=vc("gcb1", o), sem="sop")
        bk.bank_set_free(4, "sop", av)
        bk.W("tensor", "sop", av)
    h2 = gccols[:, 12:14]
    for o in range(2):
        bk.bank_wait(5)
        for kc in range(4):
            bk.mm(banks[5][:, 0:1], gc2_sb[:, o * 512 + kc * P: o * 512 + (kc + 1) * P],
                  h1[:, kc:kc + 1], start=(kc == 0), stop=(kc == 3))
        tv = bk.E("tensor", lambda e: e.drain(), sem="tmm")
        bk.W("scalar", "tmm", tv)
        av = bk.act(h2[:, o:o + 1], banks[5][:, 0:1], AF.Gelu,
                    bias=vc("gcb2", o), sem="sop")
        bk.bank_set_free(5, "sop", av)
        bk.W("tensor", "sop", av)
    bk.bank_wait(4)
    for kc in range(2):
        bk.mm(banks[4][0:1, 0:1], vc("gcw3", kc), h2[:, kc:kc + 1],
              start=(kc == 0), stop=(kc == 1))
    tv = bk.E("tensor", lambda e: e.drain(), sem="tmm")
    bk.W("scalar", "tmm", tv)
    gfs = gccols[:, 14:15]
    av = bk.act(gfs[0:1, :], banks[4][0:1, 0:1], AF.Sigmoid,
                bias=vc("gcb3")[0:1, :], sem="sop")
    bk.bank_set_free(4, "sop", av)
    bk.W("tensor", "sop", av)
    bk.bank_wait(5)
    bk.mm(banks[5][:, 0:1], ones_row, gfs[0:1, :], start=True, stop=True)
    tv = bk.E("tensor", lambda e: e.drain(), sem="tmm")
    bk.W("vector", "tmm", tv)
    gf_col = gccols[:, 15:16]
    bk.vop("tensor_copy", gf_col, banks[5][:, 0:1], sem="vop")
    bk.bank_set_free(5, "vop", bk.cnt["vop"])
    bk.vop("tensor_scalar", dvc("cgA"), vc("ar"), 0.6, None, ALU.mult)
    bk.vop("tensor_tensor", tcol, vc("ar"), gf_col, ALU.mult)
    bk.vop("tensor_scalar", tcol, tcol, 1.4, None, ALU.mult)
    bk.vop("tensor_tensor", tcol, tcol, vc("ar"), ALU.subtract)
    bk.vop("tensor_scalar", dvc("cgB"), tcol, 0.1, None, ALU.add, sem="vop")
    if "gfdbg" in bk.dbg:
        bk.vop("memset", gfdbg_sb[:], 0.0)
        bk.vop("tensor_copy", gfdbg_sb[:, 0:1], gf_col)
        bk.vop("tensor_copy", gfdbg_sb[:, 1:2], dvc("cgA"))
        bk.vop("tensor_copy", gfdbg_sb[:, 2:3], dvc("cgB"))
        bk.vop("tensor_copy", gfdbg_sb[:, 3:4], ctx[:, 0:1], sem="vop")

    bk.barrier(0)

    def xn_t(kc):
        return pool[:, kc * 512:(kc + 1) * 512]

    def hn_t(dt):
        return pool[:, dt * 512:(dt + 1) * 512]

    def h_t(o):
        return pool[:, 8192 + o * 512:8192 + (o + 1) * 512]

    def scano(j):
        return pool[:, 16384 + j * 513:16384 + (j + 1) * 513]

    def cvc(s):
        return pool[:, 28696 + s * 1026:28696 + s * 1026 + 513]

    def cvv(s):
        return pool[:, 28696 + s * 1026 + 513:28696 + s * 1026 + 1026]

    def zs_sh(s):
        return pool[:, 30748 + s * 1536:30748 + s * 1536 + 512]

    def zs_hb(s):
        return pool[:, 30748 + s * 1536 + 512:30748 + s * 1536 + 1024]

    def zs_gm(s):
        return pool[:, 30748 + s * 1536 + 1024:30748 + s * 1536 + 1536]

    def sq_t(s):
        return zs_sh(s)

    s_col_ap = pool[:, 33820:34332]
    ring = Ring(bk, pool[:, 34332:43000], 8668)

    bk.vop("memset", cvc(0)[:, 0:1], 0.0)
    bk.vop("memset", cvv(0)[:, 0:1], 1.0)
    bk.vop("memset", cvc(1)[:, 0:1], 0.0)
    bk.vop("memset", cvv(1)[:, 0:1], 1.0, sem="vop")

    def rms_stats(tiles_f, ntiles, zs_guards=(0, 0)):
        sq_guard = [None, None]
        mmv = None
        for dt in range(ntiles):
            s = dt % 2
            if sq_guard[s] is not None:
                bk.W("scalar", "tmm", sq_guard[s])
            if dt < 2:
                bk.W("scalar", "vop", zs_guards[s])
            sv = bk.act(sq_t(s), tiles_f(dt), AF.Square, sem="sop")
            bk.W("tensor", "sop", sv)
            if dt == 0:
                bk.bank_wait(6)
            mmv = bk.mm(banks[6][0:1, :], ones_col, sq_t(s),
                        start=(dt == 0), stop=(dt == ntiles - 1), sem="tmm")
            sq_guard[s] = mmv
        bk.W("scalar", "tmm", mmv)
        nv = bk.act(row(0), banks[6][0:1, :], AF.Sqrt, sem="sop")
        bk.bank_set_free(6, "sop", nv)
        bk.W("vector", "sop", nv)
        bk.vop("tensor_scalar", row(0), row(0), 1e-12, None, ALU.max)
        bk.vop("reciprocal", row(1), row(0))
        iv = bk.vop("tensor_scalar", row(1), row(1), SQD, None, ALU.mult, sem="vop")
        bk.W("tensor", "vop", iv)
        bk.bank_wait(7)
        bcv = bk.mm(banks[7][:, :], ones_row, row(1), start=True, stop=True,
                    sem="tmm")
        return bcv

    ch_guard_sop = 0
    ch_guard_zs = [0, 0]
    ch_guard_cv = [0, 0]
    scan_val = 0
    wo_mm_last = 0
    whg_mm_last = 0
    wf_mm_last = 0
    sstore_val = 0
    for c in range(NCH):
        c0 = c * TC
        bk.W("gpsimd", "tmm", wf_mm_last)
        xnv = bk.gdma(pool[:, 0:4096].rearrange("p (dt t) -> p dt t", dt=8),
                      sc["xn_buf"][:].transpose([1, 0, 2])[:, :, c0:c0 + TC])
        bk.W("tensor", "gd", xnv)
        wp_evac = None
        for o in range(16):
            ap, idx = ring.load(dr["wp_r"][:][o], 1024)
            ring.consume_wait(idx)
            bknum = 4 + o % 2
            bk.bank_wait(bknum)
            for kc in range(8):
                v = bk.mm(banks[bknum][:], ap[:, kc * P:(kc + 1) * P], xn_t(kc),
                          start=(kc == 0), stop=(kc == 7),
                          sem=("tmm" if kc == 7 else None))
            ring.free(idx)
            bk.W("vector", "tmm", v)
            if c > 0 and o == 0:
                bk.W("vector", "sop", ch_guard_sop)
            wp_evac = bk.vop("tensor_scalar", h_t(o), banks[bknum][:],
                             vc("bp", o), None, ALU.add, sem="vop")
            bk.bank_set_free(bknum, "vop", wp_evac)

        for i in range(NL):
            bk.W("scalar", "vop", bk.cnt["vop"])
            bcv = rms_stats(h_t, 16, zs_guards=tuple(ch_guard_zs))
            bk.W("vector", "tmm", bcv)
            if i > 0:
                bk.W("vector", "tmm", whg_mm_last)
            hnv = None
            for dt in range(16):
                hnv = bk.vop("scalar_tensor_tensor", hn_t(dt), h_t(dt),
                             dvc(f"g1p1_{i}", dt), banks[7][:],
                             ALU.mult, ALU.mult,
                             sem=("vop" if dt == 15 else None))
            bk.bank_set_free(7, "vop", hnv)
            bk.W("tensor", "vop", hnv)

            for j in range(24):
                ap, idx = ring.load(dr["whg_r"][:][i, j], 4096)
                ring.consume_wait(idx)
                s = j % 2
                bh, bg = (0, 1) if s == 0 else (2, 3)
                bk.bank_wait(bh)
                bk.bank_wait(bg)
                for kc in range(16):
                    bk.mm(banks[bh][:], ap[:, kc * P:(kc + 1) * P], hn_t(kc),
                          start=(kc == 0), stop=(kc == 15))
                for kc in range(16):
                    v = bk.mm(banks[bg][:], ap[:, 2048 + kc * P:2048 + (kc + 1) * P],
                              hn_t(kc), start=(kc == 0), stop=(kc == 15),
                              sem=("tmm" if kc == 15 else None))
                whg_mm_last = v
                ring.free(idx)
                bk.W("scalar", "tmm", v)
                bk.W("scalar", "vop", ch_guard_cv[s])
                cav = bk.act(cvc(s)[:, 1:513], banks[bg][:], AF.Sigmoid,
                             scale=-1.0, bias=dvc(f"nbg_{i}", j), sem="sop")
                bk.bank_set_free(bg, "sop", cav)
                bk.W("scalar", "vop", ch_guard_zs[s])
                shv = bk.act(zs_sh(s), banks[bh][:], AF.Sigmoid,
                             bias=vc(f"bhid_{i}", j), sem="sop")
                bk.W("vector", "sop", shv)
                hbv = bk.vop("tensor_scalar", zs_hb(s), banks[bh][:],
                             vc(f"bhid_{i}", j), None, ALU.add, sem="vop")
                bk.bank_set_free(bh, "vop", hbv)
                bk.vop("tensor_scalar", zs_gm(s), zs_sh(s), 0.5, None, ALU.min)
                bk.vop("scalar_tensor_tensor", zs_gm(s), zs_hb(s), 0.0,
                       zs_gm(s), ALU.max, ALU.add)
                bk.vop("tensor_scalar", zs_sh(s), cvc(s)[:, 1:513], -1.0, 1.0,
                       ALU.mult, ALU.add)
                zv = bk.vop("tensor_tensor", cvv(s)[:, 1:513], zs_sh(s),
                            zs_gm(s), ALU.mult, sem="vop")
                ch_guard_zs[s] = zv
                bk.vop("tensor_copy", cvv(s)[:, 0:1], carry[:, i * 24 + j:i * 24 + j + 1])
                if j == 0:
                    bk.W("vector", "tmm", wo_mm_last)
                bk.vop("tensor_tensor_scan", scano(j), cvc(s), cvv(s), 0.0,
                       ALU.mult, ALU.add)
                scan_val = bk.vop("tensor_copy",
                                  carry[:, i * 24 + j:i * 24 + j + 1],
                                  scano(j)[:, 512:513], sem="vop")
                ch_guard_cv[s] = scan_val

            bk.W("tensor", "vop", scan_val)
            for o in range(16):
                ap, idx = ring.load(dr["wo_r"][:][i, o], 3072)
                ring.consume_wait(idx)
                bknum = 4 + o % 2
                bk.bank_wait(bknum)
                for kc in range(24):
                    v = bk.mm(banks[bknum][:], ap[:, kc * P:(kc + 1) * P],
                              scano(kc)[:, 1:513], start=(kc == 0),
                              stop=(kc == 23), sem=("tmm" if kc == 23 else None))
                wo_mm_last = v
                ring.free(idx)
                bk.W("vector", "tmm", v)
                ev = bk.vop("scalar_tensor_tensor", h_t(o), banks[bknum][:],
                            vc(f"bo_{i}", o), h_t(o), ALU.add, ALU.add,
                            sem="vop")
                bk.bank_set_free(bknum, "vop", ev)

            bk.W("scalar", "vop", bk.cnt["vop"])
            bcv = rms_stats(h_t, 16, zs_guards=tuple(ch_guard_zs))
            ch_guard_sop = bk.cnt["sop"]
            bk.W("vector", "tmm", bcv)
            upd = None
            for dt in range(16):
                bk.vop("scalar_tensor_tensor", zs_gm(dt % 2), h_t(dt),
                       dvc(f"g2p1_{i}", dt), banks[7][:], ALU.mult, ALU.mult)
                upd = bk.vop("tensor_tensor", h_t(dt), h_t(dt), zs_gm(dt % 2),
                             ALU.add, sem=("vop" if dt == 15 else None))
            bk.bank_set_free(7, "vop", upd)
            ch_guard_zs = [upd, upd]

        bk.W("scalar", "vop", bk.cnt["vop"])
        bcv = rms_stats(h_t, 16, zs_guards=tuple(ch_guard_zs))
        ch_guard_sop = bk.cnt["sop"]
        bk.W("vector", "tmm", bcv)
        bk.W("vector", "tmm", whg_mm_last)
        hnv = None
        for dt in range(16):
            hnv = bk.vop("scalar_tensor_tensor", hn_t(dt), h_t(dt),
                         dvc("gfp1", dt), banks[7][:], ALU.mult, ALU.mult,
                         sem=("vop" if dt == 15 else None))
        bk.bank_set_free(7, "vop", hnv)
        bk.W("tensor", "vop", hnv)
        for o in range(8):
            ap, idx = ring.load(dr["wf_r"][:][o], 2048)
            ring.consume_wait(idx)
            bknum = 4 + o % 2
            bk.bank_wait(bknum)
            for kc in range(16):
                v = bk.mm(banks[bknum][:], ap[:, kc * P:(kc + 1) * P], hn_t(kc),
                          start=(kc == 0), stop=(kc == 15),
                          sem=("tmm" if kc == 15 else None))
            wf_mm_last = v
            ring.free(idx)
            bk.W("vector", "tmm", v)
            bk.W("vector", "gd", sstore_val)
            sv = bk.vop("tensor_scalar", s_col_ap, banks[bknum][:], vc("bf", o),
                        None, ALU.add, sem="vop")
            bk.bank_set_free(bknum, "vop", sv)
            bk.W("gpsimd", "vop", sv)
            sstore_val = bk.gdma(sc["s_buf"][:][o, :, c0:c0 + TC], s_col_ap)

    bk.barrier(1)

    def p2t(base, i, w=512):
        return pool[:, base + i * w:base + (i + 1) * w]

    OA, OB, XT, SX, GG, H1, H2, H3 = 0, 4096, 8192, 12288, 16384, 20480, 24576, 26624
    CPL, SQ2, CN1, CN2, CN3 = 27648, 28160, 29184, 37376, 41472
    cn1v = bk.gdma(pool[:, CN1:CN1 + 8192].rearrange("p (a b) -> p a b", a=8),
                   dr["cn1_r"][:].transpose([1, 0, 2]))
    cn2v = bk.gdma(pool[:, CN2:CN2 + 4096].rearrange("p (a b) -> p a b", a=4),
                   dr["cn2_r"][:].transpose([1, 0, 2]))
    cn3v = bk.gdma(pool[:, CN3:CN3 + 1024].rearrange("p (a b) -> p a b", a=2),
                   dr["cn3_r"][:].transpose([1, 0, 2]))
    bk.W("tensor", "gd", cn3v)

    TW_SRC = [4, 5, 6, 7, 0, 1, 2, 3]
    TW_SGN = [1, 1, -1, -1, -1, -1, 1, 1]

    for c in range(NCH):
        c0 = c * TC
        bk.W("gpsimd", "tmm", bk.cnt["tmm"])
        bk.W("gpsimd", "vop", bk.cnt["vop"])
        bk.W("gpsimd", "sop", bk.cnt["sop"])
        ldv0 = bk.gdma(pool[:, OA:OA + 4096].rearrange("p (dt t) -> p dt t", dt=8),
                       sc["xn_buf"][:].transpose([1, 0, 2])[:, :, c0:c0 + TC])
        bk.gdma(pool[:, XT:XT + 4096].rearrange("p (dt t) -> p dt t", dt=8),
                sc["xt_buf"][:].transpose([1, 0, 2])[:, :, c0:c0 + TC])
        ldv = bk.gdma(pool[:, SX:SX + 4096].rearrange("p (dt t) -> p dt t", dt=8),
                      sc["s_buf"][:].transpose([1, 0, 2])[:, :, c0:c0 + TC])
        cur_base, nxt_base = OA, OB
        cur_ready_v, cur_ready_sem = ldv, "gd"
        for cyc in range(3):
            bk.W("tensor", cur_ready_sem, cur_ready_v)
            h1_last = None
            for o in range(8):
                bknum = o % 2
                bk.bank_wait(bknum)
                for kc in range(8):
                    v = bk.mm(banks[bknum][:], pool[:, CN1 + o * 1024 + kc * P:CN1 + o * 1024 + (kc + 1) * P],
                              p2t(cur_base, kc), start=(kc == 0), stop=(kc == 7),
                              sem=("tmm" if kc == 7 else None))
                bk.W("scalar", "tmm", v)
                h1_last = bk.act(p2t(H1, o), banks[bknum][:], AF.Gelu,
                                 bias=vc("cnb1", o), sem="sop")
                bk.bank_set_free(bknum, "sop", h1_last)
            bk.W("tensor", "sop", h1_last)
            h2_last = None
            for o in range(4):
                bk.bank_wait(2)
                for kc in range(8):
                    v = bk.mm(banks[2][:], pool[:, CN2 + o * 1024 + kc * P:CN2 + o * 1024 + (kc + 1) * P],
                              p2t(H1, kc), start=(kc == 0), stop=(kc == 7),
                              sem=("tmm" if kc == 7 else None))
                bk.W("scalar", "tmm", v)
                h2_last = bk.act(p2t(H2, o), banks[2][:], AF.Gelu,
                                 bias=vc("cnb2", o), sem="sop")
                bk.bank_set_free(2, "sop", h2_last)
            bk.W("tensor", "sop", h2_last)
            h3_last = None
            for o in range(2):
                bk.bank_wait(3)
                for kc in range(4):
                    v = bk.mm(banks[3][:], pool[:, CN3 + o * 512 + kc * P:CN3 + o * 512 + (kc + 1) * P],
                              p2t(H2, kc), start=(kc == 0), stop=(kc == 3),
                              sem=("tmm" if kc == 3 else None))
                bk.W("scalar", "tmm", v)
                h3_last = bk.act(p2t(H3, o), banks[3][:], AF.Gelu,
                                 bias=vc("cnb3", o), sem="sop")
                bk.bank_set_free(3, "sop", h3_last)
            bk.W("tensor", "sop", h3_last)
            bk.bank_wait(6)
            for kc in range(2):
                v = bk.mm(banks[6][0:1, :], vc("cnw4", kc), p2t(H3, kc),
                          start=(kc == 0), stop=(kc == 1),
                          sem=("tmm" if kc == 1 else None))
            bk.W("scalar", "tmm", v)
            tfv = bk.act(row(2), banks[6][0:1, :], AF.Sigmoid,
                         bias=vc("cnb4")[0:1, :], sem="sop")
            bk.bank_set_free(6, "sop", tfv)
            bk.W("tensor", "sop", tfv)
            bk.bank_wait(7)
            bcv = bk.mm(banks[7][:], ones_row, row(2), start=True, stop=True,
                        sem="tmm")
            bk.W("vector", "tmm", bcv)
            cplv = bk.vop("tensor_scalar", pool[:, CPL:CPL + 512], banks[7][:],
                          dvc("cgA"), dvc("cgB"), ALU.mult, ALU.add, sem="vop")
            bk.bank_set_free(7, "vop", cplv)
            upd = None
            for j in range(8):
                bk.vop("tensor_tensor", p2t(SQ2, 0), pool[:, CPL:CPL + 512],
                       p2t(cur_base, TW_SRC[j]), ALU.mult)
                upd = bk.vop("tensor_tensor", p2t(nxt_base, j),
                             p2t(cur_base, j), p2t(SQ2, 0),
                             ALU.add if TW_SGN[j] > 0 else ALU.subtract,
                             sem=("vop" if j == 7 else None))
            cur_base, nxt_base = nxt_base, cur_base
            cur_ready_v, cur_ready_sem = upd, "vop"
        bk.W("scalar", "vop", cur_ready_v)
        gv = None
        for dt in range(8):
            gv = bk.act(p2t(GG, dt), p2t(cur_base, dt), AF.Silu,
                        sem=("sop" if dt == 7 else None))
        bk.W("vector", "sop", gv)
        bk.W("vector", "gd", ldv)
        x1v = None
        for dt in range(8):
            bk.vop("tensor_tensor", p2t(GG, dt), p2t(GG, dt), p2t(SX, dt),
                   ALU.mult)
            x1v = bk.vop("tensor_tensor", p2t(XT, dt), p2t(XT, dt), p2t(GG, dt),
                         ALU.add, sem=("vop" if dt == 7 else None))
        bk.W("gpsimd", "vop", x1v)
        bk.gdma(sc["x1_buf"][:].transpose([1, 0, 2])[:, :, c0:c0 + TC],
                pool[:, XT:XT + 4096].rearrange("p (dt t) -> p dt t", dt=8))
        bk.W("tensor", "vop", x1v)
        bk.W("scalar", "vop", x1v)
        sq_guard = [None, None]
        for dt in range(8):
            s = dt % 2
            if sq_guard[s] is not None:
                bk.W("scalar", "tmm", sq_guard[s])
            sv = bk.act(p2t(SQ2, s), p2t(XT, dt), AF.Square, sem="sop")
            bk.W("tensor", "sop", sv)
            if dt == 0:
                bk.bank_wait(0)
                bk.bank_wait(1)
            bk.mm(banks[0][0:1, :], ones_col, p2t(XT, dt),
                  start=(dt == 0), stop=(dt == 7))
            mv = bk.mm(banks[1][0:1, :], ones_col, p2t(SQ2, s),
                       start=(dt == 0), stop=(dt == 7), sem="tmm")
            sq_guard[s] = mv
        bk.W("vector", "tmm", mv)
        bk.vop("tensor_scalar", row(0), banks[0][0:1, :], 1.0 / D, None, ALU.mult)
        bk.vop("tensor_scalar", row(3), banks[1][0:1, :], 1.0 / D, None, ALU.mult)
        bk.vop("tensor_tensor", row(1), row(0), row(0), ALU.mult)
        vv = bk.vop("tensor_tensor", row(3), row(3), row(1), ALU.subtract,
                    sem="vop")
        bk.bank_set_free(0, "vop", vv)
        bk.bank_set_free(1, "vop", vv)
        bk.W("scalar", "vop", vv)
        sdv = bk.act(row(3), row(3), AF.Sqrt, bias=vc("c_eps")[0:1, :], sem="sop")
        bk.W("vector", "sop", sdv)
        iv = bk.vop("reciprocal", row(1), row(3), sem="vop")
        bk.W("tensor", "vop", iv)
        bk.bank_wait(2)
        bk.bank_wait(3)
        bk.mm(banks[2][:], ones_row, row(0), start=True, stop=True)
        bcv = bk.mm(banks[3][:], ones_row, row(1), start=True, stop=True,
                    sem="tmm")
        bk.W("vector", "tmm", bcv)
        xnv2 = None
        for dt in range(8):
            bk.vop("tensor_tensor", p2t(SX, dt), p2t(XT, dt), banks[2][:],
                   ALU.subtract)
            xnv2 = bk.vop("tensor_tensor", p2t(SX, dt), p2t(SX, dt),
                          banks[3][:], ALU.mult,
                          sem=("vop" if dt == 7 else None))
        bk.bank_set_free(2, "vop", xnv2)
        bk.bank_set_free(3, "vop", xnv2)
        bk.W("gpsimd", "vop", xnv2)
        bk.gdma(sc["xn2_buf"][:].transpose([1, 0, 2])[:, :, c0:c0 + TC],
                pool[:, SX:SX + 4096].rearrange("p (dt t) -> p dt t", dt=8))

    bk.barrier(2)

    WQ, WK, WV, QEV, VEV = 4096, 12288, 20480, 28672, 29696
    wq_v = bk.gdma(pool[:, WQ:WQ + 8192].rearrange("p (a b) -> p a b", a=8),
                   dr["wqt_r"][:].transpose([1, 0, 2]))
    wk_v = bk.gdma(pool[:, WK:WK + 8192].rearrange("p (a b) -> p a b", a=8),
                   dr["wkt_r"][:].transpose([1, 0, 2]))
    wv_v = bk.gdma(pool[:, WV:WV + 8192].rearrange("p (a b) -> p a b", a=8),
                   dr["wvt_r"][:].transpose([1, 0, 2]))
    bk.W("tensor", "gd", wv_v)
    for c in range(NCH):
        c0 = c * TC
        bk.W("gpsimd", "tmm", bk.cnt["tmm"])
        bk.W("gpsimd", "vop", bk.cnt["vop"])
        ld3 = bk.gdma(pool[:, 0:4096].rearrange("p (dt t) -> p dt t", dt=8),
                      sc["xn2_buf"][:].transpose([1, 0, 2])[:, :, c0:c0 + TC])
        bk.W("tensor", "gd", ld3)
        for which, wbase, bname, buf in (("q", WQ, "bq", "qt_buf"),
                                         ("k", WK, "bk", "kt_buf")):
            for h in range(NH):
                bknum = h % 4
                bk.bank_wait(bknum)
                for kc in range(8):
                    v = bk.mm(banks[bknum][0:64, :],
                              pool[:, wbase + kc * 1024 + h * HD:wbase + kc * 1024 + (h + 1) * HD],
                              pool[:, kc * 512:(kc + 1) * 512],
                              start=(kc == 0), stop=(kc == 7),
                              sem=("tmm" if kc == 7 else None))
                bk.W("vector", "tmm", v)
                ev = pool[0:64, QEV + (h % 2) * 512:QEV + (h % 2) * 512 + 512]
                bk.W("vector", "gd", bk.cnt["gd"])
                evv = bk.vop("tensor_scalar", ev, banks[bknum][0:64, :],
                             vc(bname, h)[0:64, :], None, ALU.add, sem="vop")
                bk.bank_set_free(bknum, "vop", evv)
                bk.W("gpsimd", "vop", evv)
                bk.gdma(sc[buf][:][h, :, c0:c0 + TC], ev)
        for tsub in range(4):
            tok0 = tsub * P
            for half in range(2):
                bknum = 4 + (tsub * 2 + half) % 2
                bk.bank_wait(bknum)
                for kc in range(8):
                    bk.mm(banks[bknum][:],
                          pool[:, kc * 512 + tok0:kc * 512 + tok0 + P],
                          pool[:, WV + kc * 1024 + half * 512:WV + kc * 1024 + (half + 1) * 512],
                          start=(kc == 0), stop=False)
                v = bk.mm(banks[bknum][:], ones_row,
                          rowv[0:1, P + half * 512:P + (half + 1) * 512],
                          start=False, stop=True, sem="tmm")
                bk.W("vector", "tmm", v)
                ev = pool[:, VEV + half * 512:VEV + (half + 1) * 512]
                bk.W("vector", "gd", bk.cnt["gd"])
                evv = bk.vop("tensor_copy", ev, banks[bknum][:], sem="vop")
                bk.bank_set_free(bknum, "vop", evv)
                bk.W("gpsimd", "vop", evv)
                bk.gdma(sc["v_buf"][:][c * 4 + tsub, :, half * 512:(half + 1) * 512], ev)

    bk.barrier(3)

    QA, KA = 0, 4096
    VA, VB_, PT4, LR4, OSA, OSB4 = 8192, 10272, 12352, 14400, 15424, 15936
    vexts = [pool[:, VA:VA + 2080], pool[:, VB_:VB_ + 2080]]
    vx = bk.vop("memset", vexts[0], 1.0)
    vx = bk.vop("memset", vexts[1], 1.0, sem="vop")
    bk.W("gpsimd", "vop", vx)
    qk = [(pool[0:64, QA:QA + T], pool[0:64, KA:KA + T]),
          (pool[64:128, QA:QA + T], pool[64:128, KA:KA + T])]
    lrows = [pool[64:65, LR4:LR4 + 512], pool[64:65, LR4 + 512:LR4 + 1024]]
    osts = [pool[64:128, OSA:OSA + 512], pool[64:128, OSB4:OSB4 + 512]]

    def pt4(x, par):
        off = PT4 + (x * 2 + par) * 512
        return pool[:, off:off + 512]

    prev_pv = {}
    for hp in range(NH // 2):
        bk.W("gpsimd", "tmm", bk.cnt["tmm"])
        for x in range(2):
            h = hp * 2 + x
            bk.gdma(qk[x][0], sc["qt_buf"][:][h])
            bk.gdma(qk[x][1], sc["kt_buf"][:][h])
            vldv = bk.gdma(
                vexts[x].rearrange("p (tt c) -> p tt c", c=65)[:, :, 0:64],
                sc["v_buf"][:].transpose([1, 0, 2])[:, :, h * HD:(h + 1) * HD])
        bk.W("tensor", "gd", vldv)
        for sg in range(8):
            bk.bank_wait(4)
            bk.bank_wait(5)
            exv = {}
            for tb in range(33):
                if tb < 32:
                    b0, b1 = tb % 2, 2 + tb % 2
                    bk.bank_wait(b0)
                    bk.bank_wait(b1)
                    bk.E("tensor", lambda e, o=banks[b0][:],
                         a=qk[0][1][:, tb * P:(tb + 1) * P],
                         r=qk[0][0][:, sg * 512:(sg + 1) * 512]:
                         e.matmul(o, a, r, start=True, stop=True,
                                  tile_position=(0, 0)))
                    sv = bk.E("tensor", lambda e, o=banks[b1][:],
                              a=qk[1][1][:, tb * P:(tb + 1) * P],
                              r=qk[1][0][:, sg * 512:(sg + 1) * 512]:
                              e.matmul(o, a, r, start=True, stop=True,
                                       tile_position=(64, 0)), sem="tmm")
                    for x in range(2):
                        bk.W("scalar", "tmm", sv)
                        bk.W("scalar", "tmm", prev_pv.get((x, tb % 2), 0))
                        ev = bk.act(pt4(x, tb % 2), banks[(b0, b1)[x]][:],
                                    AF.Exp, scale=0.125, sem="sop")
                        bk.bank_set_free((b0, b1)[x], "sop", ev)
                        exv[(x, tb)] = ev
                if tb > 0:
                    tp = tb - 1
                    for x in range(2):
                        bk.W("tensor", "sop", exv.pop((x, tp)))
                        pv = bk.mm(banks[4 + x][0:65, :],
                                   vexts[x][:, tp * 65:(tp + 1) * 65],
                                   pt4(x, tp % 2), start=(tp == 0),
                                   stop=(tp == 31), sem="tmm")
                        prev_pv[(x, tp % 2)] = pv
            for x in range(2):
                h = hp * 2 + x
                bk.W("scalar", "tmm", prev_pv[(x, 1)])
                bk.W("scalar", "gd", bk.cnt["gd"])
                lcp = bk.sop("copy", lrows[x], banks[4 + x][64:65, :],
                             sem="sop")
                bk.W("vector", "tmm", prev_pv[(x, 1)])
                bk.W("vector", "gd", bk.cnt["gd"])
                ocp = bk.vop("tensor_copy", osts[x], banks[4 + x][0:64, :],
                             sem="vop")
                bk.bank_set_free(4 + x, "vop", ocp)
                bk.W("gpsimd", "sop", lcp)
                bk.W("gpsimd", "vop", ocp)
                bk.gdma(sc["l_buf"][:][2 * sg:2 * sg + 2, h, :], lrows[x])
                bk.gdma(sc["o_buf"][:][h // 2, (h % 2) * HD:(h % 2 + 1) * HD,
                                       sg * 512:(sg + 1) * 512], osts[x])

    bk.barrier(4)

    XN5, X15, OT5, S25, GP5, UP5 = 0, 2048, 4096, 6144, 8192, 16384
    XQ5, XQ25, AM5, SCB5, IVB5, IVB25 = 16896, 17920, 22016, 22272, 22528, 22784
    RND5, YSB5, LR5 = 23040, 24576, 26624
    lr_ap = pool[0:1, LR5:LR5 + 4096]
    ring5 = Ring(bk, pool[:, 30736:43000], 12264, gate=("ph", bk.cnt["ph"]))

    def t5(base, i, w=256):
        return pool[:, base + i * w:base + (i + 1) * w]

    def xq_t(dt):
        return pool[:, XQ5 + dt * 128:XQ5 + (dt + 1) * 128].bitcast(BF16)

    def xq2_t(dt):
        return pool[:, XQ25 + dt * 128:XQ25 + (dt + 1) * 128].bitcast(BF16)

    def act_quant(src_f, ntiles, xq_f, invb_off):
        acc = pool[:, AM5:AM5 + 256]
        tmp = pool[:, RND5:RND5 + 256]
        mcl = pool[:, AM5 + 256:AM5 + 258]
        for dt in range(ntiles):
            if dt == 0:
                bk.vop("scalar_tensor_tensor", acc, src_f(0), -1.0, src_f(0),
                       ALU.mult, ALU.max)
            else:
                bk.vop("scalar_tensor_tensor", tmp, src_f(dt), -1.0, src_f(dt),
                       ALU.mult, ALU.max)
                bk.vop("tensor_tensor", acc, acc, tmp, ALU.max)
        av = bk.E("vector", lambda e: e.drain(), sem="vop")
        bk.W("tensor", "vop", av)
        bk.bank_wait(6)
        for b in range(2):
            tv = bk.tr(banks[6][:, b * P:(b + 1) * P], acc[:, b * P:(b + 1) * P],
                       ident[:], sem=("tmm" if b == 1 else None))
        bk.W("vector", "tmm", tv)
        bk.vop("reduce_max", mcl[:, 0:1], banks[6][:, 0:P], AX.X)
        mv = bk.vop("reduce_max", mcl[:, 1:2], banks[6][:, P:2 * P], AX.X,
                    sem="vop")
        bk.bank_set_free(6, "vop", mv)
        bk.vop("tensor_scalar", mcl[:, 0:2], mcl[:, 0:2], 1e-5, None, ALU.max)
        bk.vop("reciprocal", tmp[:, 0:2], mcl[:, 0:2])
        bk.vop("tensor_scalar", tmp[:, 0:2], tmp[:, 0:2], 127.0, None, ALU.mult)
        rv = bk.vop("tensor_scalar", mcl[:, 0:2], mcl[:, 0:2], 1.0 / 127.0,
                    None, ALU.mult, sem="vop")
        bk.W("tensor", "vop", rv)
        bk.bank_wait(6)
        for b in range(2):
            bk.tr(banks[6][0:1, b * P:(b + 1) * P], tmp[:, b:b + 1], ident[:])
        for b in range(2):
            tv = bk.tr(banks[6][0:1, 256 + b * P:256 + (b + 1) * P],
                       mcl[:, b:b + 1], ident[:],
                       sem=("tmm" if b == 1 else None))
        bk.W("scalar", "tmm", tv)
        rcv = bk.sop("copy", row(2)[0:1, 0:512], banks[6][0:1, 0:512],
                     sem="sop")
        bk.bank_set_free(6, "sop", rcv)
        bk.W("tensor", "sop", rcv)
        bk.bank_wait(7)
        bk.mm(banks[7][:, 0:256], ones_row, row(2)[0:1, 0:256],
              start=True, stop=True)
        bv = bk.mm(banks[7][:, 256:512], ones_row, row(2)[0:1, 256:512],
                   start=True, stop=True, sem="tmm")
        bk.W("vector", "tmm", bv)
        bk.vop("tensor_copy", pool[:, SCB5:SCB5 + 256], banks[7][:, 0:256])
        cbv = bk.vop("tensor_copy", pool[:, invb_off:invb_off + 256],
                     banks[7][:, 256:512], sem="vop")
        bk.bank_set_free(7, "vop", cbv)
        qq = pool[:, RND5 + 256:RND5 + 512]
        r0 = pool[:, RND5 + 512:RND5 + 768]
        di = pool[:, RND5 + 768:RND5 + 1024]
        gi = pool[:, RND5 + 1024:RND5 + 1280]
        iq = pool[:, RND5 + 1280:RND5 + 1536].bitcast(I32)
        lastv = None
        for dt in range(ntiles):
            bk.vop("tensor_tensor", qq, src_f(dt), pool[:, SCB5:SCB5 + 256],
                   ALU.mult)
            bk.vop("tensor_copy", iq, qq)
            bk.vop("tensor_copy", r0, iq)
            bk.vop("tensor_tensor", di, qq, r0, ALU.subtract)
            bk.vop("tensor_scalar", gi, di, 0.5, None, ALU.is_gt)
            bk.vop("tensor_tensor", r0, r0, gi, ALU.add)
            bk.vop("tensor_scalar", gi, di, -0.5, None, ALU.is_lt)
            bk.vop("tensor_tensor", r0, r0, gi, ALU.subtract)
            lastv = bk.vop("tensor_copy", xq_f(dt), r0,
                           sem=("vop" if dt == ntiles - 1 else None))
        return lastv

    for cc in range(NCH5):
        c0 = cc * TC5
        bk.W("gpsimd", "tmm", bk.cnt["tmm"])
        bk.W("gpsimd", "vop", bk.cnt["vop"])
        bk.W("gpsimd", "sop", bk.cnt["sop"])
        bk.W("vector", "tmm", bk.cnt["tmm"])
        bk.gdma(pool[:, XN5:XN5 + 2048].rearrange("p (dt t) -> p dt t", dt=8),
                sc["xn2_buf"][:].transpose([1, 0, 2])[:, :, c0:c0 + TC5])
        bk.gdma(pool[:, X15:X15 + 2048].rearrange("p (dt t) -> p dt t", dt=8),
                sc["x1_buf"][:].transpose([1, 0, 2])[:, :, c0:c0 + TC5])
        bk.gdma(pool[:, OT5:OT5 + 2048].rearrange("p (dt t) -> p dt t", dt=8),
                sc["o_buf"][:].transpose([1, 0, 2])[:, :, c0:c0 + TC5])
        ld5 = bk.gdma(lr_ap, sc["l_buf"][:][cc])
        bk.W("vector", "gd", ld5)
        lrv = bk.vop("reciprocal", lr_ap, lr_ap, sem="vop")
        bk.W("tensor", "vop", lrv)
        onv = None
        for dt in range(8):
            bknum = 6 + dt % 2
            bk.bank_wait(bknum)
            bk.mm(banks[bknum][0:64, 0:256], ones_row[0:1, 0:64],
                  lr_ap[0:1, (2 * dt) * 256:(2 * dt + 1) * 256],
                  start=True, stop=True)
            bv = bk.mm(banks[bknum][64:128, 0:256], ones_row[0:1, 0:64],
                       lr_ap[0:1, (2 * dt + 1) * 256:(2 * dt + 2) * 256],
                       start=True, stop=True, sem="tmm")
            bk.W("vector", "tmm", bv)
            onv = bk.vop("tensor_tensor", t5(OT5, dt), t5(OT5, dt),
                         banks[bknum][:, 0:256], ALU.mult, sem="vop")
            bk.bank_set_free(bknum, "vop", onv)
        xqv = act_quant(lambda dt: t5(XN5, dt), 8, xq_t, IVB5)
        bk.W("tensor", "vop", xqv)
        gev = None
        for o in range(32):
            ap, idx = ring5.load(sc["wig"][:][o], 512, bf16=True)
            ring5.consume_wait(idx)
            bknum = o % 4
            bk.bank_wait(bknum)
            apb = ap.bitcast(BF16)
            for kc in range(8):
                v = bk.mm(banks[bknum][:, 0:256], apb[:, kc * P:(kc + 1) * P],
                          xq_t(kc), start=(kc == 0), stop=(kc == 7),
                          sem=("tmm" if kc == 7 else None))
            ring5.free(idx)
            bk.W("vector", "tmm", v)
            gev = bk.vop("scalar_tensor_tensor", t5(GP5, o), banks[bknum][:, 0:256],
                         dvc("qrs_g"), pool[:, IVB5:IVB5 + 256],
                         ALU.mult, ALU.mult, sem="vop")
            bk.bank_set_free(bknum, "vop", gev)
        bk.W("scalar", "vop", gev)
        sgv = None
        for o in range(32):
            sgv = bk.act(t5(GP5, o), t5(GP5, o), AF.Silu,
                         sem=("sop" if o == 31 else None))
        bk.W("vector", "sop", sgv)
        pv = None
        for o in range(32):
            ap, idx = ring5.load(sc["wiu"][:][o], 512, bf16=True)
            ring5.consume_wait(idx)
            bknum = o % 4
            bk.bank_wait(bknum)
            apb = ap.bitcast(BF16)
            for kc in range(8):
                v = bk.mm(banks[bknum][:, 0:256], apb[:, kc * P:(kc + 1) * P],
                          xq_t(kc), start=(kc == 0), stop=(kc == 7),
                          sem=("tmm" if kc == 7 else None))
            ring5.free(idx)
            bk.W("vector", "tmm", v)
            bk.vop("scalar_tensor_tensor", pool[:, UP5:UP5 + 256],
                   banks[bknum][:, 0:256], dvc("qrs_u"),
                   pool[:, IVB5:IVB5 + 256], ALU.mult, ALU.mult)
            pv = bk.vop("tensor_tensor", t5(GP5, o), t5(GP5, o),
                        pool[:, UP5:UP5 + 256], ALU.mult, sem="vop")
            bk.bank_set_free(bknum, "vop", pv)
        xq2v = act_quant(lambda dt: t5(GP5, dt), 32, xq2_t, IVB25)
        bk.W("tensor", "vop", xq2v)
        g2v = None
        for o in range(8):
            ap, idx = ring5.load(sc["wid"][:][o], 2048, bf16=True)
            ring5.consume_wait(idx)
            bknum = 4 + o % 2
            bk.bank_wait(bknum)
            apb = ap.bitcast(BF16)
            for kc in range(32):
                v = bk.mm(banks[bknum][:, 0:256], apb[:, kc * P:(kc + 1) * P],
                          xq2_t(kc), start=(kc == 0), stop=(kc == 31),
                          sem=("tmm" if kc == 31 else None))
            ring5.free(idx)
            bk.W("vector", "tmm", v)
            g2v = bk.vop("scalar_tensor_tensor", t5(XN5, o), banks[bknum][:, 0:256],
                         dvc("qrs_d"), pool[:, IVB25:IVB25 + 256],
                         ALU.mult, ALU.mult, sem="vop")
            bk.bank_set_free(bknum, "vop", g2v)
        bk.W("scalar", "vop", g2v)
        s2v = None
        for o in range(8):
            s2v = bk.act(t5(XN5, o), t5(XN5, o), AF.Silu,
                         sem=("sop" if o == 7 else None))
        for o in range(8):
            ap, idx = ring5.load(dr["wot_r"][:][o], 1024)
            ring5.consume_wait(idx)
            bknum = 4 + o % 2
            bk.bank_wait(bknum)
            for kc in range(8):
                v = bk.mm(banks[bknum][:, 0:256],
                          ap[:, kc * P:(kc + 1) * P],
                          t5(OT5, kc), start=(kc == 0), stop=(kc == 7),
                          sem=("tmm" if kc == 7 else None))
            ring5.free(idx)
            bk.W("vector", "tmm", v)
            ev = bk.vop("tensor_scalar", t5(S25, o), banks[bknum][:, 0:256],
                        vc("bo_at", o), None, ALU.add, sem="vop")
            bk.bank_set_free(bknum, "vop", ev)
        bk.W("vector", "sop", s2v)
        yv = None
        for dt in range(8):
            bk.vop("tensor_tensor", t5(S25, dt), t5(S25, dt), t5(XN5, dt),
                   ALU.mult)
            yv = bk.vop("tensor_tensor", t5(X15, dt), t5(X15, dt), t5(S25, dt),
                        ALU.add, sem=("vop" if dt == 7 else None))
        bk.W("tensor", "vop", yv)
        for ts in range(2):
            tok0 = ts * P
            for half in range(2):
                bknum = half
                bk.bank_wait(bknum)
                for k in range(4):
                    dt = half * 4 + k
                    bk.tr(banks[bknum][:, k * P:(k + 1) * P],
                          pool[:, X15 + dt * 256 + tok0:X15 + dt * 256 + tok0 + P],
                          ident[:], sem=("tmm" if k == 3 else None))
                tv = bk.cnt["tmm"]
                bk.W("scalar", "tmm", tv)
                bk.W("scalar", "gd", bk.cnt["gd"])
                cpv = bk.sop("copy", pool[:, YSB5 + half * 512:YSB5 + (half + 1) * 512],
                             banks[bknum][:], sem="sop")
                bk.bank_set_free(bknum, "sop", cpv)
            bk.W("gpsimd", "sop", bk.cnt["sop"])
            bk.gdma(y_out[:][c0 + tok0:c0 + tok0 + P, :],
                    pool[:, YSB5:YSB5 + 1024])

    bk.barrier(5)

    for name in bk.dbg:
        if name == "gfdbg":
            bk.gdma(dbg_outs[name][:], gfdbg_sb[:])
        else:
            bk.gdma(dbg_outs[name][:], sc[name][:])
    bk.W("gpsimd", "gd", bk.cnt["gd"])
    return bk, nc


def kernel(**inputs):
    shared = stage_shared(inputs)
    x = np.ascontiguousarray(np.asarray(inputs["L0_attn_x"], np.float32))
    bk, nc = build_program(dbg=())
    nc = bk.finalize()
    in_maps = []
    for b in range(8):
        m = dict(shared)
        m["x"] = np.ascontiguousarray(x[b])
        in_maps.append(m)
    res = run_bass_kernel_spmd(nc, in_maps, core_ids=list(range(8)))
    return np.stack([r["y"] for r in res.results])


# version 06
# speedup vs baseline: 1.4426x; 1.1898x over previous
import math
import numpy as np
from contextlib import ExitStack

import concourse.bass as bass
import concourse.mybir as mybir
from concourse.bass_utils import run_bass_kernel_spmd

F32 = mybir.dt.float32
BF16 = mybir.dt.bfloat16
I32 = mybir.dt.int32
U8 = mybir.dt.uint8
ALU = mybir.AluOpType
AF = mybir.ActivationFunctionType
AX = mybir.AxisListType

P = 128
T = 4096
D = 1024
HID = 2048
DI = 3072
NL = 4
TC = 512
NCH = T // TC
TC5 = 256
NCH5 = T // TC5
NH = 16
HD = 64
EPS = 1e-5
SQD = float(np.float32(math.sqrt(HID)))

ENGINES = ("sync", "gpsimd", "tensor", "vector", "scalar")
NWL = 6
SEMS = ("gd", "wfr", "tmm", "vop", "sop", "ph") + tuple(f"wl{j}" for j in range(NWL))


def vec_layout():
    lay, off = {}, 0

    def add(name, n):
        nonlocal off
        lay[name] = (off, n)
        off += n

    add("ones", 1)
    add("bp", 16)
    for i in range(NL):
        add(f"g1_{i}", 16)
        add(f"bhid_{i}", 24)
        add(f"bgate_{i}", 24)
        add(f"bo_{i}", 16)
        add(f"g2_{i}", 16)
    add("gfv", 16)
    add("bf", 8)
    add("cnb1", 8)
    add("cnb2", 4)
    add("cnb3", 2)
    add("cnb4", 1)
    add("gcb1", 4)
    add("gcb2", 2)
    add("gcb3", 1)
    add("ar", 1)
    add("cnw4", 2)
    add("gcw3", 2)
    add("bq", 16)
    add("bk", 16)
    add("bv", 16)
    add("bo_at", 8)
    add("c_eps", 1)
    return lay, off


def dvec_layout():
    lay, off = {}, 0

    def add(name, n):
        nonlocal off
        lay[name] = (off, n)
        off += n

    for i in range(NL):
        add(f"g1p1_{i}", 16)
        add(f"g2p1_{i}", 16)
        add(f"nbg_{i}", 24)
    add("gfp1", 16)
    add("cgA", 1)
    add("cgB", 1)
    add("qsc_g", 1)
    add("qsc_u", 1)
    add("qsc_d", 1)
    add("qrs_g", 1)
    add("qrs_u", 1)
    add("qrs_d", 1)
    return lay, off


def col_pack(v, ncols):
    return np.ascontiguousarray(v.reshape(ncols, P).T)


def stage_shared(I):
    f32 = np.float32
    out = {}

    def blk(w, njb):
        K, M = w.shape
        nk = K // P
        r = w.reshape(nk, P, njb, M // njb)
        return np.ascontiguousarray(r.transpose(2, 1, 0, 3).reshape(njb, P, nk * (M // njb)))

    out["wp_r"] = blk(np.asarray(I["L0_attn_Wp"], f32), 16)
    whg = np.asarray(I["L0_attn_Whg"], f32)
    whg_r = np.empty((NL, 24, P, 4096), f32)
    for i in range(NL):
        hidb = blk(whg[i][:, :DI], 24)
        gateb = blk(whg[i][:, DI:], 24)
        whg_r[i][:, :, :2048] = hidb
        whg_r[i][:, :, 2048:] = gateb
    out["whg_r"] = whg_r
    wo = np.asarray(I["L0_attn_Wo"], f32)
    out["wo_r"] = np.stack([blk(wo[i], 16) for i in range(NL)])
    out["wf_r"] = blk(np.asarray(I["L0_attn_Wf"], f32), 8)
    out["cn1_r"] = blk(np.asarray(I["L0_ffn_cn_w1"], f32), 8)
    out["cn2_r"] = blk(np.asarray(I["L0_ffn_cn_w2"], f32), 4)
    out["cn3_r"] = blk(np.asarray(I["L0_ffn_cn_w3"], f32), 2)
    out["gc1_r"] = blk(np.asarray(I["L0_ffn_gc_w1"], f32), 4)
    out["gc2_r"] = blk(np.asarray(I["L0_ffn_gc_w2"], f32), 2)

    def headT(w):
        return np.ascontiguousarray(
            w.transpose(2, 0, 1).reshape(8, P, NH * HD))

    out["wqt_r"] = headT(np.asarray(I["L1_attn_Wq"], f32))
    out["wkt_r"] = headT(np.asarray(I["L1_attn_Wk"], f32))
    out["wvt_r"] = headT(np.asarray(I["L1_attn_Wv"], f32))
    out["wot_r"] = blk(np.ascontiguousarray(np.asarray(I["L1_attn_Wo"], f32).T), 8)
    out["wg_r"] = blk(np.ascontiguousarray(np.asarray(I["L1_ffn_w_gate"], f32).T), 32)
    out["wu_r"] = blk(np.ascontiguousarray(np.asarray(I["L1_ffn_w_up"], f32).T), 32)
    out["wd_r"] = blk(np.ascontiguousarray(np.asarray(I["L1_ffn_w_down"], f32).T), 8)
    out["ident"] = np.eye(P, dtype=f32)

    lay, nv = vec_layout()
    vecs = np.zeros((P, nv), f32)

    def put(name, arr):
        o, n = lay[name]
        vecs[:, o:o + n] = arr

    put("ones", np.ones((P, 1), f32))
    put("bp", col_pack(np.asarray(I["L0_attn_bp"], f32), 16))
    for i in range(NL):
        put(f"g1_{i}", col_pack(np.asarray(I["L0_attn_g1"], f32)[i], 16))
        bhg = np.asarray(I["L0_attn_bhg"], f32)[i]
        put(f"bhid_{i}", col_pack(bhg[:DI], 24))
        put(f"bgate_{i}", col_pack(bhg[DI:], 24))
        put(f"bo_{i}", col_pack(np.asarray(I["L0_attn_bo"], f32)[i], 16))
        put(f"g2_{i}", col_pack(np.asarray(I["L0_attn_g2"], f32)[i], 16))
    put("gfv", col_pack(np.asarray(I["L0_attn_gf"], f32), 16))
    put("bf", col_pack(np.asarray(I["L0_attn_bf"], f32), 8))
    put("cnb1", col_pack(np.asarray(I["L0_ffn_cn_b1"], f32), 8))
    put("cnb2", col_pack(np.asarray(I["L0_ffn_cn_b2"], f32), 4))
    put("cnb3", col_pack(np.asarray(I["L0_ffn_cn_b3"], f32), 2))
    b4 = np.zeros((P, 1), f32); b4[0, 0] = np.asarray(I["L0_ffn_cn_b4"], f32)[0]
    put("cnb4", b4)
    put("gcb1", col_pack(np.asarray(I["L0_ffn_gc_b1"], f32), 4))
    put("gcb2", col_pack(np.asarray(I["L0_ffn_gc_b2"], f32), 2))
    g3 = np.zeros((P, 1), f32); g3[0, 0] = np.asarray(I["L0_ffn_gc_b3"], f32)[0]
    put("gcb3", g3)
    arr = np.zeros((P, 1), f32); arr[:, 0] = np.asarray(I["L0_ffn_adaptive_range"], f32)[0]
    put("ar", arr)
    put("cnw4", col_pack(np.asarray(I["L0_ffn_cn_w4"], f32)[:, 0], 2))
    put("gcw3", col_pack(np.asarray(I["L0_ffn_gc_w3"], f32)[:, 0], 2))
    for nm, key in (("bq", "L1_attn_bq"), ("bk", "L1_attn_bk"), ("bv", "L1_attn_bv")):
        a = np.zeros((P, NH), f32)
        a[0:HD, :] = np.asarray(I[key], f32).T
        put(nm, a)
    put("bo_at", col_pack(np.asarray(I["L1_attn_bo"], f32), 8))
    put("c_eps", np.full((P, 1), 1e-5, f32))
    out["vecs"] = vecs

    rowvecs = np.zeros((1, 1152), f32)
    rowvecs[0, 0:P] = 1.0
    rowvecs[0, P:P + D] = np.asarray(I["L1_attn_bv"], f32).reshape(-1)
    out["rowvecs"] = rowvecs
    return out


class BK:
    def __init__(self, dbg=()):
        self.nc = bass.Bass("TRN2", target_bir_lowering=False)
        self.q = {e: [] for e in ENGINES}
        self.cnt = {s: 0 for s in SEMS}
        self.maxwait = {s: 0 for s in SEMS}
        self.sem = {}
        self.dbg = set(dbg)
        self.es = ExitStack()
        self.banks = None
        self.bank_free = {}

    def E(self, eng, f, sem=None, k=1):
        if sem is None:
            self.q[eng].append(f)
            return None
        self.cnt[sem] += k
        v = self.cnt[sem]
        name = sem

        def g(e, f=f, name=name, k=k):
            f(e).then_inc(self.sem[name], k)
        self.q[eng].append(g)
        return v

    def W(self, eng, sem, v):
        if v is None or v <= 0:
            return
        self.maxwait[sem] = max(self.maxwait[sem], v)

        def g(e, name=sem, v=v):
            e.wait_ge(self.sem[name], v)
        self.q[eng].append(g)

    def dma(self, eng, out_ap, in_ap, sem):
        return self.E(eng, lambda e, o=out_ap, i=in_ap: e.dma_start(out=o, in_=i),
                      sem=sem, k=16)

    def gdma(self, out_ap, in_ap):
        v = self.dma("gpsimd", out_ap, in_ap, "gd")
        self.W("gpsimd", "gd", v)
        return v

    def mm(self, out, lhsT, rhs, start, stop, sem=None):
        return self.E("tensor",
                      lambda e, o=out, a=lhsT, b=rhs, s=start, p=stop:
                      e.matmul(o, a, b, start=s, stop=p), sem=sem)

    def tr(self, out, in_, ident, sem=None):
        return self.E("tensor",
                      lambda e, o=out, i=in_, d=ident: e.transpose(o, i, d),
                      sem=sem)

    def act(self, out, in_, func, bias=0.0, scale=1.0, accum=None, sem=None):
        def f(e, o=out, i=in_, fn=func, b=bias, s=scale, a=accum):
            return e.activation(o, i, fn, bias=b, scale=s, accum_out=a)
        v = self.E("scalar", f, sem=sem)
        self.q["scalar"].append(lambda e: e.drain())
        return v

    def vop(self, meth, *args, sem=None, **kw):
        def f(e, m=meth, aa=args, kk=kw):
            return getattr(e, m)(*aa, **kk)
        v = self.E("vector", f, sem=sem)
        self.q["vector"].append(lambda e: e.drain())
        return v

    def sop(self, meth, *args, sem=None, **kw):
        def f(e, m=meth, aa=args, kk=kw):
            return getattr(e, m)(*aa, **kk)
        v = self.E("scalar", f, sem=sem)
        self.q["scalar"].append(lambda e: e.drain())
        return v

    def bank_wait(self, b):
        fw = self.bank_free.get(b)
        if fw is not None:
            self.W("tensor", fw[0], fw[1])
            self.bank_free[b] = None

    def bank_set_free(self, b, sem, val):
        self.bank_free[b] = (sem, val)

    def barrier(self, idx):
        for j in range(NWL):
            self.W("sync", f"wl{j}", self.cnt[f"wl{j}"])
        self.W("gpsimd", "gd", self.cnt["gd"])
        for e in ENGINES:
            self.E(e, lambda en: en.drain(), sem="ph")
        target = self.cnt["ph"]
        for e in ENGINES:
            self.W(e, "ph", target)
        self.bank_free = {}

    def finalize(self):
        for s in SEMS:
            assert self.maxwait[s] <= self.cnt[s], \
                f"sem {s}: wait {self.maxwait[s]} > incs {self.cnt[s]}"
        es = self.es
        for s in SEMS:
            self.sem[s] = es.enter_context(self.nc.semaphore(s))
        block = es.enter_context(self.nc.Block())
        with es:
            for eng in ENGINES:
                ops = self.q[eng]

                def body(e, ops=ops):
                    for f in ops:
                        f(e)
                getattr(block, eng)(body)
        return self.nc


class Ring:

    def __init__(self, bk, pool_ap, ncols, gate=None, eng="tensor"):
        self.bk = bk
        self.pool = pool_ap
        self.n = ncols
        self.pos = 0
        self.eng = eng
        self.owner = np.full(ncols, -1, dtype=np.int64)
        self.blocks = []
        self.base = bk.cnt["wfr"]
        self.gate = gate
        self.gated = gate is None

    def load(self, dram_ap, ncols, bf16=False):
        bk = self.bk
        if not self.gated:
            bk.W("sync", self.gate[0], self.gate[1])
            self.gated = True
        if self.pos + ncols > self.n:
            self.pos = 0
        sl = slice(self.pos, self.pos + ncols)
        idx = len(self.blocks)
        prev = int(self.owner[sl].max())
        need = max(prev + 1, idx - NWL + 1)
        if need > 0:
            bk.W("sync", "wfr", self.base + need)
        self.owner[sl] = idx
        ap = self.pool[:, sl]
        sem = f"wl{idx % NWL}"
        v = bk.dma("sync", ap.bitcast(BF16) if bf16 else ap, dram_ap, sem)
        self.blocks.append((sem, v))
        self.pos += ncols
        return ap, idx

    def consume_wait(self, idx):
        sem, v = self.blocks[idx]
        self.bk.W(self.eng, sem, v)

    def free(self, idx):
        v = self.bk.E(self.eng, lambda e: e.drain(), sem="wfr")
        assert v == self.base + idx + 1, (v, self.base, idx)


def build_program(dbg=()):
    bk = BK(dbg)
    nc = bk.nc
    es = bk.es

    x_in = nc.declare_dram_parameter("x", [T, D], F32, isOutput=False)
    y_out = nc.declare_dram_parameter("y", [T, D], F32, isOutput=True)
    dr = {}
    for name, shape in (
        ("wp_r", (16, P, 1024)), ("whg_r", (NL, 24, P, 4096)),
        ("wo_r", (NL, 16, P, 3072)), ("wf_r", (8, P, 2048)),
        ("cn1_r", (8, P, 1024)), ("cn2_r", (4, P, 1024)), ("cn3_r", (2, P, 512)),
        ("gc1_r", (4, P, 1024)), ("gc2_r", (2, P, 512)),
        ("wqt_r", (8, P, 1024)), ("wkt_r", (8, P, 1024)), ("wvt_r", (8, P, 1024)),
        ("wot_r", (8, P, 1024)), ("wg_r", (32, P, 1024)), ("wu_r", (32, P, 1024)),
        ("wd_r", (8, P, 4096)), ("ident", (P, P)), ("vecs", (P, vec_layout()[1])),
        ("rowvecs", (1, 1152)),
    ):
        dr[name] = nc.declare_dram_parameter(name, list(shape), F32, isOutput=False)

    sc = {}
    for name, shape, dt in (
        ("xn_buf", (8, P, T), F32), ("xt_buf", (8, P, T), F32),
        ("s_buf", (8, P, T), F32), ("x1_buf", (8, P, T), F32),
        ("xn2_buf", (8, P, T), F32),
        ("qt_buf", (NH, HD, T), F32), ("kt_buf", (NH, HD, T), F32),
        ("v_buf", (32, P, D), F32), ("o_buf", (8, P, T), F32),
        ("l_buf", (NCH5, NH, TC5), F32),
        ("wig", (32, P, 1024), BF16), ("wiu", (32, P, 1024), BF16),
        ("wid", (8, P, 4096), BF16), ("whg_s", (NL, 24, P, 8192), BF16),
    ):
        sc[name] = nc.dram_tensor(name, list(shape), dt)

    dbg_outs = {}
    for name in bk.dbg:
        shape = {"xn_buf": (8, P, T), "xt_buf": (8, P, T), "s_buf": (8, P, T),
                 "x1_buf": (8, P, T), "xn2_buf": (8, P, T),
                 "qt_buf": (NH, HD, T), "kt_buf": (NH, HD, T),
                 "v_buf": (32, P, D), "o_buf": (8, P, T),
                 "wig": (32, P, 1024), "wiu": (32, P, 1024),
                 "wid": (8, P, 4096), "gfdbg": (P, 8)}[name]
        dt = BF16 if name in ("wig", "wiu", "wid") else F32
        dbg_outs[name] = nc.declare_dram_parameter(
            "dbg_" + name, list(shape), dt, isOutput=True)

    VL, NV = vec_layout()
    DL, ND = dvec_layout()
    vecs = es.enter_context(nc.sbuf_tensor("s_vecs", [P, NV], F32))
    dv = es.enter_context(nc.sbuf_tensor("s_dv", [P, ND], F32))
    rowv = es.enter_context(nc.sbuf_tensor("s_rowv", [1, 1152], F32))
    ident = es.enter_context(nc.sbuf_tensor("identt", [P, P], F32))
    carry = es.enter_context(nc.sbuf_tensor("s_carry", [P, NL * 24], F32))
    rows = es.enter_context(nc.sbuf_tensor("s_rows", [1, 2048], F32))
    gfdbg_sb = es.enter_context(nc.sbuf_tensor("s_gfdbg", [P, 8], F32))
    POOL = 43000
    pool = es.enter_context(nc.sbuf_tensor("s_pool", [P, POOL], F32))
    banks = [es.enter_context(nc.psum_tensor(f"bank{i}", [P, 512], F32))
             for i in range(8)]

    def vc(name, j0=0, n=1):
        o, nn = VL[name]
        assert j0 + n <= nn, name
        return vecs[:, o + j0:o + j0 + n]

    def dvc(name, j0=0, n=1):
        o, nn = DL[name]
        assert j0 + n <= nn, name
        return dv[:, o + j0:o + j0 + n]

    ones_col = vc("ones")
    ones_row = rowv[0:1, 0:P]

    def row(i):
        return rows[0:1, i * 512:(i + 1) * 512]

    v0 = bk.gdma(vecs[:], dr["vecs"][:])
    bk.gdma(rowv[:], dr["rowvecs"][:])
    v1 = bk.gdma(ident[:], dr["ident"][:])
    bk.W("vector", "gd", v1)
    for i in range(NL):
        bk.vop("tensor_scalar", dvc(f"g1p1_{i}", 0, 16), vc(f"g1_{i}", 0, 16),
               1.0, None, ALU.add)
        bk.vop("tensor_scalar", dvc(f"g2p1_{i}", 0, 16), vc(f"g2_{i}", 0, 16),
               1.0, None, ALU.add)
        bk.vop("tensor_scalar", dvc(f"nbg_{i}", 0, 24), vc(f"bgate_{i}", 0, 24),
               -1.0, None, ALU.mult)
    bk.vop("tensor_scalar", dvc("gfp1", 0, 16), vc("gfv", 0, 16), 1.0, None, ALU.add)
    bk.vop("memset", carry[:], 1.0, sem="vop")
    vop_pv = bk.cnt["vop"]
    bk.W("tensor", "vop", vop_pv)
    bk.W("tensor", "gd", v1)

    WT0, WT1, ACCC, TMPC = 0, 4096, 8192, 8193
    QQ5, R05, DI5, GI5, IQ5, WB5 = 8448, 12544, 16640, 20736, 24832, 28928
    for drname, nb, ncols, scname, rsname, wbuf in (
        ("wg_r", 32, 1024, "qsc_g", "qrs_g", "wig"),
        ("wu_r", 32, 1024, "qsc_u", "qrs_u", "wiu"),
        ("wd_r", 8, 4096, "qsc_d", "qrs_d", "wid"),
    ):
        nelem = nb * P * ncols
        slot_guard = [0, 0]
        for b in range(nb):
            s = b % 2
            wt = pool[:, (WT0 if s == 0 else WT1):(WT0 if s == 0 else WT1) + ncols]
            bk.W("gpsimd", "vop", slot_guard[s])
            lv = bk.gdma(wt, dr[drname][:][b])
            bk.W("vector", "gd", lv)
            if b == 0:
                rv = bk.vop("reduce_sum", pool[:, ACCC:ACCC + 1], wt, AX.X,
                            apply_absolute_value=True, sem="vop")
            else:
                bk.vop("reduce_sum", pool[:, TMPC:TMPC + 1], wt, AX.X,
                       apply_absolute_value=True)
                rv = bk.vop("tensor_tensor", pool[:, ACCC:ACCC + 1],
                            pool[:, ACCC:ACCC + 1], pool[:, TMPC:TMPC + 1],
                            ALU.add, sem="vop")
            slot_guard[s] = rv
        bk.W("tensor", "vop", rv)
        bk.bank_wait(0)
        mv = bk.mm(banks[0][0:1, 0:1], ones_col, pool[:, ACCC:ACCC + 1],
                   start=True, stop=True, sem="tmm")
        bk.W("vector", "tmm", mv)
        r3 = row(3)
        bk.vop("tensor_scalar", r3[0:1, 0:1], banks[0][0:1, 0:1],
               1.0 / nelem, None, ALU.mult, sem="vop")
        bk.bank_set_free(0, "vop", bk.cnt["vop"])
        bk.vop("tensor_scalar", r3[0:1, 0:1], r3[0:1, 0:1], 1e-5, None, ALU.max)
        bk.vop("reciprocal", r3[0:1, 1:2], r3[0:1, 0:1])
        sv = bk.vop("reciprocal", r3[0:1, 2:3], r3[0:1, 1:2], sem="vop")
        bk.W("tensor", "vop", sv)
        bk.bank_wait(1)
        bk.mm(banks[1][:, 0:1], ones_row, r3[0:1, 1:2], start=True, stop=True)
        bv = bk.mm(banks[1][:, 1:2], ones_row, r3[0:1, 2:3], start=True,
                   stop=True, sem="tmm")
        bk.W("vector", "tmm", bv)
        bk.vop("tensor_copy", dvc(scname), banks[1][:, 0:1])
        cv2 = bk.vop("tensor_copy", dvc(rsname), banks[1][:, 1:2], sem="vop")
        bk.bank_set_free(1, "vop", cv2)
        qq = pool[:, QQ5:QQ5 + ncols]
        r0 = pool[:, R05:R05 + ncols]
        di = pool[:, DI5:DI5 + ncols]
        gi = pool[:, GI5:GI5 + ncols]
        iq = pool[:, IQ5:IQ5 + ncols].bitcast(I32)
        wb = pool[:, WB5:WB5 + ncols // 2].bitcast(BF16)
        for b in range(nb):
            s = b % 2
            wt = pool[:, (WT0 if s == 0 else WT1):(WT0 if s == 0 else WT1) + ncols]
            bk.W("gpsimd", "vop", slot_guard[s])
            lv = bk.gdma(wt, dr[drname][:][b])
            bk.W("vector", "gd", lv)
            bk.vop("tensor_scalar", qq, wt, dvc(scname), None, ALU.mult,
                   sem="vop")
            slot_guard[s] = bk.cnt["vop"]
            bk.vop("tensor_copy", iq, qq)
            bk.vop("tensor_copy", r0, iq)
            bk.vop("tensor_tensor", di, qq, r0, ALU.subtract)
            bk.vop("tensor_scalar", gi, di, 0.5, None, ALU.is_gt)
            bk.vop("tensor_tensor", r0, r0, gi, ALU.add)
            bk.vop("tensor_scalar", gi, di, -0.5, None, ALU.is_lt)
            bk.vop("tensor_tensor", r0, r0, gi, ALU.subtract)
            bk.vop("tensor_scalar", r0, r0, 1.0, None, ALU.min)
            bk.vop("tensor_scalar", r0, r0, -1.0, None, ALU.max)
            bk.W("vector", "gd", bk.cnt["gd"])
            wv = bk.vop("tensor_copy", wb, r0, sem="vop")
            bk.W("gpsimd", "vop", wv)
            bk.gdma(sc[wbuf][:][b], wb)

    bk.barrier(-1)

    OB0, OB1, TMPS = 0, 4096, 8192
    ring_ps = Ring(bk, pool[:, 10240:43000], 32760, eng="vector")
    ps_guard = [0, 0]
    for i in range(NL):
        for j in range(24):
            ap, idx = ring_ps.load(dr["whg_r"][:][i, j], 4096)
            ring_ps.consume_wait(idx)
            g = (i * 24 + j) % 2
            ob = pool[:, (OB0 if g == 0 else OB1):(OB0 if g == 0 else OB1) + 4096]
            obb = ob.bitcast(BF16)
            tmp = pool[:, TMPS:TMPS + 2048]
            bk.W("vector", "gd", ps_guard[g])
            for hf in range(2):
                aph = ap[:, hf * 2048:(hf + 1) * 2048]
                bk.vop("tensor_copy", obb[:, hf * 4096:hf * 4096 + 2048], aph)
                bk.vop("tensor_copy", tmp, obb[:, hf * 4096:hf * 4096 + 2048])
                bk.vop("tensor_tensor", tmp, aph, tmp, ALU.subtract)
                bk.vop("tensor_copy", obb[:, hf * 4096 + 2048:hf * 4096 + 4096],
                       tmp)
            wv = bk.E("vector", lambda e: e.drain(), sem="vop")
            ring_ps.free(idx)
            bk.W("gpsimd", "vop", wv)
            ps_guard[g] = bk.gdma(sc["whg_s"][:][i, j], obb)

    bk.barrier(-2)

    pa = 0

    def alloc(n):
        nonlocal pa
        ap = pool[:, pa:pa + n]
        pa += n
        return ap

    xt_b = [alloc(1024) for _ in range(2)]
    xn_b = alloc(1024)
    sq_col = alloc(1)
    sum_col = alloc(1)
    mean_col = alloc(1)
    ex2_col = alloc(1)
    var_col = alloc(1)
    sd_col = alloc(1)
    inv_col = alloc(1)
    tcol = alloc(1)
    sqs = alloc(1024)
    xnT = alloc(1024)
    xtT = alloc(1024)
    gccols = alloc(16)
    gc1_sb = alloc(4 * 1024)
    gc2_sb = alloc(2 * 512)

    g1v = bk.gdma(gc1_sb.rearrange("p (a b) -> p a b", a=4),
                  dr["gc1_r"][:].transpose([1, 0, 2]))
    g2v = bk.gdma(gc2_sb.rearrange("p (a b) -> p a b", a=2),
                  dr["gc2_r"][:].transpose([1, 0, 2]))

    last_store = 0
    last_xt_use = [(0, 0), (0, 0)]
    prev_mmv = 0
    for tt in range(32):
        buf = xt_b[tt % 2]
        bk.W("gpsimd", "vop", last_xt_use[tt % 2][0])
        bk.W("gpsimd", "tmm", last_xt_use[tt % 2][1])
        av = bk.gdma(buf, x_in[:][tt * P:(tt + 1) * P, :])
        bk.W("vector", "gd", av)
        bk.vop("scalar_tensor_tensor", sqs, buf, 1.0, buf, ALU.mult, ALU.mult,
               accum_out=sq_col)
        bk.vop("reduce_sum", sum_col, buf, AX.X)
        bk.vop("tensor_scalar", mean_col, sum_col, 1.0 / D, None, ALU.mult)
        bk.vop("tensor_scalar", ex2_col, sq_col, 1.0 / D, None, ALU.mult)
        bk.vop("tensor_tensor", tcol, mean_col, mean_col, ALU.mult)
        bk.vop("tensor_tensor", var_col, ex2_col, tcol, ALU.subtract, sem="vop")
        vv = bk.cnt["vop"]
        bk.W("scalar", "vop", vv)
        sv = bk.act(sd_col, var_col, AF.Sqrt, bias=vc("c_eps"), sem="sop")
        bk.W("vector", "sop", sv)
        bk.vop("reciprocal", inv_col, sd_col)
        bk.W("vector", "tmm", prev_mmv)
        bk.vop("tensor_scalar", xn_b, buf, mean_col, inv_col,
               ALU.subtract, ALU.mult, sem="vop")
        xn_ready = bk.cnt["vop"]

        bk.W("tensor", "vop", xn_ready)
        for half in range(2):
            b = banks[half]
            bk.bank_wait(half)
            for k in range(4):
                bk.tr(b[:, k * P:(k + 1) * P], xn_b[:, (half * 4 + k) * P:(half * 4 + k + 1) * P], ident[:])
            tv = bk.E("tensor", lambda e: e.drain(), sem="tmm")
            bk.W("scalar", "tmm", tv)
            cv = bk.sop("copy", xnT[:, half * 512:(half + 1) * 512], b[:], sem="sop")
            bk.bank_set_free(half, "sop", cv)
        for half in range(2):
            b = banks[2 + half]
            bk.bank_wait(2 + half)
            for k in range(4):
                bk.tr(b[:, k * P:(k + 1) * P], buf[:, (half * 4 + k) * P:(half * 4 + k + 1) * P], ident[:])
            tv = bk.E("tensor", lambda e: e.drain(), sem="tmm")
            bk.W("scalar", "tmm", tv)
            cv = bk.sop("copy", xtT[:, half * 512:(half + 1) * 512], b[:], sem="sop")
            bk.bank_set_free(2 + half, "sop", cv)
        for half in range(2):
            bk.bank_wait(6 + half)
            bk.mm(banks[6 + half][0:1, :], ones_col, xn_b[:, half * 512:(half + 1) * 512],
                  start=(tt == 0), stop=(tt == 31))
        mmv = bk.E("tensor", lambda e: e.drain(), sem="tmm")
        prev_mmv = mmv
        last_xt_use[tt % 2] = (xn_ready, mmv)

        sv_ = bk.cnt["sop"]
        bk.W("gpsimd", "sop", sv_)
        bk.gdma(sc["xn_buf"][:].transpose([1, 0, 2])[:, :, tt * P:(tt + 1) * P],
                xnT.rearrange("p (dt t) -> p dt t", dt=8))
        last_store = bk.gdma(
            sc["xt_buf"][:].transpose([1, 0, 2])[:, :, tt * P:(tt + 1) * P],
            xtT.rearrange("p (dt t) -> p dt t", dt=8))
        bk.W("scalar", "gd", last_store)

    bk.W("tensor", "gd", g2v)
    ctx = gccols[:, 0:8]
    bk.W("scalar", "tmm", bk.cnt["tmm"])
    e1 = bk.act(row(0), banks[6][0:1, :], AF.Copy, scale=1.0 / T, sem="sop")
    e2 = bk.act(row(1), banks[7][0:1, :], AF.Copy, scale=1.0 / T, sem="sop")
    bk.bank_set_free(6, "sop", e1)
    bk.bank_set_free(7, "sop", e2)
    bk.W("tensor", "sop", e2)
    for k in range(8):
        b = banks[k % 2]
        src = row(k // 4)[0:1, (k % 4) * 128:((k % 4) + 1) * 128]
        bk.bank_wait(k % 2)
        bk.tr(b[:, 0:1], src, ident[0:1, 0:1])
        tv = bk.E("tensor", lambda e: e.drain(), sem="tmm")
        bk.W("vector", "tmm", tv)
        fv = bk.vop("tensor_copy", ctx[:, k:k + 1], b[:, 0:1], sem="vop")
        bk.bank_set_free(k % 2, "vop", fv)
    h1 = gccols[:, 8:12]
    bk.W("tensor", "vop", bk.cnt["vop"])
    for o in range(4):
        bk.bank_wait(4)
        for kc in range(8):
            bk.mm(banks[4][:, 0:1], gc1_sb[:, o * 1024 + kc * P: o * 1024 + (kc + 1) * P],
                  ctx[:, kc:kc + 1], start=(kc == 0), stop=(kc == 7))
        tv = bk.E("tensor", lambda e: e.drain(), sem="tmm")
        bk.W("scalar", "tmm", tv)
        av = bk.act(h1[:, o:o + 1], banks[4][:, 0:1], AF.Gelu,
                    bias=vc("gcb1", o), sem="sop")
        bk.bank_set_free(4, "sop", av)
        bk.W("tensor", "sop", av)
    h2 = gccols[:, 12:14]
    for o in range(2):
        bk.bank_wait(5)
        for kc in range(4):
            bk.mm(banks[5][:, 0:1], gc2_sb[:, o * 512 + kc * P: o * 512 + (kc + 1) * P],
                  h1[:, kc:kc + 1], start=(kc == 0), stop=(kc == 3))
        tv = bk.E("tensor", lambda e: e.drain(), sem="tmm")
        bk.W("scalar", "tmm", tv)
        av = bk.act(h2[:, o:o + 1], banks[5][:, 0:1], AF.Gelu,
                    bias=vc("gcb2", o), sem="sop")
        bk.bank_set_free(5, "sop", av)
        bk.W("tensor", "sop", av)
    bk.bank_wait(4)
    for kc in range(2):
        bk.mm(banks[4][0:1, 0:1], vc("gcw3", kc), h2[:, kc:kc + 1],
              start=(kc == 0), stop=(kc == 1))
    tv = bk.E("tensor", lambda e: e.drain(), sem="tmm")
    bk.W("scalar", "tmm", tv)
    gfs = gccols[:, 14:15]
    av = bk.act(gfs[0:1, :], banks[4][0:1, 0:1], AF.Sigmoid,
                bias=vc("gcb3")[0:1, :], sem="sop")
    bk.bank_set_free(4, "sop", av)
    bk.W("tensor", "sop", av)
    bk.bank_wait(5)
    bk.mm(banks[5][:, 0:1], ones_row, gfs[0:1, :], start=True, stop=True)
    tv = bk.E("tensor", lambda e: e.drain(), sem="tmm")
    bk.W("vector", "tmm", tv)
    gf_col = gccols[:, 15:16]
    bk.vop("tensor_copy", gf_col, banks[5][:, 0:1], sem="vop")
    bk.bank_set_free(5, "vop", bk.cnt["vop"])
    bk.vop("tensor_scalar", dvc("cgA"), vc("ar"), 0.6, None, ALU.mult)
    bk.vop("tensor_tensor", tcol, vc("ar"), gf_col, ALU.mult)
    bk.vop("tensor_scalar", tcol, tcol, 1.4, None, ALU.mult)
    bk.vop("tensor_tensor", tcol, tcol, vc("ar"), ALU.subtract)
    bk.vop("tensor_scalar", dvc("cgB"), tcol, 0.1, None, ALU.add, sem="vop")
    if "gfdbg" in bk.dbg:
        bk.vop("memset", gfdbg_sb[:], 0.0)
        bk.vop("tensor_copy", gfdbg_sb[:, 0:1], gf_col)
        bk.vop("tensor_copy", gfdbg_sb[:, 1:2], dvc("cgA"))
        bk.vop("tensor_copy", gfdbg_sb[:, 2:3], dvc("cgB"))
        bk.vop("tensor_copy", gfdbg_sb[:, 3:4], ctx[:, 0:1], sem="vop")

    bk.barrier(0)

    def xn_t(kc):
        return pool[:, kc * 512:(kc + 1) * 512]

    def hnhi(dt):
        return pool[:, 4096 + dt * 256:4096 + (dt + 1) * 256].bitcast(BF16)

    def hnlo(dt):
        return pool[:, 8192 + dt * 256:8192 + (dt + 1) * 256].bitcast(BF16)

    def hn3_t(dt):
        return pool[:, 4096 + dt * 512:4096 + (dt + 1) * 512]

    def h_t(o):
        return pool[:, 12288 + o * 512:12288 + (o + 1) * 512]

    def scano(j):
        return pool[:, 20480 + j * 513:20480 + (j + 1) * 513]

    def cvc(s):
        return pool[:, 32792 + s * 1026:32792 + s * 1026 + 513]

    def cvv(s):
        return pool[:, 32792 + s * 1026 + 513:32792 + s * 1026 + 1026]

    def zs_sh(s):
        return pool[:, 34844 + s * 1536:34844 + s * 1536 + 512]

    def zs_hb(s):
        return pool[:, 34844 + s * 1536 + 512:34844 + s * 1536 + 1024]

    def zs_gm(s):
        return pool[:, 34844 + s * 1536 + 1024:34844 + s * 1536 + 1536]

    def sq_t(s):
        return zs_sh(s)

    s_col_ap = pool[:, 37916:38428]
    stag = pool[:, 0:512]
    stmp = pool[:, 512:1024]
    ring = Ring(bk, pool[:, 38428:43000], 4572)

    bk.vop("memset", cvc(0)[:, 0:1], 0.0)
    bk.vop("memset", cvv(0)[:, 0:1], 1.0)
    bk.vop("memset", cvc(1)[:, 0:1], 0.0)
    bk.vop("memset", cvv(1)[:, 0:1], 1.0, sem="vop")

    def rms_stats(tiles_f, ntiles, zs_guards=(0, 0)):
        sq_guard = [None, None]
        mmv = None
        for dt in range(ntiles):
            s = dt % 2
            if sq_guard[s] is not None:
                bk.W("scalar", "tmm", sq_guard[s])
            if dt < 2:
                bk.W("scalar", "vop", zs_guards[s])
            sv = bk.act(sq_t(s), tiles_f(dt), AF.Square, sem="sop")
            bk.W("tensor", "sop", sv)
            if dt == 0:
                bk.bank_wait(6)
            mmv = bk.mm(banks[6][0:1, :], ones_col, sq_t(s),
                        start=(dt == 0), stop=(dt == ntiles - 1), sem="tmm")
            sq_guard[s] = mmv
        bk.W("scalar", "tmm", mmv)
        nv = bk.act(row(0), banks[6][0:1, :], AF.Sqrt, sem="sop")
        bk.bank_set_free(6, "sop", nv)
        bk.W("vector", "sop", nv)
        bk.vop("tensor_scalar", row(0), row(0), 1e-12, None, ALU.max)
        bk.vop("reciprocal", row(1), row(0))
        iv = bk.vop("tensor_scalar", row(1), row(1), SQD, None, ALU.mult, sem="vop")
        bk.W("tensor", "vop", iv)
        bk.bank_wait(7)
        bcv = bk.mm(banks[7][:, :], ones_row, row(1), start=True, stop=True,
                    sem="tmm")
        return bcv

    ch_guard_sop = 0
    ch_guard_zs = [0, 0]
    ch_guard_cv = [0, 0]
    scan_val = 0
    wo_mm_last = 0
    whg_mm_last = 0
    wf_mm_last = 0
    sstore_val = 0
    for c in range(NCH):
        c0 = c * TC
        bk.W("gpsimd", "tmm", wf_mm_last)
        bk.W("gpsimd", "vop", bk.cnt["vop"])
        xnv = bk.gdma(pool[:, 0:4096].rearrange("p (dt t) -> p dt t", dt=8),
                      sc["xn_buf"][:].transpose([1, 0, 2])[:, :, c0:c0 + TC])
        bk.W("tensor", "gd", xnv)
        wp_evac = None
        for o in range(16):
            ap, idx = ring.load(dr["wp_r"][:][o], 1024)
            ring.consume_wait(idx)
            bknum = 4 + o % 2
            bk.bank_wait(bknum)
            for kc in range(8):
                v = bk.mm(banks[bknum][:], ap[:, kc * P:(kc + 1) * P], xn_t(kc),
                          start=(kc == 0), stop=(kc == 7),
                          sem=("tmm" if kc == 7 else None))
            ring.free(idx)
            bk.W("vector", "tmm", v)
            if c > 0 and o == 0:
                bk.W("vector", "sop", ch_guard_sop)
            wp_evac = bk.vop("tensor_scalar", h_t(o), banks[bknum][:],
                             vc("bp", o), None, ALU.add, sem="vop")
            bk.bank_set_free(bknum, "vop", wp_evac)

        for i in range(NL):
            bk.W("scalar", "vop", bk.cnt["vop"])
            bcv = rms_stats(h_t, 16, zs_guards=tuple(ch_guard_zs))
            bk.W("vector", "tmm", bcv)
            bk.W("vector", "tmm", whg_mm_last)
            hnv = None
            for dt in range(16):
                bk.vop("scalar_tensor_tensor", stag, h_t(dt),
                       dvc(f"g1p1_{i}", dt), banks[7][:], ALU.mult, ALU.mult)
                bk.vop("tensor_copy", hnhi(dt), stag)
                bk.vop("tensor_copy", stmp, hnhi(dt))
                bk.vop("tensor_tensor", stmp, stag, stmp, ALU.subtract)
                hnv = bk.vop("tensor_copy", hnlo(dt), stmp,
                             sem=("vop" if dt == 15 else None))
            bk.bank_set_free(7, "vop", hnv)
            bk.W("tensor", "vop", hnv)

            for j in range(24):
                aph, idxh = ring.load(sc["whg_s"][:][i, j][:, 0:4096], 2048,
                                      bf16=True)
                apg, idxg = ring.load(sc["whg_s"][:][i, j][:, 4096:8192], 2048,
                                      bf16=True)
                s = j % 2
                bh, bg = (0, 1) if s == 0 else (2, 3)
                ring.consume_wait(idxh)
                bk.bank_wait(bh)
                ahb = aph.bitcast(BF16)
                for kc in range(16):
                    bk.mm(banks[bh][:], ahb[:, kc * P:(kc + 1) * P], hnhi(kc),
                          start=(kc == 0), stop=False)
                    bk.mm(banks[bh][:], ahb[:, kc * P:(kc + 1) * P], hnlo(kc),
                          start=False, stop=False)
                    bk.mm(banks[bh][:], ahb[:, 2048 + kc * P:2048 + (kc + 1) * P],
                          hnhi(kc), start=False, stop=(kc == 15))
                ring.free(idxh)
                ring.consume_wait(idxg)
                bk.bank_wait(bg)
                agb = apg.bitcast(BF16)
                for kc in range(16):
                    bk.mm(banks[bg][:], agb[:, kc * P:(kc + 1) * P], hnhi(kc),
                          start=(kc == 0), stop=False)
                    bk.mm(banks[bg][:], agb[:, kc * P:(kc + 1) * P], hnlo(kc),
                          start=False, stop=False)
                    v = bk.mm(banks[bg][:], agb[:, 2048 + kc * P:2048 + (kc + 1) * P],
                              hnhi(kc), start=False, stop=(kc == 15),
                              sem=("tmm" if kc == 15 else None))
                whg_mm_last = v
                ring.free(idxg)
                bk.W("scalar", "tmm", v)
                bk.W("scalar", "vop", ch_guard_cv[s])
                cav = bk.act(cvc(s)[:, 1:513], banks[bg][:], AF.Sigmoid,
                             scale=-1.0, bias=dvc(f"nbg_{i}", j), sem="sop")
                bk.bank_set_free(bg, "sop", cav)
                bk.W("scalar", "vop", ch_guard_zs[s])
                shv = bk.act(zs_sh(s), banks[bh][:], AF.Sigmoid,
                             bias=vc(f"bhid_{i}", j), sem="sop")
                bk.W("vector", "sop", shv)
                hbv = bk.vop("tensor_scalar", zs_hb(s), banks[bh][:],
                             vc(f"bhid_{i}", j), None, ALU.add, sem="vop")
                bk.bank_set_free(bh, "vop", hbv)
                bk.vop("tensor_scalar", zs_gm(s), zs_sh(s), 0.5, None, ALU.min)
                bk.vop("scalar_tensor_tensor", zs_gm(s), zs_hb(s), 0.0,
                       zs_gm(s), ALU.max, ALU.add)
                bk.vop("tensor_scalar", zs_sh(s), cvc(s)[:, 1:513], -1.0, 1.0,
                       ALU.mult, ALU.add)
                zv = bk.vop("tensor_tensor", cvv(s)[:, 1:513], zs_sh(s),
                            zs_gm(s), ALU.mult, sem="vop")
                ch_guard_zs[s] = zv
                bk.vop("tensor_copy", cvv(s)[:, 0:1], carry[:, i * 24 + j:i * 24 + j + 1])
                if j == 0:
                    bk.W("vector", "tmm", wo_mm_last)
                bk.vop("tensor_tensor_scan", scano(j), cvc(s), cvv(s), 0.0,
                       ALU.mult, ALU.add)
                scan_val = bk.vop("tensor_copy",
                                  carry[:, i * 24 + j:i * 24 + j + 1],
                                  scano(j)[:, 512:513], sem="vop")
                ch_guard_cv[s] = scan_val

            bk.W("tensor", "vop", scan_val)
            for o in range(16):
                ap0, idx0 = ring.load(dr["wo_r"][:][i, o][:, 0:1536], 1536)
                ap1, idx1 = ring.load(dr["wo_r"][:][i, o][:, 1536:3072], 1536)
                bknum = 4 + o % 2
                ring.consume_wait(idx0)
                bk.bank_wait(bknum)
                for kc in range(12):
                    bk.mm(banks[bknum][:], ap0[:, kc * P:(kc + 1) * P],
                          scano(kc)[:, 1:513], start=(kc == 0), stop=False)
                ring.free(idx0)
                ring.consume_wait(idx1)
                for kc in range(12, 24):
                    v = bk.mm(banks[bknum][:], ap1[:, (kc - 12) * P:(kc - 11) * P],
                              scano(kc)[:, 1:513], start=False,
                              stop=(kc == 23), sem=("tmm" if kc == 23 else None))
                wo_mm_last = v
                ring.free(idx1)
                bk.W("vector", "tmm", v)
                ev = bk.vop("scalar_tensor_tensor", h_t(o), banks[bknum][:],
                            vc(f"bo_{i}", o), h_t(o), ALU.add, ALU.add,
                            sem="vop")
                bk.bank_set_free(bknum, "vop", ev)

            bk.W("scalar", "vop", bk.cnt["vop"])
            bcv = rms_stats(h_t, 16, zs_guards=tuple(ch_guard_zs))
            ch_guard_sop = bk.cnt["sop"]
            bk.W("vector", "tmm", bcv)
            upd = None
            for dt in range(16):
                bk.vop("scalar_tensor_tensor", zs_gm(dt % 2), h_t(dt),
                       dvc(f"g2p1_{i}", dt), banks[7][:], ALU.mult, ALU.mult)
                upd = bk.vop("tensor_tensor", h_t(dt), h_t(dt), zs_gm(dt % 2),
                             ALU.add, sem=("vop" if dt == 15 else None))
            bk.bank_set_free(7, "vop", upd)
            ch_guard_zs = [upd, upd]

        bk.W("scalar", "vop", bk.cnt["vop"])
        bcv = rms_stats(h_t, 16, zs_guards=tuple(ch_guard_zs))
        ch_guard_sop = bk.cnt["sop"]
        bk.W("vector", "tmm", bcv)
        bk.W("vector", "tmm", whg_mm_last)
        hnv = None
        for dt in range(16):
            hnv = bk.vop("scalar_tensor_tensor", hn3_t(dt), h_t(dt),
                         dvc("gfp1", dt), banks[7][:], ALU.mult, ALU.mult,
                         sem=("vop" if dt == 15 else None))
        bk.bank_set_free(7, "vop", hnv)
        bk.W("tensor", "vop", hnv)
        for o in range(8):
            ap, idx = ring.load(dr["wf_r"][:][o], 2048)
            ring.consume_wait(idx)
            bknum = 4 + o % 2
            bk.bank_wait(bknum)
            for kc in range(16):
                v = bk.mm(banks[bknum][:], ap[:, kc * P:(kc + 1) * P], hn3_t(kc),
                          start=(kc == 0), stop=(kc == 15),
                          sem=("tmm" if kc == 15 else None))
            wf_mm_last = v
            ring.free(idx)
            bk.W("vector", "tmm", v)
            bk.W("vector", "gd", sstore_val)
            sv = bk.vop("tensor_scalar", s_col_ap, banks[bknum][:], vc("bf", o),
                        None, ALU.add, sem="vop")
            bk.bank_set_free(bknum, "vop", sv)
            bk.W("gpsimd", "vop", sv)
            sstore_val = bk.gdma(sc["s_buf"][:][o, :, c0:c0 + TC], s_col_ap)

    bk.barrier(1)

    def p2t(base, i, w=512):
        return pool[:, base + i * w:base + (i + 1) * w]

    OA, OB, XT, SX, GG, H1, H2, H3 = 0, 4096, 8192, 12288, 16384, 20480, 24576, 26624
    CPL, SQ2, CN1, CN2, CN3 = 27648, 28160, 29184, 37376, 41472
    cn1v = bk.gdma(pool[:, CN1:CN1 + 8192].rearrange("p (a b) -> p a b", a=8),
                   dr["cn1_r"][:].transpose([1, 0, 2]))
    cn2v = bk.gdma(pool[:, CN2:CN2 + 4096].rearrange("p (a b) -> p a b", a=4),
                   dr["cn2_r"][:].transpose([1, 0, 2]))
    cn3v = bk.gdma(pool[:, CN3:CN3 + 1024].rearrange("p (a b) -> p a b", a=2),
                   dr["cn3_r"][:].transpose([1, 0, 2]))
    bk.W("tensor", "gd", cn3v)

    TW_SRC = [4, 5, 6, 7, 0, 1, 2, 3]
    TW_SGN = [1, 1, -1, -1, -1, -1, 1, 1]

    for c in range(NCH):
        c0 = c * TC
        bk.W("gpsimd", "tmm", bk.cnt["tmm"])
        bk.W("gpsimd", "vop", bk.cnt["vop"])
        bk.W("gpsimd", "sop", bk.cnt["sop"])
        ldv0 = bk.gdma(pool[:, OA:OA + 4096].rearrange("p (dt t) -> p dt t", dt=8),
                       sc["xn_buf"][:].transpose([1, 0, 2])[:, :, c0:c0 + TC])
        bk.gdma(pool[:, XT:XT + 4096].rearrange("p (dt t) -> p dt t", dt=8),
                sc["xt_buf"][:].transpose([1, 0, 2])[:, :, c0:c0 + TC])
        ldv = bk.gdma(pool[:, SX:SX + 4096].rearrange("p (dt t) -> p dt t", dt=8),
                      sc["s_buf"][:].transpose([1, 0, 2])[:, :, c0:c0 + TC])
        cur_base, nxt_base = OA, OB
        cur_ready_v, cur_ready_sem = ldv, "gd"
        for cyc in range(3):
            bk.W("tensor", cur_ready_sem, cur_ready_v)
            h1_last = None
            for o in range(8):
                bknum = o % 2
                bk.bank_wait(bknum)
                for kc in range(8):
                    v = bk.mm(banks[bknum][:], pool[:, CN1 + o * 1024 + kc * P:CN1 + o * 1024 + (kc + 1) * P],
                              p2t(cur_base, kc), start=(kc == 0), stop=(kc == 7),
                              sem=("tmm" if kc == 7 else None))
                bk.W("scalar", "tmm", v)
                h1_last = bk.act(p2t(H1, o), banks[bknum][:], AF.Gelu,
                                 bias=vc("cnb1", o), sem="sop")
                bk.bank_set_free(bknum, "sop", h1_last)
            bk.W("tensor", "sop", h1_last)
            h2_last = None
            for o in range(4):
                bk.bank_wait(2)
                for kc in range(8):
                    v = bk.mm(banks[2][:], pool[:, CN2 + o * 1024 + kc * P:CN2 + o * 1024 + (kc + 1) * P],
                              p2t(H1, kc), start=(kc == 0), stop=(kc == 7),
                              sem=("tmm" if kc == 7 else None))
                bk.W("scalar", "tmm", v)
                h2_last = bk.act(p2t(H2, o), banks[2][:], AF.Gelu,
                                 bias=vc("cnb2", o), sem="sop")
                bk.bank_set_free(2, "sop", h2_last)
            bk.W("tensor", "sop", h2_last)
            h3_last = None
            for o in range(2):
                bk.bank_wait(3)
                for kc in range(4):
                    v = bk.mm(banks[3][:], pool[:, CN3 + o * 512 + kc * P:CN3 + o * 512 + (kc + 1) * P],
                              p2t(H2, kc), start=(kc == 0), stop=(kc == 3),
                              sem=("tmm" if kc == 3 else None))
                bk.W("scalar", "tmm", v)
                h3_last = bk.act(p2t(H3, o), banks[3][:], AF.Gelu,
                                 bias=vc("cnb3", o), sem="sop")
                bk.bank_set_free(3, "sop", h3_last)
            bk.W("tensor", "sop", h3_last)
            bk.bank_wait(6)
            for kc in range(2):
                v = bk.mm(banks[6][0:1, :], vc("cnw4", kc), p2t(H3, kc),
                          start=(kc == 0), stop=(kc == 1),
                          sem=("tmm" if kc == 1 else None))
            bk.W("scalar", "tmm", v)
            tfv = bk.act(row(2), banks[6][0:1, :], AF.Sigmoid,
                         bias=vc("cnb4")[0:1, :], sem="sop")
            bk.bank_set_free(6, "sop", tfv)
            bk.W("tensor", "sop", tfv)
            bk.bank_wait(7)
            bcv = bk.mm(banks[7][:], ones_row, row(2), start=True, stop=True,
                        sem="tmm")
            bk.W("vector", "tmm", bcv)
            cplv = bk.vop("tensor_scalar", pool[:, CPL:CPL + 512], banks[7][:],
                          dvc("cgA"), dvc("cgB"), ALU.mult, ALU.add, sem="vop")
            bk.bank_set_free(7, "vop", cplv)
            upd = None
            for j in range(8):
                bk.vop("tensor_tensor", p2t(SQ2, 0), pool[:, CPL:CPL + 512],
                       p2t(cur_base, TW_SRC[j]), ALU.mult)
                upd = bk.vop("tensor_tensor", p2t(nxt_base, j),
                             p2t(cur_base, j), p2t(SQ2, 0),
                             ALU.add if TW_SGN[j] > 0 else ALU.subtract,
                             sem=("vop" if j == 7 else None))
            cur_base, nxt_base = nxt_base, cur_base
            cur_ready_v, cur_ready_sem = upd, "vop"
        bk.W("scalar", "vop", cur_ready_v)
        gv = None
        for dt in range(8):
            gv = bk.act(p2t(GG, dt), p2t(cur_base, dt), AF.Silu,
                        sem=("sop" if dt == 7 else None))
        bk.W("vector", "sop", gv)
        bk.W("vector", "gd", ldv)
        x1v = None
        for dt in range(8):
            bk.vop("tensor_tensor", p2t(GG, dt), p2t(GG, dt), p2t(SX, dt),
                   ALU.mult)
            x1v = bk.vop("tensor_tensor", p2t(XT, dt), p2t(XT, dt), p2t(GG, dt),
                         ALU.add, sem=("vop" if dt == 7 else None))
        bk.W("gpsimd", "vop", x1v)
        bk.gdma(sc["x1_buf"][:].transpose([1, 0, 2])[:, :, c0:c0 + TC],
                pool[:, XT:XT + 4096].rearrange("p (dt t) -> p dt t", dt=8))
        bk.W("tensor", "vop", x1v)
        bk.W("scalar", "vop", x1v)
        sq_guard = [None, None]
        for dt in range(8):
            s = dt % 2
            if sq_guard[s] is not None:
                bk.W("scalar", "tmm", sq_guard[s])
            sv = bk.act(p2t(SQ2, s), p2t(XT, dt), AF.Square, sem="sop")
            bk.W("tensor", "sop", sv)
            if dt == 0:
                bk.bank_wait(0)
                bk.bank_wait(1)
            bk.mm(banks[0][0:1, :], ones_col, p2t(XT, dt),
                  start=(dt == 0), stop=(dt == 7))
            mv = bk.mm(banks[1][0:1, :], ones_col, p2t(SQ2, s),
                       start=(dt == 0), stop=(dt == 7), sem="tmm")
            sq_guard[s] = mv
        bk.W("vector", "tmm", mv)
        bk.vop("tensor_scalar", row(0), banks[0][0:1, :], 1.0 / D, None, ALU.mult)
        bk.vop("tensor_scalar", row(3), banks[1][0:1, :], 1.0 / D, None, ALU.mult)
        bk.vop("tensor_tensor", row(1), row(0), row(0), ALU.mult)
        vv = bk.vop("tensor_tensor", row(3), row(3), row(1), ALU.subtract,
                    sem="vop")
        bk.bank_set_free(0, "vop", vv)
        bk.bank_set_free(1, "vop", vv)
        bk.W("scalar", "vop", vv)
        sdv = bk.act(row(3), row(3), AF.Sqrt, bias=vc("c_eps")[0:1, :], sem="sop")
        bk.W("vector", "sop", sdv)
        iv = bk.vop("reciprocal", row(1), row(3), sem="vop")
        bk.W("tensor", "vop", iv)
        bk.bank_wait(2)
        bk.bank_wait(3)
        bk.mm(banks[2][:], ones_row, row(0), start=True, stop=True)
        bcv = bk.mm(banks[3][:], ones_row, row(1), start=True, stop=True,
                    sem="tmm")
        bk.W("vector", "tmm", bcv)
        xnv2 = None
        for dt in range(8):
            bk.vop("tensor_tensor", p2t(SX, dt), p2t(XT, dt), banks[2][:],
                   ALU.subtract)
            xnv2 = bk.vop("tensor_tensor", p2t(SX, dt), p2t(SX, dt),
                          banks[3][:], ALU.mult,
                          sem=("vop" if dt == 7 else None))
        bk.bank_set_free(2, "vop", xnv2)
        bk.bank_set_free(3, "vop", xnv2)
        bk.W("gpsimd", "vop", xnv2)
        bk.gdma(sc["xn2_buf"][:].transpose([1, 0, 2])[:, :, c0:c0 + TC],
                pool[:, SX:SX + 4096].rearrange("p (dt t) -> p dt t", dt=8))

    bk.barrier(2)

    WQ, WK, WV, QEV, VEV = 4096, 12288, 20480, 28672, 36864
    wq_v = bk.gdma(pool[:, WQ:WQ + 8192].rearrange("p (a b) -> p a b", a=8),
                   dr["wqt_r"][:].transpose([1, 0, 2]))
    wk_v = bk.gdma(pool[:, WK:WK + 8192].rearrange("p (a b) -> p a b", a=8),
                   dr["wkt_r"][:].transpose([1, 0, 2]))
    wv_v = bk.gdma(pool[:, WV:WV + 8192].rearrange("p (a b) -> p a b", a=8),
                   dr["wvt_r"][:].transpose([1, 0, 2]))
    bk.W("tensor", "gd", wv_v)
    for c in range(NCH):
        c0 = c * TC
        bk.W("gpsimd", "tmm", bk.cnt["tmm"])
        bk.W("gpsimd", "vop", bk.cnt["vop"])
        ld3 = bk.gdma(pool[:, 0:4096].rearrange("p (dt t) -> p dt t", dt=8),
                      sc["xn2_buf"][:].transpose([1, 0, 2])[:, :, c0:c0 + TC])
        bk.W("tensor", "gd", ld3)
        for which, wbase, bname, buf in (("q", WQ, "bq", "qt_buf"),
                                         ("k", WK, "bk", "kt_buf")):
            bk.W("vector", "gd", bk.cnt["gd"])
            for h in range(NH):
                bknum = h % 4
                bk.bank_wait(bknum)
                for kc in range(8):
                    v = bk.mm(banks[bknum][0:64, :],
                              pool[:, wbase + kc * 1024 + h * HD:wbase + kc * 1024 + (h + 1) * HD],
                              pool[:, kc * 512:(kc + 1) * 512],
                              start=(kc == 0), stop=(kc == 7),
                              sem=("tmm" if kc == 7 else None))
                bk.W("vector", "tmm", v)
                ev = pool[0:64, QEV + h * 512:QEV + (h + 1) * 512]
                evv = bk.vop("tensor_scalar", ev, banks[bknum][0:64, :],
                             vc(bname, h)[0:64, :], None, ALU.add, sem="vop")
                bk.bank_set_free(bknum, "vop", evv)
            bk.W("gpsimd", "vop", evv)
            bk.gdma(sc[buf][:].transpose([1, 0, 2])[:, :, c0:c0 + TC],
                    pool[0:64, QEV:QEV + 8192].rearrange("p (h t) -> p h t", h=NH))
        for tsub in range(4):
            tok0 = tsub * P
            for half in range(2):
                bknum = 4 + (tsub * 2 + half) % 2
                bk.bank_wait(bknum)
                for kc in range(8):
                    bk.mm(banks[bknum][:],
                          pool[:, kc * 512 + tok0:kc * 512 + tok0 + P],
                          pool[:, WV + kc * 1024 + half * 512:WV + kc * 1024 + (half + 1) * 512],
                          start=(kc == 0), stop=False)
                v = bk.mm(banks[bknum][:], ones_row,
                          rowv[0:1, P + half * 512:P + (half + 1) * 512],
                          start=False, stop=True, sem="tmm")
                bk.W("vector", "tmm", v)
                ev = pool[:, VEV + half * 512:VEV + (half + 1) * 512]
                bk.W("vector", "gd", bk.cnt["gd"])
                evv = bk.vop("tensor_copy", ev, banks[bknum][:], sem="vop")
                bk.bank_set_free(bknum, "vop", evv)
                bk.W("gpsimd", "vop", evv)
                bk.gdma(sc["v_buf"][:][c * 4 + tsub, :, half * 512:(half + 1) * 512], ev)

    bk.barrier(3)

    QA, KA = 0, 4096
    VA, VB_, PT4, LR4, OSA = 8192, 10272, 12352, 14400, 16448
    vexts = [pool[:, VA:VA + 2080], pool[:, VB_:VB_ + 2080]]
    vx = bk.vop("memset", vexts[0], 1.0)
    vx = bk.vop("memset", vexts[1], 1.0, sem="vop")
    bk.W("gpsimd", "vop", vx)
    qk = [(pool[0:64, QA:QA + T], pool[0:64, KA:KA + T]),
          (pool[64:128, QA:QA + T], pool[64:128, KA:KA + T])]
    lrows = [[pool[64:65, LR4 + (2 * g + x) * 512:LR4 + (2 * g + x) * 512 + 512]
              for x in range(2)] for g in range(2)]
    osts = [[pool[64:128, OSA + (2 * g + x) * 512:OSA + (2 * g + x) * 512 + 512]
             for x in range(2)] for g in range(2)]
    ost_guard = [0, 0]

    def pt4(x, par):
        off = PT4 + (x * 2 + par) * 512
        return pool[:, off:off + 512]

    prev_pv = {}
    for hp in range(NH // 2):
        bk.W("gpsimd", "tmm", bk.cnt["tmm"])
        for x in range(2):
            h = hp * 2 + x
            bk.gdma(qk[x][0], sc["qt_buf"][:][h])
            bk.gdma(qk[x][1], sc["kt_buf"][:][h])
            vldv = bk.gdma(
                vexts[x].rearrange("p (tt c) -> p tt c", c=65)[:, :, 0:64],
                sc["v_buf"][:].transpose([1, 0, 2])[:, :, h * HD:(h + 1) * HD])
        bk.W("tensor", "gd", vldv)
        for sg in range(8):
            bk.bank_wait(4)
            bk.bank_wait(5)
            exv = {}
            for tb in range(33):
                if tb < 32:
                    b0, b1 = tb % 2, 2 + tb % 2
                    bk.bank_wait(b0)
                    bk.bank_wait(b1)
                    bk.E("tensor", lambda e, o=banks[b0][:],
                         a=qk[0][1][:, tb * P:(tb + 1) * P],
                         r=qk[0][0][:, sg * 512:(sg + 1) * 512]:
                         e.matmul(o, a, r, start=True, stop=True,
                                  tile_position=(0, 0)))
                    sv = bk.E("tensor", lambda e, o=banks[b1][:],
                              a=qk[1][1][:, tb * P:(tb + 1) * P],
                              r=qk[1][0][:, sg * 512:(sg + 1) * 512]:
                              e.matmul(o, a, r, start=True, stop=True,
                                       tile_position=(64, 0)), sem="tmm")
                    for x in range(2):
                        bk.W("scalar", "tmm", sv)
                        bk.W("scalar", "tmm", prev_pv.get((x, tb % 2), 0))
                        ev = bk.act(pt4(x, tb % 2), banks[(b0, b1)[x]][:],
                                    AF.Exp, scale=0.125, sem="sop")
                        bk.bank_set_free((b0, b1)[x], "sop", ev)
                        exv[(x, tb)] = ev
                if tb > 0:
                    tp = tb - 1
                    for x in range(2):
                        bk.W("tensor", "sop", exv.pop((x, tp)))
                        pv = bk.mm(banks[4 + x][0:65, :],
                                   vexts[x][:, tp * 65:(tp + 1) * 65],
                                   pt4(x, tp % 2), start=(tp == 0),
                                   stop=(tp == 31), sem="tmm")
                        prev_pv[(x, tp % 2)] = pv
            g = sg % 2
            bk.W("scalar", "gd", ost_guard[g])
            bk.W("vector", "gd", ost_guard[g])
            for x in range(2):
                h = hp * 2 + x
                bk.W("scalar", "tmm", prev_pv[(x, 1)])
                lcp = bk.sop("copy", lrows[g][x], banks[4 + x][64:65, :],
                             sem="sop")
                bk.W("vector", "tmm", prev_pv[(x, 1)])
                ocp = bk.vop("tensor_copy", osts[g][x], banks[4 + x][0:64, :],
                             sem="vop")
                bk.bank_set_free(4 + x, "vop", ocp)
                bk.W("gpsimd", "sop", lcp)
                bk.W("gpsimd", "vop", ocp)
                bk.gdma(sc["l_buf"][:][2 * sg:2 * sg + 2, h, :], lrows[g][x])
                bk.gdma(sc["o_buf"][:][h // 2, (h % 2) * HD:(h % 2 + 1) * HD,
                                       sg * 512:(sg + 1) * 512], osts[g][x])
            ost_guard[g] = bk.cnt["gd"]

    bk.barrier(4)

    XN5, X15, OT5, S25, GP5, UP5 = 0, 2048, 4096, 6144, 8192, 16384
    XQ5, XQ25, AM5, SCB5, IVB5, IVB25 = 16896, 17920, 22016, 22272, 22528, 22784
    RND5, YSB5, LR5 = 23040, 24576, 26624
    lr_ap = pool[0:1, LR5:LR5 + 4096]
    ring5 = Ring(bk, pool[:, 30736:43000], 12264, gate=("ph", bk.cnt["ph"]))

    def t5(base, i, w=256):
        return pool[:, base + i * w:base + (i + 1) * w]

    def xq_t(dt):
        return pool[:, XQ5 + dt * 128:XQ5 + (dt + 1) * 128].bitcast(BF16)

    def xq2_t(dt):
        return pool[:, XQ25 + dt * 128:XQ25 + (dt + 1) * 128].bitcast(BF16)

    def act_quant(src_f, ntiles, xq_f, invb_off):
        acc = pool[:, AM5:AM5 + 256]
        tmp = pool[:, RND5:RND5 + 256]
        mcl = pool[:, AM5 + 256:AM5 + 258]
        for dt in range(ntiles):
            if dt == 0:
                bk.vop("scalar_tensor_tensor", acc, src_f(0), -1.0, src_f(0),
                       ALU.mult, ALU.max)
            else:
                bk.vop("scalar_tensor_tensor", tmp, src_f(dt), -1.0, src_f(dt),
                       ALU.mult, ALU.max)
                bk.vop("tensor_tensor", acc, acc, tmp, ALU.max)
        av = bk.E("vector", lambda e: e.drain(), sem="vop")
        bk.W("tensor", "vop", av)
        bk.bank_wait(6)
        for b in range(2):
            tv = bk.tr(banks[6][:, b * P:(b + 1) * P], acc[:, b * P:(b + 1) * P],
                       ident[:], sem=("tmm" if b == 1 else None))
        bk.W("vector", "tmm", tv)
        bk.vop("reduce_max", mcl[:, 0:1], banks[6][:, 0:P], AX.X)
        mv = bk.vop("reduce_max", mcl[:, 1:2], banks[6][:, P:2 * P], AX.X,
                    sem="vop")
        bk.bank_set_free(6, "vop", mv)
        bk.vop("tensor_scalar", mcl[:, 0:2], mcl[:, 0:2], 1e-5, None, ALU.max)
        bk.vop("reciprocal", tmp[:, 0:2], mcl[:, 0:2])
        bk.vop("tensor_scalar", tmp[:, 0:2], tmp[:, 0:2], 127.0, None, ALU.mult)
        rv = bk.vop("tensor_scalar", mcl[:, 0:2], mcl[:, 0:2], 1.0 / 127.0,
                    None, ALU.mult, sem="vop")
        bk.W("tensor", "vop", rv)
        bk.bank_wait(6)
        for b in range(2):
            bk.tr(banks[6][0:1, b * P:(b + 1) * P], tmp[:, b:b + 1], ident[:])
        for b in range(2):
            tv = bk.tr(banks[6][0:1, 256 + b * P:256 + (b + 1) * P],
                       mcl[:, b:b + 1], ident[:],
                       sem=("tmm" if b == 1 else None))
        bk.W("scalar", "tmm", tv)
        rcv = bk.sop("copy", row(2)[0:1, 0:512], banks[6][0:1, 0:512],
                     sem="sop")
        bk.bank_set_free(6, "sop", rcv)
        bk.W("tensor", "sop", rcv)
        bk.bank_wait(7)
        bk.mm(banks[7][:, 0:256], ones_row, row(2)[0:1, 0:256],
              start=True, stop=True)
        bv = bk.mm(banks[7][:, 256:512], ones_row, row(2)[0:1, 256:512],
                   start=True, stop=True, sem="tmm")
        bk.W("vector", "tmm", bv)
        bk.vop("tensor_copy", pool[:, SCB5:SCB5 + 256], banks[7][:, 0:256])
        cbv = bk.vop("tensor_copy", pool[:, invb_off:invb_off + 256],
                     banks[7][:, 256:512], sem="vop")
        bk.bank_set_free(7, "vop", cbv)
        qq = pool[:, RND5 + 256:RND5 + 512]
        r0 = pool[:, RND5 + 512:RND5 + 768]
        di = pool[:, RND5 + 768:RND5 + 1024]
        gi = pool[:, RND5 + 1024:RND5 + 1280]
        iq = pool[:, RND5 + 1280:RND5 + 1536].bitcast(I32)
        lastv = None
        for dt in range(ntiles):
            bk.vop("tensor_tensor", qq, src_f(dt), pool[:, SCB5:SCB5 + 256],
                   ALU.mult)
            bk.vop("tensor_copy", iq, qq)
            bk.vop("tensor_copy", r0, iq)
            bk.vop("tensor_tensor", di, qq, r0, ALU.subtract)
            bk.vop("tensor_scalar", gi, di, 0.5, None, ALU.is_gt)
            bk.vop("tensor_tensor", r0, r0, gi, ALU.add)
            bk.vop("tensor_scalar", gi, di, -0.5, None, ALU.is_lt)
            bk.vop("tensor_tensor", r0, r0, gi, ALU.subtract)
            lastv = bk.vop("tensor_copy", xq_f(dt), r0,
                           sem=("vop" if dt == ntiles - 1 else None))
        return lastv

    for cc in range(NCH5):
        c0 = cc * TC5
        bk.W("gpsimd", "tmm", bk.cnt["tmm"])
        bk.W("gpsimd", "vop", bk.cnt["vop"])
        bk.W("gpsimd", "sop", bk.cnt["sop"])
        bk.W("vector", "tmm", bk.cnt["tmm"])
        bk.gdma(pool[:, XN5:XN5 + 2048].rearrange("p (dt t) -> p dt t", dt=8),
                sc["xn2_buf"][:].transpose([1, 0, 2])[:, :, c0:c0 + TC5])
        bk.gdma(pool[:, X15:X15 + 2048].rearrange("p (dt t) -> p dt t", dt=8),
                sc["x1_buf"][:].transpose([1, 0, 2])[:, :, c0:c0 + TC5])
        bk.gdma(pool[:, OT5:OT5 + 2048].rearrange("p (dt t) -> p dt t", dt=8),
                sc["o_buf"][:].transpose([1, 0, 2])[:, :, c0:c0 + TC5])
        ld5 = bk.gdma(lr_ap, sc["l_buf"][:][cc])
        bk.W("vector", "gd", ld5)
        lrv = bk.vop("reciprocal", lr_ap, lr_ap, sem="vop")
        bk.W("tensor", "vop", lrv)
        onv = None
        for dt in range(8):
            bknum = 6 + dt % 2
            bk.bank_wait(bknum)
            bk.mm(banks[bknum][0:64, 0:256], ones_row[0:1, 0:64],
                  lr_ap[0:1, (2 * dt) * 256:(2 * dt + 1) * 256],
                  start=True, stop=True)
            bv = bk.mm(banks[bknum][64:128, 0:256], ones_row[0:1, 0:64],
                       lr_ap[0:1, (2 * dt + 1) * 256:(2 * dt + 2) * 256],
                       start=True, stop=True, sem="tmm")
            bk.W("vector", "tmm", bv)
            onv = bk.vop("tensor_tensor", t5(OT5, dt), t5(OT5, dt),
                         banks[bknum][:, 0:256], ALU.mult, sem="vop")
            bk.bank_set_free(bknum, "vop", onv)
        xqv = act_quant(lambda dt: t5(XN5, dt), 8, xq_t, IVB5)
        bk.W("tensor", "vop", xqv)
        gev = None
        for o in range(32):
            ap, idx = ring5.load(sc["wig"][:][o], 512, bf16=True)
            ring5.consume_wait(idx)
            bknum = o % 4
            bk.bank_wait(bknum)
            apb = ap.bitcast(BF16)
            for kc in range(8):
                v = bk.mm(banks[bknum][:, 0:256], apb[:, kc * P:(kc + 1) * P],
                          xq_t(kc), start=(kc == 0), stop=(kc == 7),
                          sem=("tmm" if kc == 7 else None))
            ring5.free(idx)
            bk.W("vector", "tmm", v)
            gev = bk.vop("scalar_tensor_tensor", t5(GP5, o), banks[bknum][:, 0:256],
                         dvc("qrs_g"), pool[:, IVB5:IVB5 + 256],
                         ALU.mult, ALU.mult, sem="vop")
            bk.bank_set_free(bknum, "vop", gev)
        bk.W("scalar", "vop", gev)
        sgv = None
        for o in range(32):
            sgv = bk.act(t5(GP5, o), t5(GP5, o), AF.Silu,
                         sem=("sop" if o == 31 else None))
        bk.W("vector", "sop", sgv)
        pv = None
        for o in range(32):
            ap, idx = ring5.load(sc["wiu"][:][o], 512, bf16=True)
            ring5.consume_wait(idx)
            bknum = o % 4
            bk.bank_wait(bknum)
            apb = ap.bitcast(BF16)
            for kc in range(8):
                v = bk.mm(banks[bknum][:, 0:256], apb[:, kc * P:(kc + 1) * P],
                          xq_t(kc), start=(kc == 0), stop=(kc == 7),
                          sem=("tmm" if kc == 7 else None))
            ring5.free(idx)
            bk.W("vector", "tmm", v)
            bk.vop("scalar_tensor_tensor", pool[:, UP5:UP5 + 256],
                   banks[bknum][:, 0:256], dvc("qrs_u"),
                   pool[:, IVB5:IVB5 + 256], ALU.mult, ALU.mult)
            pv = bk.vop("tensor_tensor", t5(GP5, o), t5(GP5, o),
                        pool[:, UP5:UP5 + 256], ALU.mult, sem="vop")
            bk.bank_set_free(bknum, "vop", pv)
        xq2v = act_quant(lambda dt: t5(GP5, dt), 32, xq2_t, IVB25)
        bk.W("tensor", "vop", xq2v)
        g2v = None
        for o in range(8):
            ap, idx = ring5.load(sc["wid"][:][o], 2048, bf16=True)
            ring5.consume_wait(idx)
            bknum = 4 + o % 2
            bk.bank_wait(bknum)
            apb = ap.bitcast(BF16)
            for kc in range(32):
                v = bk.mm(banks[bknum][:, 0:256], apb[:, kc * P:(kc + 1) * P],
                          xq2_t(kc), start=(kc == 0), stop=(kc == 31),
                          sem=("tmm" if kc == 31 else None))
            ring5.free(idx)
            bk.W("vector", "tmm", v)
            g2v = bk.vop("scalar_tensor_tensor", t5(XN5, o), banks[bknum][:, 0:256],
                         dvc("qrs_d"), pool[:, IVB25:IVB25 + 256],
                         ALU.mult, ALU.mult, sem="vop")
            bk.bank_set_free(bknum, "vop", g2v)
        bk.W("scalar", "vop", g2v)
        s2v = None
        for o in range(8):
            s2v = bk.act(t5(XN5, o), t5(XN5, o), AF.Silu,
                         sem=("sop" if o == 7 else None))
        for o in range(8):
            ap, idx = ring5.load(dr["wot_r"][:][o], 1024)
            ring5.consume_wait(idx)
            bknum = 4 + o % 2
            bk.bank_wait(bknum)
            for kc in range(8):
                v = bk.mm(banks[bknum][:, 0:256],
                          ap[:, kc * P:(kc + 1) * P],
                          t5(OT5, kc), start=(kc == 0), stop=(kc == 7),
                          sem=("tmm" if kc == 7 else None))
            ring5.free(idx)
            bk.W("vector", "tmm", v)
            ev = bk.vop("tensor_scalar", t5(S25, o), banks[bknum][:, 0:256],
                        vc("bo_at", o), None, ALU.add, sem="vop")
            bk.bank_set_free(bknum, "vop", ev)
        bk.W("vector", "sop", s2v)
        yv = None
        for dt in range(8):
            bk.vop("tensor_tensor", t5(S25, dt), t5(S25, dt), t5(XN5, dt),
                   ALU.mult)
            yv = bk.vop("tensor_tensor", t5(X15, dt), t5(X15, dt), t5(S25, dt),
                        ALU.add, sem=("vop" if dt == 7 else None))
        bk.W("tensor", "vop", yv)
        for ts in range(2):
            tok0 = ts * P
            for half in range(2):
                bknum = half
                bk.bank_wait(bknum)
                for k in range(4):
                    dt = half * 4 + k
                    bk.tr(banks[bknum][:, k * P:(k + 1) * P],
                          pool[:, X15 + dt * 256 + tok0:X15 + dt * 256 + tok0 + P],
                          ident[:], sem=("tmm" if k == 3 else None))
                tv = bk.cnt["tmm"]
                bk.W("scalar", "tmm", tv)
                bk.W("scalar", "gd", bk.cnt["gd"])
                cpv = bk.sop("copy", pool[:, YSB5 + half * 512:YSB5 + (half + 1) * 512],
                             banks[bknum][:], sem="sop")
                bk.bank_set_free(bknum, "sop", cpv)
            bk.W("gpsimd", "sop", bk.cnt["sop"])
            bk.gdma(y_out[:][c0 + tok0:c0 + tok0 + P, :],
                    pool[:, YSB5:YSB5 + 1024])

    bk.barrier(5)

    for name in bk.dbg:
        if name == "gfdbg":
            bk.gdma(dbg_outs[name][:], gfdbg_sb[:])
        else:
            bk.gdma(dbg_outs[name][:], sc[name][:])
    bk.W("gpsimd", "gd", bk.cnt["gd"])
    return bk, nc


def kernel(**inputs):
    shared = stage_shared(inputs)
    x = np.ascontiguousarray(np.asarray(inputs["L0_attn_x"], np.float32))
    bk, nc = build_program(dbg=())
    nc = bk.finalize()
    in_maps = []
    for b in range(8):
        m = dict(shared)
        m["x"] = np.ascontiguousarray(x[b])
        in_maps.append(m)
    res = run_bass_kernel_spmd(nc, in_maps, core_ids=list(range(8)))
    return np.stack([r["y"] for r in res.results])


# version 07 (final)
# speedup vs baseline: 1.4462x; 1.0024x over previous
import math
import numpy as np
from contextlib import ExitStack

import concourse.bass as bass
import concourse.mybir as mybir
from concourse.bass_utils import run_bass_kernel_spmd

F32 = mybir.dt.float32
BF16 = mybir.dt.bfloat16
I32 = mybir.dt.int32
U8 = mybir.dt.uint8
ALU = mybir.AluOpType
AF = mybir.ActivationFunctionType
AX = mybir.AxisListType

P = 128
T = 4096
D = 1024
HID = 2048
DI = 3072
NL = 4
TC = 512
NCH = T // TC
TC5 = 256
NCH5 = T // TC5
NH = 16
HD = 64
EPS = 1e-5
SQD = float(np.float32(math.sqrt(HID)))

ENGINES = ("sync", "gpsimd", "tensor", "vector", "scalar")
NWL = 6
SEMS = ("gd", "wfr", "tmm", "vop", "sop", "ph") + tuple(f"wl{j}" for j in range(NWL))


def vec_layout():
    lay, off = {}, 0

    def add(name, n):
        nonlocal off
        lay[name] = (off, n)
        off += n

    add("ones", 1)
    add("bp", 16)
    for i in range(NL):
        add(f"g1_{i}", 16)
        add(f"bhid_{i}", 24)
        add(f"bgate_{i}", 24)
        add(f"bo_{i}", 16)
        add(f"g2_{i}", 16)
    add("gfv", 16)
    add("bf", 8)
    add("cnb1", 8)
    add("cnb2", 4)
    add("cnb3", 2)
    add("cnb4", 1)
    add("gcb1", 4)
    add("gcb2", 2)
    add("gcb3", 1)
    add("ar", 1)
    add("cnw4", 2)
    add("gcw3", 2)
    add("bq", 16)
    add("bk", 16)
    add("bv", 16)
    add("bo_at", 8)
    add("c_eps", 1)
    return lay, off


def dvec_layout():
    lay, off = {}, 0

    def add(name, n):
        nonlocal off
        lay[name] = (off, n)
        off += n

    for i in range(NL):
        add(f"g1p1_{i}", 16)
        add(f"g2p1_{i}", 16)
        add(f"nbg_{i}", 24)
    add("gfp1", 16)
    add("cgA", 1)
    add("cgB", 1)
    add("qsc_g", 1)
    add("qsc_u", 1)
    add("qsc_d", 1)
    add("qrs_g", 1)
    add("qrs_u", 1)
    add("qrs_d", 1)
    return lay, off


def col_pack(v, ncols):
    return np.ascontiguousarray(v.reshape(ncols, P).T)


def stage_shared(I):
    f32 = np.float32
    out = {}

    def blk(w, njb):
        K, M = w.shape
        nk = K // P
        r = w.reshape(nk, P, njb, M // njb)
        return np.ascontiguousarray(r.transpose(2, 1, 0, 3).reshape(njb, P, nk * (M // njb)))

    out["wp_r"] = blk(np.asarray(I["L0_attn_Wp"], f32), 16)
    whg = np.asarray(I["L0_attn_Whg"], f32)
    whg_r = np.empty((NL, 24, P, 4096), f32)
    for i in range(NL):
        hidb = blk(whg[i][:, :DI], 24)
        gateb = blk(whg[i][:, DI:], 24)
        whg_r[i][:, :, :2048] = hidb
        whg_r[i][:, :, 2048:] = gateb
    out["whg_r"] = whg_r
    wo = np.asarray(I["L0_attn_Wo"], f32)
    out["wo_r"] = np.stack([blk(wo[i], 16) for i in range(NL)])
    out["wf_r"] = blk(np.asarray(I["L0_attn_Wf"], f32), 8)
    out["cn1_r"] = blk(np.asarray(I["L0_ffn_cn_w1"], f32), 8)
    out["cn2_r"] = blk(np.asarray(I["L0_ffn_cn_w2"], f32), 4)
    out["cn3_r"] = blk(np.asarray(I["L0_ffn_cn_w3"], f32), 2)
    out["gc1_r"] = blk(np.asarray(I["L0_ffn_gc_w1"], f32), 4)
    out["gc2_r"] = blk(np.asarray(I["L0_ffn_gc_w2"], f32), 2)

    def headT(w):
        return np.ascontiguousarray(
            w.transpose(2, 0, 1).reshape(8, P, NH * HD))

    out["wqt_r"] = headT(np.asarray(I["L1_attn_Wq"], f32))
    out["wkt_r"] = headT(np.asarray(I["L1_attn_Wk"], f32))
    out["wvt_r"] = headT(np.asarray(I["L1_attn_Wv"], f32))
    out["wot_r"] = blk(np.ascontiguousarray(np.asarray(I["L1_attn_Wo"], f32).T), 8)
    out["wg_r"] = blk(np.ascontiguousarray(np.asarray(I["L1_ffn_w_gate"], f32).T), 32)
    out["wu_r"] = blk(np.ascontiguousarray(np.asarray(I["L1_ffn_w_up"], f32).T), 32)
    out["wd_r"] = blk(np.ascontiguousarray(np.asarray(I["L1_ffn_w_down"], f32).T), 8)
    out["ident"] = np.eye(P, dtype=f32)

    lay, nv = vec_layout()
    vecs = np.zeros((P, nv), f32)

    def put(name, arr):
        o, n = lay[name]
        vecs[:, o:o + n] = arr

    put("ones", np.ones((P, 1), f32))
    put("bp", col_pack(np.asarray(I["L0_attn_bp"], f32), 16))
    for i in range(NL):
        put(f"g1_{i}", col_pack(np.asarray(I["L0_attn_g1"], f32)[i], 16))
        bhg = np.asarray(I["L0_attn_bhg"], f32)[i]
        put(f"bhid_{i}", col_pack(bhg[:DI], 24))
        put(f"bgate_{i}", col_pack(bhg[DI:], 24))
        put(f"bo_{i}", col_pack(np.asarray(I["L0_attn_bo"], f32)[i], 16))
        put(f"g2_{i}", col_pack(np.asarray(I["L0_attn_g2"], f32)[i], 16))
    put("gfv", col_pack(np.asarray(I["L0_attn_gf"], f32), 16))
    put("bf", col_pack(np.asarray(I["L0_attn_bf"], f32), 8))
    put("cnb1", col_pack(np.asarray(I["L0_ffn_cn_b1"], f32), 8))
    put("cnb2", col_pack(np.asarray(I["L0_ffn_cn_b2"], f32), 4))
    put("cnb3", col_pack(np.asarray(I["L0_ffn_cn_b3"], f32), 2))
    b4 = np.zeros((P, 1), f32); b4[0, 0] = np.asarray(I["L0_ffn_cn_b4"], f32)[0]
    put("cnb4", b4)
    put("gcb1", col_pack(np.asarray(I["L0_ffn_gc_b1"], f32), 4))
    put("gcb2", col_pack(np.asarray(I["L0_ffn_gc_b2"], f32), 2))
    g3 = np.zeros((P, 1), f32); g3[0, 0] = np.asarray(I["L0_ffn_gc_b3"], f32)[0]
    put("gcb3", g3)
    arr = np.zeros((P, 1), f32); arr[:, 0] = np.asarray(I["L0_ffn_adaptive_range"], f32)[0]
    put("ar", arr)
    put("cnw4", col_pack(np.asarray(I["L0_ffn_cn_w4"], f32)[:, 0], 2))
    put("gcw3", col_pack(np.asarray(I["L0_ffn_gc_w3"], f32)[:, 0], 2))
    for nm, key in (("bq", "L1_attn_bq"), ("bk", "L1_attn_bk"), ("bv", "L1_attn_bv")):
        a = np.zeros((P, NH), f32)
        a[0:HD, :] = np.asarray(I[key], f32).T
        put(nm, a)
    put("bo_at", col_pack(np.asarray(I["L1_attn_bo"], f32), 8))
    put("c_eps", np.full((P, 1), 1e-5, f32))
    out["vecs"] = vecs

    rowvecs = np.zeros((1, 1152), f32)
    rowvecs[0, 0:P] = 1.0
    rowvecs[0, P:P + D] = np.asarray(I["L1_attn_bv"], f32).reshape(-1)
    out["rowvecs"] = rowvecs
    return out


class BK:
    def __init__(self, dbg=()):
        self.nc = bass.Bass("TRN2", target_bir_lowering=False)
        self.q = {e: [] for e in ENGINES}
        self.cnt = {s: 0 for s in SEMS}
        self.maxwait = {s: 0 for s in SEMS}
        self.sem = {}
        self.dbg = set(dbg)
        self.es = ExitStack()
        self.banks = None
        self.bank_free = {}

    def E(self, eng, f, sem=None, k=1):
        if sem is None:
            self.q[eng].append(f)
            return None
        self.cnt[sem] += k
        v = self.cnt[sem]
        name = sem

        def g(e, f=f, name=name, k=k):
            f(e).then_inc(self.sem[name], k)
        self.q[eng].append(g)
        return v

    def W(self, eng, sem, v):
        if v is None or v <= 0:
            return
        self.maxwait[sem] = max(self.maxwait[sem], v)

        def g(e, name=sem, v=v):
            e.wait_ge(self.sem[name], v)
        self.q[eng].append(g)

    def dma(self, eng, out_ap, in_ap, sem):
        return self.E(eng, lambda e, o=out_ap, i=in_ap: e.dma_start(out=o, in_=i),
                      sem=sem, k=16)

    def gdma(self, out_ap, in_ap):
        v = self.dma("gpsimd", out_ap, in_ap, "gd")
        self.W("gpsimd", "gd", v)
        return v

    def mm(self, out, lhsT, rhs, start, stop, sem=None):
        return self.E("tensor",
                      lambda e, o=out, a=lhsT, b=rhs, s=start, p=stop:
                      e.matmul(o, a, b, start=s, stop=p), sem=sem)

    def tr(self, out, in_, ident, sem=None):
        return self.E("tensor",
                      lambda e, o=out, i=in_, d=ident: e.transpose(o, i, d),
                      sem=sem)

    def act(self, out, in_, func, bias=0.0, scale=1.0, accum=None, sem=None):
        def f(e, o=out, i=in_, fn=func, b=bias, s=scale, a=accum):
            return e.activation(o, i, fn, bias=b, scale=s, accum_out=a)
        v = self.E("scalar", f, sem=sem)
        self.q["scalar"].append(lambda e: e.drain())
        return v

    def vop(self, meth, *args, sem=None, **kw):
        def f(e, m=meth, aa=args, kk=kw):
            return getattr(e, m)(*aa, **kk)
        v = self.E("vector", f, sem=sem)
        self.q["vector"].append(lambda e: e.drain())
        return v

    def sop(self, meth, *args, sem=None, **kw):
        def f(e, m=meth, aa=args, kk=kw):
            return getattr(e, m)(*aa, **kk)
        v = self.E("scalar", f, sem=sem)
        self.q["scalar"].append(lambda e: e.drain())
        return v

    def bank_wait(self, b):
        fw = self.bank_free.get(b)
        if fw is not None:
            self.W("tensor", fw[0], fw[1])
            self.bank_free[b] = None

    def bank_set_free(self, b, sem, val):
        self.bank_free[b] = (sem, val)

    def barrier(self, idx):
        for j in range(NWL):
            self.W("sync", f"wl{j}", self.cnt[f"wl{j}"])
        self.W("gpsimd", "gd", self.cnt["gd"])
        for e in ENGINES:
            self.E(e, lambda en: en.drain(), sem="ph")
        target = self.cnt["ph"]
        for e in ENGINES:
            self.W(e, "ph", target)
        self.bank_free = {}

    def finalize(self):
        for s in SEMS:
            assert self.maxwait[s] <= self.cnt[s], \
                f"sem {s}: wait {self.maxwait[s]} > incs {self.cnt[s]}"
        es = self.es
        for s in SEMS:
            self.sem[s] = es.enter_context(self.nc.semaphore(s))
        block = es.enter_context(self.nc.Block())
        with es:
            for eng in ENGINES:
                ops = self.q[eng]

                def body(e, ops=ops):
                    for f in ops:
                        f(e)
                getattr(block, eng)(body)
        return self.nc


class Ring:

    def __init__(self, bk, pool_ap, ncols, gate=None, eng="tensor"):
        self.bk = bk
        self.pool = pool_ap
        self.n = ncols
        self.pos = 0
        self.eng = eng
        self.owner = np.full(ncols, -1, dtype=np.int64)
        self.blocks = []
        self.base = bk.cnt["wfr"]
        self.gate = gate
        self.gated = gate is None

    def load(self, dram_ap, ncols, bf16=False):
        bk = self.bk
        if not self.gated:
            bk.W("sync", self.gate[0], self.gate[1])
            self.gated = True
        if self.pos + ncols > self.n:
            self.pos = 0
        sl = slice(self.pos, self.pos + ncols)
        idx = len(self.blocks)
        prev = int(self.owner[sl].max())
        need = max(prev + 1, idx - NWL + 1)
        if need > 0:
            bk.W("sync", "wfr", self.base + need)
        self.owner[sl] = idx
        ap = self.pool[:, sl]
        sem = f"wl{idx % NWL}"
        v = bk.dma("sync", ap.bitcast(BF16) if bf16 else ap, dram_ap, sem)
        self.blocks.append((sem, v))
        self.pos += ncols
        return ap, idx

    def consume_wait(self, idx):
        sem, v = self.blocks[idx]
        self.bk.W(self.eng, sem, v)

    def free(self, idx):
        v = self.bk.E(self.eng, lambda e: e.drain(), sem="wfr")
        assert v == self.base + idx + 1, (v, self.base, idx)


def build_program(dbg=()):
    bk = BK(dbg)
    nc = bk.nc
    es = bk.es

    x_in = nc.declare_dram_parameter("x", [T, D], F32, isOutput=False)
    y_out = nc.declare_dram_parameter("y", [T, D], F32, isOutput=True)
    dr = {}
    for name, shape in (
        ("wp_r", (16, P, 1024)), ("whg_r", (NL, 24, P, 4096)),
        ("wo_r", (NL, 16, P, 3072)), ("wf_r", (8, P, 2048)),
        ("cn1_r", (8, P, 1024)), ("cn2_r", (4, P, 1024)), ("cn3_r", (2, P, 512)),
        ("gc1_r", (4, P, 1024)), ("gc2_r", (2, P, 512)),
        ("wqt_r", (8, P, 1024)), ("wkt_r", (8, P, 1024)), ("wvt_r", (8, P, 1024)),
        ("wot_r", (8, P, 1024)), ("wg_r", (32, P, 1024)), ("wu_r", (32, P, 1024)),
        ("wd_r", (8, P, 4096)), ("ident", (P, P)), ("vecs", (P, vec_layout()[1])),
        ("rowvecs", (1, 1152)),
    ):
        dr[name] = nc.declare_dram_parameter(name, list(shape), F32, isOutput=False)

    sc = {}
    for name, shape, dt in (
        ("xn_buf", (8, P, T), F32), ("xt_buf", (8, P, T), F32),
        ("s_buf", (8, P, T), F32), ("x1_buf", (8, P, T), F32),
        ("xn2_buf", (8, P, T), F32),
        ("qt_buf", (NH, HD, T), F32), ("kt_buf", (NH, HD, T), F32),
        ("v_buf", (32, P, D), F32), ("o_buf", (8, P, T), F32),
        ("l_buf", (NCH5, NH, TC5), F32),
        ("wig", (32, P, 1024), BF16), ("wiu", (32, P, 1024), BF16),
        ("wid", (8, P, 4096), BF16), ("whg_s", (NL, 24, P, 8192), BF16),
        ("wo_s", (NL, 16, P, 6144), BF16),
    ):
        sc[name] = nc.dram_tensor(name, list(shape), dt)

    dbg_outs = {}
    for name in bk.dbg:
        shape = {"xn_buf": (8, P, T), "xt_buf": (8, P, T), "s_buf": (8, P, T),
                 "x1_buf": (8, P, T), "xn2_buf": (8, P, T),
                 "qt_buf": (NH, HD, T), "kt_buf": (NH, HD, T),
                 "v_buf": (32, P, D), "o_buf": (8, P, T),
                 "wig": (32, P, 1024), "wiu": (32, P, 1024),
                 "wid": (8, P, 4096), "gfdbg": (P, 8)}[name]
        dt = BF16 if name in ("wig", "wiu", "wid") else F32
        dbg_outs[name] = nc.declare_dram_parameter(
            "dbg_" + name, list(shape), dt, isOutput=True)

    VL, NV = vec_layout()
    DL, ND = dvec_layout()
    vecs = es.enter_context(nc.sbuf_tensor("s_vecs", [P, NV], F32))
    dv = es.enter_context(nc.sbuf_tensor("s_dv", [P, ND], F32))
    rowv = es.enter_context(nc.sbuf_tensor("s_rowv", [1, 1152], F32))
    ident = es.enter_context(nc.sbuf_tensor("identt", [P, P], F32))
    carry = es.enter_context(nc.sbuf_tensor("s_carry", [P, NL * 24], F32))
    rows = es.enter_context(nc.sbuf_tensor("s_rows", [1, 2048], F32))
    gfdbg_sb = es.enter_context(nc.sbuf_tensor("s_gfdbg", [P, 8], F32))
    POOL = 43000
    pool = es.enter_context(nc.sbuf_tensor("s_pool", [P, POOL], F32))
    banks = [es.enter_context(nc.psum_tensor(f"bank{i}", [P, 512], F32))
             for i in range(8)]

    def vc(name, j0=0, n=1):
        o, nn = VL[name]
        assert j0 + n <= nn, name
        return vecs[:, o + j0:o + j0 + n]

    def dvc(name, j0=0, n=1):
        o, nn = DL[name]
        assert j0 + n <= nn, name
        return dv[:, o + j0:o + j0 + n]

    ones_col = vc("ones")
    ones_row = rowv[0:1, 0:P]

    def row(i):
        return rows[0:1, i * 512:(i + 1) * 512]

    v0 = bk.gdma(vecs[:], dr["vecs"][:])
    bk.gdma(rowv[:], dr["rowvecs"][:])
    v1 = bk.gdma(ident[:], dr["ident"][:])
    bk.W("vector", "gd", v1)
    for i in range(NL):
        bk.vop("tensor_scalar", dvc(f"g1p1_{i}", 0, 16), vc(f"g1_{i}", 0, 16),
               1.0, None, ALU.add)
        bk.vop("tensor_scalar", dvc(f"g2p1_{i}", 0, 16), vc(f"g2_{i}", 0, 16),
               1.0, None, ALU.add)
        bk.vop("tensor_scalar", dvc(f"nbg_{i}", 0, 24), vc(f"bgate_{i}", 0, 24),
               -1.0, None, ALU.mult)
    bk.vop("tensor_scalar", dvc("gfp1", 0, 16), vc("gfv", 0, 16), 1.0, None, ALU.add)
    bk.vop("memset", carry[:], 1.0, sem="vop")
    vop_pv = bk.cnt["vop"]
    bk.W("tensor", "vop", vop_pv)
    bk.W("tensor", "gd", v1)

    WT0, WT1, ACCC, TMPC = 0, 4096, 8192, 8193
    QQ5, R05, DI5, GI5, IQ5, WB5 = 8448, 12544, 16640, 20736, 24832, 28928
    for drname, nb, ncols, scname, rsname, wbuf in (
        ("wg_r", 32, 1024, "qsc_g", "qrs_g", "wig"),
        ("wu_r", 32, 1024, "qsc_u", "qrs_u", "wiu"),
        ("wd_r", 8, 4096, "qsc_d", "qrs_d", "wid"),
    ):
        nelem = nb * P * ncols
        slot_guard = [0, 0]
        for b in range(nb):
            s = b % 2
            wt = pool[:, (WT0 if s == 0 else WT1):(WT0 if s == 0 else WT1) + ncols]
            bk.W("gpsimd", "vop", slot_guard[s])
            lv = bk.gdma(wt, dr[drname][:][b])
            bk.W("vector", "gd", lv)
            if b == 0:
                rv = bk.vop("reduce_sum", pool[:, ACCC:ACCC + 1], wt, AX.X,
                            apply_absolute_value=True, sem="vop")
            else:
                bk.vop("reduce_sum", pool[:, TMPC:TMPC + 1], wt, AX.X,
                       apply_absolute_value=True)
                rv = bk.vop("tensor_tensor", pool[:, ACCC:ACCC + 1],
                            pool[:, ACCC:ACCC + 1], pool[:, TMPC:TMPC + 1],
                            ALU.add, sem="vop")
            slot_guard[s] = rv
        bk.W("tensor", "vop", rv)
        bk.bank_wait(0)
        mv = bk.mm(banks[0][0:1, 0:1], ones_col, pool[:, ACCC:ACCC + 1],
                   start=True, stop=True, sem="tmm")
        bk.W("vector", "tmm", mv)
        r3 = row(3)
        bk.vop("tensor_scalar", r3[0:1, 0:1], banks[0][0:1, 0:1],
               1.0 / nelem, None, ALU.mult, sem="vop")
        bk.bank_set_free(0, "vop", bk.cnt["vop"])
        bk.vop("tensor_scalar", r3[0:1, 0:1], r3[0:1, 0:1], 1e-5, None, ALU.max)
        bk.vop("reciprocal", r3[0:1, 1:2], r3[0:1, 0:1])
        sv = bk.vop("reciprocal", r3[0:1, 2:3], r3[0:1, 1:2], sem="vop")
        bk.W("tensor", "vop", sv)
        bk.bank_wait(1)
        bk.mm(banks[1][:, 0:1], ones_row, r3[0:1, 1:2], start=True, stop=True)
        bv = bk.mm(banks[1][:, 1:2], ones_row, r3[0:1, 2:3], start=True,
                   stop=True, sem="tmm")
        bk.W("vector", "tmm", bv)
        bk.vop("tensor_copy", dvc(scname), banks[1][:, 0:1])
        cv2 = bk.vop("tensor_copy", dvc(rsname), banks[1][:, 1:2], sem="vop")
        bk.bank_set_free(1, "vop", cv2)
        qq = pool[:, QQ5:QQ5 + ncols]
        r0 = pool[:, R05:R05 + ncols]
        di = pool[:, DI5:DI5 + ncols]
        gi = pool[:, GI5:GI5 + ncols]
        iq = pool[:, IQ5:IQ5 + ncols].bitcast(I32)
        wb = pool[:, WB5:WB5 + ncols // 2].bitcast(BF16)
        for b in range(nb):
            s = b % 2
            wt = pool[:, (WT0 if s == 0 else WT1):(WT0 if s == 0 else WT1) + ncols]
            bk.W("gpsimd", "vop", slot_guard[s])
            lv = bk.gdma(wt, dr[drname][:][b])
            bk.W("vector", "gd", lv)
            bk.vop("tensor_scalar", qq, wt, dvc(scname), None, ALU.mult,
                   sem="vop")
            slot_guard[s] = bk.cnt["vop"]
            bk.vop("tensor_copy", iq, qq)
            bk.vop("tensor_copy", r0, iq)
            bk.vop("tensor_tensor", di, qq, r0, ALU.subtract)
            bk.vop("tensor_scalar", gi, di, 0.5, None, ALU.is_gt)
            bk.vop("tensor_tensor", r0, r0, gi, ALU.add)
            bk.vop("tensor_scalar", gi, di, -0.5, None, ALU.is_lt)
            bk.vop("tensor_tensor", r0, r0, gi, ALU.subtract)
            bk.vop("tensor_scalar", r0, r0, 1.0, None, ALU.min)
            bk.vop("tensor_scalar", r0, r0, -1.0, None, ALU.max)
            bk.W("vector", "gd", bk.cnt["gd"])
            wv = bk.vop("tensor_copy", wb, r0, sem="vop")
            bk.W("gpsimd", "vop", wv)
            bk.gdma(sc[wbuf][:][b], wb)

    bk.barrier(-1)

    OB0, OB1, TMPS = 0, 4096, 8192
    ring_ps = Ring(bk, pool[:, 10240:43000], 32760, eng="vector")
    ps_guard = [0, 0]
    for i in range(NL):
        for j in range(24):
            ap, idx = ring_ps.load(dr["whg_r"][:][i, j], 4096)
            ring_ps.consume_wait(idx)
            g = (i * 24 + j) % 2
            ob = pool[:, (OB0 if g == 0 else OB1):(OB0 if g == 0 else OB1) + 4096]
            obb = ob.bitcast(BF16)
            tmp = pool[:, TMPS:TMPS + 2048]
            bk.W("vector", "gd", ps_guard[g])
            for hf in range(2):
                aph = ap[:, hf * 2048:(hf + 1) * 2048]
                bk.vop("tensor_copy", obb[:, hf * 4096:hf * 4096 + 2048], aph)
                bk.vop("tensor_copy", tmp, obb[:, hf * 4096:hf * 4096 + 2048])
                bk.vop("tensor_tensor", tmp, aph, tmp, ALU.subtract)
                bk.vop("tensor_copy", obb[:, hf * 4096 + 2048:hf * 4096 + 4096],
                       tmp)
            wv = bk.E("vector", lambda e: e.drain(), sem="vop")
            ring_ps.free(idx)
            bk.W("gpsimd", "vop", wv)
            ps_guard[g] = bk.gdma(sc["whg_s"][:][i, j], obb)
    for i in range(NL):
        for o in range(16):
            ap, idx = ring_ps.load(dr["wo_r"][:][i, o], 3072)
            ring_ps.consume_wait(idx)
            g = (i * 16 + o) % 2
            ob = pool[:, (OB0 if g == 0 else OB1):(OB0 if g == 0 else OB1) + 3072]
            obb = ob.bitcast(BF16)
            tmp = pool[:, TMPS:TMPS + 1536]
            bk.W("vector", "gd", ps_guard[g])
            for hf in range(2):
                aph = ap[:, hf * 1536:(hf + 1) * 1536]
                bk.vop("tensor_copy", obb[:, hf * 3072:hf * 3072 + 1536], aph)
                bk.vop("tensor_copy", tmp, obb[:, hf * 3072:hf * 3072 + 1536])
                bk.vop("tensor_tensor", tmp, aph, tmp, ALU.subtract)
                bk.vop("tensor_copy", obb[:, hf * 3072 + 1536:hf * 3072 + 3072],
                       tmp)
            wv = bk.E("vector", lambda e: e.drain(), sem="vop")
            ring_ps.free(idx)
            bk.W("gpsimd", "vop", wv)
            ps_guard[g] = bk.gdma(sc["wo_s"][:][i, o], obb)

    bk.barrier(-2)

    pa = 0

    def alloc(n):
        nonlocal pa
        ap = pool[:, pa:pa + n]
        pa += n
        return ap

    xt_b = [alloc(1024) for _ in range(2)]
    xn_b = alloc(1024)
    sq_col = alloc(1)
    sum_col = alloc(1)
    mean_col = alloc(1)
    ex2_col = alloc(1)
    var_col = alloc(1)
    sd_col = alloc(1)
    inv_col = alloc(1)
    tcol = alloc(1)
    sqs = alloc(1024)
    xnT = alloc(1024)
    xtT = alloc(1024)
    gccols = alloc(16)
    gc1_sb = alloc(4 * 1024)
    gc2_sb = alloc(2 * 512)

    g1v = bk.gdma(gc1_sb.rearrange("p (a b) -> p a b", a=4),
                  dr["gc1_r"][:].transpose([1, 0, 2]))
    g2v = bk.gdma(gc2_sb.rearrange("p (a b) -> p a b", a=2),
                  dr["gc2_r"][:].transpose([1, 0, 2]))

    last_store = 0
    last_xt_use = [(0, 0), (0, 0)]
    prev_mmv = 0
    for tt in range(32):
        buf = xt_b[tt % 2]
        bk.W("gpsimd", "vop", last_xt_use[tt % 2][0])
        bk.W("gpsimd", "tmm", last_xt_use[tt % 2][1])
        av = bk.gdma(buf, x_in[:][tt * P:(tt + 1) * P, :])
        bk.W("vector", "gd", av)
        bk.vop("scalar_tensor_tensor", sqs, buf, 1.0, buf, ALU.mult, ALU.mult,
               accum_out=sq_col)
        bk.vop("reduce_sum", sum_col, buf, AX.X)
        bk.vop("tensor_scalar", mean_col, sum_col, 1.0 / D, None, ALU.mult)
        bk.vop("tensor_scalar", ex2_col, sq_col, 1.0 / D, None, ALU.mult)
        bk.vop("tensor_tensor", tcol, mean_col, mean_col, ALU.mult)
        bk.vop("tensor_tensor", var_col, ex2_col, tcol, ALU.subtract, sem="vop")
        vv = bk.cnt["vop"]
        bk.W("scalar", "vop", vv)
        sv = bk.act(sd_col, var_col, AF.Sqrt, bias=vc("c_eps"), sem="sop")
        bk.W("vector", "sop", sv)
        bk.vop("reciprocal", inv_col, sd_col)
        bk.W("vector", "tmm", prev_mmv)
        bk.vop("tensor_scalar", xn_b, buf, mean_col, inv_col,
               ALU.subtract, ALU.mult, sem="vop")
        xn_ready = bk.cnt["vop"]

        bk.W("tensor", "vop", xn_ready)
        for half in range(2):
            b = banks[half]
            bk.bank_wait(half)
            for k in range(4):
                bk.tr(b[:, k * P:(k + 1) * P], xn_b[:, (half * 4 + k) * P:(half * 4 + k + 1) * P], ident[:])
            tv = bk.E("tensor", lambda e: e.drain(), sem="tmm")
            bk.W("scalar", "tmm", tv)
            cv = bk.sop("copy", xnT[:, half * 512:(half + 1) * 512], b[:], sem="sop")
            bk.bank_set_free(half, "sop", cv)
        for half in range(2):
            b = banks[2 + half]
            bk.bank_wait(2 + half)
            for k in range(4):
                bk.tr(b[:, k * P:(k + 1) * P], buf[:, (half * 4 + k) * P:(half * 4 + k + 1) * P], ident[:])
            tv = bk.E("tensor", lambda e: e.drain(), sem="tmm")
            bk.W("scalar", "tmm", tv)
            cv = bk.sop("copy", xtT[:, half * 512:(half + 1) * 512], b[:], sem="sop")
            bk.bank_set_free(2 + half, "sop", cv)
        for half in range(2):
            bk.bank_wait(6 + half)
            bk.mm(banks[6 + half][0:1, :], ones_col, xn_b[:, half * 512:(half + 1) * 512],
                  start=(tt == 0), stop=(tt == 31))
        mmv = bk.E("tensor", lambda e: e.drain(), sem="tmm")
        prev_mmv = mmv
        last_xt_use[tt % 2] = (xn_ready, mmv)

        sv_ = bk.cnt["sop"]
        bk.W("gpsimd", "sop", sv_)
        bk.gdma(sc["xn_buf"][:].transpose([1, 0, 2])[:, :, tt * P:(tt + 1) * P],
                xnT.rearrange("p (dt t) -> p dt t", dt=8))
        last_store = bk.gdma(
            sc["xt_buf"][:].transpose([1, 0, 2])[:, :, tt * P:(tt + 1) * P],
            xtT.rearrange("p (dt t) -> p dt t", dt=8))
        bk.W("scalar", "gd", last_store)

    bk.W("tensor", "gd", g2v)
    ctx = gccols[:, 0:8]
    bk.W("scalar", "tmm", bk.cnt["tmm"])
    e1 = bk.act(row(0), banks[6][0:1, :], AF.Copy, scale=1.0 / T, sem="sop")
    e2 = bk.act(row(1), banks[7][0:1, :], AF.Copy, scale=1.0 / T, sem="sop")
    bk.bank_set_free(6, "sop", e1)
    bk.bank_set_free(7, "sop", e2)
    bk.W("tensor", "sop", e2)
    for k in range(8):
        b = banks[k % 2]
        src = row(k // 4)[0:1, (k % 4) * 128:((k % 4) + 1) * 128]
        bk.bank_wait(k % 2)
        bk.tr(b[:, 0:1], src, ident[0:1, 0:1])
        tv = bk.E("tensor", lambda e: e.drain(), sem="tmm")
        bk.W("vector", "tmm", tv)
        fv = bk.vop("tensor_copy", ctx[:, k:k + 1], b[:, 0:1], sem="vop")
        bk.bank_set_free(k % 2, "vop", fv)
    h1 = gccols[:, 8:12]
    bk.W("tensor", "vop", bk.cnt["vop"])
    for o in range(4):
        bk.bank_wait(4)
        for kc in range(8):
            bk.mm(banks[4][:, 0:1], gc1_sb[:, o * 1024 + kc * P: o * 1024 + (kc + 1) * P],
                  ctx[:, kc:kc + 1], start=(kc == 0), stop=(kc == 7))
        tv = bk.E("tensor", lambda e: e.drain(), sem="tmm")
        bk.W("scalar", "tmm", tv)
        av = bk.act(h1[:, o:o + 1], banks[4][:, 0:1], AF.Gelu,
                    bias=vc("gcb1", o), sem="sop")
        bk.bank_set_free(4, "sop", av)
        bk.W("tensor", "sop", av)
    h2 = gccols[:, 12:14]
    for o in range(2):
        bk.bank_wait(5)
        for kc in range(4):
            bk.mm(banks[5][:, 0:1], gc2_sb[:, o * 512 + kc * P: o * 512 + (kc + 1) * P],
                  h1[:, kc:kc + 1], start=(kc == 0), stop=(kc == 3))
        tv = bk.E("tensor", lambda e: e.drain(), sem="tmm")
        bk.W("scalar", "tmm", tv)
        av = bk.act(h2[:, o:o + 1], banks[5][:, 0:1], AF.Gelu,
                    bias=vc("gcb2", o), sem="sop")
        bk.bank_set_free(5, "sop", av)
        bk.W("tensor", "sop", av)
    bk.bank_wait(4)
    for kc in range(2):
        bk.mm(banks[4][0:1, 0:1], vc("gcw3", kc), h2[:, kc:kc + 1],
              start=(kc == 0), stop=(kc == 1))
    tv = bk.E("tensor", lambda e: e.drain(), sem="tmm")
    bk.W("scalar", "tmm", tv)
    gfs = gccols[:, 14:15]
    av = bk.act(gfs[0:1, :], banks[4][0:1, 0:1], AF.Sigmoid,
                bias=vc("gcb3")[0:1, :], sem="sop")
    bk.bank_set_free(4, "sop", av)
    bk.W("tensor", "sop", av)
    bk.bank_wait(5)
    bk.mm(banks[5][:, 0:1], ones_row, gfs[0:1, :], start=True, stop=True)
    tv = bk.E("tensor", lambda e: e.drain(), sem="tmm")
    bk.W("vector", "tmm", tv)
    gf_col = gccols[:, 15:16]
    bk.vop("tensor_copy", gf_col, banks[5][:, 0:1], sem="vop")
    bk.bank_set_free(5, "vop", bk.cnt["vop"])
    bk.vop("tensor_scalar", dvc("cgA"), vc("ar"), 0.6, None, ALU.mult)
    bk.vop("tensor_tensor", tcol, vc("ar"), gf_col, ALU.mult)
    bk.vop("tensor_scalar", tcol, tcol, 1.4, None, ALU.mult)
    bk.vop("tensor_tensor", tcol, tcol, vc("ar"), ALU.subtract)
    bk.vop("tensor_scalar", dvc("cgB"), tcol, 0.1, None, ALU.add, sem="vop")
    if "gfdbg" in bk.dbg:
        bk.vop("memset", gfdbg_sb[:], 0.0)
        bk.vop("tensor_copy", gfdbg_sb[:, 0:1], gf_col)
        bk.vop("tensor_copy", gfdbg_sb[:, 1:2], dvc("cgA"))
        bk.vop("tensor_copy", gfdbg_sb[:, 2:3], dvc("cgB"))
        bk.vop("tensor_copy", gfdbg_sb[:, 3:4], ctx[:, 0:1], sem="vop")

    bk.barrier(0)

    def xn_t(kc):
        return pool[:, kc * 512:(kc + 1) * 512]

    def hnhi(dt):
        return pool[:, 4096 + dt * 256:4096 + (dt + 1) * 256].bitcast(BF16)

    def hnlo(dt):
        return pool[:, 8192 + dt * 256:8192 + (dt + 1) * 256].bitcast(BF16)

    def hn3_t(dt):
        return pool[:, 4096 + dt * 512:4096 + (dt + 1) * 512]

    def h_t(o):
        return pool[:, 12288 + o * 512:12288 + (o + 1) * 512]

    def scano(j):
        return pool[:, 20480 + j * 513:20480 + (j + 1) * 513]

    def cvc(s):
        return pool[:, 32792 + s * 1026:32792 + s * 1026 + 513]

    def cvv(s):
        return pool[:, 32792 + s * 1026 + 513:32792 + s * 1026 + 1026]

    def zs_sh(s):
        return pool[:, 34844 + s * 1536:34844 + s * 1536 + 512]

    def zs_hb(s):
        return pool[:, 34844 + s * 1536 + 512:34844 + s * 1536 + 1024]

    def zs_gm(s):
        return pool[:, 34844 + s * 1536 + 1024:34844 + s * 1536 + 1536]

    def sq_t(s):
        return zs_sh(s)

    s_col_ap = pool[:, 37916:38428]
    stag = pool[:, 0:512]
    stmp = pool[:, 512:1024]
    ring = Ring(bk, pool[:, 38428:43000], 4572)

    bk.vop("memset", cvc(0)[:, 0:1], 0.0)
    bk.vop("memset", cvv(0)[:, 0:1], 1.0)
    bk.vop("memset", cvc(1)[:, 0:1], 0.0)
    bk.vop("memset", cvv(1)[:, 0:1], 1.0, sem="vop")

    def rms_stats(tiles_f, ntiles, zs_guards=(0, 0)):
        sq_guard = [None, None]
        mmv = None
        for dt in range(ntiles):
            s = dt % 2
            if sq_guard[s] is not None:
                bk.W("scalar", "tmm", sq_guard[s])
            if dt < 2:
                bk.W("scalar", "vop", zs_guards[s])
            sv = bk.act(sq_t(s), tiles_f(dt), AF.Square, sem="sop")
            bk.W("tensor", "sop", sv)
            if dt == 0:
                bk.bank_wait(6)
            mmv = bk.mm(banks[6][0:1, :], ones_col, sq_t(s),
                        start=(dt == 0), stop=(dt == ntiles - 1), sem="tmm")
            sq_guard[s] = mmv
        bk.W("scalar", "tmm", mmv)
        nv = bk.act(row(0), banks[6][0:1, :], AF.Sqrt, sem="sop")
        bk.bank_set_free(6, "sop", nv)
        bk.W("vector", "sop", nv)
        bk.vop("tensor_scalar", row(0), row(0), 1e-12, None, ALU.max)
        bk.vop("reciprocal", row(1), row(0))
        iv = bk.vop("tensor_scalar", row(1), row(1), SQD, None, ALU.mult, sem="vop")
        bk.W("tensor", "vop", iv)
        bk.bank_wait(7)
        bcv = bk.mm(banks[7][:, :], ones_row, row(1), start=True, stop=True,
                    sem="tmm")
        return bcv

    ch_guard_sop = 0
    ch_guard_zs = [0, 0]
    ch_guard_cv = [0, 0]
    scan_val = 0
    wo_mm_last = 0
    whg_mm_last = 0
    wf_mm_last = 0
    sstore_val = 0
    for c in range(NCH):
        c0 = c * TC
        bk.W("gpsimd", "tmm", wf_mm_last)
        bk.W("gpsimd", "vop", bk.cnt["vop"])
        xnv = bk.gdma(pool[:, 0:4096].rearrange("p (dt t) -> p dt t", dt=8),
                      sc["xn_buf"][:].transpose([1, 0, 2])[:, :, c0:c0 + TC])
        bk.W("tensor", "gd", xnv)
        wp_evac = None
        for o in range(16):
            ap, idx = ring.load(dr["wp_r"][:][o], 1024)
            ring.consume_wait(idx)
            bknum = 4 + o % 2
            bk.bank_wait(bknum)
            for kc in range(8):
                v = bk.mm(banks[bknum][:], ap[:, kc * P:(kc + 1) * P], xn_t(kc),
                          start=(kc == 0), stop=(kc == 7),
                          sem=("tmm" if kc == 7 else None))
            ring.free(idx)
            bk.W("vector", "tmm", v)
            if c > 0 and o == 0:
                bk.W("vector", "sop", ch_guard_sop)
            wp_evac = bk.vop("tensor_scalar", h_t(o), banks[bknum][:],
                             vc("bp", o), None, ALU.add, sem="vop")
            bk.bank_set_free(bknum, "vop", wp_evac)

        for i in range(NL):
            bk.W("scalar", "vop", bk.cnt["vop"])
            bcv = rms_stats(h_t, 16, zs_guards=tuple(ch_guard_zs))
            bk.W("vector", "tmm", bcv)
            bk.W("vector", "tmm", whg_mm_last)
            hnv = None
            for dt in range(16):
                bk.vop("scalar_tensor_tensor", stag, h_t(dt),
                       dvc(f"g1p1_{i}", dt), banks[7][:], ALU.mult, ALU.mult)
                bk.vop("tensor_copy", hnhi(dt), stag)
                bk.vop("tensor_copy", stmp, hnhi(dt))
                bk.vop("tensor_tensor", stmp, stag, stmp, ALU.subtract)
                hnv = bk.vop("tensor_copy", hnlo(dt), stmp,
                             sem=("vop" if dt == 15 else None))
            bk.bank_set_free(7, "vop", hnv)
            bk.W("tensor", "vop", hnv)

            for j in range(24):
                aph, idxh = ring.load(sc["whg_s"][:][i, j][:, 0:4096], 2048,
                                      bf16=True)
                apg, idxg = ring.load(sc["whg_s"][:][i, j][:, 4096:8192], 2048,
                                      bf16=True)
                s = j % 2
                bh, bg = (0, 1) if s == 0 else (2, 3)
                ring.consume_wait(idxh)
                bk.bank_wait(bh)
                ahb = aph.bitcast(BF16)
                for kc in range(16):
                    bk.mm(banks[bh][:], ahb[:, kc * P:(kc + 1) * P], hnhi(kc),
                          start=(kc == 0), stop=False)
                    bk.mm(banks[bh][:], ahb[:, kc * P:(kc + 1) * P], hnlo(kc),
                          start=False, stop=False)
                    bk.mm(banks[bh][:], ahb[:, 2048 + kc * P:2048 + (kc + 1) * P],
                          hnhi(kc), start=False, stop=(kc == 15))
                ring.free(idxh)
                ring.consume_wait(idxg)
                bk.bank_wait(bg)
                agb = apg.bitcast(BF16)
                for kc in range(16):
                    bk.mm(banks[bg][:], agb[:, kc * P:(kc + 1) * P], hnhi(kc),
                          start=(kc == 0), stop=False)
                    bk.mm(banks[bg][:], agb[:, kc * P:(kc + 1) * P], hnlo(kc),
                          start=False, stop=False)
                    v = bk.mm(banks[bg][:], agb[:, 2048 + kc * P:2048 + (kc + 1) * P],
                              hnhi(kc), start=False, stop=(kc == 15),
                              sem=("tmm" if kc == 15 else None))
                whg_mm_last = v
                ring.free(idxg)
                bk.W("scalar", "tmm", v)
                bk.W("scalar", "vop", ch_guard_cv[s])
                cav = bk.act(cvc(s)[:, 1:513], banks[bg][:], AF.Sigmoid,
                             scale=-1.0, bias=dvc(f"nbg_{i}", j), sem="sop")
                bk.bank_set_free(bg, "sop", cav)
                bk.W("scalar", "vop", ch_guard_zs[s])
                shv = bk.act(zs_sh(s), banks[bh][:], AF.Sigmoid,
                             bias=vc(f"bhid_{i}", j), sem="sop")
                bk.W("vector", "sop", shv)
                hbv = bk.vop("tensor_scalar", zs_hb(s), banks[bh][:],
                             vc(f"bhid_{i}", j), None, ALU.add, sem="vop")
                bk.bank_set_free(bh, "vop", hbv)
                bk.vop("tensor_scalar", zs_gm(s), zs_sh(s), 0.5, None, ALU.min)
                bk.vop("scalar_tensor_tensor", zs_gm(s), zs_hb(s), 0.0,
                       zs_gm(s), ALU.max, ALU.add)
                bk.vop("tensor_scalar", zs_sh(s), cvc(s)[:, 1:513], -1.0, 1.0,
                       ALU.mult, ALU.add)
                zv = bk.vop("tensor_tensor", cvv(s)[:, 1:513], zs_sh(s),
                            zs_gm(s), ALU.mult, sem="vop")
                ch_guard_zs[s] = zv
                bk.vop("tensor_copy", cvv(s)[:, 0:1], carry[:, i * 24 + j:i * 24 + j + 1])
                if j == 0:
                    bk.W("vector", "tmm", wo_mm_last)
                bk.vop("tensor_tensor_scan", scano(j), cvc(s), cvv(s), 0.0,
                       ALU.mult, ALU.add)
                bk.vop("tensor_copy", carry[:, i * 24 + j:i * 24 + j + 1],
                       scano(j)[:, 512:513])
                s_hi = stag[:, 0:256].bitcast(BF16)
                s_lo = stag[:, 256:512].bitcast(BF16)
                bk.vop("tensor_copy", s_hi, scano(j)[:, 1:513])
                bk.vop("tensor_copy", stmp, s_hi)
                bk.vop("tensor_tensor", stmp, scano(j)[:, 1:513], stmp,
                       ALU.subtract)
                bk.vop("tensor_copy", s_lo, stmp)
                scan_val = bk.vop("tensor_copy", scano(j)[:, 0:512], stag,
                                  sem="vop")
                ch_guard_cv[s] = scan_val

            bk.W("tensor", "vop", scan_val)
            for o in range(16):
                ap0, idx0 = ring.load(sc["wo_s"][:][i, o][:, 0:3072], 1536,
                                      bf16=True)
                ap1, idx1 = ring.load(sc["wo_s"][:][i, o][:, 3072:6144], 1536,
                                      bf16=True)
                bknum = 4 + o % 2
                ring.consume_wait(idx0)
                bk.bank_wait(bknum)
                a0 = ap0.bitcast(BF16)
                for kc in range(12):
                    shi = scano(kc)[:, 0:256].bitcast(BF16)
                    slo = scano(kc)[:, 256:512].bitcast(BF16)
                    bk.mm(banks[bknum][:], a0[:, kc * P:(kc + 1) * P], shi,
                          start=(kc == 0), stop=False)
                    bk.mm(banks[bknum][:], a0[:, kc * P:(kc + 1) * P], slo,
                          start=False, stop=False)
                    bk.mm(banks[bknum][:], a0[:, 1536 + kc * P:1536 + (kc + 1) * P],
                          shi, start=False, stop=False)
                ring.free(idx0)
                ring.consume_wait(idx1)
                a1 = ap1.bitcast(BF16)
                for kc in range(12, 24):
                    shi = scano(kc)[:, 0:256].bitcast(BF16)
                    slo = scano(kc)[:, 256:512].bitcast(BF16)
                    k = kc - 12
                    bk.mm(banks[bknum][:], a1[:, k * P:(k + 1) * P], shi,
                          start=False, stop=False)
                    bk.mm(banks[bknum][:], a1[:, k * P:(k + 1) * P], slo,
                          start=False, stop=False)
                    v = bk.mm(banks[bknum][:], a1[:, 1536 + k * P:1536 + (k + 1) * P],
                              shi, start=False, stop=(kc == 23),
                              sem=("tmm" if kc == 23 else None))
                wo_mm_last = v
                ring.free(idx1)
                bk.W("vector", "tmm", v)
                ev = bk.vop("scalar_tensor_tensor", h_t(o), banks[bknum][:],
                            vc(f"bo_{i}", o), h_t(o), ALU.add, ALU.add,
                            sem="vop")
                bk.bank_set_free(bknum, "vop", ev)

            bk.W("scalar", "vop", bk.cnt["vop"])
            bcv = rms_stats(h_t, 16, zs_guards=tuple(ch_guard_zs))
            ch_guard_sop = bk.cnt["sop"]
            bk.W("vector", "tmm", bcv)
            upd = None
            for dt in range(16):
                bk.vop("scalar_tensor_tensor", zs_gm(dt % 2), h_t(dt),
                       dvc(f"g2p1_{i}", dt), banks[7][:], ALU.mult, ALU.mult)
                upd = bk.vop("tensor_tensor", h_t(dt), h_t(dt), zs_gm(dt % 2),
                             ALU.add, sem=("vop" if dt == 15 else None))
            bk.bank_set_free(7, "vop", upd)
            ch_guard_zs = [upd, upd]

        bk.W("scalar", "vop", bk.cnt["vop"])
        bcv = rms_stats(h_t, 16, zs_guards=tuple(ch_guard_zs))
        ch_guard_sop = bk.cnt["sop"]
        bk.W("vector", "tmm", bcv)
        bk.W("vector", "tmm", whg_mm_last)
        hnv = None
        for dt in range(16):
            hnv = bk.vop("scalar_tensor_tensor", hn3_t(dt), h_t(dt),
                         dvc("gfp1", dt), banks[7][:], ALU.mult, ALU.mult,
                         sem=("vop" if dt == 15 else None))
        bk.bank_set_free(7, "vop", hnv)
        bk.W("tensor", "vop", hnv)
        for o in range(8):
            ap, idx = ring.load(dr["wf_r"][:][o], 2048)
            ring.consume_wait(idx)
            bknum = 4 + o % 2
            bk.bank_wait(bknum)
            for kc in range(16):
                v = bk.mm(banks[bknum][:], ap[:, kc * P:(kc + 1) * P], hn3_t(kc),
                          start=(kc == 0), stop=(kc == 15),
                          sem=("tmm" if kc == 15 else None))
            wf_mm_last = v
            ring.free(idx)
            bk.W("vector", "tmm", v)
            bk.W("vector", "gd", sstore_val)
            sv = bk.vop("tensor_scalar", s_col_ap, banks[bknum][:], vc("bf", o),
                        None, ALU.add, sem="vop")
            bk.bank_set_free(bknum, "vop", sv)
            bk.W("gpsimd", "vop", sv)
            sstore_val = bk.gdma(sc["s_buf"][:][o, :, c0:c0 + TC], s_col_ap)

    bk.barrier(1)

    def p2t(base, i, w=512):
        return pool[:, base + i * w:base + (i + 1) * w]

    OA, OB, XT, SX, GG, H1, H2, H3 = 0, 4096, 8192, 12288, 16384, 20480, 24576, 26624
    CPL, SQ2, CN1, CN2, CN3 = 27648, 28160, 29184, 37376, 41472
    cn1v = bk.gdma(pool[:, CN1:CN1 + 8192].rearrange("p (a b) -> p a b", a=8),
                   dr["cn1_r"][:].transpose([1, 0, 2]))
    cn2v = bk.gdma(pool[:, CN2:CN2 + 4096].rearrange("p (a b) -> p a b", a=4),
                   dr["cn2_r"][:].transpose([1, 0, 2]))
    cn3v = bk.gdma(pool[:, CN3:CN3 + 1024].rearrange("p (a b) -> p a b", a=2),
                   dr["cn3_r"][:].transpose([1, 0, 2]))
    bk.W("tensor", "gd", cn3v)

    TW_SRC = [4, 5, 6, 7, 0, 1, 2, 3]
    TW_SGN = [1, 1, -1, -1, -1, -1, 1, 1]

    for c in range(NCH):
        c0 = c * TC
        bk.W("gpsimd", "tmm", bk.cnt["tmm"])
        bk.W("gpsimd", "vop", bk.cnt["vop"])
        bk.W("gpsimd", "sop", bk.cnt["sop"])
        ldv0 = bk.gdma(pool[:, OA:OA + 4096].rearrange("p (dt t) -> p dt t", dt=8),
                       sc["xn_buf"][:].transpose([1, 0, 2])[:, :, c0:c0 + TC])
        bk.gdma(pool[:, XT:XT + 4096].rearrange("p (dt t) -> p dt t", dt=8),
                sc["xt_buf"][:].transpose([1, 0, 2])[:, :, c0:c0 + TC])
        ldv = bk.gdma(pool[:, SX:SX + 4096].rearrange("p (dt t) -> p dt t", dt=8),
                      sc["s_buf"][:].transpose([1, 0, 2])[:, :, c0:c0 + TC])
        cur_base, nxt_base = OA, OB
        cur_ready_v, cur_ready_sem = ldv, "gd"
        for cyc in range(3):
            bk.W("tensor", cur_ready_sem, cur_ready_v)
            h1_last = None
            for o in range(8):
                bknum = o % 2
                bk.bank_wait(bknum)
                for kc in range(8):
                    v = bk.mm(banks[bknum][:], pool[:, CN1 + o * 1024 + kc * P:CN1 + o * 1024 + (kc + 1) * P],
                              p2t(cur_base, kc), start=(kc == 0), stop=(kc == 7),
                              sem=("tmm" if kc == 7 else None))
                bk.W("scalar", "tmm", v)
                h1_last = bk.act(p2t(H1, o), banks[bknum][:], AF.Gelu,
                                 bias=vc("cnb1", o), sem="sop")
                bk.bank_set_free(bknum, "sop", h1_last)
            bk.W("tensor", "sop", h1_last)
            h2_last = None
            for o in range(4):
                bk.bank_wait(2)
                for kc in range(8):
                    v = bk.mm(banks[2][:], pool[:, CN2 + o * 1024 + kc * P:CN2 + o * 1024 + (kc + 1) * P],
                              p2t(H1, kc), start=(kc == 0), stop=(kc == 7),
                              sem=("tmm" if kc == 7 else None))
                bk.W("scalar", "tmm", v)
                h2_last = bk.act(p2t(H2, o), banks[2][:], AF.Gelu,
                                 bias=vc("cnb2", o), sem="sop")
                bk.bank_set_free(2, "sop", h2_last)
            bk.W("tensor", "sop", h2_last)
            h3_last = None
            for o in range(2):
                bk.bank_wait(3)
                for kc in range(4):
                    v = bk.mm(banks[3][:], pool[:, CN3 + o * 512 + kc * P:CN3 + o * 512 + (kc + 1) * P],
                              p2t(H2, kc), start=(kc == 0), stop=(kc == 3),
                              sem=("tmm" if kc == 3 else None))
                bk.W("scalar", "tmm", v)
                h3_last = bk.act(p2t(H3, o), banks[3][:], AF.Gelu,
                                 bias=vc("cnb3", o), sem="sop")
                bk.bank_set_free(3, "sop", h3_last)
            bk.W("tensor", "sop", h3_last)
            bk.bank_wait(6)
            for kc in range(2):
                v = bk.mm(banks[6][0:1, :], vc("cnw4", kc), p2t(H3, kc),
                          start=(kc == 0), stop=(kc == 1),
                          sem=("tmm" if kc == 1 else None))
            bk.W("scalar", "tmm", v)
            tfv = bk.act(row(2), banks[6][0:1, :], AF.Sigmoid,
                         bias=vc("cnb4")[0:1, :], sem="sop")
            bk.bank_set_free(6, "sop", tfv)
            bk.W("tensor", "sop", tfv)
            bk.bank_wait(7)
            bcv = bk.mm(banks[7][:], ones_row, row(2), start=True, stop=True,
                        sem="tmm")
            bk.W("vector", "tmm", bcv)
            cplv = bk.vop("tensor_scalar", pool[:, CPL:CPL + 512], banks[7][:],
                          dvc("cgA"), dvc("cgB"), ALU.mult, ALU.add, sem="vop")
            bk.bank_set_free(7, "vop", cplv)
            upd = None
            for j in range(8):
                bk.vop("tensor_tensor", p2t(SQ2, 0), pool[:, CPL:CPL + 512],
                       p2t(cur_base, TW_SRC[j]), ALU.mult)
                upd = bk.vop("tensor_tensor", p2t(nxt_base, j),
                             p2t(cur_base, j), p2t(SQ2, 0),
                             ALU.add if TW_SGN[j] > 0 else ALU.subtract,
                             sem=("vop" if j == 7 else None))
            cur_base, nxt_base = nxt_base, cur_base
            cur_ready_v, cur_ready_sem = upd, "vop"
        bk.W("scalar", "vop", cur_ready_v)
        gv = None
        for dt in range(8):
            gv = bk.act(p2t(GG, dt), p2t(cur_base, dt), AF.Silu,
                        sem=("sop" if dt == 7 else None))
        bk.W("vector", "sop", gv)
        bk.W("vector", "gd", ldv)
        x1v = None
        for dt in range(8):
            bk.vop("tensor_tensor", p2t(GG, dt), p2t(GG, dt), p2t(SX, dt),
                   ALU.mult)
            x1v = bk.vop("tensor_tensor", p2t(XT, dt), p2t(XT, dt), p2t(GG, dt),
                         ALU.add, sem=("vop" if dt == 7 else None))
        bk.W("gpsimd", "vop", x1v)
        bk.gdma(sc["x1_buf"][:].transpose([1, 0, 2])[:, :, c0:c0 + TC],
                pool[:, XT:XT + 4096].rearrange("p (dt t) -> p dt t", dt=8))
        bk.W("tensor", "vop", x1v)
        bk.W("scalar", "vop", x1v)
        sq_guard = [None, None]
        for dt in range(8):
            s = dt % 2
            if sq_guard[s] is not None:
                bk.W("scalar", "tmm", sq_guard[s])
            sv = bk.act(p2t(SQ2, s), p2t(XT, dt), AF.Square, sem="sop")
            bk.W("tensor", "sop", sv)
            if dt == 0:
                bk.bank_wait(0)
                bk.bank_wait(1)
            bk.mm(banks[0][0:1, :], ones_col, p2t(XT, dt),
                  start=(dt == 0), stop=(dt == 7))
            mv = bk.mm(banks[1][0:1, :], ones_col, p2t(SQ2, s),
                       start=(dt == 0), stop=(dt == 7), sem="tmm")
            sq_guard[s] = mv
        bk.W("vector", "tmm", mv)
        bk.vop("tensor_scalar", row(0), banks[0][0:1, :], 1.0 / D, None, ALU.mult)
        bk.vop("tensor_scalar", row(3), banks[1][0:1, :], 1.0 / D, None, ALU.mult)
        bk.vop("tensor_tensor", row(1), row(0), row(0), ALU.mult)
        vv = bk.vop("tensor_tensor", row(3), row(3), row(1), ALU.subtract,
                    sem="vop")
        bk.bank_set_free(0, "vop", vv)
        bk.bank_set_free(1, "vop", vv)
        bk.W("scalar", "vop", vv)
        sdv = bk.act(row(3), row(3), AF.Sqrt, bias=vc("c_eps")[0:1, :], sem="sop")
        bk.W("vector", "sop", sdv)
        iv = bk.vop("reciprocal", row(1), row(3), sem="vop")
        bk.W("tensor", "vop", iv)
        bk.bank_wait(2)
        bk.bank_wait(3)
        bk.mm(banks[2][:], ones_row, row(0), start=True, stop=True)
        bcv = bk.mm(banks[3][:], ones_row, row(1), start=True, stop=True,
                    sem="tmm")
        bk.W("vector", "tmm", bcv)
        xnv2 = None
        for dt in range(8):
            bk.vop("tensor_tensor", p2t(SX, dt), p2t(XT, dt), banks[2][:],
                   ALU.subtract)
            xnv2 = bk.vop("tensor_tensor", p2t(SX, dt), p2t(SX, dt),
                          banks[3][:], ALU.mult,
                          sem=("vop" if dt == 7 else None))
        bk.bank_set_free(2, "vop", xnv2)
        bk.bank_set_free(3, "vop", xnv2)
        bk.W("gpsimd", "vop", xnv2)
        bk.gdma(sc["xn2_buf"][:].transpose([1, 0, 2])[:, :, c0:c0 + TC],
                pool[:, SX:SX + 4096].rearrange("p (dt t) -> p dt t", dt=8))

    bk.barrier(2)

    WQ, WK, WV, QEV, VEV = 4096, 12288, 20480, 28672, 36864
    wq_v = bk.gdma(pool[:, WQ:WQ + 8192].rearrange("p (a b) -> p a b", a=8),
                   dr["wqt_r"][:].transpose([1, 0, 2]))
    wk_v = bk.gdma(pool[:, WK:WK + 8192].rearrange("p (a b) -> p a b", a=8),
                   dr["wkt_r"][:].transpose([1, 0, 2]))
    wv_v = bk.gdma(pool[:, WV:WV + 8192].rearrange("p (a b) -> p a b", a=8),
                   dr["wvt_r"][:].transpose([1, 0, 2]))
    bk.W("tensor", "gd", wv_v)
    for c in range(NCH):
        c0 = c * TC
        bk.W("gpsimd", "tmm", bk.cnt["tmm"])
        bk.W("gpsimd", "vop", bk.cnt["vop"])
        ld3 = bk.gdma(pool[:, 0:4096].rearrange("p (dt t) -> p dt t", dt=8),
                      sc["xn2_buf"][:].transpose([1, 0, 2])[:, :, c0:c0 + TC])
        bk.W("tensor", "gd", ld3)
        for which, wbase, bname, buf in (("q", WQ, "bq", "qt_buf"),
                                         ("k", WK, "bk", "kt_buf")):
            bk.W("vector", "gd", bk.cnt["gd"])
            for h in range(NH):
                bknum = h % 4
                bk.bank_wait(bknum)
                for kc in range(8):
                    v = bk.mm(banks[bknum][0:64, :],
                              pool[:, wbase + kc * 1024 + h * HD:wbase + kc * 1024 + (h + 1) * HD],
                              pool[:, kc * 512:(kc + 1) * 512],
                              start=(kc == 0), stop=(kc == 7),
                              sem=("tmm" if kc == 7 else None))
                bk.W("vector", "tmm", v)
                ev = pool[0:64, QEV + h * 512:QEV + (h + 1) * 512]
                evv = bk.vop("tensor_scalar", ev, banks[bknum][0:64, :],
                             vc(bname, h)[0:64, :], None, ALU.add, sem="vop")
                bk.bank_set_free(bknum, "vop", evv)
            bk.W("gpsimd", "vop", evv)
            bk.gdma(sc[buf][:].transpose([1, 0, 2])[:, :, c0:c0 + TC],
                    pool[0:64, QEV:QEV + 8192].rearrange("p (h t) -> p h t", h=NH))
        for tsub in range(4):
            tok0 = tsub * P
            for half in range(2):
                bknum = 4 + (tsub * 2 + half) % 2
                bk.bank_wait(bknum)
                for kc in range(8):
                    bk.mm(banks[bknum][:],
                          pool[:, kc * 512 + tok0:kc * 512 + tok0 + P],
                          pool[:, WV + kc * 1024 + half * 512:WV + kc * 1024 + (half + 1) * 512],
                          start=(kc == 0), stop=False)
                v = bk.mm(banks[bknum][:], ones_row,
                          rowv[0:1, P + half * 512:P + (half + 1) * 512],
                          start=False, stop=True, sem="tmm")
                bk.W("vector", "tmm", v)
                ev = pool[:, VEV + half * 512:VEV + (half + 1) * 512]
                bk.W("vector", "gd", bk.cnt["gd"])
                evv = bk.vop("tensor_copy", ev, banks[bknum][:], sem="vop")
                bk.bank_set_free(bknum, "vop", evv)
                bk.W("gpsimd", "vop", evv)
                bk.gdma(sc["v_buf"][:][c * 4 + tsub, :, half * 512:(half + 1) * 512], ev)

    bk.barrier(3)

    QA, KA = 0, 4096
    VA, VB_, PT4, LR4, OSA = 8192, 10272, 12352, 14400, 16448
    vexts = [pool[:, VA:VA + 2080], pool[:, VB_:VB_ + 2080]]
    vx = bk.vop("memset", vexts[0], 1.0)
    vx = bk.vop("memset", vexts[1], 1.0, sem="vop")
    bk.W("gpsimd", "vop", vx)
    qk = [(pool[0:64, QA:QA + T], pool[0:64, KA:KA + T]),
          (pool[64:128, QA:QA + T], pool[64:128, KA:KA + T])]
    lrows = [[pool[64:65, LR4 + (2 * g + x) * 512:LR4 + (2 * g + x) * 512 + 512]
              for x in range(2)] for g in range(2)]
    osts = [[pool[64:128, OSA + (2 * g + x) * 512:OSA + (2 * g + x) * 512 + 512]
             for x in range(2)] for g in range(2)]
    ost_guard = [0, 0]

    def pt4(x, par):
        off = PT4 + (x * 2 + par) * 512
        return pool[:, off:off + 512]

    prev_pv = {}
    for hp in range(NH // 2):
        bk.W("gpsimd", "tmm", bk.cnt["tmm"])
        for x in range(2):
            h = hp * 2 + x
            bk.gdma(qk[x][0], sc["qt_buf"][:][h])
            bk.gdma(qk[x][1], sc["kt_buf"][:][h])
            vldv = bk.gdma(
                vexts[x].rearrange("p (tt c) -> p tt c", c=65)[:, :, 0:64],
                sc["v_buf"][:].transpose([1, 0, 2])[:, :, h * HD:(h + 1) * HD])
        bk.W("tensor", "gd", vldv)
        for sg in range(8):
            bk.bank_wait(4)
            bk.bank_wait(5)
            exv = {}
            for tb in range(33):
                if tb < 32:
                    b0, b1 = tb % 2, 2 + tb % 2
                    bk.bank_wait(b0)
                    bk.bank_wait(b1)
                    bk.E("tensor", lambda e, o=banks[b0][:],
                         a=qk[0][1][:, tb * P:(tb + 1) * P],
                         r=qk[0][0][:, sg * 512:(sg + 1) * 512]:
                         e.matmul(o, a, r, start=True, stop=True,
                                  tile_position=(0, 0)))
                    sv = bk.E("tensor", lambda e, o=banks[b1][:],
                              a=qk[1][1][:, tb * P:(tb + 1) * P],
                              r=qk[1][0][:, sg * 512:(sg + 1) * 512]:
                              e.matmul(o, a, r, start=True, stop=True,
                                       tile_position=(64, 0)), sem="tmm")
                    for x in range(2):
                        bk.W("scalar", "tmm", sv)
                        bk.W("scalar", "tmm", prev_pv.get((x, tb % 2), 0))
                        ev = bk.act(pt4(x, tb % 2), banks[(b0, b1)[x]][:],
                                    AF.Exp, scale=0.125, sem="sop")
                        bk.bank_set_free((b0, b1)[x], "sop", ev)
                        exv[(x, tb)] = ev
                if tb > 0:
                    tp = tb - 1
                    for x in range(2):
                        bk.W("tensor", "sop", exv.pop((x, tp)))
                        pv = bk.mm(banks[4 + x][0:65, :],
                                   vexts[x][:, tp * 65:(tp + 1) * 65],
                                   pt4(x, tp % 2), start=(tp == 0),
                                   stop=(tp == 31), sem="tmm")
                        prev_pv[(x, tp % 2)] = pv
            g = sg % 2
            bk.W("scalar", "gd", ost_guard[g])
            bk.W("vector", "gd", ost_guard[g])
            for x in range(2):
                h = hp * 2 + x
                bk.W("scalar", "tmm", prev_pv[(x, 1)])
                lcp = bk.sop("copy", lrows[g][x], banks[4 + x][64:65, :],
                             sem="sop")
                bk.W("vector", "tmm", prev_pv[(x, 1)])
                ocp = bk.vop("tensor_copy", osts[g][x], banks[4 + x][0:64, :],
                             sem="vop")
                bk.bank_set_free(4 + x, "vop", ocp)
                bk.W("gpsimd", "sop", lcp)
                bk.W("gpsimd", "vop", ocp)
                bk.gdma(sc["l_buf"][:][2 * sg:2 * sg + 2, h, :], lrows[g][x])
                bk.gdma(sc["o_buf"][:][h // 2, (h % 2) * HD:(h % 2 + 1) * HD,
                                       sg * 512:(sg + 1) * 512], osts[g][x])
            ost_guard[g] = bk.cnt["gd"]

    bk.barrier(4)

    XN5, X15, OT5, S25, GP5, UP5 = 0, 2048, 4096, 6144, 8192, 16384
    XQ5, XQ25, AM5, SCB5, IVB5, IVB25 = 16896, 17920, 22016, 22272, 22528, 22784
    RND5, YSB5, LR5 = 23040, 24576, 26624
    lr_ap = pool[0:1, LR5:LR5 + 4096]
    ring5 = Ring(bk, pool[:, 30736:43000], 12264, gate=("ph", bk.cnt["ph"]))

    def t5(base, i, w=256):
        return pool[:, base + i * w:base + (i + 1) * w]

    def xq_t(dt):
        return pool[:, XQ5 + dt * 128:XQ5 + (dt + 1) * 128].bitcast(BF16)

    def xq2_t(dt):
        return pool[:, XQ25 + dt * 128:XQ25 + (dt + 1) * 128].bitcast(BF16)

    def act_quant(src_f, ntiles, xq_f, invb_off):
        acc = pool[:, AM5:AM5 + 256]
        tmp = pool[:, RND5:RND5 + 256]
        mcl = pool[:, AM5 + 256:AM5 + 258]
        for dt in range(ntiles):
            if dt == 0:
                bk.vop("scalar_tensor_tensor", acc, src_f(0), -1.0, src_f(0),
                       ALU.mult, ALU.max)
            else:
                bk.vop("scalar_tensor_tensor", tmp, src_f(dt), -1.0, src_f(dt),
                       ALU.mult, ALU.max)
                bk.vop("tensor_tensor", acc, acc, tmp, ALU.max)
        av = bk.E("vector", lambda e: e.drain(), sem="vop")
        bk.W("tensor", "vop", av)
        bk.bank_wait(6)
        for b in range(2):
            tv = bk.tr(banks[6][:, b * P:(b + 1) * P], acc[:, b * P:(b + 1) * P],
                       ident[:], sem=("tmm" if b == 1 else None))
        bk.W("vector", "tmm", tv)
        bk.vop("reduce_max", mcl[:, 0:1], banks[6][:, 0:P], AX.X)
        mv = bk.vop("reduce_max", mcl[:, 1:2], banks[6][:, P:2 * P], AX.X,
                    sem="vop")
        bk.bank_set_free(6, "vop", mv)
        bk.vop("tensor_scalar", mcl[:, 0:2], mcl[:, 0:2], 1e-5, None, ALU.max)
        bk.vop("reciprocal", tmp[:, 0:2], mcl[:, 0:2])
        bk.vop("tensor_scalar", tmp[:, 0:2], tmp[:, 0:2], 127.0, None, ALU.mult)
        rv = bk.vop("tensor_scalar", mcl[:, 0:2], mcl[:, 0:2], 1.0 / 127.0,
                    None, ALU.mult, sem="vop")
        bk.W("tensor", "vop", rv)
        bk.bank_wait(6)
        for b in range(2):
            bk.tr(banks[6][0:1, b * P:(b + 1) * P], tmp[:, b:b + 1], ident[:])
        for b in range(2):
            tv = bk.tr(banks[6][0:1, 256 + b * P:256 + (b + 1) * P],
                       mcl[:, b:b + 1], ident[:],
                       sem=("tmm" if b == 1 else None))
        bk.W("scalar", "tmm", tv)
        rcv = bk.sop("copy", row(2)[0:1, 0:512], banks[6][0:1, 0:512],
                     sem="sop")
        bk.bank_set_free(6, "sop", rcv)
        bk.W("tensor", "sop", rcv)
        bk.bank_wait(7)
        bk.mm(banks[7][:, 0:256], ones_row, row(2)[0:1, 0:256],
              start=True, stop=True)
        bv = bk.mm(banks[7][:, 256:512], ones_row, row(2)[0:1, 256:512],
                   start=True, stop=True, sem="tmm")
        bk.W("vector", "tmm", bv)
        bk.vop("tensor_copy", pool[:, SCB5:SCB5 + 256], banks[7][:, 0:256])
        cbv = bk.vop("tensor_copy", pool[:, invb_off:invb_off + 256],
                     banks[7][:, 256:512], sem="vop")
        bk.bank_set_free(7, "vop", cbv)
        qq = pool[:, RND5 + 256:RND5 + 512]
        r0 = pool[:, RND5 + 512:RND5 + 768]
        di = pool[:, RND5 + 768:RND5 + 1024]
        gi = pool[:, RND5 + 1024:RND5 + 1280]
        iq = pool[:, RND5 + 1280:RND5 + 1536].bitcast(I32)
        lastv = None
        for dt in range(ntiles):
            bk.vop("tensor_tensor", qq, src_f(dt), pool[:, SCB5:SCB5 + 256],
                   ALU.mult)
            bk.vop("tensor_copy", iq, qq)
            bk.vop("tensor_copy", r0, iq)
            bk.vop("tensor_tensor", di, qq, r0, ALU.subtract)
            bk.vop("tensor_scalar", gi, di, 0.5, None, ALU.is_gt)
            bk.vop("tensor_tensor", r0, r0, gi, ALU.add)
            bk.vop("tensor_scalar", gi, di, -0.5, None, ALU.is_lt)
            bk.vop("tensor_tensor", r0, r0, gi, ALU.subtract)
            lastv = bk.vop("tensor_copy", xq_f(dt), r0,
                           sem=("vop" if dt == ntiles - 1 else None))
        return lastv

    for cc in range(NCH5):
        c0 = cc * TC5
        bk.W("gpsimd", "tmm", bk.cnt["tmm"])
        bk.W("gpsimd", "vop", bk.cnt["vop"])
        bk.W("gpsimd", "sop", bk.cnt["sop"])
        bk.W("vector", "tmm", bk.cnt["tmm"])
        bk.gdma(pool[:, XN5:XN5 + 2048].rearrange("p (dt t) -> p dt t", dt=8),
                sc["xn2_buf"][:].transpose([1, 0, 2])[:, :, c0:c0 + TC5])
        bk.gdma(pool[:, X15:X15 + 2048].rearrange("p (dt t) -> p dt t", dt=8),
                sc["x1_buf"][:].transpose([1, 0, 2])[:, :, c0:c0 + TC5])
        bk.gdma(pool[:, OT5:OT5 + 2048].rearrange("p (dt t) -> p dt t", dt=8),
                sc["o_buf"][:].transpose([1, 0, 2])[:, :, c0:c0 + TC5])
        ld5 = bk.gdma(lr_ap, sc["l_buf"][:][cc])
        bk.W("vector", "gd", ld5)
        lrv = bk.vop("reciprocal", lr_ap, lr_ap, sem="vop")
        bk.W("tensor", "vop", lrv)
        onv = None
        for dt in range(8):
            bknum = 6 + dt % 2
            bk.bank_wait(bknum)
            bk.mm(banks[bknum][0:64, 0:256], ones_row[0:1, 0:64],
                  lr_ap[0:1, (2 * dt) * 256:(2 * dt + 1) * 256],
                  start=True, stop=True)
            bv = bk.mm(banks[bknum][64:128, 0:256], ones_row[0:1, 0:64],
                       lr_ap[0:1, (2 * dt + 1) * 256:(2 * dt + 2) * 256],
                       start=True, stop=True, sem="tmm")
            bk.W("vector", "tmm", bv)
            onv = bk.vop("tensor_tensor", t5(OT5, dt), t5(OT5, dt),
                         banks[bknum][:, 0:256], ALU.mult, sem="vop")
            bk.bank_set_free(bknum, "vop", onv)
        xqv = act_quant(lambda dt: t5(XN5, dt), 8, xq_t, IVB5)
        bk.W("tensor", "vop", xqv)
        gev = None
        for o in range(32):
            ap, idx = ring5.load(sc["wig"][:][o], 512, bf16=True)
            ring5.consume_wait(idx)
            bknum = o % 4
            bk.bank_wait(bknum)
            apb = ap.bitcast(BF16)
            for kc in range(8):
                v = bk.mm(banks[bknum][:, 0:256], apb[:, kc * P:(kc + 1) * P],
                          xq_t(kc), start=(kc == 0), stop=(kc == 7),
                          sem=("tmm" if kc == 7 else None))
            ring5.free(idx)
            bk.W("vector", "tmm", v)
            gev = bk.vop("scalar_tensor_tensor", t5(GP5, o), banks[bknum][:, 0:256],
                         dvc("qrs_g"), pool[:, IVB5:IVB5 + 256],
                         ALU.mult, ALU.mult, sem="vop")
            bk.bank_set_free(bknum, "vop", gev)
        bk.W("scalar", "vop", gev)
        sgv = None
        for o in range(32):
            sgv = bk.act(t5(GP5, o), t5(GP5, o), AF.Silu,
                         sem=("sop" if o == 31 else None))
        bk.W("vector", "sop", sgv)
        pv = None
        for o in range(32):
            ap, idx = ring5.load(sc["wiu"][:][o], 512, bf16=True)
            ring5.consume_wait(idx)
            bknum = o % 4
            bk.bank_wait(bknum)
            apb = ap.bitcast(BF16)
            for kc in range(8):
                v = bk.mm(banks[bknum][:, 0:256], apb[:, kc * P:(kc + 1) * P],
                          xq_t(kc), start=(kc == 0), stop=(kc == 7),
                          sem=("tmm" if kc == 7 else None))
            ring5.free(idx)
            bk.W("vector", "tmm", v)
            bk.vop("scalar_tensor_tensor", pool[:, UP5:UP5 + 256],
                   banks[bknum][:, 0:256], dvc("qrs_u"),
                   pool[:, IVB5:IVB5 + 256], ALU.mult, ALU.mult)
            pv = bk.vop("tensor_tensor", t5(GP5, o), t5(GP5, o),
                        pool[:, UP5:UP5 + 256], ALU.mult, sem="vop")
            bk.bank_set_free(bknum, "vop", pv)
        xq2v = act_quant(lambda dt: t5(GP5, dt), 32, xq2_t, IVB25)
        bk.W("tensor", "vop", xq2v)
        g2v = None
        for o in range(8):
            ap, idx = ring5.load(sc["wid"][:][o], 2048, bf16=True)
            ring5.consume_wait(idx)
            bknum = 4 + o % 2
            bk.bank_wait(bknum)
            apb = ap.bitcast(BF16)
            for kc in range(32):
                v = bk.mm(banks[bknum][:, 0:256], apb[:, kc * P:(kc + 1) * P],
                          xq2_t(kc), start=(kc == 0), stop=(kc == 31),
                          sem=("tmm" if kc == 31 else None))
            ring5.free(idx)
            bk.W("vector", "tmm", v)
            g2v = bk.vop("scalar_tensor_tensor", t5(XN5, o), banks[bknum][:, 0:256],
                         dvc("qrs_d"), pool[:, IVB25:IVB25 + 256],
                         ALU.mult, ALU.mult, sem="vop")
            bk.bank_set_free(bknum, "vop", g2v)
        bk.W("scalar", "vop", g2v)
        s2v = None
        for o in range(8):
            s2v = bk.act(t5(XN5, o), t5(XN5, o), AF.Silu,
                         sem=("sop" if o == 7 else None))
        for o in range(8):
            ap, idx = ring5.load(dr["wot_r"][:][o], 1024)
            ring5.consume_wait(idx)
            bknum = 4 + o % 2
            bk.bank_wait(bknum)
            for kc in range(8):
                v = bk.mm(banks[bknum][:, 0:256],
                          ap[:, kc * P:(kc + 1) * P],
                          t5(OT5, kc), start=(kc == 0), stop=(kc == 7),
                          sem=("tmm" if kc == 7 else None))
            ring5.free(idx)
            bk.W("vector", "tmm", v)
            ev = bk.vop("tensor_scalar", t5(S25, o), banks[bknum][:, 0:256],
                        vc("bo_at", o), None, ALU.add, sem="vop")
            bk.bank_set_free(bknum, "vop", ev)
        bk.W("vector", "sop", s2v)
        yv = None
        for dt in range(8):
            bk.vop("tensor_tensor", t5(S25, dt), t5(S25, dt), t5(XN5, dt),
                   ALU.mult)
            yv = bk.vop("tensor_tensor", t5(X15, dt), t5(X15, dt), t5(S25, dt),
                        ALU.add, sem=("vop" if dt == 7 else None))
        bk.W("tensor", "vop", yv)
        for ts in range(2):
            tok0 = ts * P
            for half in range(2):
                bknum = half
                bk.bank_wait(bknum)
                for k in range(4):
                    dt = half * 4 + k
                    bk.tr(banks[bknum][:, k * P:(k + 1) * P],
                          pool[:, X15 + dt * 256 + tok0:X15 + dt * 256 + tok0 + P],
                          ident[:], sem=("tmm" if k == 3 else None))
                tv = bk.cnt["tmm"]
                bk.W("scalar", "tmm", tv)
                bk.W("scalar", "gd", bk.cnt["gd"])
                cpv = bk.sop("copy", pool[:, YSB5 + half * 512:YSB5 + (half + 1) * 512],
                             banks[bknum][:], sem="sop")
                bk.bank_set_free(bknum, "sop", cpv)
            bk.W("gpsimd", "sop", bk.cnt["sop"])
            bk.gdma(y_out[:][c0 + tok0:c0 + tok0 + P, :],
                    pool[:, YSB5:YSB5 + 1024])

    bk.barrier(5)

    for name in bk.dbg:
        if name == "gfdbg":
            bk.gdma(dbg_outs[name][:], gfdbg_sb[:])
        else:
            bk.gdma(dbg_outs[name][:], sc[name][:])
    bk.W("gpsimd", "gd", bk.cnt["gd"])
    return bk, nc


def kernel(**inputs):
    shared = stage_shared(inputs)
    x = np.ascontiguousarray(np.asarray(inputs["L0_attn_x"], np.float32))
    bk, nc = build_program(dbg=())
    nc = bk.finalize()
    in_maps = []
    for b in range(8):
        m = dict(shared)
        m["x"] = np.ascontiguousarray(x[b])
        in_maps.append(m)
    res = run_bass_kernel_spmd(nc, in_maps, core_ids=list(range(8)))
    return np.stack([r["y"] for r in res.results])
